# version 04
# baseline (speedup 1.0000x reference)
import contextlib
import numpy as np

import concourse.bass as bass
import concourse.mybir as mybir
from concourse.bass_utils import run_bass_kernel_spmd

F32 = mybir.dt.float32
BF16 = mybir.dt.bfloat16
AX = mybir.AxisListType
ALU = mybir.AluOpType
AF = mybir.ActivationFunctionType

FUSED = False


ENGS = ("pe", "act", "dve", "pool", "sp")
_ENG_ATTR = {"pe": "tensor", "act": "scalar", "dve": "vector", "pool": "gpsimd",
             "sp": "sync"}
_SEM_ROT = 48000


class _Op:
    __slots__ = ("method", "args", "kwargs", "waits", "inc", "idx", "tag")

    def __init__(self, method, args, kwargs, idx, tag):
        self.method = method
        self.args = args
        self.kwargs = kwargs
        self.waits = []
        self.inc = None
        self.idx = idx
        self.tag = tag


class Fw:
    def __init__(self, nc):
        self.nc = nc
        self.ops = {e: [] for e in ENGS}
        self.ndma = {e: 0 for e in ENGS}
        self.named = {}
        self.barrier_totals = {e: 0 for e in ENGS}
        self.needed = {}
        self.state = {}
        self.waited = {e: {} for e in ENGS}
        self.last_compute = {e: None for e in ENGS}
        self.verbose = False

    def _key(self, ap):
        if isinstance(ap, str):
            return ap
        t = getattr(ap, "tensor", ap)
        return getattr(t, "name", str(t))

    def _st(self, key):
        if key not in self.state:
            self.state[key] = {"w": None, "r": []}
        return self.state[key]

    def _need_wait(self, eng, dep):
        stream, idx = dep
        cur = self.waited[eng].get(stream, -1)
        if idx <= cur:
            return False
        self.waited[eng][stream] = idx
        return True

    def _resolve(self, eng, deps, o, what=""):
        for d in deps:
            stream, idx = d
            if stream == "eng:" + eng and eng == "pe":
                continue
            if stream == "eng:" + eng and idx == o.idx:
                continue
            if stream.startswith("dmabulk:"):
                q = stream.split(":")[1]
                if idx <= self.barrier_totals[q]:
                    continue
                raise RuntimeError(
                    f"dep on un-barriered bulk DMA {d} (from {what})")
            if self._need_wait(eng, d):
                o.waits.append(d)
                self.needed[d] = True

    def op(self, eng, method, *args, reads=None, writes=None, tag="", **kwargs):
        assert eng in ENGS
        _OUT_KW = ("accum_out",)
        if writes is None:
            writes = (args[0],)
            if kwargs.get("accum_out") is not None:
                writes = writes + (kwargs["accum_out"],)
        if reads is None:
            reads = tuple(a for a in args[1:] if hasattr(a, "tensor"))
            reads = reads + tuple(v for k, v in kwargs.items()
                                  if k not in _OUT_KW and hasattr(v, "tensor"))
        idx = len(self.ops[eng])
        o = _Op(method, args, kwargs, idx, tag)
        deps = []
        for ap in reads:
            st = self._st(self._key(ap))
            if st["w"] is not None:
                deps.append(st["w"])
        for ap in writes:
            st = self._st(self._key(ap))
            if st["w"] is not None:
                deps.append(st["w"])
            deps.extend(st["r"])
        self._resolve(eng, deps, o, tag or method)
        mytok = ("eng:" + eng, idx)
        for ap in reads:
            self._st(self._key(ap))["r"].append(mytok)
        for ap in writes:
            st = self._st(self._key(ap))
            st["w"] = mytok
            st["r"] = []
        self.ops[eng].append(o)
        self.last_compute[eng] = idx
        return mytok

    def dma(self, q, out, in_, sem=None, rkey=None, wkey=None, tag="", **kwargs):
        assert q in ENGS
        if sem is None:
            side = out if "SBTensor" in type(getattr(out, "tensor", out)).__name__ else in_
            sem = "b_" + self._key(side)
        elif sem is False:
            sem = None
        idx = len(self.ops[q])
        o = _Op("dma_start", (), dict(out=out, in_=in_, **kwargs), idx, tag)
        deps = []
        st_in = self._st(rkey if rkey is not None else self._key(in_))
        if st_in["w"] is not None:
            deps.append(st_in["w"])
        st_out = self._st(wkey if wkey is not None else self._key(out))
        if st_out["w"] is not None:
            deps.append(st_out["w"])
        deps.extend(st_out["r"])
        self._resolve(q, deps, o, tag or "dma")
        if sem is None:
            self.ndma[q] += 1
            tok = ("dmabulk:" + q, self.ndma[q])
        else:
            self.named[sem] = self.named.get(sem, 0) + 1
            tok = ("dsem:" + sem, self.named[sem])
        o.inc = tok
        st_in["r"].append(tok)
        st_out["w"] = tok
        st_out["r"] = []
        self.ops[q].append(o)
        return tok

    def cc(self, kind, op, groups, in_ap, out_ap):
        q = "pool"
        name = "cc%d" % (len(self.named) + 1000)
        idx = len(self.ops[q])
        o = _Op("collective_compute", (kind, op),
                dict(replica_groups=groups, ins=[in_ap.opt()],
                     outs=[out_ap.opt()]), idx, "cc")
        deps = []
        st_in = self._st(self._key(in_ap))
        if st_in["w"] is not None:
            deps.append(st_in["w"])
        st_out = self._st(self._key(out_ap))
        if st_out["w"] is not None:
            deps.append(st_out["w"])
        deps.extend(st_out["r"])
        self._resolve(q, deps, o, "cc")
        self.named[name] = self.named.get(name, 0) + 1
        tok = ("dsem:" + name, self.named[name])
        o.inc = tok
        st_in["r"].append(tok)
        st_out["w"] = tok
        st_out["r"] = []
        self.ops[q].append(o)
        return tok

    def barrier(self):
        toks = []
        for e in ENGS:
            if self.last_compute[e] is not None:
                toks.append(("eng:" + e, self.last_compute[e]))
            if self.ndma[e]:
                toks.append(("dmabulk:" + e, self.ndma[e]))
        for nm, c in self.named.items():
            if c:
                toks.append(("dsem:" + nm, c))
        for e in ENGS:
            o = _Op("engine_nop", (), {}, len(self.ops[e]), "barrier")
            for d in toks:
                if d[0] == "eng:" + e:
                    continue
                if self._need_wait(e, d):
                    o.waits.append(d)
                    self.needed[d] = True
            self.ops[e].append(o)
        for e in ENGS:
            self.barrier_totals[e] = self.ndma[e]

    def finish(self):
        self.barrier()

    def flush(self, es):
        nc = self.nc
        eng_incpts = {e: sorted(i for (s, i) in self.needed if s == "eng:" + e)
                      for e in ENGS}

        def mk_sems(name, nvals):
            n_sems = max(1, (nvals + _SEM_ROT - 1) // _SEM_ROT)
            return [es.enter_context(nc.semaphore(f"{name}{i}"))
                    for i in range(n_sems)]

        eng_sems = {e: mk_sems("S" + e, len(eng_incpts[e])) for e in ENGS}
        eng_rank = {e: {idx: r + 1 for r, idx in enumerate(eng_incpts[e])}
                    for e in ENGS}
        bulk_sems = {e: mk_sems("D" + e, self.ndma[e] * 16) for e in ENGS}
        named_sems = {}
        for nm, c in self.named.items():
            assert c * 16 < _SEM_ROT, f"named dma sem {nm} overflows: {c}"
            named_sems[nm] = es.enter_context(nc.semaphore("N" + nm))
        rot = _SEM_ROT

        def sem_val(stream, idx):
            kind, name = stream.split(":", 1)
            if kind == "eng":
                r = eng_rank[name][idx]
                si, v = divmod(r - 1, rot)
                return eng_sems[name][si], v + 1
            if kind == "dmabulk":
                per = rot // 16
                si, v = divmod(idx - 1, per)
                return bulk_sems[name][si], (v + 1) * 16
            return named_sems[name], idx * 16

        def inc_for_eng(e, idx):
            r = eng_rank[e].get(idx)
            if r is None:
                return None
            return eng_sems[e][(r - 1) // rot], 1

        def inc_for_dma(stream, idx):
            kind, name = stream.split(":", 1)
            if kind == "dmabulk":
                per = rot // 16
                return bulk_sems[name][(idx - 1) // per], 16
            return named_sems[name], 16

        if self.verbose:
            stats = {e: (len(self.ops[e]), len(eng_incpts[e])) for e in ENGS}
            nw = sum(len(o.waits) for e in ENGS for o in self.ops[e])
            print("fw flush ops(incs):", stats, "nsems:",
                  len(self.named), "waits:", nw)

        with nc.Block() as block:
            for e in ENGS:
                ops = self.ops[e]
                if not ops:
                    continue

                def body(engine, ops=ops, e=e):
                    for o in ops:
                        for d in o.waits:
                            s, v = sem_val(*d)
                            engine.wait_ge(s, v)
                        if o.method == "engine_nop":
                            inst = engine.nop()
                        else:
                            inst = getattr(engine, o.method)(*o.args, **o.kwargs)
                        if o.inc is not None:
                            s, a = inc_for_dma(*o.inc)
                            inst.then_inc(s, a)
                        else:
                            ic = inc_for_eng(e, o.idx)
                            if ic is not None:
                                inst.then_inc(ic[0], ic[1])

                getattr(block, _ENG_ATTR[e])(body)



class Cfg:
    def __init__(self, full=True):
        if full:
            self.D = 4096
            self.T = 1024
            self.H0 = 32
            self.KVH = 8
            self.H1 = 128
            self.G1 = 8
            self.E = 64
            self.KTOP = 8
            self.MK = 64
            self.MTOP = 16
            self.R = 16
        else:
            self.D = 512
            self.T = 512
            self.H0 = 4
            self.KVH = 1
            self.H1 = 16
            self.G1 = 2
            self.E = 64
            self.KTOP = 8
            self.MK = 64
            self.MTOP = 16
            self.R = 16
        self.HD = 128
        self.P1 = 64
        self.N1 = 128
        self.CH = 128
        self.NZ = self.T // self.CH
        self.KD = self.KVH * self.HD
        self.QD = self.H0 * self.HD
        self.INTER = self.H1 * self.P1
        self.CONV = self.INTER + 2 * self.G1 * self.N1
        self.IPJ = self.INTER + self.CONV + self.H1
        self.KR = self.MK * self.R
        self.NK = self.D // 128
        self.fused = True
        self.groups = [[2 * i, 2 * i + 1] for i in range(4)]



def declare_tensors(cfg):
    c = cfg
    t = {}
    t["x"] = ([c.D, c.T], F32)
    t["wq"] = ([c.D, c.QD], F32)
    t["wk"] = ([c.D, c.KD], F32)
    t["wv"] = ([c.D, c.KD], F32)
    t["wg"] = ([c.D, c.QD], F32)
    t["wwin"] = ([c.D, c.H0], F32)
    t["dtb0"] = ([1, c.H0], F32)
    t["alog0"] = ([1, c.H0], F32)
    t["gnw"] = ([1, c.HD], F32)
    t["wo"] = ([c.QD, c.D], F32)
    t["rw"] = ([c.D, c.E], F32)
    t["rb"] = ([1, c.E], F32)
    t["eb"] = ([c.E, c.D], F32)
    t["ipj"] = ([c.D, c.IPJ], F32)
    t["cw"] = ([128, (c.CONV // 128) * 4], F32)
    t["cb"] = ([128, c.CONV // 128], F32)
    t["dtb1"] = ([c.H1, 1], F32)
    t["alog1"] = ([c.H1, 1], F32)
    t["dvec"] = ([1, c.H1], F32)
    t["nw"] = ([1, c.INTER], F32)
    t["opj"] = ([c.INTER, c.D], F32)
    t["afl"] = ([c.D, c.KR], F32)
    t["bfl"] = ([c.KR, c.D], F32)
    t["wr"] = ([c.D, c.MK], F32)
    t["rcos"] = ([c.HD, c.T], F32)
    t["rsin"] = ([c.HD, c.T], F32)
    t["eye"] = ([128, 128], F32)
    t["triu"] = ([128, 128], F32)
    t["mneg"] = ([128, 128], F32)
    t["smask"] = ([1, 1], F32)
    t["rmask"] = ([1, 1], F32)
    t["y"] = ([c.D, c.T], F32)
    t["xn0"] = ([c.D, c.T], BF16)
    t["qsp"] = ([c.QD, c.T], BF16)
    t["ksp"] = ([c.KD, c.T], BF16)
    t["vsp"] = ([c.T, c.KD], BF16)
    t["g0sp"] = ([c.T, c.QD], BF16)
    t["dt0"] = ([c.T, c.H0], F32)
    t["g0cum"] = ([c.T, c.H0], F32)
    t["b0"] = ([c.T, c.H0], F32)
    t["s0loc"] = ([c.NZ * 128, c.H0 * c.HD], BF16)
    t["p0"] = ([c.NZ, c.H0], F32)
    t["s0fin"] = ([128, c.H0 * c.HD], F32)
    t["s0init"] = ([128, c.H0 * c.HD], F32)
    t["ygfm"] = ([c.QD, c.T], BF16)
    t["x1"] = ([c.D, c.T], F32)
    t["x1bf"] = ([c.D, c.T], BF16)
    t["xn1"] = ([c.D, c.T], BF16)
    t["x2"] = ([c.D, c.T], F32)
    t["x2bf"] = ([c.D, c.T], BF16)
    t["xn2"] = ([c.D, c.T], BF16)
    t["ipg"] = ([c.T, c.INTER], BF16)
    t["iph"] = ([c.CONV, c.T], BF16)
    t["dt1"] = ([c.H1, c.T], F32)
    t["lndt1"] = ([c.H1, c.T], F32)
    t["adt1"] = ([c.H1, c.T], F32)
    t["halo"] = ([128, (c.CONV // 128) * 3], F32)
    t["haloin"] = ([128, (c.CONV // 128) * 3], F32)
    t["hstm"] = ([c.T, c.INTER], BF16)
    t["bcfm"] = ([2 * c.G1 * c.N1, c.T], BF16)
    t["acumt"] = ([c.T, c.H1], F32)
    t["b1sp"] = ([c.T, c.H1], F32)
    t["acfl"] = ([c.NZ, c.H1 * 128], F32)
    t["p1"] = ([c.NZ, c.H1], F32)
    t["s1loc"] = ([c.NZ * c.N1, c.H1 * c.P1], BF16)
    t["s1fin"] = ([c.N1, c.H1 * c.P1], F32)
    t["s1init"] = ([c.N1, c.H1 * c.P1], F32)
    t["ggfm"] = ([c.INTER, c.T], BF16)
    t["x3"] = ([c.D, c.T], F32)
    t["x3bf"] = ([c.D, c.T], BF16)
    t["xn3"] = ([c.D, c.T], BF16)
    return t


INPUT_NAMES = ["x", "wq", "wk", "wv", "wg", "wwin", "dtb0", "alog0", "gnw",
               "wo", "rw", "rb", "eb", "ipj", "cw", "cb", "dtb1", "alog1",
               "dvec", "nw", "opj", "afl", "bfl", "wr", "rcos", "rsin",
               "eye", "triu", "mneg", "smask", "rmask"]



class P:

    def __init__(self, nc, cfg, es):
        self.nc = nc
        self.cfg = cfg
        self.scopes = [es]
        self.fw = Fw(nc)
        self.d = {}
        self.uid = 0

    def sb(self, name, shape, dt=F32):
        return self.scopes[-1].enter_context(self.nc.sbuf_tensor(name, shape, dt))

    def sb_root(self, name, shape, dt=F32):
        return self.scopes[0].enter_context(self.nc.sbuf_tensor(name, shape, dt))

    def ps(self, name, shape, dt=F32):
        return self.scopes[-1].enter_context(self.nc.psum_tensor(name, shape, dt))

    def u(self, base):
        self.uid += 1
        return f"{base}{self.uid}"

    @contextlib.contextmanager
    def scope(self):
        es = contextlib.ExitStack()
        self.scopes.append(es)
        try:
            yield
        finally:
            self.fw.barrier()
            self.scopes.pop()
            es.close()


def ph_consts(p):
    c, fw = p.cfg, p.fw
    p.ones1 = p.sb("ones1", [1, 128], F32)
    fw.op("dve", "memset", p.ones1[0:1, :], 1.0)
    p.onescol = p.sb("onescol", [128, 1], BF16)
    fw.op("dve", "memset", p.onescol[:, :], 1.0)
    p.eye32 = p.sb("c_eye32", [128, 128], F32)
    fw.dma("sp", p.eye32[:, :], p.d["eye"])
    p.eyeb = p.sb("c_eyeb", [128, 128], BF16)
    fw.dma("pool", p.eyeb[:, :], p.d["eye"])
    p.triu = p.sb("c_triu", [128, 128], F32)
    fw.dma("sp", p.triu[:, :], p.d["triu"])
    p.mneg = p.sb("c_mneg", [128, 128], F32)
    fw.dma("sp", p.mneg[:, :], p.d["mneg"])
    p.ps_bc = p.ps("ps_bc", [128, 512], F32)
    sm = p.sb("sm_row", [1, 2], F32)
    fw.dma("sp", sm[0:1, 0:1], p.d["smask"])
    fw.dma("sp", sm[0:1, 1:2], p.d["rmask"])
    p.smrm = p.sb("smrm", [128, 2], F32)
    fw.op("pe", "matmul", p.ps_bc[:, 0:2], p.ones1[0:1, 0:128], sm[0:1, 0:2],
          start=True, stop=True)
    fw.op("dve", "tensor_copy", p.smrm[:, :], p.ps_bc[:, 0:2])
    p.smcol = p.smrm[:, 0:1]
    p.rmcol = p.smrm[:, 1:2]
    consts = [1e-5, 1e-30, -0.5, 1.0 / c.D, 1.0 / c.HD,
              1.0 / (c.INTER // c.G1), 0.5]
    p.cst = p.sb("c_vals", [128, len(consts)], F32)
    p.rows = p.sb("c_rows", [1, 4 * c.T], F32)
    p.rows_off = 0
    for i, v in enumerate(consts):
        fw.op("dve", "memset", p.cst[:, i:i + 1], v, writes=(p.cst,))
        p.nc.const_aps.register(F32, v, p.cst[:, i:i + 1])
    fw.barrier()


def bcast(p, row_ap, w, out=None, name=None, read_key=None):
    fw = p.fw
    if out is None:
        out = p.sb(name or p.u("bc"), [128, w], F32)
    for o in range(0, w, 512):
        ww = min(512, w - o)
        kw = {}
        fw.op("pe", "matmul", p.ps_bc[:, 0:ww], p.ones1[0:1, 0:128],
              row_ap[0:1, o:o + ww], start=True, stop=True,
              reads=(p.ones1[:, :], read_key or row_ap,), writes=(p.ps_bc,))
        fw.op("dve", "tensor_copy", out[:, o:o + ww], p.ps_bc[:, 0:ww])
    return out


def load_fm(p, sb_tile, dram_name, nrows, T, q="sp", cast=False, cols=None):
    c0, cl = cols if cols else (0, T)
    src = p.d[dram_name]
    for cch in range(nrows // 128):
        dst = sb_tile[:, cch * cl:(cch + 1) * cl]
        s = src[cch * 128:(cch + 1) * 128, c0:c0 + cl]
        p.fw.dma("pool" if cast else q, dst, s, tag=f"ld_{dram_name}{cch}")


def softplus(p, out, raw, tmp1, tmp2):
    fw = p.fw
    fw.op("act", "activation", tmp1, raw, AF.Abs, bias=0.0, scale=1.0)
    fw.op("act", "activation", tmp1, tmp1, AF.Exp, bias=0.0, scale=-1.0)
    fw.op("act", "activation", tmp1, tmp1, AF.Ln, bias=1.0, scale=1.0)
    fw.op("dve", "tensor_scalar", tmp2, raw, 0.0, None, ALU.max)
    fw.op("dve", "tensor_add", out, tmp2, tmp1)



def ph_ln(p, xbf_src, xn_dst, cast_from=None):
    c, fw = p.cfg, p.fw
    with p.scope():
        xbf = p.sb(p.u("lnxbf"), [128, c.NK * c.T], BF16)
        if cast_from is not None:
            load_fm(p, xbf, cast_from, c.D, c.T, cast=True)
        else:
            load_fm(p, xbf, xbf_src, c.D, c.T, q="sp")
        ps_sum = p.ps(p.u("lnps"), [1, c.T], F32)
        ps_sq = p.ps(p.u("lnpq"), [1, c.T], F32)
        sq = p.sb(p.u("lnsq"), [128, 512], BF16)
        for tt in range(c.T // 512):
            for cc in range(c.NK):
                xsl = xbf[:, cc * c.T + tt * 512: cc * c.T + tt * 512 + 512]
                fw.op("pe", "matmul", ps_sum[0:1, tt * 512:(tt + 1) * 512],
                      p.onescol[:, 0:1], xsl, start=(cc == 0), stop=(cc == c.NK - 1))
                fw.op("act", "activation", sq[:, :], xsl, AF.Square,
                      bias=0.0, scale=1.0, tag="lnsq")
                fw.op("pe", "matmul", ps_sq[0:1, tt * 512:(tt + 1) * 512],
                      p.onescol[:, 0:1], sq[:, :], start=(cc == 0), stop=(cc == c.NK - 1))
        mean = p.sb(p.u("lnm"), [1, c.T], F32)
        var = p.sb(p.u("lnv"), [1, c.T], F32)
        rstd = p.sb(p.u("lnr"), [1, c.T], F32)
        mrs = p.sb(p.u("lnmr"), [1, c.T], F32)
        fw.op("act", "activation", mean[0:1, :], ps_sum[0:1, :], AF.Copy,
              bias=0.0, scale=1.0 / c.D)
        fw.op("dve", "tensor_mul", var[0:1, :], mean[0:1, :], mean[0:1, :])
        fw.op("dve", "scalar_tensor_tensor", var[0:1, :], ps_sq[0:1, :],
              1.0 / c.D, var[0:1, :], ALU.mult, ALU.subtract)
        fw.op("act", "activation", rstd[0:1, :], var[0:1, :], AF.Ln,
              bias=1e-5, scale=1.0)
        fw.op("act", "activation", rstd[0:1, :], rstd[0:1, :], AF.Exp,
              bias=0.0, scale=-0.5)
        fw.op("dve", "tensor_mul", mrs[0:1, :], mean[0:1, :], rstd[0:1, :])
        tmp = p.sb(p.u("lntmp"), [128, 512], F32)
        xnt = p.sb(p.u("lnxn"), [128, 512], BF16)
        for tt in range(c.T // 512):
            rbc = bcast(p, rstd[0:1, tt * 512:(tt + 1) * 512], 512, name=p.u("lnrb"))
            mbc = bcast(p, mrs[0:1, tt * 512:(tt + 1) * 512], 512, name=p.u("lnmb"))
            for cc in range(c.NK):
                xsl = xbf[:, cc * c.T + tt * 512: cc * c.T + tt * 512 + 512]
                fw.op("dve", "tensor_mul", tmp[:, :], xsl, rbc[:, :])
                fw.op("dve", "tensor_sub", xnt[:, :], tmp[:, :], mbc[:, :])
                p.fw.dma("sp", p.d[xn_dst][cc * 128:(cc + 1) * 128,
                                           tt * 512:tt * 512 + 512],
                         xnt[:, :], tag="st_xn")



def ph_l0_proj(p):
    c, fw = p.cfg, p.fw
    parts = getattr(c, "l0proj_parts", "qk,vg,dt")
    with p.scope():
        xns = p.sb("xns", [128, c.NK * c.T], BF16)
        load_fm(p, xns, "xn0", c.D, c.T)
        cosf = p.sb("cosf", [128, c.T], F32)
        sinf = p.sb("sinf", [128, c.T], F32)
        fw.dma("sp", cosf[:, :], p.d["rcos"])
        fw.dma("sp", sinf[:, :], p.d["rsin"])

        wt = [p.sb(f"l0w{i}", [128, 128], BF16) for i in range(2)]
        psq = [p.ps(f"l0psq{i}", [128, 512], F32) for i in range(2)]
        t1 = p.sb("ropet1", [128, 512], F32)
        rot = p.sb("roperot", [128, 512], F32)
        t2 = p.sb("ropet2", [128, 512], F32)
        qout = p.sb("ropeq", [128, 512], BF16)

        def proj_rope(wname, mch, dst):
            for m in range(mch):
                w = wt[m % 2]
                for k in range(c.NK):
                    fw.dma("pool", w[:, :],
                           p.d[wname][k * 128:(k + 1) * 128, m * 128:(m + 1) * 128],
                           tag=f"{wname}{m}_{k}")
                    for tt in range(c.T // 512):
                        fw.op("pe", "matmul", psq[tt][:, 0:512], w[:, :],
                              xns[:, k * c.T + tt * 512:k * c.T + tt * 512 + 512],
                              start=(k == 0), stop=(k == c.NK - 1))
                for tt in range(c.T // 512):
                    if getattr(c, "norope", False):
                        fw.op("dve", "tensor_copy", qout[:, :], psq[tt][:, 0:512])
                        fw.dma("sp", p.d[dst][m * 128:(m + 1) * 128,
                                              tt * 512:tt * 512 + 512],
                               qout[:, :], tag=f"st_{dst}")
                        continue
                    cs = cosf[:, tt * 512:(tt + 1) * 512]
                    sn = sinf[:, tt * 512:(tt + 1) * 512]
                    rv = getattr(c, "ropevar", 0)
                    if rv == 1:
                        fw.op("dve", "tensor_mul", t1[:, :], psq[tt][:, 0:512], cs)
                        fw.op("act", "activation", qout[:, :], t1[:, :], AF.Copy,
                              bias=0.0, scale=1.0)
                        fw.dma("sp", p.d[dst][m * 128:(m + 1) * 128,
                                              tt * 512:tt * 512 + 512],
                               qout[:, :], tag=f"st_{dst}")
                        continue
                    if rv == 2:
                        fw.op("act", "activation", rot[0:64, :], psq[tt][64:128, 0:512],
                              AF.Copy, bias=0.0, scale=-1.0, writes=(rot,),
                              reads=(psq[tt],))
                        fw.op("act", "activation", rot[64:128, :], psq[tt][0:64, 0:512],
                              AF.Copy, bias=0.0, scale=1.0, writes=(rot,),
                              reads=(psq[tt],))
                        fw.op("dve", "tensor_mul", t2[:, :], rot[:, :], sn)
                        fw.op("act", "activation", qout[:, :], t2[:, :], AF.Copy,
                              bias=0.0, scale=1.0)
                        fw.dma("sp", p.d[dst][m * 128:(m + 1) * 128,
                                              tt * 512:tt * 512 + 512],
                               qout[:, :], tag=f"st_{dst}")
                        continue
                    fw.op("dve", "tensor_mul", t1[:, :], psq[tt][:, 0:512], cs)
                    fw.op("dve", "tensor_scalar", rot[0:64, :],
                          psq[tt][64:128, 0:512], -1.0, None, ALU.mult,
                          writes=(rot,), reads=(psq[tt],))
                    fw.op("dve", "tensor_copy", rot[64:128, :],
                          psq[tt][0:64, 0:512], writes=(rot,), reads=(psq[tt],))
                    fw.op("dve", "tensor_mul", t2[:, :], rot[:, :], sn)
                    fw.op("dve", "tensor_add", qout[:, :], t1[:, :], t2[:, :])
                    fw.dma("sp", p.d[dst][m * 128:(m + 1) * 128,
                                          tt * 512:tt * 512 + 512],
                           qout[:, :], tag=f"st_{dst}")

        if "qk" in parts:
            proj_rope("wq", c.H0, "qsp")
            proj_rope("wk", c.KVH, "ksp")

        psv = p.ps("l0psv", [128, 512], F32)
        vst = p.sb("l0vst", [128, c.NK * 512], BF16)
        ev = p.sb("l0ev", [128, 512], BF16)

        def proj_tm(wname, width, dst, silu):
            for n0 in range(0, width, 512):
                w5 = min(512, width - n0)
                for k in range(c.NK):
                    fw.dma("pool", vst[:, k * 512:k * 512 + w5],
                           p.d[wname][k * 128:(k + 1) * 128, n0:n0 + w5],
                           tag=f"{wname}s{n0}_{k}")
                for tz in range(c.NZ):
                    for k in range(c.NK):
                        fw.op("pe", "matmul", psv[:, 0:w5],
                              xns[:, k * c.T + tz * 128:k * c.T + tz * 128 + 128],
                              vst[:, k * 512:k * 512 + w5],
                              start=(k == 0), stop=(k == c.NK - 1))
                    if silu:
                        fw.op("act", "activation", ev[:, 0:w5], psv[:, 0:w5],
                              AF.Silu, bias=0.0, scale=1.0)
                    else:
                        fw.op("dve", "tensor_copy", ev[:, 0:w5], psv[:, 0:w5])
                    fw.dma("sp", p.d[dst][tz * 128:(tz + 1) * 128, n0:n0 + w5],
                           ev[:, 0:w5], tag=f"st_{dst}")

        if "vg" in parts:
            proj_tm("wv", c.KD, "vsp", False)
            proj_tm("wg", c.QD, "g0sp", True)

        if "dt" not in parts:
            return
        h0 = c.H0
        wdt = p.sb("l0wdt", [128, c.NK * h0], BF16)
        for k in range(c.NK):
            fw.dma("pool", wdt[:, k * h0:(k + 1) * h0],
                   p.d["wwin"][k * 128:(k + 1) * 128, :], tag=f"wwin{k}")
        dtbrow = p.sb("dtbrow", [1, h0], F32)
        fw.dma("sp", dtbrow[0:1, :], p.d["dtb0"])
        dtb_bc = bcast(p, dtbrow, h0, name="dtb_bc")
        alrow = p.sb("alrow", [1, h0], F32)
        fw.dma("sp", alrow[0:1, :], p.d["alog0"])
        fw.op("act", "activation", alrow[0:1, :], alrow[0:1, :], AF.Exp,
              bias=0.0, scale=1.0)
        a_bc = bcast(p, alrow, h0, name="a_bc")
        psd = p.ps("l0psd", [128, h0], F32)
        psg = p.ps("l0psg", [128, h0], F32)
        raw = p.sb("dtraw", [128, h0], F32)
        s1 = p.sb("dts1", [128, h0], F32)
        s2 = p.sb("dts2", [128, h0], F32)
        dts = p.sb("dts", [128, h0], F32)
        lnd = p.sb("dtln", [128, h0], F32)
        gts = p.sb("dtg", [128, h0], F32)
        Gs = p.sb("dtG", [128, h0], F32)
        b0s = p.sb("dtb0s", [128, h0], F32)
        for tz in range(c.NZ):
            for k in range(c.NK):
                fw.op("pe", "matmul", psd[:, 0:h0],
                      xns[:, k * c.T + tz * 128:k * c.T + tz * 128 + 128],
                      wdt[:, k * h0:(k + 1) * h0],
                      start=(k == 0), stop=(k == c.NK - 1))
            fw.op("dve", "tensor_add", raw[:, :], psd[:, 0:h0], dtb_bc[:, :])
            softplus(p, dts[:, :], raw[:, :], s1[:, :], s2[:, :])
            fw.dma("sp", p.d["dt0"][tz * 128:(tz + 1) * 128, :], dts[:, :],
                   tag="st_dt0")
            fw.op("act", "activation", lnd[:, :], dts[:, :], AF.Ln,
                  bias=1e-30, scale=1.0)
            fw.op("dve", "scalar_tensor_tensor", gts[:, :], dts[:, :], -1.0,
                  a_bc[:, :], ALU.mult, ALU.mult)
            fw.op("pe", "matmul", psg[:, 0:h0], p.triu[:, :], gts[:, :],
                  start=True, stop=True)
            fw.op("dve", "tensor_copy", Gs[:, :], psg[:, 0:h0])
            fw.dma("sp", p.d["g0cum"][tz * 128:(tz + 1) * 128, :], Gs[:, :],
                   tag="st_g0cum")
            fw.op("dve", "tensor_sub", b0s[:, :], lnd[:, :], Gs[:, :])
            fw.dma("sp", p.d["b0"][tz * 128:(tz + 1) * 128, :], b0s[:, :],
                   tag="st_b0")



def ph_l0_cs(p):
    c, fw = p.cfg, p.fw
    with p.scope():
        ks = p.sb("ks", [128, c.KVH * c.T], BF16)
        for kc in range(c.KVH):
            fw.dma("sp", ks[:, kc * c.T:(kc + 1) * c.T],
                   p.d["ksp"][kc * 128:(kc + 1) * 128, :], tag="ld_ks")
        S = p.sb("S0", [128, c.H0 * c.HD], F32)
        fw.op("dve", "memset", S[:, :], 0.0)
        prow = p.sb("prow0", [1, c.H0], F32)
        fw.op("dve", "memset", prow[0:1, :], 1.0)
        vch = p.sb("vch0", [128, c.KD], BF16)
        Gt = p.sb("Gt0", [128, c.H0], F32)
        b0c = p.sb("b0c0", [128, c.H0], F32)
        glrow = p.sb("glrow0", [1, c.H0], F32)
        expb = p.sb("expb0", [128, c.H0], F32)
        scl = p.sb("scl0", [128, c.H0], F32)
        decbc = p.sb("decbc0", [128, c.H0], F32)
        pst = p.ps("pst0", [128, 128], BF16)
        ktt = p.sb("ktt0", [128, c.KVH * 128], BF16)
        kst = [p.sb(f"kst0{i}", [128, 128], BF16) for i in range(2)]
        ps_cs = [p.ps(f"pscs0{i}", [128, c.HD], F32) for i in range(2)]
        ssend = p.sb("ssend0", [128, c.H0 * c.HD], F32)
        for z in range(c.NZ):
            fw.dma("sp", vch[:, :], p.d["vsp"][z * 128:(z + 1) * 128, :],
                   tag="ld_vch")
            fw.dma("sp", Gt[:, :], p.d["g0cum"][z * 128:(z + 1) * 128, :])
            fw.dma("sp", b0c[:, :], p.d["b0"][z * 128:(z + 1) * 128, :])
            fw.dma("pool", glrow[0:1, :], Gt[127:128, :], tag="fl_gl")
            glb = bcast(p, glrow, c.H0, out=expb)
            fw.op("dve", "tensor_tensor", scl[:, :], b0c[:, :], glb[:, :],
                  ALU.add)
            fw.op("act", "activation", scl[:, :], scl[:, :], AF.Exp,
                  bias=0.0, scale=1.0)
            fw.op("act", "activation", glrow[0:1, :], glrow[0:1, :], AF.Exp,
                  bias=0.0, scale=1.0)
            fw.dma("sp", p.d["p0"][z:z + 1, :], prow[0:1, :], tag="st_p0")
            fw.op("dve", "tensor_mul", prow[0:1, :], prow[0:1, :], glrow[0:1, :])
            bcast(p, glrow, c.H0, out=decbc)
            fw.dma("pool", p.d["s0loc"][z * 128:(z + 1) * 128, :], S[:, :],
                   tag="st_s0loc", sem="s0loc")
            for kc in range(c.KVH):
                fw.op("pe", "transpose", pst[:, :],
                      ks[:, kc * c.T + z * 128:kc * c.T + z * 128 + 128],
                      p.eyeb[:, :])
                fw.op("dve", "tensor_copy", ktt[:, kc * 128:(kc + 1) * 128],
                      pst[:, :])
            for h in range(c.H0):
                kc = h // (c.H0 // c.KVH)
                kb = kst[h % 2]
                pc = ps_cs[h % 2]
                fw.op("dve", "tensor_scalar", kb[:, :],
                      ktt[:, kc * 128:(kc + 1) * 128], scl[:, h:h + 1], None,
                      ALU.mult)
                fw.op("pe", "matmul", pc[:, 0:c.HD], kb[:, :],
                      vch[:, kc * c.HD:(kc + 1) * c.HD], start=True, stop=True)
                hsl = S[:, h * c.HD:(h + 1) * c.HD]
                fw.op("dve", "tensor_scalar", hsl, hsl, decbc[:, h:h + 1],
                      None, ALU.mult, reads=(S,), writes=(S,))
                fw.op("dve", "tensor_tensor", hsl, hsl, pc[:, 0:c.HD], ALU.add,
                      reads=(S, pc), writes=(S,))
        fw.op("dve", "tensor_scalar", ssend[:, :], S[:, :], p.smcol, None,
              ALU.mult)
        fw.dma("sp", p.d["s0fin"], ssend[:, :], tag="st_s0fin")
        if c.fused:
            fw.cc("AllReduce", ALU.add, p.cfg.groups, p.d["s0fin"], p.d["s0init"])



def ph_l0_att(p):
    c, fw = p.cfg, p.fw
    with p.scope():
        src = "s0init"
        Sini = p.sb("Sini0", [128, c.H0 * c.HD], F32)
        fw.dma("sp", Sini[:, :], p.d[src], tag="ld_s0init")
        fw.op("dve", "tensor_scalar", Sini[:, :], Sini[:, :], p.rmcol, None,
              ALU.mult)
        ks = p.sb("ks_a", [128, c.KVH * c.T], BF16)
        for kc in range(c.KVH):
            fw.dma("sp", ks[:, kc * c.T:(kc + 1) * c.T],
                   p.d["ksp"][kc * 128:(kc + 1) * 128, :], tag="ld_ksa")
        gnwrow = p.sb("gnwrow", [1, c.HD], F32)
        fw.dma("sp", gnwrow[0:1, :], p.d["gnw"])
        gnw_bc = bcast(p, gnwrow, c.HD, name="gnw_bc")
        qsb = p.sb("qsb", [128, c.H0 * 128], BF16)
        vch = p.sb("vch_a", [128, c.KD], BF16)
        gch = p.sb("gch_a", [128, c.QD], BF16)
        Gt = p.sb("Gt_a", [128, c.H0], F32)
        b0c = p.sb("b0c_a", [128, c.H0], F32)
        expG = p.sb("expG_a", [128, c.H0], F32)
        p0row = p.sb("p0row", [1, c.H0], F32)
        pbc = p.sb("pbc0", [128, c.H0], F32)
        slsb = p.sb("slsb0", [128, c.H0 * c.HD], BF16)
        pst32 = p.ps("pst32", [128, 128], F32)
        gfm = p.sb("gfm0", [c.H0, 128], F32)
        gflat = p.sb("gflat0", [1, c.H0 * 128], F32)
        szb = [p.sb(f"szb{i}", [128, c.HD], BF16) for i in range(2)]
        dm = [p.sb(f"dm0{i}", [128, 128], F32) for i in range(2)]
        dec = [p.sb(f"dec0{i}", [128, 128], F32) for i in range(2)]
        mt = [p.sb(f"mt0{i}", [128, 128], BF16) for i in range(2)]
        t1e = [p.sb(f"t1e{i}", [128, c.HD], F32) for i in range(2)]
        bankA = [p.ps(f"l0bankA{i}", [128, 256], F32) for i in range(2)]
        bankB = [p.ps(f"l0bankB{i}", [128, 256], F32) for i in range(2)]
        ps_sc = [bankA[i][:, 0:128] for i in range(2)]
        ps_a1 = [bankA[i][:, 128:256] for i in range(2)]
        ps_yi = [bankB[i][:, 0:c.HD] for i in range(2)]
        ps_ya = [bankB[i][:, 128:128 + c.HD] for i in range(2)]
        kq = [dict(sc=f"k_sc{i}", a1=f"k_a1{i}", yi=f"k_yi{i}", ya=f"k_ya{i}")
              for i in range(2)]
        ych = p.sb("ych", [128, c.QD], F32)
        ss = p.sb("ss0", [128, c.H0], F32)
        sqj = p.sb("sqj0", [128, 128], BF16)
        rstd = p.sb("rstd0", [128, c.H0], F32)
        yg = p.sb("yg0", [128, c.QD], BF16)
        yg2 = p.sb("yg20", [128, c.QD], BF16)
        pstb = p.ps("pstb0", [128, 128], BF16)
        tb = [p.sb(f"tb0{i}", [128, 128], BF16) for i in range(2)]
        for z in range(c.NZ):
            fw.dma("sp", qsb[:, :].rearrange("pp (c t) -> pp c t", t=128),
                   p.d["qsp"].rearrange("(c pp) t -> pp c t", pp=128)[:, :, z * 128:(z + 1) * 128],
                   tag="ld_qsb")
            fw.dma("sp", vch[:, :], p.d["vsp"][z * 128:(z + 1) * 128, :],
                   tag="ld_vcha")
            fw.dma("sp", gch[:, :], p.d["g0sp"][z * 128:(z + 1) * 128, :],
                   tag="ld_gch")
            fw.dma("sp", Gt[:, :], p.d["g0cum"][z * 128:(z + 1) * 128, :])
            fw.dma("sp", b0c[:, :], p.d["b0"][z * 128:(z + 1) * 128, :])
            fw.dma("sp", p0row[0:1, :], p.d["p0"][z:z + 1, :], tag="ld_p0")
            fw.dma("sp", slsb[:, :], p.d["s0loc"][z * 128:(z + 1) * 128, :],
                   tag="ld_s0loc")
            fw.op("act", "activation", expG[:, :], Gt[:, :], AF.Exp,
                  bias=0.0, scale=1.0)
            bcast(p, p0row, c.H0, out=pbc)
            fw.op("pe", "transpose", pst32[0:c.H0, 0:128], Gt[:, :],
                  p.eye32[:, :], writes=(pst32,))
            fw.op("dve", "tensor_copy", gfm[0:c.H0, :], pst32[0:c.H0, 0:128],
                  reads=(pst32,), writes=(gfm,))
            fw.dma("pool", gflat[0:1, :], gfm[0:c.H0, :], tag="fl_gfm")
            for h in range(c.H0):
                kc = h // (c.H0 // c.KVH)
                i2 = h % 2
                hsl = slice(h * c.HD, (h + 1) * c.HD)
                fw.op("dve", "scalar_tensor_tensor", szb[i2][:, :],
                      Sini[:, hsl], pbc[:, h:h + 1], slsb[:, hsl],
                      ALU.mult, ALU.add, reads=(Sini, pbc, slsb))
                fw.op("pe", "matmul", ps_a1[i2], p.ones1[0:1, 0:128],
                      gflat[0:1, h * 128:(h + 1) * 128], start=True, stop=True,
                      reads=(p.ones1, gflat), writes=(kq[i2]["a1"],))
                fw.op("pe", "matmul", ps_sc[i2],
                      ks[:, kc * c.T + z * 128:kc * c.T + z * 128 + 128],
                      qsb[:, h * 128:(h + 1) * 128], start=True, stop=True,
                      reads=(ks, qsb), writes=(kq[i2]["sc"],))
                fw.op("dve", "tensor_tensor", dm[i2][:, :], ps_a1[i2],
                      p.mneg[:, :], ALU.add,
                      reads=(kq[i2]["a1"], kq[i2]["sc"], p.mneg),
                      writes=(dm[i2],))
                fw.op("act", "activation", dec[i2][:, :], dm[i2][:, :], AF.Exp,
                      bias=b0c[:, h:h + 1], scale=1.0)
                fw.op("dve", "tensor_tensor", mt[i2][:, :], ps_sc[i2],
                      dec[i2][:, :], ALU.mult,
                      reads=(kq[i2]["sc"], dec[i2]), writes=(mt[i2],))
                fw.op("pe", "matmul", ps_yi[i2],
                      qsb[:, h * 128:(h + 1) * 128], szb[i2][:, :],
                      start=True, stop=True,
                      reads=(qsb, szb[i2]), writes=(kq[i2]["yi"],))
                fw.op("pe", "matmul", ps_ya[i2], mt[i2][:, :],
                      vch[:, kc * c.HD:(kc + 1) * c.HD], start=True, stop=True,
                      reads=(mt[i2], vch), writes=(kq[i2]["ya"],))
                fw.op("dve", "tensor_scalar", t1e[i2][:, :], ps_yi[i2],
                      expG[:, h:h + 1], None, ALU.mult,
                      reads=(kq[i2]["yi"], kq[i2]["ya"], expG), writes=(t1e[i2],))
                fw.op("dve", "tensor_tensor", ych[:, hsl], t1e[i2][:, :],
                      ps_ya[i2], ALU.add,
                      reads=(t1e[i2], kq[i2]["ya"]), writes=(ych,))
            for h in range(c.H0):
                hsl = slice(h * c.HD, (h + 1) * c.HD)
                fw.op("act", "activation", sqj[:, :], ych[:, hsl], AF.Square,
                      bias=0.0, scale=1.0, accum_out=ss[:, h:h + 1],
                      reads=(ych,), writes=(sqj, ss))
            fw.op("act", "activation", rstd[:, :], ss[:, :], AF.Ln,
                  bias=1e-5, scale=1.0 / c.HD)
            fw.op("act", "activation", rstd[:, :], rstd[:, :], AF.Exp,
                  bias=0.0, scale=-0.5)
            for h in range(c.H0):
                hsl = slice(h * c.HD, (h + 1) * c.HD)
                fw.op("dve", "scalar_tensor_tensor", yg[:, hsl], ych[:, hsl],
                      rstd[:, h:h + 1], gnw_bc[:, :], ALU.mult, ALU.mult,
                      reads=(ych, rstd, gnw_bc), writes=(yg,))
            for j in range(c.QD // 512):
                fw.op("dve", "tensor_tensor", yg2[:, j * 512:(j + 1) * 512],
                      yg[:, j * 512:(j + 1) * 512],
                      gch[:, j * 512:(j + 1) * 512], ALU.mult,
                      reads=(yg, gch), writes=(yg2,))
            for m in range(c.QD // 128):
                fw.op("pe", "transpose", pstb[:, :], yg2[:, m * 128:(m + 1) * 128],
                      p.eyeb[:, :], reads=(yg2, p.eyeb))
                fw.op("dve", "tensor_copy", tb[m % 2][:, :], pstb[:, :])
                fw.dma("sp", p.d["ygfm"][m * 128:(m + 1) * 128,
                                         z * 128:(z + 1) * 128],
                       tb[m % 2][:, :], tag="st_ygfm")



def ln_rows(p, ps_s, ps_q, ks, kq):
    c, fw = p.cfg, p.fw
    o = 0
    mean = p.rows[0:1, (o + 0) * c.T:(o + 1) * c.T]
    var = p.rows[0:1, (o + 1) * c.T:(o + 2) * c.T]
    rstd = p.rows[0:1, (o + 2) * c.T:(o + 3) * c.T]
    mrs = p.rows[0:1, (o + 3) * c.T:(o + 4) * c.T]
    fw.op("act", "activation", mean, ps_s[0:1, :], AF.Copy,
          bias=0.0, scale=1.0 / c.D, reads=(ks,), writes=(mean,))
    fw.op("dve", "tensor_mul", var, mean, mean)
    fw.op("dve", "scalar_tensor_tensor", var, ps_q[0:1, :],
          1.0 / c.D, var, ALU.mult, ALU.subtract,
          reads=(kq, var), writes=(var,))
    fw.op("act", "activation", rstd, var, AF.Ln, bias=1e-5, scale=1.0)
    fw.op("act", "activation", rstd, rstd, AF.Exp, bias=0.0, scale=-0.5)
    fw.op("dve", "tensor_mul", mrs, mean, rstd)
    return rstd, mrs


def ln_apply(p, rstd, mrs, xbf, xn, store=None):
    c, fw = p.cfg, p.fw
    tmp = p.sb(p.u("lap"), [128, 512], F32)
    for tt in range(c.T // 512):
        rbc = bcast(p, rstd[0:1, tt * 512:(tt + 1) * 512], 512, name=p.u("larb"))
        mbc = bcast(p, mrs[0:1, tt * 512:(tt + 1) * 512], 512, name=p.u("lamb"))
        for cc in range(c.NK):
            sl = slice(cc * c.T + tt * 512, cc * c.T + tt * 512 + 512)
            fw.op("dve", "tensor_mul", tmp[:, :], xbf[:, sl], rbc[:, :],
                  reads=(xbf, rbc), writes=(tmp,))
            fw.op("dve", "tensor_tensor", xn[:, sl], tmp[:, :], mbc[:, :],
                  ALU.subtract, reads=(tmp, mbc), writes=(xn,))
            if store:
                fw.dma("sp", p.d[store][cc * 128:(cc + 1) * 128,
                                        tt * 512:tt * 512 + 512],
                       xn[:, sl], tag="st_" + store, sem="st_" + store)


def ln_apply_dram(p, rstd, mrs, xbf_dram, xn, store=None):
    c, fw = p.cfg, p.fw
    tmp = p.sb(p.u("lap"), [128, 512], F32)
    xt = [p.sb(p.u("lax"), [128, 512], BF16) for _ in range(2)]
    for tt in range(c.T // 512):
        rbc = bcast(p, rstd[0:1, tt * 512:(tt + 1) * 512], 512, name=p.u("larb"))
        mbc = bcast(p, mrs[0:1, tt * 512:(tt + 1) * 512], 512, name=p.u("lamb"))
        for cc in range(c.NK):
            i2 = cc % 2
            fw.dma("sp", xt[i2][:, :],
                   p.d[xbf_dram][cc * 128:(cc + 1) * 128, tt * 512:tt * 512 + 512],
                   tag="ld_" + xbf_dram)
            sl = slice(cc * c.T + tt * 512, cc * c.T + tt * 512 + 512)
            fw.op("dve", "tensor_mul", tmp[:, :], xt[i2][:, :], rbc[:, :],
                  reads=(xt[i2], rbc), writes=(tmp,))
            fw.op("dve", "tensor_tensor", xn[:, sl], tmp[:, :], mbc[:, :],
                  ALU.subtract, reads=(tmp, mbc), writes=(xn,))
            if store:
                fw.dma("sp", p.d[store][cc * 128:(cc + 1) * 128,
                                        tt * 512:tt * 512 + 512],
                       xn[:, sl], tag="st_" + store, sem="st_" + store)


def topk_gate(p, sc, scw, thr, mk, kiter):
    fw = p.fw
    fw.op("dve", "tensor_copy", scw[:, :], sc[:, :])
    for it in range(kiter):
        fw.op("dve", "tensor_reduce", thr[:, :], scw[:, :], AX.X, ALU.max)
        if it < kiter - 1:
            fw.op("dve", "tensor_scalar", mk[:, :], scw[:, :], thr[:, 0:1],
                  None, ALU.is_ge)
            fw.op("dve", "scalar_tensor_tensor", scw[:, :], mk[:, :], -1e30,
                  scw[:, :], ALU.mult, ALU.add)
    fw.op("dve", "tensor_scalar", mk[:, :], sc[:, :], thr[:, 0:1], None,
          ALU.is_ge)



def ph_l0_post(p):
    c, fw = p.cfg, p.fw
    with p.scope():
        rows1 = []
        with p.scope():
            ygr = p.sb("ygr", [128, (c.QD // 128) * c.T], BF16)
            load_fm(p, ygr, "ygfm", c.QD, c.T)
            ps_s = p.ps("ps_s1", [1, c.T], F32)
            ps_q = p.ps("ps_q1", [1, c.T], F32)
            ps_o = [p.ps(f"ps_o{i}", [128, 512], F32) for i in range(2)]
            wt = [p.sb(f"wo_t{i}", [128, 128], BF16) for i in range(2)]
            xold = [p.sb(f"xo{i}", [128, 512], F32) for i in range(2)]
            x1s = [p.sb(f"x1s{i}", [128, 512], F32) for i in range(2)]
            sqt = [p.sb(f"sq1{i}", [128, 512], BF16) for i in range(2)]
            xb1 = [p.sb(f"xb1{i}", [128, 512], BF16) for i in range(2)]
            for m in range(c.NK):
                for k in range(c.QD // 128):
                    w = wt[k % 2]
                    fw.dma("pool", w[:, :],
                           p.d["wo"][k * 128:(k + 1) * 128, m * 128:(m + 1) * 128],
                           tag=f"wo{m}_{k}")
                    for tt in range(c.T // 512):
                        fw.op("pe", "matmul", ps_o[tt][:, 0:512], w[:, :],
                              ygr[:, k * c.T + tt * 512:k * c.T + tt * 512 + 512],
                              start=(k == 0), stop=(k == c.QD // 128 - 1))
                for tt in range(c.T // 512):
                    i2 = tt % 2
                    fw.dma("sp", xold[i2][:, :],
                           p.d["x"][m * 128:(m + 1) * 128, tt * 512:tt * 512 + 512],
                           tag="ld_x0")
                    fw.op("dve", "tensor_tensor", x1s[i2][:, :], ps_o[tt][:, 0:512],
                          xold[i2][:, :], ALU.add)
                    fw.dma("sp", p.d["x1"][m * 128:(m + 1) * 128,
                                           tt * 512:tt * 512 + 512],
                           x1s[i2][:, :], tag="st_x1", sem="st_x1")
                    bsl = xb1[i2][:, :]
                    fw.op("act", "activation", bsl, x1s[i2][:, :], AF.Copy,
                          bias=0.0, scale=1.0, reads=(x1s[i2],), writes=(xb1[i2],))
                    fw.dma("sp", p.d["x1bf"][m * 128:(m + 1) * 128,
                                             tt * 512:tt * 512 + 512],
                           bsl, tag="st_x1bf")
                    fw.op("act", "activation", sqt[i2][:, :], bsl, AF.Square,
                          bias=0.0, scale=1.0, reads=(xb1[i2],), writes=(sqt[i2],))
                    fw.op("pe", "matmul", ps_s[0:1, tt * 512:(tt + 1) * 512],
                          p.onescol[:, 0:1], bsl, start=(m == 0),
                          stop=(m == c.NK - 1), reads=(p.onescol, xb1[i2]),
                          writes=("k_lns1",))
                    fw.op("pe", "matmul", ps_q[0:1, tt * 512:(tt + 1) * 512],
                          p.onescol[:, 0:1], sqt[i2][:, :], start=(m == 0),
                          stop=(m == c.NK - 1), reads=(p.onescol, sqt[i2]),
                          writes=("k_lnq1",))
            rows1 = ln_rows(p, ps_s, ps_q, "k_lns1", "k_lnq1")
        xn1 = p.sb("xn1_r", [128, c.NK * c.T], BF16)
        ln_apply_dram(p, rows1[0], rows1[1], "x1bf", xn1)
        rows2 = None
        with p.scope():
            rws = p.sb("rws", [128, c.NK * c.E], BF16)
            for k in range(c.NK):
                fw.dma("pool", rws[:, k * c.E:(k + 1) * c.E],
                       p.d["rw"][k * 128:(k + 1) * 128, :], tag=f"rw{k}")
            rbrow = p.sb("rbrow", [1, c.E], F32)
            fw.dma("sp", rbrow[0:1, :], p.d["rb"])
            rtb_bc = bcast(p, rbrow, c.E, name="rtb_bc")
            ebs = p.sb("ebs", [c.E, c.D], BF16)
            fw.dma("pool", ebs[:, :], p.d["eb"], tag="ld_eb")
            wdall = p.sb("wdall", [c.E, c.T], BF16)
            wsall = p.sb("wsall", [1, c.T], F32)
            ps_l = p.ps("ps_l", [128, 512], F32)
            pst32r = p.ps("pst32r", [128, 128], F32)
            lg = p.sb("lg", [128, c.E], F32)
            nmx = p.sb("nmx", [128, 1], F32)
            ex = p.sb("ex", [128, c.E], F32)
            ssum = p.sb("ssum", [128, 1], F32)
            sc = p.sb("sc_r", [128, c.E], F32)
            scw = p.sb("scw_r", [128, c.E], F32)
            thr = p.sb("thr_r", [128, 1], F32)
            mk = p.sb("mk_r", [128, c.E], F32)
            wd = p.sb("wd_r", [128, c.E], F32)
            ws = p.sb("ws_r", [128, 1], F32)
            for tz in range(c.NZ):
                for k in range(c.NK):
                    fw.op("pe", "matmul", ps_l[:, 0:c.E],
                          xn1[:, k * c.T + tz * 128:k * c.T + tz * 128 + 128],
                          rws[:, k * c.E:(k + 1) * c.E],
                          start=(k == 0), stop=(k == c.NK - 1))
                fw.op("dve", "tensor_tensor", lg[:, :], ps_l[:, 0:c.E],
                      rtb_bc[:, :], ALU.add)
                fw.op("dve", "tensor_reduce", nmx[:, :], lg[:, :], AX.X,
                      ALU.max, negate=True)
                fw.op("act", "activation", ex[:, :], lg[:, :], AF.Exp,
                      bias=nmx[:, 0:1], scale=1.0)
                fw.op("dve", "tensor_reduce", ssum[:, :], ex[:, :], AX.X, ALU.add)
                fw.op("dve", "reciprocal", ssum[:, :], ssum[:, :])
                fw.op("dve", "tensor_scalar", sc[:, :], ex[:, :], ssum[:, 0:1],
                      None, ALU.mult)
                topk_gate(p, sc, scw, thr, mk, c.KTOP)
                fw.op("dve", "tensor_mul", wd[:, :], mk[:, :], sc[:, :])
                fw.op("dve", "tensor_reduce", ws[:, :], wd[:, :], AX.X, ALU.add)
                fw.dma("pool", wsall[0:1, tz * 128:(tz + 1) * 128], ws[:, 0:1],
                       tag="fl_ws")
                fw.op("pe", "transpose", pst32r[0:c.E, 0:128], wd[:, :],
                      p.eye32[:, :], reads=(wd, p.eye32), writes=("k_wdt",))
                fw.op("dve", "tensor_copy", wdall[0:c.E, tz * 128:(tz + 1) * 128],
                      pst32r[0:c.E, 0:128], reads=("k_wdt",), writes=(wdall,))
            wsb = p.sb("wsb", [128, c.T], F32)
            for tt in range(c.T // 512):
                bcast(p, wsall[0:1, tt * 512:(tt + 1) * 512], 512,
                      out=wsb[:, tt * 512:(tt + 1) * 512])
            ps_b = p.ps("ps_b", [128, 512], F32)
            ps_s2 = p.ps("ps_s2", [1, c.T], F32)
            ps_q2 = p.ps("ps_q2", [1, c.T], F32)
            tsb = [p.sb(f"tsb{i}", [128, 512], F32) for i in range(2)]
            xo1 = [p.sb(f"xo1{i}", [128, 512], F32) for i in range(2)]
            x2s = [p.sb(f"x2s{i}", [128, 512], F32) for i in range(2)]
            sq2 = [p.sb(f"sq2{i}", [128, 512], BF16) for i in range(2)]
            xb2 = [p.sb(f"xb2{i}", [128, 512], BF16) for i in range(2)]
            for m in range(c.NK):
                for tt in range(c.T // 512):
                    i2 = tt % 2
                    fw.op("pe", "matmul", ps_b[:, 0:512],
                          ebs[:, m * 128:(m + 1) * 128],
                          wdall[:, tt * 512:(tt + 1) * 512],
                          start=True, stop=True)
                    fw.op("dve", "tensor_mul", tsb[i2][:, :],
                          xn1[:, m * c.T + tt * 512:m * c.T + tt * 512 + 512],
                          wsb[:, tt * 512:(tt + 1) * 512],
                          reads=(xn1, wsb), writes=(tsb[i2],))
                    fw.op("dve", "tensor_tensor", tsb[i2][:, :], tsb[i2][:, :],
                          ps_b[:, 0:512], ALU.add)
                    fw.dma("sp", xo1[i2][:, :],
                           p.d["x1"][m * 128:(m + 1) * 128, tt * 512:tt * 512 + 512],
                           tag="ld_x1")
                    fw.op("dve", "tensor_tensor", x2s[i2][:, :], tsb[i2][:, :],
                          xo1[i2][:, :], ALU.add)
                    fw.dma("sp", p.d["x2"][m * 128:(m + 1) * 128,
                                           tt * 512:tt * 512 + 512],
                           x2s[i2][:, :], tag="st_x2", sem="st_x2")
                    bsl = xb2[i2][:, :]
                    fw.op("act", "activation", bsl, x2s[i2][:, :], AF.Copy,
                          bias=0.0, scale=1.0, reads=(x2s[i2],), writes=(xb2[i2],))
                    fw.dma("sp", p.d["x2bf"][m * 128:(m + 1) * 128,
                                             tt * 512:tt * 512 + 512],
                           bsl, tag="st_x2bf")
                    fw.op("act", "activation", sq2[i2][:, :], bsl, AF.Square,
                          bias=0.0, scale=1.0, reads=(xb2[i2],), writes=(sq2[i2],))
                    fw.op("pe", "matmul", ps_s2[0:1, tt * 512:(tt + 1) * 512],
                          p.onescol[:, 0:1], bsl, start=(m == 0),
                          stop=(m == c.NK - 1), reads=(p.onescol, xb2[i2]),
                          writes=("k_lns2",))
                    fw.op("pe", "matmul", ps_q2[0:1, tt * 512:(tt + 1) * 512],
                          p.onescol[:, 0:1], sq2[i2][:, :], start=(m == 0),
                          stop=(m == c.NK - 1), reads=(p.onescol, sq2[i2]),
                          writes=("k_lnq2",))
            rows2 = ln_rows(p, ps_s2, ps_q2, "k_lns2", "k_lnq2")
        xn2 = p.sb("xn2_r", [128, c.NK * c.T], BF16)
        ln_apply_dram(p, rows2[0], rows2[1], "x2bf", xn2, store="xn2")



def ph_l1_inproj(p):
    c, fw = p.cfg, p.fw
    with p.scope():
        xns = p.sb("xns1", [128, c.NK * c.T], BF16)
        load_fm(p, xns, "xn2", c.D, c.T)
        gws = p.sb("gws", [128, c.NK * 512], BF16)
        psv = p.ps("l1psv", [128, 512], F32)
        ev = [p.sb(f"l1ev{i}", [128, 512], BF16) for i in range(2)]
        for n in range(c.INTER // 512):
            for k in range(c.NK):
                fw.dma("pool", gws[:, k * 512:(k + 1) * 512],
                       p.d["ipj"][k * 128:(k + 1) * 128, n * 512:(n + 1) * 512],
                       tag=f"ipg{n}_{k}")
            for tz in range(c.NZ):
                i2 = tz % 2
                for k in range(c.NK):
                    fw.op("pe", "matmul", psv[:, 0:512],
                          xns[:, k * c.T + tz * 128:k * c.T + tz * 128 + 128],
                          gws[:, k * 512:(k + 1) * 512],
                          start=(k == 0), stop=(k == c.NK - 1))
                fw.op("act", "activation", ev[i2][:, :], psv[:, 0:512],
                      AF.Silu, bias=0.0, scale=1.0)
                fw.dma("sp", p.d["ipg"][tz * 128:(tz + 1) * 128,
                                        n * 512:n * 512 + 512],
                       ev[i2][:, :], tag="st_ipg")
        wt = [p.sb(f"l1w{i}", [128, 128], BF16) for i in range(2)]
        ps2 = [p.ps(f"l1ps2{i}", [128, 512], F32) for i in range(2)]
        ev2 = [p.sb(f"l1eh{i}", [128, 512], BF16) for i in range(2)]
        nc3 = c.CONV // 128
        hal = p.sb("halsb", [128, nc3 * 3], F32)
        for m in range(c.CONV // 128):
            for k in range(c.NK):
                w = wt[k % 2]
                fw.dma("pool", w[:, :],
                       p.d["ipj"][k * 128:(k + 1) * 128,
                                  c.INTER + m * 128:c.INTER + (m + 1) * 128],
                       tag=f"iph{m}_{k}")
                for tt in range(c.T // 512):
                    fw.op("pe", "matmul", ps2[tt][:, 0:512], w[:, :],
                          xns[:, k * c.T + tt * 512:k * c.T + tt * 512 + 512],
                          start=(k == 0), stop=(k == c.NK - 1))
            for tt in range(c.T // 512):
                fw.op("dve", "tensor_copy", ev2[tt % 2][:, :], ps2[tt][:, 0:512])
                fw.dma("sp", p.d["iph"][m * 128:(m + 1) * 128,
                                        tt * 512:tt * 512 + 512],
                       ev2[tt % 2][:, :], tag="st_iph")
                if tt == c.T // 512 - 1:
                    fw.op("dve", "tensor_copy", hal[:, m * 3:(m + 1) * 3],
                          ev2[tt % 2][:, 509:512], reads=(ev2[tt % 2],),
                          writes=(hal,))
        h1 = c.H1
        wdt = [p.sb(f"l1wd{i}", [128, h1], BF16) for i in range(2)]
        psd = p.ps("l1psd", [128, 512], F32)
        dtb1c = p.sb("dtb1c", [h1, 1], F32)
        fw.dma("sp", dtb1c[:, :], p.d["dtb1"])
        acol = p.sb("acol1", [h1, 1], F32)
        fw.dma("sp", acol[:, :], p.d["alog1"])
        fw.op("act", "activation", acol[:, :], acol[:, :], AF.Exp,
              bias=0.0, scale=1.0)
        raw = p.sb("l1raw", [h1, 512], F32)
        s1 = p.sb("l1s1", [h1, 512], F32)
        s2 = p.sb("l1s2", [h1, 512], F32)
        dts = p.sb("l1dt", [h1, 512], F32)
        lnd = p.sb("l1ln", [h1, 512], F32)
        adt = p.sb("l1ad", [h1, 512], F32)
        for tt in range(c.T // 512):
            for k in range(c.NK):
                w = wdt[k % 2]
                fw.dma("pool", w[:, 0:h1],
                       p.d["ipj"][k * 128:(k + 1) * 128,
                                  c.INTER + c.CONV:c.INTER + c.CONV + h1],
                       tag=f"ipd{tt}_{k}")
                fw.op("pe", "matmul", psd[0:h1, 0:512], w[:, 0:h1],
                      xns[:, k * c.T + tt * 512:k * c.T + tt * 512 + 512],
                      start=(k == 0), stop=(k == c.NK - 1))
            fw.op("dve", "tensor_scalar", raw[:, :], psd[0:h1, 0:512],
                  dtb1c[:, 0:1], None, ALU.add, reads=(psd, dtb1c), writes=(raw,))
            softplus(p, dts[:, :], raw[:, :], s1[:, :], s2[:, :])
            fw.op("dve", "tensor_scalar_min", dts[:, :], dts[:, :], 100.0)
            fw.dma("sp", p.d["dt1"][:, tt * 512:(tt + 1) * 512], dts[:, :],
                   tag="st_dt1", sem="st_dt1")
            fw.op("act", "activation", lnd[:, :], dts[:, :], AF.Ln,
                  bias=1e-30, scale=1.0)
            fw.dma("sp", p.d["lndt1"][:, tt * 512:(tt + 1) * 512], lnd[:, :],
                   tag="st_lndt1", sem="st_lndt1")
            fw.op("dve", "tensor_scalar", adt[:, :], dts[:, :], acol[:, 0:1],
                  -1.0, ALU.mult, ALU.mult, reads=(dts, acol), writes=(adt,))
            fw.dma("sp", p.d["adt1"][:, tt * 512:(tt + 1) * 512], adt[:, :],
                   tag="st_adt1", sem="st_adt1")
        fw.op("dve", "tensor_scalar", hal[:, :], hal[:, :], p.smcol, None,
              ALU.mult)
        fw.dma("sp", p.d["halo"], hal[:, :], tag="st_halo")
        if c.fused:
            fw.cc("AllReduce", ALU.add, c.groups, p.d["halo"], p.d["haloin"])



def ph_l1_convcs(p):
    c, fw = p.cfg, p.fw
    h1 = c.H1
    nc3 = c.CONV // 128
    nhs = c.INTER // 128
    with p.scope():
        dtr = p.sb("dt1r", [h1, c.T], F32)
        fw.dma("sp", dtr[:, :], p.d["dt1"])
        lndr = p.sb("lnd1r", [h1, c.T], F32)
        fw.dma("sp", lndr[:, :], p.d["lndt1"])
        adtr = p.sb("adt1r", [h1, c.T], F32)
        fw.dma("sp", adtr[:, :], p.d["adt1"])
        cwsb = p.sb("cwsb", [128, nc3 * 4], F32)
        fw.dma("sp", cwsb[:, :], p.d["cw"])
        cbsb = p.sb("cbsb", [128, nc3], F32)
        fw.dma("sp", cbsb[:, :], p.d["cb"])
        halb = p.sb("halb", [128, nc3 * 3], BF16)
        hali = p.sb("hali", [128, nc3 * 3], F32)
        fw.dma("sp", hali[:, :], p.d["haloin"], tag="ld_haloin")
        fw.op("dve", "tensor_scalar", halb[:, :], hali[:, :], p.rmcol, None,
              ALU.mult)
        S = p.sb("S1", [128, h1 * c.P1], F32)
        fw.op("dve", "memset", S[:, :], 0.0)
        prow = p.sb("prow1", [1, h1], F32)
        fw.op("dve", "memset", prow[0:1, :], 1.0)
        pst32 = p.ps("l1pst32", [128, 128], F32)
        ps_ac = p.ps("l1psac", [128, 128], F32)
        pstb = p.ps("l1pstb", [128, 128], BF16)
        adtT = p.sb("adtT", [128, h1], F32)
        lndT = p.sb("lndT", [128, h1], F32)
        acum = p.sb("acum1", [128, h1], F32)
        b1s = p.sb("b1s", [128, h1], F32)
        expb1 = p.sb("expb1", [128, h1], F32)
        acfm = p.sb("acfm", [h1, 128], F32)
        acflr = p.sb("acflr", [1, h1 * 128], F32)
        decrow = p.sb("decrow1", [1, h1], F32)
        decbc = p.sb("decbc1", [128, h1], F32)
        scl = p.sb("scl1", [128, h1], F32)
        hb = [p.sb(f"hb{i}", [128, 131], BF16) for i in range(2)]
        acc = [p.sb(f"cacc{i}", [128, 128], F32) for i in range(2)]
        hstm = p.sb("hstm_sb", [128, c.INTER], BF16)
        bcz = p.sb("bcz", [128, (nc3 - nhs) * 128], BF16)
        bt = p.sb("bt1", [128, c.G1 * 128], BF16)
        bsc = [p.sb(f"bsc{i}", [128, 128], BF16) for i in range(2)]
        ps_cs = [p.ps(f"l1pscs{i}", [128, c.P1], F32) for i in range(2)]
        ssend = p.sb("ssend1", [128, h1 * c.P1], F32)
        for z in range(c.NZ):
            fw.op("pe", "transpose", pst32[:, 0:h1],
                  adtr[0:h1, z * 128:(z + 1) * 128], p.eye32[0:h1, 0:h1],
                  reads=(adtr, p.eye32), writes=(pst32,))
            fw.op("dve", "tensor_copy", adtT[:, :], pst32[:, 0:h1],
                  reads=(pst32,), writes=(adtT,))
            fw.op("pe", "matmul", ps_ac[:, 0:h1], p.triu[:, :], adtT[:, :],
                  start=True, stop=True)
            fw.op("dve", "tensor_copy", acum[:, :], ps_ac[:, 0:h1])
            fw.dma("sp", p.d["acumt"][z * 128:(z + 1) * 128, :], acum[:, :],
                   tag="st_acumt", sem="st_acumt")
            fw.op("pe", "transpose", pst32[:, 0:h1],
                  lndr[0:h1, z * 128:(z + 1) * 128], p.eye32[0:h1, 0:h1],
                  reads=(lndr, p.eye32), writes=(pst32,))
            fw.op("dve", "tensor_copy", lndT[:, :], pst32[:, 0:h1],
                  reads=(pst32,), writes=(lndT,))
            fw.op("dve", "tensor_sub", b1s[:, :], lndT[:, :], acum[:, :])
            fw.dma("sp", p.d["b1sp"][z * 128:(z + 1) * 128, :], b1s[:, :],
                   tag="st_b1sp", sem="st_b1sp")

            fw.op("pe", "transpose", pst32[0:h1, 0:128], acum[:, 0:h1],
                  p.eye32[:, :], reads=(acum, p.eye32), writes=(pst32,))
            fw.op("dve", "tensor_copy", acfm[0:h1, :], pst32[0:h1, :],
                  reads=(pst32,), writes=(acfm,))
            fw.dma("pool", acflr[0:1, :], acfm[0:h1, :], tag="fl_acfm")
            fw.dma("sp", p.d["acfl"][z:z + 1, :], acflr[0:1, :], tag="st_acfl",
                   sem="st_acfl")
            fw.dma("pool", decrow[0:1, :], acum[127:128, 0:h1], tag="fl_dec")
            glb1 = bcast(p, decrow, h1, out=expb1)
            fw.op("dve", "tensor_tensor", scl[:, :], b1s[:, :], glb1[:, :],
                  ALU.add)
            fw.op("act", "activation", scl[:, :], scl[:, :], AF.Exp,
                  bias=0.0, scale=1.0)
            fw.op("act", "activation", decrow[0:1, :], decrow[0:1, :], AF.Exp,
                  bias=0.0, scale=1.0)
            fw.dma("sp", p.d["p1"][z:z + 1, :], prow[0:1, :], tag="st_p1",
                   sem="st_p1")
            fw.op("dve", "tensor_mul", prow[0:1, :], prow[0:1, :], decrow[0:1, :])
            bcast(p, decrow, h1, out=decbc)
            for pc in range(nc3):
                i2 = pc % 2
                if z == 0:
                    fw.op("dve", "tensor_copy", hb[i2][:, 0:3],
                          halb[:, pc * 3:(pc + 1) * 3], reads=(halb,),
                          writes=(hb[i2],))
                    fw.dma("sp", hb[i2][:, 3:131],
                           p.d["iph"][pc * 128:(pc + 1) * 128, 0:128],
                           tag="ld_hb")
                else:
                    fw.dma("sp", hb[i2][:, 0:131],
                           p.d["iph"][pc * 128:(pc + 1) * 128,
                                      z * 128 - 3:z * 128 + 128],
                           tag="ld_hb")
                fw.op("dve", "tensor_scalar", acc[i2][:, :], hb[i2][:, 0:128],
                      cwsb[:, pc * 4:pc * 4 + 1], None, ALU.mult,
                      reads=(hb[i2], cwsb), writes=(acc[i2],))
                fw.op("dve", "scalar_tensor_tensor", acc[i2][:, :],
                      hb[i2][:, 1:129], cwsb[:, pc * 4 + 1:pc * 4 + 2],
                      acc[i2][:, :], ALU.mult, ALU.add,
                      reads=(hb[i2], cwsb, acc[i2]), writes=(acc[i2],))
                fw.op("dve", "scalar_tensor_tensor", acc[i2][:, :],
                      hb[i2][:, 2:130], cwsb[:, pc * 4 + 2:pc * 4 + 3],
                      acc[i2][:, :], ALU.mult, ALU.add,
                      reads=(hb[i2], cwsb, acc[i2]), writes=(acc[i2],))
                fw.op("dve", "scalar_tensor_tensor", acc[i2][:, :],
                      hb[i2][:, 3:131], cwsb[:, pc * 4 + 3:pc * 4 + 4],
                      acc[i2][:, :], ALU.mult, ALU.add,
                      reads=(hb[i2], cwsb, acc[i2]), writes=(acc[i2],))
                if pc < nhs:
                    fw.op("act", "activation", hb[i2][:, 0:128], acc[i2][:, :],
                          AF.Silu, bias=cbsb[:, pc:pc + 1], scale=1.0,
                          reads=(acc[i2], cbsb), writes=(hb[i2],))
                    fw.op("pe", "transpose", pstb[:, :], hb[i2][:, 0:128],
                          p.eyeb[:, :], reads=(hb[i2], p.eyeb), writes=(pstb,))
                    fw.op("dve", "tensor_copy", hstm[:, pc * 128:(pc + 1) * 128],
                          pstb[:, :], reads=(pstb,), writes=(hstm,))
                else:
                    fw.op("act", "activation",
                          bcz[:, (pc - nhs) * 128:(pc - nhs + 1) * 128],
                          acc[i2][:, :], AF.Silu, bias=cbsb[:, pc:pc + 1],
                          scale=1.0, reads=(acc[i2], cbsb), writes=(bcz,))
            fw.dma("sp", p.d["hstm"][z * 128:(z + 1) * 128, :], hstm[:, :],
                   tag="st_hstm", sem="st_hstm")
            for bcid in range(nc3 - nhs):
                fw.dma("sp", p.d["bcfm"][bcid * 128:(bcid + 1) * 128,
                                         z * 128:(z + 1) * 128],
                       bcz[:, bcid * 128:(bcid + 1) * 128], tag="st_bcfm",
                       sem="st_bcfm")
            for g in range(c.G1):
                fw.op("pe", "transpose", pstb[:, :], bcz[:, g * 128:(g + 1) * 128],
                      p.eyeb[:, :], reads=(bcz, p.eyeb), writes=(pstb,))
                fw.op("dve", "tensor_copy", bt[:, g * 128:(g + 1) * 128],
                      pstb[:, :], reads=(pstb,), writes=(bt,))
            fw.dma("pool", p.d["s1loc"][z * c.N1:(z + 1) * c.N1, :], S[:, :],
                   tag="st_s1loc", sem="s1loc")
            for h in range(h1):
                g = h // (h1 // c.G1)
                i2 = h % 2
                fw.op("dve", "tensor_scalar", bsc[i2][:, :],
                      bt[:, g * 128:(g + 1) * 128], scl[:, h:h + 1], None,
                      ALU.mult, reads=(bt, scl), writes=(bsc[i2],))
                fw.op("pe", "matmul", ps_cs[i2][:, 0:c.P1], bsc[i2][:, :],
                      hstm[:, h * c.P1:(h + 1) * c.P1], start=True, stop=True,
                      reads=(bsc[i2], hstm), writes=(f"k_cs{i2}",))
                hsl = S[:, h * c.P1:(h + 1) * c.P1]
                fw.op("dve", "tensor_scalar", hsl, hsl, decbc[:, h:h + 1],
                      None, ALU.mult, reads=(S, decbc), writes=(S,))
                fw.op("dve", "tensor_tensor", hsl, hsl, ps_cs[i2][:, 0:c.P1],
                      ALU.add, reads=(S, f"k_cs{i2}"), writes=(S,))
        fw.op("dve", "tensor_scalar", ssend[:, :], S[:, :], p.smcol, None,
              ALU.mult)
        fw.dma("sp", p.d["s1fin"], ssend[:, :], tag="st_s1fin")
        if c.fused:
            fw.cc("AllReduce", ALU.add, c.groups, p.d["s1fin"], p.d["s1init"])



def ph_l1_ssd(p):
    c, fw = p.cfg, p.fw
    h1 = c.H1
    nhs = c.INTER // 128
    nbc = 2 * c.G1 * c.N1 // 128
    with p.scope():
        Sini = p.sb("Sini1", [128, h1 * c.P1], BF16)
        sini32 = p.sb("sini32", [128, 512], F32)
        for o in range(0, h1 * c.P1, 512):
            fw.dma("sp", sini32[:, :], p.d["s1init"][:, o:o + 512],
                   tag="ld_s1init")
            fw.op("dve", "tensor_scalar", Sini[:, o:o + 512], sini32[:, :],
                  p.rmcol, None, ALU.mult, reads=(sini32, p.smrm),
                  writes=(Sini,))
        nwrow = p.sb("nwrow", [1, c.INTER], F32)
        fw.dma("sp", nwrow[0:1, :], p.d["nw"])
        nw_bc = p.sb("nw_bc", [128, c.INTER], BF16)
        for o in range(0, c.INTER, 512):
            fw.op("pe", "matmul", p.ps_bc[:, 0:512], p.ones1[0:1, 0:128],
                  nwrow[0:1, o:o + 512], start=True, stop=True,
                  reads=(p.ones1, nwrow), writes=(p.ps_bc,))
            fw.op("dve", "tensor_copy", nw_bc[:, o:o + 512], p.ps_bc[:, 0:512],
                  reads=(p.ps_bc,), writes=(nw_bc,))
        dvrow = p.sb("dvrow", [1, h1], F32)
        fw.dma("sp", dvrow[0:1, :], p.d["dvec"])
        dcols = bcast(p, dvrow, h1, name="dcols")
        slsb = p.sb("slsb1", [128, h1 * c.P1], BF16)
        hsz = p.sb("hsz", [128, c.INTER], BF16)
        bcch = p.sb("bcch", [128, nbc * 128], BF16)
        acumz = p.sb("acumz", [128, h1], F32)
        b1z = p.sb("b1z", [128, h1], F32)
        expA = p.sb("expA1", [128, h1], F32)
        p1row = p.sb("p1row", [1, h1], F32)
        pbc1 = p.sb("pbc1", [128, h1], F32)
        segw = min(4096, h1 * 128)
        acflr = p.sb("acflr_s", [1, segw], F32)
        szb = p.sb("szb1", [128, h1 * c.P1], BF16)
        cbg = [p.sb(f"cbg{i}", [128, 128], F32) for i in range(2)]
        ps_cb = [p.ps(f"pscb{i}", [128, 128], F32) for i in range(2)]
        bankA1 = [p.ps(f"l1bankA{i}", [128, 128], F32) for i in range(2)]
        bankY = [p.ps(f"l1bankY{i}", [128, 128], F32) for i in range(2)]
        ps_a1 = [bankA1[i][:, 0:128] for i in range(2)]
        ps_yd = [bankY[i][:, 0:c.P1] for i in range(2)]
        ps_yo = [bankY[i][:, c.P1:2 * c.P1] for i in range(2)]
        dm = [p.sb(f"dm1{i}", [128, 128], F32) for i in range(2)]
        dec = dm
        mt = [p.sb(f"mt1{i}", [128, 128], BF16) for i in range(2)]
        u1 = [p.sb(f"u1_{i}", [128, c.P1], F32) for i in range(2)]
        u2 = u1
        ych = p.sb("ych1", [128, c.INTER], BF16)
        gch = p.sb("gch1", [128, c.INTER], BF16)
        gg = slsb
        ss = p.sb("ss1", [128, c.G1], F32)
        sqj = p.sb("sqj1", [128, c.INTER // c.G1], BF16)
        rstd = p.sb("rstd1", [128, c.G1], F32)
        ggn = ych
        pstb = p.ps("pstb1", [128, 128], BF16)
        tb = [p.sb(f"tb1{i}", [128, 128], BF16) for i in range(2)]
        for z in range(c.NZ):
            fw.dma("sp", slsb[:, :], p.d["s1loc"][z * c.N1:(z + 1) * c.N1, :],
                   tag="ld_s1loc")
            fw.dma("sp", hsz[:, :], p.d["hstm"][z * 128:(z + 1) * 128, :],
                   tag="ld_hstm")
            fw.dma("sp", bcch[:, :].rearrange("pp (cc t) -> pp cc t", t=128),
                   p.d["bcfm"].rearrange("(cc pp) t -> pp cc t", pp=128)[:, :, z * 128:(z + 1) * 128],
                   tag="ld_bcfm")
            fw.dma("sp", gch[:, :], p.d["ipg"][z * 128:(z + 1) * 128, :],
                   tag="ld_ipg")
            fw.dma("sp", acumz[:, :], p.d["acumt"][z * 128:(z + 1) * 128, :])
            fw.dma("sp", b1z[:, :], p.d["b1sp"][z * 128:(z + 1) * 128, :])
            fw.dma("sp", p1row[0:1, :], p.d["p1"][z:z + 1, :], tag="ld_p1")
            acfl_seg = [-1]
            fw.op("act", "activation", expA[:, :], acumz[:, :], AF.Exp,
                  bias=0.0, scale=1.0)
            bcast(p, p1row, h1, out=pbc1)
            for h in range(h1):
                hsl = slice(h * c.P1, (h + 1) * c.P1)
                fw.op("dve", "scalar_tensor_tensor", szb[:, hsl],
                      Sini[:, hsl], pbc1[:, h:h + 1], slsb[:, hsl],
                      ALU.mult, ALU.add, reads=(Sini, pbc1, slsb), writes=(szb,))
            for g in range(c.G1):
                ig = g % 2
                fw.op("pe", "matmul", ps_cb[ig][:, 0:128],
                      bcch[:, g * 128:(g + 1) * 128],
                      bcch[:, (c.G1 + g) * 128:(c.G1 + g + 1) * 128],
                      start=True, stop=True, reads=(bcch,),
                      writes=(f"k_cb{ig}",))
                fw.op("dve", "tensor_copy", cbg[ig][:, :], ps_cb[ig][:, 0:128],
                      reads=(f"k_cb{ig}",), writes=(cbg[ig],))
                for hh in range(h1 // c.G1):
                    h = g * (h1 // c.G1) + hh
                    i2 = h % 2
                    hsl = slice(h * c.P1, (h + 1) * c.P1)
                    seg = (h * 128) // segw
                    if seg != acfl_seg[0]:
                        fw.dma("sp", acflr[0:1, :],
                               p.d["acfl"][z:z + 1, seg * segw:(seg + 1) * segw],
                               tag="ld_acfl")
                        acfl_seg[0] = seg
                    ho = (h * 128) % segw
                    fw.op("pe", "matmul", ps_a1[i2], p.ones1[0:1, 0:128],
                          acflr[0:1, ho:ho + 128], start=True,
                          stop=True, reads=(p.ones1, acflr),
                          writes=(f"k1_a1{i2}",))
                    fw.op("dve", "tensor_tensor", dm[i2][:, :], ps_a1[i2],
                          p.mneg[:, :], ALU.add, reads=(f"k1_a1{i2}", p.mneg),
                          writes=(dm[i2],))
                    fw.op("act", "activation", dec[i2][:, :], dm[i2][:, :],
                          AF.Exp, bias=b1z[:, h:h + 1], scale=1.0)
                    fw.op("dve", "tensor_tensor", mt[i2][:, :], cbg[ig][:, :],
                          dec[i2][:, :], ALU.mult, reads=(cbg[ig], dec[i2]),
                          writes=(mt[i2],))
                    fw.op("pe", "matmul", ps_yd[i2], mt[i2][:, :],
                          hsz[:, hsl], start=True, stop=True,
                          reads=(mt[i2], hsz), writes=(f"k1_yd{i2}",))
                    fw.op("pe", "matmul", ps_yo[i2],
                          bcch[:, (c.G1 + g) * 128:(c.G1 + g + 1) * 128],
                          szb[:, hsl], start=True, stop=True,
                          reads=(bcch, szb), writes=(f"k1_yo{i2}",))
                    fw.op("dve", "tensor_scalar", u1[i2][:, :], ps_yo[i2],
                          expA[:, h:h + 1], None, ALU.mult,
                          reads=(f"k1_yo{i2}", f"k1_yd{i2}", expA),
                          writes=(u1[i2],))
                    fw.op("dve", "tensor_tensor", u2[i2][:, :], u1[i2][:, :],
                          ps_yd[i2], ALU.add, reads=(u1[i2], f"k1_yd{i2}"),
                          writes=(u2[i2],))
                    fw.op("dve", "scalar_tensor_tensor", ych[:, hsl],
                          hsz[:, hsl], dcols[:, h:h + 1], u2[i2][:, :],
                          ALU.mult, ALU.add, reads=(hsz, dcols, u2[i2]),
                          writes=(ych,))
            for j in range(c.INTER // 512):
                fw.op("dve", "tensor_tensor", gg[:, j * 512:(j + 1) * 512],
                      ych[:, j * 512:(j + 1) * 512],
                      gch[:, j * 512:(j + 1) * 512], ALU.mult,
                      reads=(ych, gch), writes=(gg,))
            gsz = c.INTER // c.G1
            for g in range(c.G1):
                fw.op("act", "activation", sqj[:, :], gg[:, g * gsz:(g + 1) * gsz],
                      AF.Square, bias=0.0, scale=1.0,
                      accum_out=ss[:, g:g + 1], reads=(gg,), writes=(sqj, ss))
            fw.op("act", "activation", rstd[:, :], ss[:, :], AF.Ln,
                  bias=1e-5, scale=1.0 / gsz)
            fw.op("act", "activation", rstd[:, :], rstd[:, :], AF.Exp,
                  bias=0.0, scale=-0.5)
            for g in range(c.G1):
                fw.op("dve", "scalar_tensor_tensor", ggn[:, g * gsz:(g + 1) * gsz],
                      gg[:, g * gsz:(g + 1) * gsz], rstd[:, g:g + 1],
                      nw_bc[:, g * gsz:(g + 1) * gsz], ALU.mult, ALU.mult,
                      reads=(gg, rstd, nw_bc), writes=(ggn,))
            for m in range(c.INTER // 128):
                fw.op("pe", "transpose", pstb[:, :], ggn[:, m * 128:(m + 1) * 128],
                      p.eyeb[:, :], reads=(ggn, p.eyeb), writes=("k_ggT",))
                fw.op("dve", "tensor_copy", tb[m % 2][:, :], pstb[:, :],
                      reads=("k_ggT",), writes=(tb[m % 2],))
                fw.dma("sp", p.d["ggfm"][m * 128:(m + 1) * 128,
                                         z * 128:(z + 1) * 128],
                       tb[m % 2][:, :], tag="st_ggfm")



def ph_l1_post(p):
    c, fw = p.cfg, p.fw
    nhs = c.INTER // 128
    with p.scope():
        rows3 = None
        with p.scope():
            ggr = p.sb("ggr", [128, nhs * c.T], BF16)
            load_fm(p, ggr, "ggfm", c.INTER, c.T)
            ps_s = p.ps("ps_s3", [1, c.T], F32)
            ps_q = p.ps("ps_q3", [1, c.T], F32)
            ps_o = [p.ps(f"ps3_o{i}", [128, 512], F32) for i in range(2)]
            wt = [p.sb(f"op_t{i}", [128, 128], BF16) for i in range(2)]
            xold = [p.sb(f"xo3{i}", [128, 512], F32) for i in range(2)]
            x3s = [p.sb(f"x3s{i}", [128, 512], F32) for i in range(2)]
            sqt = [p.sb(f"sq3{i}", [128, 512], BF16) for i in range(2)]
            xb3 = [p.sb(f"xb3{i}", [128, 512], BF16) for i in range(2)]
            for m in range(c.NK):
                for k in range(nhs):
                    w = wt[k % 2]
                    fw.dma("pool", w[:, :],
                           p.d["opj"][k * 128:(k + 1) * 128, m * 128:(m + 1) * 128],
                           tag=f"opj{m}_{k}")
                    for tt in range(c.T // 512):
                        fw.op("pe", "matmul", ps_o[tt][:, 0:512], w[:, :],
                              ggr[:, k * c.T + tt * 512:k * c.T + tt * 512 + 512],
                              start=(k == 0), stop=(k == nhs - 1))
                for tt in range(c.T // 512):
                    i2 = tt % 2
                    fw.dma("sp", xold[i2][:, :],
                           p.d["x2"][m * 128:(m + 1) * 128, tt * 512:tt * 512 + 512],
                           tag="ld_x2")
                    fw.op("dve", "tensor_tensor", x3s[i2][:, :], ps_o[tt][:, 0:512],
                          xold[i2][:, :], ALU.add)
                    fw.dma("sp", p.d["x3"][m * 128:(m + 1) * 128,
                                           tt * 512:tt * 512 + 512],
                           x3s[i2][:, :], tag="st_x3", sem="st_x3")
                    bsl = xb3[i2][:, :]
                    fw.op("act", "activation", bsl, x3s[i2][:, :], AF.Copy,
                          bias=0.0, scale=1.0, reads=(x3s[i2],), writes=(xb3[i2],))
                    fw.dma("sp", p.d["x3bf"][m * 128:(m + 1) * 128,
                                             tt * 512:tt * 512 + 512],
                           bsl, tag="st_x3bf")
                    fw.op("act", "activation", sqt[i2][:, :], bsl, AF.Square,
                          bias=0.0, scale=1.0, reads=(xb3[i2],), writes=(sqt[i2],))
                    fw.op("pe", "matmul", ps_s[0:1, tt * 512:(tt + 1) * 512],
                          p.onescol[:, 0:1], bsl, start=(m == 0),
                          stop=(m == c.NK - 1), reads=(p.onescol, xb3[i2]),
                          writes=("k_lns3",))
                    fw.op("pe", "matmul", ps_q[0:1, tt * 512:(tt + 1) * 512],
                          p.onescol[:, 0:1], sqt[i2][:, :], start=(m == 0),
                          stop=(m == c.NK - 1), reads=(p.onescol, sqt[i2]),
                          writes=("k_lnq3",))
            rows3 = ln_rows(p, ps_s, ps_q, "k_lns3", "k_lnq3")
        xn3 = p.sb("xn3_r", [128, c.NK * c.T], BF16)
        ln_apply_dram(p, rows3[0], rows3[1], "x3bf", xn3)
        Gall = p.sb("Gall", [128, c.NZ * c.MK], F32)
        with p.scope():
            wrs = p.sb("wrs", [128, c.NK * c.MK], BF16)
            for k in range(c.NK):
                fw.dma("pool", wrs[:, k * c.MK:(k + 1) * c.MK],
                       p.d["wr"][k * 128:(k + 1) * 128, :], tag=f"wr{k}")
            ps_l = p.ps("ps_l3", [128, 512], F32)
            lg = p.sb("lg3", [128, c.MK], F32)
            nmx = p.sb("nmx3", [128, 1], F32)
            ex = p.sb("ex3", [128, c.MK], F32)
            ssum = p.sb("ssum3", [128, 1], F32)
            scw = p.sb("scw3", [128, c.MK], F32)
            thr = p.sb("thr3", [128, 1], F32)
            mk = p.sb("mk3", [128, c.MK], F32)
            eg = p.sb("eg3", [128, c.MK], F32)
            for tz in range(c.NZ):
                for k in range(c.NK):
                    fw.op("pe", "matmul", ps_l[:, 0:c.MK],
                          xn3[:, k * c.T + tz * 128:k * c.T + tz * 128 + 128],
                          wrs[:, k * c.MK:(k + 1) * c.MK],
                          start=(k == 0), stop=(k == c.NK - 1))
                fw.op("dve", "tensor_copy", lg[:, :], ps_l[:, 0:c.MK])
                fw.op("dve", "tensor_reduce", nmx[:, :], lg[:, :], AX.X,
                      ALU.max, negate=True)
                fw.op("act", "activation", ex[:, :], lg[:, :], AF.Exp,
                      bias=nmx[:, 0:1], scale=1.0)
                topk_gate(p, ex, scw, thr, mk, c.MTOP)
                fw.op("dve", "tensor_mul", eg[:, :], ex[:, :], mk[:, :])
                fw.op("dve", "tensor_reduce", ssum[:, :], eg[:, :], AX.X, ALU.add)
                fw.op("dve", "reciprocal", ssum[:, :], ssum[:, :])
                fw.op("dve", "tensor_scalar", Gall[:, tz * c.MK:(tz + 1) * c.MK],
                      eg[:, :], ssum[:, 0:1], None, ALU.mult,
                      reads=(eg, ssum), writes=(Gall,))
        hgf = p.sb("hgf", [128, (c.KR // 128) * c.T], BF16)
        with p.scope():
            ast = p.sb("ast", [128, c.NK * 512], BF16)
            ps_h = p.ps("ps_h3", [128, 512], F32)
            hgt = p.sb("hgt", [128, 512], BF16)
            pstb3 = p.ps("pstb3", [128, 128], BF16)
            tb3 = [p.sb(f"tb3{i}", [128, 128], BF16) for i in range(2)]
            for n in range(c.KR // 512):
                for k in range(c.NK):
                    fw.dma("pool", ast[:, k * 512:(k + 1) * 512],
                           p.d["afl"][k * 128:(k + 1) * 128, n * 512:(n + 1) * 512],
                           tag=f"afl{n}_{k}")
                for tz in range(c.NZ):
                    for k in range(c.NK):
                        fw.op("pe", "matmul", ps_h[:, 0:512],
                              xn3[:, k * c.T + tz * 128:k * c.T + tz * 128 + 128],
                              ast[:, k * 512:(k + 1) * 512],
                              start=(k == 0), stop=(k == c.NK - 1))
                    nex = 512 // c.R
                    gv = Gall[:, tz * c.MK + n * nex: tz * c.MK + (n + 1) * nex]
                    fw.op("dve", "tensor_tensor",
                          hgt[:, :].rearrange("pp (e r) -> pp e r", r=c.R),
                          ps_h[:, 0:512].rearrange("pp (e r) -> pp e r", r=c.R),
                          gv.unsqueeze(2).broadcast_to([128, nex, c.R]),
                          ALU.mult, reads=(ps_h, Gall), writes=(hgt,))
                    for j in range(4):
                        fw.op("pe", "transpose", pstb3[:, :],
                              hgt[:, j * 128:(j + 1) * 128], p.eyeb[:, :],
                              reads=(hgt, p.eyeb), writes=("k_hgT",))
                        fw.op("dve", "tensor_copy", tb3[j % 2][:, :],
                              pstb3[:, :], reads=("k_hgT",), writes=(tb3[j % 2],))
                        kr = n * 4 + j
                        fw.op("dve", "tensor_copy",
                              hgf[:, kr * c.T + tz * 128: kr * c.T + tz * 128 + 128],
                              tb3[j % 2][:, :], reads=(tb3[j % 2],), writes=(hgf,))
        with p.scope():
            wt3 = [p.sb(f"bfl_t{i}", [128, 128], BF16) for i in range(2)]
            ps_f = [p.ps(f"ps_f{i}", [128, 512], F32) for i in range(2)]
            xo4 = [p.sb(f"xo4{i}", [128, 512], F32) for i in range(2)]
            yt = [p.sb(f"yt4{i}", [128, 512], F32) for i in range(2)]
            for m in range(c.NK):
                for k in range(c.KR // 128):
                    w = wt3[k % 2]
                    fw.dma("pool", w[:, :],
                           p.d["bfl"][k * 128:(k + 1) * 128, m * 128:(m + 1) * 128],
                           tag=f"bfl{m}_{k}")
                    for tt in range(c.T // 512):
                        fw.op("pe", "matmul", ps_f[tt][:, 0:512], w[:, :],
                              hgf[:, k * c.T + tt * 512:k * c.T + tt * 512 + 512],
                              start=(k == 0), stop=(k == c.KR // 128 - 1))
                for tt in range(c.T // 512):
                    i2 = tt % 2
                    fw.dma("sp", xo4[i2][:, :],
                           p.d["x3"][m * 128:(m + 1) * 128, tt * 512:tt * 512 + 512],
                           tag="ld_x3")
                    fw.op("dve", "scalar_tensor_tensor", yt[i2][:, :],
                          ps_f[tt][:, 0:512], 2.0, xo4[i2][:, :],
                          ALU.mult, ALU.add)
                    fw.dma("sp", p.d["y"][m * 128:(m + 1) * 128,
                                          tt * 512:tt * 512 + 512],
                           yt[i2][:, :], tag="st_y", sem="st_y")



def ph_ln0(p):
    ph_ln(p, None, "xn0", cast_from="x")


CONSTS = ["eye", "triu", "mneg", "smask", "rmask"]
PHASE_IO = {
    "ln0": (ph_ln0, ["x"], ["xn0"]),
    "l0proj": (ph_l0_proj, ["xn0", "wq", "wk", "wv", "wg", "wwin", "dtb0",
                            "alog0", "rcos", "rsin"],
               ["qsp", "ksp", "vsp", "g0sp", "dt0", "g0cum", "b0"]),
    "l0cs": (ph_l0_cs, ["ksp", "vsp", "g0cum", "b0"],
             ["s0loc", "p0", "s0fin"]),
    "l0att": (ph_l0_att, ["qsp", "ksp", "vsp", "g0sp", "g0cum", "b0", "p0",
                          "s0loc", "s0init", "gnw"], ["ygfm"]),
    "l0post": (ph_l0_post, ["ygfm", "x", "wo", "rw", "rb", "eb"],
               ["x1", "x2", "xn2", "x1bf", "x2bf"]),
    "l1inproj": (ph_l1_inproj, ["xn2", "ipj", "dtb1", "alog1"],
                 ["ipg", "iph", "dt1", "lndt1", "adt1", "halo"]),
    "l1convcs": (ph_l1_convcs, ["iph", "haloin", "cw", "cb", "dt1", "lndt1",
                                "adt1"],
                 ["hstm", "bcfm", "acumt", "b1sp", "acfl", "p1", "s1loc",
                  "s1fin"]),
    "l1ssd": (ph_l1_ssd, ["s1loc", "s1init", "hstm", "bcfm", "ipg", "acumt",
                          "b1sp", "acfl", "p1", "nw", "dvec"], ["ggfm"]),
    "l1post": (ph_l1_post, ["ggfm", "x2", "opj", "wr", "afl", "bfl"],
               ["x3", "y", "x3bf"]),
}
CC_ROUTE = {"s0init": "s0fin", "haloin": "halo", "s1init": "s1fin"}
CC_PROD = {"s0fin": "l0cs", "halo": "l1inproj", "s1fin": "l1convcs"}

LAUNCHES = [["ln0", "l0proj", "l0cs"], ["l0att"], ["l0post"],
            ["l1inproj"], ["l1convcs"], ["l1ssd"], ["l1post"]]

_BUILD_CACHE = {}


def build_launch(cfg, phases, key):
    if key in _BUILD_CACHE:
        return _BUILD_CACHE[key]
    reg = declare_tensors(cfg)
    nc = bass.Bass("TRN2", target_bir_lowering=False)
    writes_here = set()
    reads_here = set(CONSTS)
    for ph in phases:
        _, rds, wrs = PHASE_IO[ph]
        for r in rds:
            if r not in writes_here:
                reads_here.add(r)
        writes_here.update(wrs)
        if cfg.fused:
            for ini, fin in CC_ROUTE.items():
                if fin in wrs:
                    writes_here.add(ini)
    later_reads = set()
    if not cfg.fused:
        for ph, (fn, rds, wrs) in PHASE_IO.items():
            if ph in phases:
                continue
            later_reads.update(rds)
            for ini in CC_ROUTE:
                if ini in rds:
                    later_reads.add(CC_ROUTE[ini])
    ins, outs = [], []
    with contextlib.ExitStack() as es:
        p = P(nc, cfg, es)
        for name, (shape, dt) in reg.items():
            if name in reads_here and name not in writes_here:
                kind = "ExternalInput"
                ins.append(name)
            elif name == "y" and "y" in writes_here:
                kind = "ExternalOutput"
                outs.append(name)
            elif name in writes_here and (not cfg.fused) and (
                    name in later_reads or name in CC_ROUTE.values()):
                kind = "ExternalOutput"
                outs.append(name)
            elif name in writes_here or name in reads_here:
                kind = "Internal"
            else:
                continue
            p.d[name] = nc.dram_tensor(name, list(shape), dt, kind=kind).ap()
        ph_consts(p)
        for ph in phases:
            PHASE_IO[ph][0](p)
        p.fw.finish()
        p.fw.flush(es)
    _BUILD_CACHE[key] = (nc, ins, outs)
    return _BUILD_CACHE[key]


def rope_tables(L, D=128, theta=1e4):
    inv = 1.0 / theta ** (np.arange(0, D, 2, dtype=np.float32) / D)
    ang = np.arange(L, dtype=np.float32)[:, None] * inv
    cos = np.concatenate([np.cos(ang), np.cos(ang)], -1).astype(np.float32)
    sin = np.concatenate([np.sin(ang), np.sin(ang)], -1).astype(np.float32)
    return cos, sin


def prep_inputs(cfg, inp):
    c = cfg
    ct = lambda a: np.ascontiguousarray(a, dtype=np.float32)
    shared = {}
    shared["wq"] = ct(inp["L0_attn_Wq"].T)
    shared["wk"] = ct(inp["L0_attn_Wk"].T)
    shared["wv"] = ct(inp["L0_attn_Wv"].T)
    shared["wg"] = ct(inp["L0_attn_Wg"].T)
    shared["wwin"] = ct(inp["L0_attn_Win"].T)
    shared["dtb0"] = ct(inp["L0_attn_dt_bias"][None, :])
    shared["alog0"] = ct(inp["L0_attn_A_log_bias"][None, :])
    shared["gnw"] = ct(inp["L0_attn_gnorm_w"][None, :])
    shared["wo"] = ct(inp["L0_attn_Wo"].T)
    shared["rw"] = ct(inp["L0_ffn_router_weight"].T)
    shared["rb"] = ct(inp["L0_ffn_router_bias"][None, :])
    shared["eb"] = ct(inp["L0_ffn_expert_bias"])
    shared["ipj"] = ct(inp["L1_attn_in_proj_w"].T)
    cw = inp["L1_attn_conv_w"][:, 0, :]
    nc3 = c.CONV // 128
    shared["cw"] = ct(cw.reshape(nc3, 128, 4).transpose(1, 0, 2).reshape(128, nc3 * 4))
    shared["cb"] = ct(inp["L1_attn_conv_b"].reshape(nc3, 128).T)
    shared["dtb1"] = ct(inp["L1_attn_dt_bias"][:, None])
    shared["alog1"] = ct(inp["L1_attn_A_log"][:, None])
    shared["dvec"] = ct(inp["L1_attn_D"][None, :])
    shared["nw"] = ct(inp["L1_attn_norm_w"][None, :])
    shared["opj"] = ct(inp["L1_attn_out_proj_w"].T)
    shared["afl"] = ct(inp["L1_ffn_A"].reshape(c.KR, c.D).T)
    shared["bfl"] = ct(inp["L1_ffn_B"].transpose(0, 2, 1).reshape(c.KR, c.D))
    shared["wr"] = ct(inp["L1_ffn_Wr"].T)
    shared["eye"] = np.eye(128, dtype=np.float32)
    tri = np.triu(np.ones((128, 128), dtype=np.float32))
    shared["triu"] = tri
    shared["mneg"] = np.where(tri > 0, 0.0, -1e30).astype(np.float32)
    x = inp["L0_attn_hidden_states"]
    B, L, _ = x.shape
    cos, sin = rope_tables(L)
    percore = []
    for core in range(8):
        b, half = core // 2, core % 2
        pc = {}
        pc["x"] = ct(x[b, half * c.T:(half + 1) * c.T, :].T)
        pc["rcos"] = ct(cos[half * c.T:(half + 1) * c.T, :].T)
        pc["rsin"] = ct(sin[half * c.T:(half + 1) * c.T, :].T)
        pc["smask"] = np.full((1, 1), 1.0 if half == 0 else 0.0, np.float32)
        pc["rmask"] = np.full((1, 1), 0.0 if half == 0 else 1.0, np.float32)
        percore.append(pc)
    return shared, percore


def kernel(**inputs):
    cfg = Cfg(True)
    cfg.fused = FUSED
    shared, percore = prep_inputs(cfg, inputs)
    env = [dict(shared, **pc) for pc in percore]
    launches = [[ph for grp in LAUNCHES for ph in grp]] if cfg.fused else LAUNCHES
    for li, grp in enumerate(launches):
        nc, ins, outs = build_launch(cfg, grp, (cfg.fused, li))
        in_maps = [{k: env[core][k] for k in ins} for core in range(8)]
        res = run_bass_kernel_spmd(nc, in_maps, core_ids=list(range(8)))
        for core in range(8):
            for k in outs:
                env[core][k] = res.results[core][k]
        if not cfg.fused:
            for ini, fin in CC_ROUTE.items():
                if fin in outs:
                    for core in range(8):
                        env[core][ini] = res.results[core ^ 1][fin]
    out = np.empty((4, 2048, 4096), dtype=np.float32)
    for core in range(8):
        b, half = core // 2, core % 2
        out[b, half * cfg.T:(half + 1) * cfg.T, :] = env[core]["y"].T
    return out


# version 07
# speedup vs baseline: 1.0021x; 1.0021x over previous
import contextlib
import numpy as np

import concourse.bass as bass
import concourse.mybir as mybir
from concourse.bass_utils import run_bass_kernel_spmd

F32 = mybir.dt.float32
BF16 = mybir.dt.bfloat16
AX = mybir.AxisListType
ALU = mybir.AluOpType
AF = mybir.ActivationFunctionType

FUSED = True


ENGS = ("pe", "act", "dve", "pool", "sp")
_ENG_ATTR = {"pe": "tensor", "act": "scalar", "dve": "vector", "pool": "gpsimd",
             "sp": "sync"}
_SEM_ROT = 48000


class _Op:
    __slots__ = ("method", "args", "kwargs", "waits", "inc", "idx", "tag")

    def __init__(self, method, args, kwargs, idx, tag):
        self.method = method
        self.args = args
        self.kwargs = kwargs
        self.waits = []
        self.inc = None
        self.idx = idx
        self.tag = tag


class Fw:
    def __init__(self, nc):
        self.nc = nc
        self.ops = {e: [] for e in ENGS}
        self.ndma = {e: 0 for e in ENGS}
        self.named = {}
        self.barrier_totals = {e: 0 for e in ENGS}
        self.needed = {}
        self.state = {}
        self.waited = {e: {} for e in ENGS}
        self.last_compute = {e: None for e in ENGS}
        self.epoch = 0
        self.named_first = {}
        self.named_last = {}
        self.named_queue = {}
        self.verbose = False

    def _key(self, ap):
        if isinstance(ap, str):
            return ap
        t = getattr(ap, "tensor", ap)
        return getattr(t, "name", str(t))

    def _st(self, key):
        if key not in self.state:
            self.state[key] = {"w": None, "r": []}
        return self.state[key]

    def _need_wait(self, eng, dep):
        stream, idx = dep
        cur = self.waited[eng].get(stream, -1)
        if idx <= cur:
            return False
        self.waited[eng][stream] = idx
        return True

    def _resolve(self, eng, deps, o, what=""):
        for d in deps:
            stream, idx = d
            if stream == "eng:" + eng and eng == "pe":
                continue
            if stream == "eng:" + eng and idx == o.idx:
                continue
            if stream.startswith("dmabulk:"):
                q = stream.split(":")[1]
                if idx <= self.barrier_totals[q]:
                    continue
                raise RuntimeError(
                    f"dep on un-barriered bulk DMA {d} (from {what})")
            if self._need_wait(eng, d):
                o.waits.append(d)
                self.needed[d] = True

    def op(self, eng, method, *args, reads=None, writes=None, tag="", **kwargs):
        assert eng in ENGS
        _OUT_KW = ("accum_out",)
        if writes is None:
            writes = (args[0],)
            if kwargs.get("accum_out") is not None:
                writes = writes + (kwargs["accum_out"],)
        if reads is None:
            reads = tuple(a for a in args[1:] if hasattr(a, "tensor"))
            reads = reads + tuple(v for k, v in kwargs.items()
                                  if k not in _OUT_KW and hasattr(v, "tensor"))
        idx = len(self.ops[eng])
        o = _Op(method, args, kwargs, idx, tag)
        deps = []
        for ap in reads:
            st = self._st(self._key(ap))
            if st["w"] is not None:
                deps.append(st["w"])
        for ap in writes:
            st = self._st(self._key(ap))
            if st["w"] is not None:
                deps.append(st["w"])
            deps.extend(st["r"])
        self._resolve(eng, deps, o, tag or method)
        mytok = ("eng:" + eng, idx)
        for ap in reads:
            self._st(self._key(ap))["r"].append(mytok)
        for ap in writes:
            st = self._st(self._key(ap))
            st["w"] = mytok
            st["r"] = []
        self.ops[eng].append(o)
        self.last_compute[eng] = idx
        return mytok

    def dma(self, q, out, in_, sem=None, rkey=None, wkey=None, tag="", **kwargs):
        assert q in ENGS
        if sem is None:
            side = out if "SBTensor" in type(getattr(out, "tensor", out)).__name__ else in_
            sem = "b_" + self._key(side) + "@" + q
        elif sem is False:
            sem = None
        idx = len(self.ops[q])
        o = _Op("dma_start", (), dict(out=out, in_=in_, **kwargs), idx, tag)
        deps = []
        st_in = self._st(rkey if rkey is not None else self._key(in_))
        if st_in["w"] is not None:
            deps.append(st_in["w"])
        st_out = self._st(wkey if wkey is not None else self._key(out))
        if st_out["w"] is not None:
            deps.append(st_out["w"])
        deps.extend(st_out["r"])
        self._resolve(q, deps, o, tag or "dma")
        if sem is None:
            self.ndma[q] += 1
            tok = ("dmabulk:" + q, self.ndma[q])
        else:
            self.named[sem] = self.named.get(sem, 0) + 1
            self.named_first.setdefault(sem, self.epoch)
            self.named_last[sem] = self.epoch
            q0 = self.named_queue.setdefault(sem, q)
            assert q0 == q, f"named sem {sem} used from queues {q0} and {q}"
            tok = ("dsem:" + sem, self.named[sem])
        o.inc = tok
        st_in["r"].append(tok)
        st_out["w"] = tok
        st_out["r"] = []
        self.ops[q].append(o)
        return tok

    def cc(self, kind, op, groups, in_ap, out_ap):
        tok = self.op("pool", "collective_compute", kind, op,
                      reads=(in_ap,), writes=(out_ap,),
                      replica_groups=groups, ins=[in_ap.opt()],
                      outs=[out_ap.opt()], tag="cc")
        self.op("pool", "engine_nop", reads=(out_ap,), writes=("_ccnop",),
                tag="ccwait")
        return tok

    def barrier(self):
        toks = []
        for e in ENGS:
            if self.last_compute[e] is not None:
                toks.append(("eng:" + e, self.last_compute[e]))
            if self.ndma[e]:
                toks.append(("dmabulk:" + e, self.ndma[e]))
        for nm, c in self.named.items():
            if c:
                toks.append(("dsem:" + nm, c))
        for e in ENGS:
            o = _Op("engine_nop", (), {}, len(self.ops[e]), "barrier")
            for d in toks:
                if d[0] == "eng:" + e:
                    continue
                if self._need_wait(e, d):
                    o.waits.append(d)
                    self.needed[d] = True
            self.ops[e].append(o)
        for e in ENGS:
            self.barrier_totals[e] = self.ndma[e]
        self.epoch += 1

    def finish(self):
        self.barrier()

    def flush(self, es):
        nc = self.nc
        eng_incpts = {e: sorted(i for (s, i) in self.needed if s == "eng:" + e)
                      for e in ENGS}

        def mk_sems(name, nvals):
            n_sems = max(1, (nvals + _SEM_ROT - 1) // _SEM_ROT)
            return [es.enter_context(nc.semaphore(f"{name}{i}"))
                    for i in range(n_sems)]

        eng_sems = {e: mk_sems("S" + e, len(eng_incpts[e])) for e in ENGS}
        eng_rank = {e: {idx: r + 1 for r, idx in enumerate(eng_incpts[e])}
                    for e in ENGS}
        bulk_sems = {e: mk_sems("D" + e, self.ndma[e] * 16) for e in ENGS}
        slots = []
        slot_count = []
        name_slot = {}
        name_base = {}
        for nm in sorted(self.named, key=lambda n: (self.named_first[n],
                                                    self.named_last[n])):
            fe, le = self.named_first[nm], self.named_last[nm]
            qn = self.named_queue[nm]
            pick = None
            for si in range(len(slots)):
                if (slots[si][0] == qn and slots[si][1] < fe
                        and (slot_count[si] + self.named[nm]) * 16 < _SEM_ROT):
                    pick = si
                    break
            if pick is None:
                pick = len(slots)
                slots.append((qn, -1))
                slot_count.append(0)
            name_slot[nm] = pick
            name_base[nm] = slot_count[pick]
            slot_count[pick] += self.named[nm]
            slots[pick] = (qn, le)
        named_slot_sems = [es.enter_context(nc.semaphore(f"NS{si}"))
                           for si in range(len(slots))]
        rot = _SEM_ROT

        def sem_val(stream, idx):
            kind, name = stream.split(":", 1)
            if kind == "eng":
                r = eng_rank[name][idx]
                si, v = divmod(r - 1, rot)
                return eng_sems[name][si], v + 1
            if kind == "dmabulk":
                per = rot // 16
                si, v = divmod(idx - 1, per)
                return bulk_sems[name][si], (v + 1) * 16
            return (named_slot_sems[name_slot[name]],
                    (name_base[name] + idx) * 16)

        def inc_for_eng(e, idx):
            r = eng_rank[e].get(idx)
            if r is None:
                return None
            return eng_sems[e][(r - 1) // rot], 1

        def inc_for_dma(stream, idx):
            kind, name = stream.split(":", 1)
            if kind == "dmabulk":
                per = rot // 16
                return bulk_sems[name][(idx - 1) // per], 16
            return named_slot_sems[name_slot[name]], 16

        if self.verbose:
            stats = {e: (len(self.ops[e]), len(eng_incpts[e])) for e in ENGS}
            nw = sum(len(o.waits) for e in ENGS for o in self.ops[e])
            print("fw flush ops(incs):", stats, "names:", len(self.named),
                  "slots:", len(slots), "waits:", nw)

        with nc.Block() as block:
            for e in ENGS:
                ops = self.ops[e]
                if not ops:
                    continue

                def body(engine, ops=ops, e=e):
                    for o in ops:
                        for d in o.waits:
                            s, v = sem_val(*d)
                            engine.wait_ge(s, v)
                        if o.method == "engine_nop":
                            inst = engine.nop()
                        else:
                            inst = getattr(engine, o.method)(*o.args, **o.kwargs)
                        if o.inc is not None:
                            s, a = inc_for_dma(*o.inc)
                            inst.then_inc(s, a)
                        else:
                            ic = inc_for_eng(e, o.idx)
                            if ic is not None:
                                inst.then_inc(ic[0], ic[1])

                getattr(block, _ENG_ATTR[e])(body)



class Cfg:
    def __init__(self, full=True):
        if full:
            self.D = 4096
            self.T = 1024
            self.H0 = 32
            self.KVH = 8
            self.H1 = 128
            self.G1 = 8
            self.E = 64
            self.KTOP = 8
            self.MK = 64
            self.MTOP = 16
            self.R = 16
        else:
            self.D = 512
            self.T = 512
            self.H0 = 4
            self.KVH = 1
            self.H1 = 16
            self.G1 = 2
            self.E = 64
            self.KTOP = 8
            self.MK = 64
            self.MTOP = 16
            self.R = 16
        self.HD = 128
        self.P1 = 64
        self.N1 = 128
        self.CH = 128
        self.NZ = self.T // self.CH
        self.KD = self.KVH * self.HD
        self.QD = self.H0 * self.HD
        self.INTER = self.H1 * self.P1
        self.CONV = self.INTER + 2 * self.G1 * self.N1
        self.IPJ = self.INTER + self.CONV + self.H1
        self.KR = self.MK * self.R
        self.NK = self.D // 128
        self.fused = True
        self.groups = [[2 * i, 2 * i + 1] for i in range(4)]



def declare_tensors(cfg):
    c = cfg
    t = {}
    t["x"] = ([c.D, c.T], F32)
    t["wq"] = ([c.D, c.QD], F32)
    t["wk"] = ([c.D, c.KD], F32)
    t["wv"] = ([c.D, c.KD], F32)
    t["wg"] = ([c.D, c.QD], F32)
    t["wwin"] = ([c.D, c.H0], F32)
    t["dtb0"] = ([1, c.H0], F32)
    t["alog0"] = ([1, c.H0], F32)
    t["gnw"] = ([1, c.HD], F32)
    t["wo"] = ([c.QD, c.D], F32)
    t["rw"] = ([c.D, c.E], F32)
    t["rb"] = ([1, c.E], F32)
    t["eb"] = ([c.E, c.D], F32)
    t["ipj"] = ([c.D, c.IPJ], F32)
    t["cw"] = ([128, (c.CONV // 128) * 4], F32)
    t["cb"] = ([128, c.CONV // 128], F32)
    t["dtb1"] = ([c.H1, 1], F32)
    t["alog1"] = ([c.H1, 1], F32)
    t["dvec"] = ([1, c.H1], F32)
    t["nw"] = ([1, c.INTER], F32)
    t["opj"] = ([c.INTER, c.D], F32)
    t["afl"] = ([c.D, c.KR], F32)
    t["bfl"] = ([c.KR, c.D], F32)
    t["wr"] = ([c.D, c.MK], F32)
    t["rcos"] = ([c.HD, c.T], F32)
    t["rsin"] = ([c.HD, c.T], F32)
    t["eye"] = ([128, 128], F32)
    t["triu"] = ([128, 128], F32)
    t["mneg"] = ([128, 128], F32)
    t["smask"] = ([1, 1], F32)
    t["rmask"] = ([1, 1], F32)
    t["y"] = ([c.D, c.T], F32)
    t["xn0"] = ([c.D, c.T], BF16)
    t["qsp"] = ([c.QD, c.T], BF16)
    t["ksp"] = ([c.KD, c.T], BF16)
    t["vsp"] = ([c.T, c.KD], BF16)
    t["g0sp"] = ([c.T, c.QD], BF16)
    t["dt0"] = ([c.T, c.H0], F32)
    t["g0cum"] = ([c.T, c.H0], F32)
    t["b0"] = ([c.T, c.H0], F32)
    t["s0loc"] = ([c.NZ * 128, c.H0 * c.HD], BF16)
    t["p0"] = ([c.NZ, c.H0], F32)
    t["s0fin"] = ([128, c.H0 * c.HD], F32)
    t["s0init"] = ([128, c.H0 * c.HD], F32)
    t["ygfm"] = ([c.QD, c.T], BF16)
    t["x1"] = ([c.D, c.T], F32)
    t["x1bf"] = ([c.D, c.T], BF16)
    t["xn1"] = ([c.D, c.T], BF16)
    t["x2"] = ([c.D, c.T], F32)
    t["x2bf"] = ([c.D, c.T], BF16)
    t["xn2"] = ([c.D, c.T], BF16)
    t["ipg"] = ([c.T, c.INTER], BF16)
    t["iph"] = ([c.CONV, c.T], BF16)
    t["dt1"] = ([c.H1, c.T], F32)
    t["lndt1"] = ([c.H1, c.T], F32)
    t["adt1"] = ([c.H1, c.T], F32)
    t["halo"] = ([128, (c.CONV // 128) * 3], F32)
    t["haloin"] = ([128, (c.CONV // 128) * 3], F32)
    t["hstm"] = ([c.T, c.INTER], BF16)
    t["bcfm"] = ([2 * c.G1 * c.N1, c.T], BF16)
    t["acumt"] = ([c.T, c.H1], F32)
    t["b1sp"] = ([c.T, c.H1], F32)
    t["acfl"] = ([c.NZ, c.H1 * 128], F32)
    t["p1"] = ([c.NZ, c.H1], F32)
    t["s1loc"] = ([c.NZ * c.N1, c.H1 * c.P1], BF16)
    t["s1fin"] = ([c.N1, c.H1 * c.P1], F32)
    t["s1init"] = ([c.N1, c.H1 * c.P1], F32)
    t["ggfm"] = ([c.INTER, c.T], BF16)
    t["x3"] = ([c.D, c.T], F32)
    t["x3bf"] = ([c.D, c.T], BF16)
    t["xn3"] = ([c.D, c.T], BF16)
    return t


INPUT_NAMES = ["x", "wq", "wk", "wv", "wg", "wwin", "dtb0", "alog0", "gnw",
               "wo", "rw", "rb", "eb", "ipj", "cw", "cb", "dtb1", "alog1",
               "dvec", "nw", "opj", "afl", "bfl", "wr", "rcos", "rsin",
               "eye", "triu", "mneg", "smask", "rmask"]



class P:

    def __init__(self, nc, cfg, es):
        self.nc = nc
        self.cfg = cfg
        self.scopes = [es]
        self.fw = Fw(nc)
        self.d = {}
        self.uid = 0

    def sb(self, name, shape, dt=F32):
        return self.scopes[-1].enter_context(self.nc.sbuf_tensor(name, shape, dt))

    def sb_root(self, name, shape, dt=F32):
        return self.scopes[0].enter_context(self.nc.sbuf_tensor(name, shape, dt))

    def ps(self, name, shape, dt=F32):
        return self.scopes[-1].enter_context(self.nc.psum_tensor(name, shape, dt))

    def u(self, base):
        self.uid += 1
        return f"{base}{self.uid}"

    @contextlib.contextmanager
    def scope(self):
        es = contextlib.ExitStack()
        self.scopes.append(es)
        try:
            yield
        finally:
            self.fw.barrier()
            self.scopes.pop()
            es.close()


def ph_consts(p):
    c, fw = p.cfg, p.fw
    p.ones1 = p.sb("ones1", [1, 128], F32)
    fw.op("dve", "memset", p.ones1[0:1, :], 1.0)
    p.onescol = p.sb("onescol", [128, 1], BF16)
    fw.op("dve", "memset", p.onescol[:, :], 1.0)
    p.eye32 = p.sb("c_eye32", [128, 128], F32)
    fw.dma("sp", p.eye32[:, :], p.d["eye"])
    p.eyeb = p.sb("c_eyeb", [128, 128], BF16)
    fw.dma("pool", p.eyeb[:, :], p.d["eye"])
    p.triu = p.sb("c_triu", [128, 128], F32)
    fw.dma("sp", p.triu[:, :], p.d["triu"])
    p.mneg = p.sb("c_mneg", [128, 128], F32)
    fw.dma("sp", p.mneg[:, :], p.d["mneg"])
    p.ps_bc = p.ps("ps_bc", [128, 512], F32)
    sm = p.sb("sm_row", [1, 2], F32)
    fw.dma("sp", sm[0:1, 0:1], p.d["smask"])
    fw.dma("sp", sm[0:1, 1:2], p.d["rmask"])
    p.smrm = p.sb("smrm", [128, 2], F32)
    fw.op("pe", "matmul", p.ps_bc[:, 0:2], p.ones1[0:1, 0:128], sm[0:1, 0:2],
          start=True, stop=True)
    fw.op("dve", "tensor_copy", p.smrm[:, :], p.ps_bc[:, 0:2])
    p.smcol = p.smrm[:, 0:1]
    p.rmcol = p.smrm[:, 1:2]
    consts = [1e-5, 1e-30, -0.5, 1.0 / c.D, 1.0 / c.HD,
              1.0 / (c.INTER // c.G1), 0.5]
    p.cst = p.sb("c_vals", [128, len(consts)], F32)
    p.rows = p.sb("c_rows", [1, 4 * c.T], F32)
    p.rows_off = 0
    for i, v in enumerate(consts):
        fw.op("dve", "memset", p.cst[:, i:i + 1], v, writes=(p.cst,))
        p.nc.const_aps.register(F32, v, p.cst[:, i:i + 1])
    fw.barrier()


def bcast(p, row_ap, w, out=None, name=None, read_key=None):
    fw = p.fw
    if out is None:
        out = p.sb(name or p.u("bc"), [128, w], F32)
    for o in range(0, w, 512):
        ww = min(512, w - o)
        kw = {}
        fw.op("pe", "matmul", p.ps_bc[:, 0:ww], p.ones1[0:1, 0:128],
              row_ap[0:1, o:o + ww], start=True, stop=True,
              reads=(p.ones1[:, :], read_key or row_ap,), writes=(p.ps_bc,))
        fw.op("dve", "tensor_copy", out[:, o:o + ww], p.ps_bc[:, 0:ww])
    return out


def load_fm(p, sb_tile, dram_name, nrows, T, q="sp", cast=False, cols=None):
    c0, cl = cols if cols else (0, T)
    src = p.d[dram_name]
    for cch in range(nrows // 128):
        dst = sb_tile[:, cch * cl:(cch + 1) * cl]
        s = src[cch * 128:(cch + 1) * 128, c0:c0 + cl]
        p.fw.dma("pool" if cast else q, dst, s, tag=f"ld_{dram_name}{cch}")


def softplus(p, out, raw, tmp1, tmp2):
    fw = p.fw
    fw.op("act", "activation", tmp1, raw, AF.Abs, bias=0.0, scale=1.0)
    fw.op("act", "activation", tmp1, tmp1, AF.Exp, bias=0.0, scale=-1.0)
    fw.op("act", "activation", tmp1, tmp1, AF.Ln, bias=1.0, scale=1.0)
    fw.op("dve", "tensor_scalar", tmp2, raw, 0.0, None, ALU.max)
    fw.op("dve", "tensor_add", out, tmp2, tmp1)



def ph_ln(p, xbf_src, xn_dst, cast_from=None):
    c, fw = p.cfg, p.fw
    with p.scope():
        xbf = p.sb(p.u("lnxbf"), [128, c.NK * c.T], BF16)
        if cast_from is not None:
            load_fm(p, xbf, cast_from, c.D, c.T, cast=True)
        else:
            load_fm(p, xbf, xbf_src, c.D, c.T, q="sp")
        ps_sum = p.ps(p.u("lnps"), [1, c.T], F32)
        ps_sq = p.ps(p.u("lnpq"), [1, c.T], F32)
        sq = p.sb(p.u("lnsq"), [128, 512], BF16)
        for tt in range(c.T // 512):
            for cc in range(c.NK):
                xsl = xbf[:, cc * c.T + tt * 512: cc * c.T + tt * 512 + 512]
                fw.op("pe", "matmul", ps_sum[0:1, tt * 512:(tt + 1) * 512],
                      p.onescol[:, 0:1], xsl, start=(cc == 0), stop=(cc == c.NK - 1))
                fw.op("act", "activation", sq[:, :], xsl, AF.Square,
                      bias=0.0, scale=1.0, tag="lnsq")
                fw.op("pe", "matmul", ps_sq[0:1, tt * 512:(tt + 1) * 512],
                      p.onescol[:, 0:1], sq[:, :], start=(cc == 0), stop=(cc == c.NK - 1))
        mean = p.sb(p.u("lnm"), [1, c.T], F32)
        var = p.sb(p.u("lnv"), [1, c.T], F32)
        rstd = p.sb(p.u("lnr"), [1, c.T], F32)
        mrs = p.sb(p.u("lnmr"), [1, c.T], F32)
        fw.op("act", "activation", mean[0:1, :], ps_sum[0:1, :], AF.Copy,
              bias=0.0, scale=1.0 / c.D)
        fw.op("dve", "tensor_mul", var[0:1, :], mean[0:1, :], mean[0:1, :])
        fw.op("dve", "scalar_tensor_tensor", var[0:1, :], ps_sq[0:1, :],
              1.0 / c.D, var[0:1, :], ALU.mult, ALU.subtract)
        fw.op("act", "activation", rstd[0:1, :], var[0:1, :], AF.Ln,
              bias=1e-5, scale=1.0)
        fw.op("act", "activation", rstd[0:1, :], rstd[0:1, :], AF.Exp,
              bias=0.0, scale=-0.5)
        fw.op("dve", "tensor_mul", mrs[0:1, :], mean[0:1, :], rstd[0:1, :])
        tmp = p.sb(p.u("lntmp"), [128, 512], F32)
        xnt = p.sb(p.u("lnxn"), [128, 512], BF16)
        for tt in range(c.T // 512):
            rbc = bcast(p, rstd[0:1, tt * 512:(tt + 1) * 512], 512, name=p.u("lnrb"))
            mbc = bcast(p, mrs[0:1, tt * 512:(tt + 1) * 512], 512, name=p.u("lnmb"))
            for cc in range(c.NK):
                xsl = xbf[:, cc * c.T + tt * 512: cc * c.T + tt * 512 + 512]
                fw.op("dve", "tensor_mul", tmp[:, :], xsl, rbc[:, :])
                fw.op("dve", "tensor_sub", xnt[:, :], tmp[:, :], mbc[:, :])
                p.fw.dma("sp", p.d[xn_dst][cc * 128:(cc + 1) * 128,
                                           tt * 512:tt * 512 + 512],
                         xnt[:, :], tag="st_xn")



def ph_l0_proj(p):
    c, fw = p.cfg, p.fw
    parts = getattr(c, "l0proj_parts", "qk,vg,dt")
    with p.scope():
        xns = p.sb("xns", [128, c.NK * c.T], BF16)
        load_fm(p, xns, "xn0", c.D, c.T)
        cosf = p.sb("cosf", [128, c.T], F32)
        sinf = p.sb("sinf", [128, c.T], F32)
        fw.dma("sp", cosf[:, :], p.d["rcos"])
        fw.dma("sp", sinf[:, :], p.d["rsin"])

        wt = [p.sb(f"l0w{i}", [128, 128], BF16) for i in range(2)]
        psq = [p.ps(f"l0psq{i}", [128, 512], F32) for i in range(2)]
        t1 = p.sb("ropet1", [128, 512], F32)
        rot = p.sb("roperot", [128, 512], F32)
        t2 = p.sb("ropet2", [128, 512], F32)
        qout = p.sb("ropeq", [128, 512], BF16)

        def proj_rope(wname, mch, dst):
            for m in range(mch):
                w = wt[m % 2]
                for k in range(c.NK):
                    fw.dma("pool", w[:, :],
                           p.d[wname][k * 128:(k + 1) * 128, m * 128:(m + 1) * 128],
                           tag=f"{wname}{m}_{k}")
                    for tt in range(c.T // 512):
                        fw.op("pe", "matmul", psq[tt][:, 0:512], w[:, :],
                              xns[:, k * c.T + tt * 512:k * c.T + tt * 512 + 512],
                              start=(k == 0), stop=(k == c.NK - 1))
                for tt in range(c.T // 512):
                    if getattr(c, "norope", False):
                        fw.op("dve", "tensor_copy", qout[:, :], psq[tt][:, 0:512])
                        fw.dma("sp", p.d[dst][m * 128:(m + 1) * 128,
                                              tt * 512:tt * 512 + 512],
                               qout[:, :], tag=f"st_{dst}")
                        continue
                    cs = cosf[:, tt * 512:(tt + 1) * 512]
                    sn = sinf[:, tt * 512:(tt + 1) * 512]
                    rv = getattr(c, "ropevar", 0)
                    if rv == 1:
                        fw.op("dve", "tensor_mul", t1[:, :], psq[tt][:, 0:512], cs)
                        fw.op("act", "activation", qout[:, :], t1[:, :], AF.Copy,
                              bias=0.0, scale=1.0)
                        fw.dma("sp", p.d[dst][m * 128:(m + 1) * 128,
                                              tt * 512:tt * 512 + 512],
                               qout[:, :], tag=f"st_{dst}")
                        continue
                    if rv == 2:
                        fw.op("act", "activation", rot[0:64, :], psq[tt][64:128, 0:512],
                              AF.Copy, bias=0.0, scale=-1.0, writes=(rot,),
                              reads=(psq[tt],))
                        fw.op("act", "activation", rot[64:128, :], psq[tt][0:64, 0:512],
                              AF.Copy, bias=0.0, scale=1.0, writes=(rot,),
                              reads=(psq[tt],))
                        fw.op("dve", "tensor_mul", t2[:, :], rot[:, :], sn)
                        fw.op("act", "activation", qout[:, :], t2[:, :], AF.Copy,
                              bias=0.0, scale=1.0)
                        fw.dma("sp", p.d[dst][m * 128:(m + 1) * 128,
                                              tt * 512:tt * 512 + 512],
                               qout[:, :], tag=f"st_{dst}")
                        continue
                    fw.op("dve", "tensor_mul", t1[:, :], psq[tt][:, 0:512], cs)
                    fw.op("dve", "tensor_scalar", rot[0:64, :],
                          psq[tt][64:128, 0:512], -1.0, None, ALU.mult,
                          writes=(rot,), reads=(psq[tt],))
                    fw.op("dve", "tensor_copy", rot[64:128, :],
                          psq[tt][0:64, 0:512], writes=(rot,), reads=(psq[tt],))
                    fw.op("dve", "tensor_mul", t2[:, :], rot[:, :], sn)
                    fw.op("dve", "tensor_add", qout[:, :], t1[:, :], t2[:, :])
                    fw.dma("sp", p.d[dst][m * 128:(m + 1) * 128,
                                          tt * 512:tt * 512 + 512],
                           qout[:, :], tag=f"st_{dst}")

        if "qk" in parts:
            proj_rope("wq", c.H0, "qsp")
            proj_rope("wk", c.KVH, "ksp")

        psv = p.ps("l0psv", [128, 512], F32)
        vst = p.sb("l0vst", [128, c.NK * 512], BF16)
        ev = p.sb("l0ev", [128, 512], BF16)

        def proj_tm(wname, width, dst, silu):
            for n0 in range(0, width, 512):
                w5 = min(512, width - n0)
                for k in range(c.NK):
                    fw.dma("pool", vst[:, k * 512:k * 512 + w5],
                           p.d[wname][k * 128:(k + 1) * 128, n0:n0 + w5],
                           tag=f"{wname}s{n0}_{k}")
                for tz in range(c.NZ):
                    for k in range(c.NK):
                        fw.op("pe", "matmul", psv[:, 0:w5],
                              xns[:, k * c.T + tz * 128:k * c.T + tz * 128 + 128],
                              vst[:, k * 512:k * 512 + w5],
                              start=(k == 0), stop=(k == c.NK - 1))
                    if silu:
                        fw.op("act", "activation", ev[:, 0:w5], psv[:, 0:w5],
                              AF.Silu, bias=0.0, scale=1.0)
                    else:
                        fw.op("dve", "tensor_copy", ev[:, 0:w5], psv[:, 0:w5])
                    fw.dma("sp", p.d[dst][tz * 128:(tz + 1) * 128, n0:n0 + w5],
                           ev[:, 0:w5], tag=f"st_{dst}")

        if "vg" in parts:
            proj_tm("wv", c.KD, "vsp", False)
            proj_tm("wg", c.QD, "g0sp", True)

        if "dt" not in parts:
            return
        h0 = c.H0
        wdt = p.sb("l0wdt", [128, c.NK * h0], BF16)
        for k in range(c.NK):
            fw.dma("pool", wdt[:, k * h0:(k + 1) * h0],
                   p.d["wwin"][k * 128:(k + 1) * 128, :], tag=f"wwin{k}")
        dtbrow = p.sb("dtbrow", [1, h0], F32)
        fw.dma("sp", dtbrow[0:1, :], p.d["dtb0"])
        dtb_bc = bcast(p, dtbrow, h0, name="dtb_bc")
        alrow = p.sb("alrow", [1, h0], F32)
        fw.dma("sp", alrow[0:1, :], p.d["alog0"])
        fw.op("act", "activation", alrow[0:1, :], alrow[0:1, :], AF.Exp,
              bias=0.0, scale=1.0)
        a_bc = bcast(p, alrow, h0, name="a_bc")
        psd = p.ps("l0psd", [128, h0], F32)
        psg = p.ps("l0psg", [128, h0], F32)
        raw = p.sb("dtraw", [128, h0], F32)
        s1 = p.sb("dts1", [128, h0], F32)
        s2 = p.sb("dts2", [128, h0], F32)
        dts = p.sb("dts", [128, h0], F32)
        lnd = p.sb("dtln", [128, h0], F32)
        gts = p.sb("dtg", [128, h0], F32)
        Gs = p.sb("dtG", [128, h0], F32)
        b0s = p.sb("dtb0s", [128, h0], F32)
        for tz in range(c.NZ):
            for k in range(c.NK):
                fw.op("pe", "matmul", psd[:, 0:h0],
                      xns[:, k * c.T + tz * 128:k * c.T + tz * 128 + 128],
                      wdt[:, k * h0:(k + 1) * h0],
                      start=(k == 0), stop=(k == c.NK - 1))
            fw.op("dve", "tensor_add", raw[:, :], psd[:, 0:h0], dtb_bc[:, :])
            softplus(p, dts[:, :], raw[:, :], s1[:, :], s2[:, :])
            fw.dma("sp", p.d["dt0"][tz * 128:(tz + 1) * 128, :], dts[:, :],
                   tag="st_dt0")
            fw.op("act", "activation", lnd[:, :], dts[:, :], AF.Ln,
                  bias=1e-30, scale=1.0)
            fw.op("dve", "scalar_tensor_tensor", gts[:, :], dts[:, :], -1.0,
                  a_bc[:, :], ALU.mult, ALU.mult)
            fw.op("pe", "matmul", psg[:, 0:h0], p.triu[:, :], gts[:, :],
                  start=True, stop=True)
            fw.op("dve", "tensor_copy", Gs[:, :], psg[:, 0:h0])
            fw.dma("sp", p.d["g0cum"][tz * 128:(tz + 1) * 128, :], Gs[:, :],
                   tag="st_g0cum")
            fw.op("dve", "tensor_sub", b0s[:, :], lnd[:, :], Gs[:, :])
            fw.dma("sp", p.d["b0"][tz * 128:(tz + 1) * 128, :], b0s[:, :],
                   tag="st_b0")



def ph_l0_cs(p):
    c, fw = p.cfg, p.fw
    with p.scope():
        ks = p.sb("ks", [128, c.KVH * c.T], BF16)
        for kc in range(c.KVH):
            fw.dma("sp", ks[:, kc * c.T:(kc + 1) * c.T],
                   p.d["ksp"][kc * 128:(kc + 1) * 128, :], tag="ld_ks")
        S = p.sb("S0", [128, c.H0 * c.HD], F32)
        fw.op("dve", "memset", S[:, :], 0.0)
        prow = p.sb("prow0", [1, c.H0], F32)
        fw.op("dve", "memset", prow[0:1, :], 1.0)
        vch = p.sb("vch0", [128, c.KD], BF16)
        Gt = p.sb("Gt0", [128, c.H0], F32)
        b0c = p.sb("b0c0", [128, c.H0], F32)
        glrow = p.sb("glrow0", [1, c.H0], F32)
        expb = p.sb("expb0", [128, c.H0], F32)
        scl = p.sb("scl0", [128, c.H0], F32)
        decbc = p.sb("decbc0", [128, c.H0], F32)
        pst = p.ps("pst0", [128, 128], BF16)
        ktt = p.sb("ktt0", [128, c.KVH * 128], BF16)
        kst = [p.sb(f"kst0{i}", [128, 128], BF16) for i in range(2)]
        ps_cs = [p.ps(f"pscs0{i}", [128, c.HD], F32) for i in range(2)]
        ssend = p.sb("ssend0", [128, c.H0 * c.HD], F32)
        for z in range(c.NZ):
            fw.dma("sp", vch[:, :], p.d["vsp"][z * 128:(z + 1) * 128, :],
                   tag="ld_vch")
            fw.dma("sp", Gt[:, :], p.d["g0cum"][z * 128:(z + 1) * 128, :])
            fw.dma("sp", b0c[:, :], p.d["b0"][z * 128:(z + 1) * 128, :])
            fw.dma("pool", glrow[0:1, :], Gt[127:128, :], tag="fl_gl")
            glb = bcast(p, glrow, c.H0, out=expb)
            fw.op("dve", "tensor_tensor", scl[:, :], b0c[:, :], glb[:, :],
                  ALU.add)
            fw.op("act", "activation", scl[:, :], scl[:, :], AF.Exp,
                  bias=0.0, scale=1.0)
            fw.op("act", "activation", glrow[0:1, :], glrow[0:1, :], AF.Exp,
                  bias=0.0, scale=1.0)
            fw.dma("sp", p.d["p0"][z:z + 1, :], prow[0:1, :], tag="st_p0")
            fw.op("dve", "tensor_mul", prow[0:1, :], prow[0:1, :], glrow[0:1, :])
            bcast(p, glrow, c.H0, out=decbc)
            fw.dma("pool", p.d["s0loc"][z * 128:(z + 1) * 128, :], S[:, :],
                   tag="st_s0loc", sem="s0loc")
            for kc in range(c.KVH):
                fw.op("pe", "transpose", pst[:, :],
                      ks[:, kc * c.T + z * 128:kc * c.T + z * 128 + 128],
                      p.eyeb[:, :])
                fw.op("dve", "tensor_copy", ktt[:, kc * 128:(kc + 1) * 128],
                      pst[:, :])
            for h in range(c.H0):
                kc = h // (c.H0 // c.KVH)
                kb = kst[h % 2]
                pc = ps_cs[h % 2]
                fw.op("dve", "tensor_scalar", kb[:, :],
                      ktt[:, kc * 128:(kc + 1) * 128], scl[:, h:h + 1], None,
                      ALU.mult)
                fw.op("pe", "matmul", pc[:, 0:c.HD], kb[:, :],
                      vch[:, kc * c.HD:(kc + 1) * c.HD], start=True, stop=True)
                hsl = S[:, h * c.HD:(h + 1) * c.HD]
                fw.op("dve", "tensor_scalar", hsl, hsl, decbc[:, h:h + 1],
                      None, ALU.mult, reads=(S,), writes=(S,))
                fw.op("dve", "tensor_tensor", hsl, hsl, pc[:, 0:c.HD], ALU.add,
                      reads=(S, pc), writes=(S,))
        fw.op("dve", "tensor_scalar", ssend[:, :], S[:, :], p.smcol, None,
              ALU.mult)
        fw.dma("sp", p.d["s0fin"], ssend[:, :], tag="st_s0fin")
        if c.fused:
            fw.cc("AllReduce", ALU.add, p.cfg.groups, p.d["s0fin"], p.d["s0init"])



def ph_l0_att(p):
    c, fw = p.cfg, p.fw
    with p.scope():
        src = "s0init"
        Sini = p.sb("Sini0", [128, c.H0 * c.HD], F32)
        fw.dma("sp", Sini[:, :], p.d[src], tag="ld_s0init")
        fw.op("dve", "tensor_scalar", Sini[:, :], Sini[:, :], p.rmcol, None,
              ALU.mult)
        ks = p.sb("ks_a", [128, c.KVH * c.T], BF16)
        for kc in range(c.KVH):
            fw.dma("sp", ks[:, kc * c.T:(kc + 1) * c.T],
                   p.d["ksp"][kc * 128:(kc + 1) * 128, :], tag="ld_ksa")
        gnwrow = p.sb("gnwrow", [1, c.HD], F32)
        fw.dma("sp", gnwrow[0:1, :], p.d["gnw"])
        gnw_bc = bcast(p, gnwrow, c.HD, name="gnw_bc")
        qsb = p.sb("qsb", [128, c.H0 * 128], BF16)
        vch = p.sb("vch_a", [128, c.KD], BF16)
        gch = p.sb("gch_a", [128, c.QD], BF16)
        Gt = p.sb("Gt_a", [128, c.H0], F32)
        b0c = p.sb("b0c_a", [128, c.H0], F32)
        expG = p.sb("expG_a", [128, c.H0], F32)
        p0row = p.sb("p0row", [1, c.H0], F32)
        pbc = p.sb("pbc0", [128, c.H0], F32)
        slsb = p.sb("slsb0", [128, c.H0 * c.HD], BF16)
        pst32 = p.ps("pst32", [128, 128], F32)
        gfm = p.sb("gfm0", [c.H0, 128], F32)
        gflat = p.sb("gflat0", [1, c.H0 * 128], F32)
        szb = [p.sb(f"szb{i}", [128, c.HD], BF16) for i in range(2)]
        dm = [p.sb(f"dm0{i}", [128, 128], F32) for i in range(2)]
        dec = [p.sb(f"dec0{i}", [128, 128], F32) for i in range(2)]
        mt = [p.sb(f"mt0{i}", [128, 128], BF16) for i in range(2)]
        t1e = [p.sb(f"t1e{i}", [128, c.HD], F32) for i in range(2)]
        bankA = [p.ps(f"l0bankA{i}", [128, 256], F32) for i in range(2)]
        bankB = [p.ps(f"l0bankB{i}", [128, 256], F32) for i in range(2)]
        ps_sc = [bankA[i][:, 0:128] for i in range(2)]
        ps_a1 = [bankA[i][:, 128:256] for i in range(2)]
        ps_yi = [bankB[i][:, 0:c.HD] for i in range(2)]
        ps_ya = [bankB[i][:, 128:128 + c.HD] for i in range(2)]
        kq = [dict(sc=f"k_sc{i}", a1=f"k_a1{i}", yi=f"k_yi{i}", ya=f"k_ya{i}")
              for i in range(2)]
        ych = p.sb("ych", [128, c.QD], F32)
        ss = p.sb("ss0", [128, c.H0], F32)
        sqj = p.sb("sqj0", [128, 128], BF16)
        rstd = p.sb("rstd0", [128, c.H0], F32)
        yg = p.sb("yg0", [128, c.QD], BF16)
        yg2 = p.sb("yg20", [128, c.QD], BF16)
        pstb = p.ps("pstb0", [128, 128], BF16)
        tb = [p.sb(f"tb0{i}", [128, 128], BF16) for i in range(2)]
        for z in range(c.NZ):
            fw.dma("sp", qsb[:, :].rearrange("pp (c t) -> pp c t", t=128),
                   p.d["qsp"].rearrange("(c pp) t -> pp c t", pp=128)[:, :, z * 128:(z + 1) * 128],
                   tag="ld_qsb")
            fw.dma("sp", vch[:, :], p.d["vsp"][z * 128:(z + 1) * 128, :],
                   tag="ld_vcha")
            fw.dma("sp", gch[:, :], p.d["g0sp"][z * 128:(z + 1) * 128, :],
                   tag="ld_gch")
            fw.dma("sp", Gt[:, :], p.d["g0cum"][z * 128:(z + 1) * 128, :])
            fw.dma("sp", b0c[:, :], p.d["b0"][z * 128:(z + 1) * 128, :])
            fw.dma("sp", p0row[0:1, :], p.d["p0"][z:z + 1, :], tag="ld_p0")
            fw.dma("sp", slsb[:, :], p.d["s0loc"][z * 128:(z + 1) * 128, :],
                   tag="ld_s0loc")
            fw.op("act", "activation", expG[:, :], Gt[:, :], AF.Exp,
                  bias=0.0, scale=1.0)
            bcast(p, p0row, c.H0, out=pbc)
            fw.op("pe", "transpose", pst32[0:c.H0, 0:128], Gt[:, :],
                  p.eye32[:, :], writes=(pst32,))
            fw.op("dve", "tensor_copy", gfm[0:c.H0, :], pst32[0:c.H0, 0:128],
                  reads=(pst32,), writes=(gfm,))
            fw.dma("pool", gflat[0:1, :], gfm[0:c.H0, :], tag="fl_gfm")
            for h in range(c.H0):
                kc = h // (c.H0 // c.KVH)
                i2 = h % 2
                hsl = slice(h * c.HD, (h + 1) * c.HD)
                fw.op("dve", "scalar_tensor_tensor", szb[i2][:, :],
                      Sini[:, hsl], pbc[:, h:h + 1], slsb[:, hsl],
                      ALU.mult, ALU.add, reads=(Sini, pbc, slsb))
                fw.op("pe", "matmul", ps_a1[i2], p.ones1[0:1, 0:128],
                      gflat[0:1, h * 128:(h + 1) * 128], start=True, stop=True,
                      reads=(p.ones1, gflat), writes=(kq[i2]["a1"],))
                fw.op("pe", "matmul", ps_sc[i2],
                      ks[:, kc * c.T + z * 128:kc * c.T + z * 128 + 128],
                      qsb[:, h * 128:(h + 1) * 128], start=True, stop=True,
                      reads=(ks, qsb), writes=(kq[i2]["sc"],))
                fw.op("dve", "tensor_tensor", dm[i2][:, :], ps_a1[i2],
                      p.mneg[:, :], ALU.add,
                      reads=(kq[i2]["a1"], kq[i2]["sc"], p.mneg),
                      writes=(dm[i2],))
                fw.op("act", "activation", dec[i2][:, :], dm[i2][:, :], AF.Exp,
                      bias=b0c[:, h:h + 1], scale=1.0)
                fw.op("dve", "tensor_tensor", mt[i2][:, :], ps_sc[i2],
                      dec[i2][:, :], ALU.mult,
                      reads=(kq[i2]["sc"], dec[i2]), writes=(mt[i2],))
                fw.op("pe", "matmul", ps_yi[i2],
                      qsb[:, h * 128:(h + 1) * 128], szb[i2][:, :],
                      start=True, stop=True,
                      reads=(qsb, szb[i2]), writes=(kq[i2]["yi"],))
                fw.op("pe", "matmul", ps_ya[i2], mt[i2][:, :],
                      vch[:, kc * c.HD:(kc + 1) * c.HD], start=True, stop=True,
                      reads=(mt[i2], vch), writes=(kq[i2]["ya"],))
                fw.op("dve", "tensor_scalar", t1e[i2][:, :], ps_yi[i2],
                      expG[:, h:h + 1], None, ALU.mult,
                      reads=(kq[i2]["yi"], kq[i2]["ya"], expG), writes=(t1e[i2],))
                fw.op("dve", "tensor_tensor", ych[:, hsl], t1e[i2][:, :],
                      ps_ya[i2], ALU.add,
                      reads=(t1e[i2], kq[i2]["ya"]), writes=(ych,))
            for h in range(c.H0):
                hsl = slice(h * c.HD, (h + 1) * c.HD)
                fw.op("act", "activation", sqj[:, :], ych[:, hsl], AF.Square,
                      bias=0.0, scale=1.0, accum_out=ss[:, h:h + 1],
                      reads=(ych,), writes=(sqj, ss))
            fw.op("act", "activation", rstd[:, :], ss[:, :], AF.Ln,
                  bias=1e-5, scale=1.0 / c.HD)
            fw.op("act", "activation", rstd[:, :], rstd[:, :], AF.Exp,
                  bias=0.0, scale=-0.5)
            for h in range(c.H0):
                hsl = slice(h * c.HD, (h + 1) * c.HD)
                fw.op("dve", "scalar_tensor_tensor", yg[:, hsl], ych[:, hsl],
                      rstd[:, h:h + 1], gnw_bc[:, :], ALU.mult, ALU.mult,
                      reads=(ych, rstd, gnw_bc), writes=(yg,))
            for j in range(c.QD // 512):
                fw.op("dve", "tensor_tensor", yg2[:, j * 512:(j + 1) * 512],
                      yg[:, j * 512:(j + 1) * 512],
                      gch[:, j * 512:(j + 1) * 512], ALU.mult,
                      reads=(yg, gch), writes=(yg2,))
            for m in range(c.QD // 128):
                fw.op("pe", "transpose", pstb[:, :], yg2[:, m * 128:(m + 1) * 128],
                      p.eyeb[:, :], reads=(yg2, p.eyeb))
                fw.op("dve", "tensor_copy", tb[m % 2][:, :], pstb[:, :])
                fw.dma("sp", p.d["ygfm"][m * 128:(m + 1) * 128,
                                         z * 128:(z + 1) * 128],
                       tb[m % 2][:, :], tag="st_ygfm")



def ln_rows(p, ps_s, ps_q, ks, kq):
    c, fw = p.cfg, p.fw
    o = 0
    mean = p.rows[0:1, (o + 0) * c.T:(o + 1) * c.T]
    var = p.rows[0:1, (o + 1) * c.T:(o + 2) * c.T]
    rstd = p.rows[0:1, (o + 2) * c.T:(o + 3) * c.T]
    mrs = p.rows[0:1, (o + 3) * c.T:(o + 4) * c.T]
    fw.op("act", "activation", mean, ps_s[0:1, :], AF.Copy,
          bias=0.0, scale=1.0 / c.D, reads=(ks,), writes=(mean,))
    fw.op("dve", "tensor_mul", var, mean, mean)
    fw.op("dve", "scalar_tensor_tensor", var, ps_q[0:1, :],
          1.0 / c.D, var, ALU.mult, ALU.subtract,
          reads=(kq, var), writes=(var,))
    fw.op("act", "activation", rstd, var, AF.Ln, bias=1e-5, scale=1.0)
    fw.op("act", "activation", rstd, rstd, AF.Exp, bias=0.0, scale=-0.5)
    fw.op("dve", "tensor_mul", mrs, mean, rstd)
    return rstd, mrs


def ln_apply(p, rstd, mrs, xbf, xn, store=None):
    c, fw = p.cfg, p.fw
    tmp = p.sb(p.u("lap"), [128, 512], F32)
    for tt in range(c.T // 512):
        rbc = bcast(p, rstd[0:1, tt * 512:(tt + 1) * 512], 512, name=p.u("larb"))
        mbc = bcast(p, mrs[0:1, tt * 512:(tt + 1) * 512], 512, name=p.u("lamb"))
        for cc in range(c.NK):
            sl = slice(cc * c.T + tt * 512, cc * c.T + tt * 512 + 512)
            fw.op("dve", "tensor_mul", tmp[:, :], xbf[:, sl], rbc[:, :],
                  reads=(xbf, rbc), writes=(tmp,))
            fw.op("dve", "tensor_tensor", xn[:, sl], tmp[:, :], mbc[:, :],
                  ALU.subtract, reads=(tmp, mbc), writes=(xn,))
            if store:
                fw.dma("sp", p.d[store][cc * 128:(cc + 1) * 128,
                                        tt * 512:tt * 512 + 512],
                       xn[:, sl], tag="st_" + store, sem="st_" + store)


def ln_apply_dram(p, rstd, mrs, xbf_dram, xn, store=None):
    c, fw = p.cfg, p.fw
    tmp = p.sb(p.u("lap"), [128, 512], F32)
    xt = [p.sb(p.u("lax"), [128, 512], BF16) for _ in range(2)]
    for tt in range(c.T // 512):
        rbc = bcast(p, rstd[0:1, tt * 512:(tt + 1) * 512], 512, name=p.u("larb"))
        mbc = bcast(p, mrs[0:1, tt * 512:(tt + 1) * 512], 512, name=p.u("lamb"))
        for cc in range(c.NK):
            i2 = cc % 2
            fw.dma("sp", xt[i2][:, :],
                   p.d[xbf_dram][cc * 128:(cc + 1) * 128, tt * 512:tt * 512 + 512],
                   tag="ld_" + xbf_dram)
            sl = slice(cc * c.T + tt * 512, cc * c.T + tt * 512 + 512)
            fw.op("dve", "tensor_mul", tmp[:, :], xt[i2][:, :], rbc[:, :],
                  reads=(xt[i2], rbc), writes=(tmp,))
            fw.op("dve", "tensor_tensor", xn[:, sl], tmp[:, :], mbc[:, :],
                  ALU.subtract, reads=(tmp, mbc), writes=(xn,))
            if store:
                fw.dma("sp", p.d[store][cc * 128:(cc + 1) * 128,
                                        tt * 512:tt * 512 + 512],
                       xn[:, sl], tag="st_" + store, sem="st_" + store)


def topk_gate(p, sc, scw, thr, mk, kiter):
    fw = p.fw
    fw.op("dve", "tensor_copy", scw[:, :], sc[:, :])
    for it in range(kiter):
        fw.op("dve", "tensor_reduce", thr[:, :], scw[:, :], AX.X, ALU.max)
        if it < kiter - 1:
            fw.op("dve", "tensor_scalar", mk[:, :], scw[:, :], thr[:, 0:1],
                  None, ALU.is_ge)
            fw.op("dve", "scalar_tensor_tensor", scw[:, :], mk[:, :], -1e30,
                  scw[:, :], ALU.mult, ALU.add)
    fw.op("dve", "tensor_scalar", mk[:, :], sc[:, :], thr[:, 0:1], None,
          ALU.is_ge)



def ph_l0_post(p):
    c, fw = p.cfg, p.fw
    with p.scope():
        rows1 = []
        with p.scope():
            ygr = p.sb("ygr", [128, (c.QD // 128) * c.T], BF16)
            load_fm(p, ygr, "ygfm", c.QD, c.T)
            ps_s = p.ps("ps_s1", [1, c.T], F32)
            ps_q = p.ps("ps_q1", [1, c.T], F32)
            ps_o = [p.ps(f"ps_o{i}", [128, 512], F32) for i in range(2)]
            wt = [p.sb(f"wo_t{i}", [128, 128], BF16) for i in range(2)]
            xold = [p.sb(f"xo{i}", [128, 512], F32) for i in range(2)]
            x1s = [p.sb(f"x1s{i}", [128, 512], F32) for i in range(2)]
            sqt = [p.sb(f"sq1{i}", [128, 512], BF16) for i in range(2)]
            xb1 = [p.sb(f"xb1{i}", [128, 512], BF16) for i in range(2)]
            for m in range(c.NK):
                for k in range(c.QD // 128):
                    w = wt[k % 2]
                    fw.dma("pool", w[:, :],
                           p.d["wo"][k * 128:(k + 1) * 128, m * 128:(m + 1) * 128],
                           tag=f"wo{m}_{k}")
                    for tt in range(c.T // 512):
                        fw.op("pe", "matmul", ps_o[tt][:, 0:512], w[:, :],
                              ygr[:, k * c.T + tt * 512:k * c.T + tt * 512 + 512],
                              start=(k == 0), stop=(k == c.QD // 128 - 1))
                for tt in range(c.T // 512):
                    i2 = tt % 2
                    fw.dma("sp", xold[i2][:, :],
                           p.d["x"][m * 128:(m + 1) * 128, tt * 512:tt * 512 + 512],
                           tag="ld_x0")
                    fw.op("dve", "tensor_tensor", x1s[i2][:, :], ps_o[tt][:, 0:512],
                          xold[i2][:, :], ALU.add)
                    fw.dma("sp", p.d["x1"][m * 128:(m + 1) * 128,
                                           tt * 512:tt * 512 + 512],
                           x1s[i2][:, :], tag="st_x1", sem="st_x1")
                    bsl = xb1[i2][:, :]
                    fw.op("act", "activation", bsl, x1s[i2][:, :], AF.Copy,
                          bias=0.0, scale=1.0, reads=(x1s[i2],), writes=(xb1[i2],))
                    fw.dma("sp", p.d["x1bf"][m * 128:(m + 1) * 128,
                                             tt * 512:tt * 512 + 512],
                           bsl, tag="st_x1bf")
                    fw.op("act", "activation", sqt[i2][:, :], bsl, AF.Square,
                          bias=0.0, scale=1.0, reads=(xb1[i2],), writes=(sqt[i2],))
                    fw.op("pe", "matmul", ps_s[0:1, tt * 512:(tt + 1) * 512],
                          p.onescol[:, 0:1], bsl, start=(m == 0),
                          stop=(m == c.NK - 1), reads=(p.onescol, xb1[i2]),
                          writes=("k_lns1",))
                    fw.op("pe", "matmul", ps_q[0:1, tt * 512:(tt + 1) * 512],
                          p.onescol[:, 0:1], sqt[i2][:, :], start=(m == 0),
                          stop=(m == c.NK - 1), reads=(p.onescol, sqt[i2]),
                          writes=("k_lnq1",))
            rows1 = ln_rows(p, ps_s, ps_q, "k_lns1", "k_lnq1")
        xn1 = p.sb("xn1_r", [128, c.NK * c.T], BF16)
        ln_apply_dram(p, rows1[0], rows1[1], "x1bf", xn1)
        rows2 = None
        with p.scope():
            rws = p.sb("rws", [128, c.NK * c.E], BF16)
            for k in range(c.NK):
                fw.dma("pool", rws[:, k * c.E:(k + 1) * c.E],
                       p.d["rw"][k * 128:(k + 1) * 128, :], tag=f"rw{k}")
            rbrow = p.sb("rbrow", [1, c.E], F32)
            fw.dma("sp", rbrow[0:1, :], p.d["rb"])
            rtb_bc = bcast(p, rbrow, c.E, name="rtb_bc")
            ebs = p.sb("ebs", [c.E, c.D], BF16)
            fw.dma("pool", ebs[:, :], p.d["eb"], tag="ld_eb")
            wdall = p.sb("wdall", [c.E, c.T], BF16)
            wsall = p.sb("wsall", [1, c.T], F32)
            ps_l = p.ps("ps_l", [128, 512], F32)
            pst32r = p.ps("pst32r", [128, 128], F32)
            lg = p.sb("lg", [128, c.E], F32)
            nmx = p.sb("nmx", [128, 1], F32)
            ex = p.sb("ex", [128, c.E], F32)
            ssum = p.sb("ssum", [128, 1], F32)
            sc = p.sb("sc_r", [128, c.E], F32)
            scw = p.sb("scw_r", [128, c.E], F32)
            thr = p.sb("thr_r", [128, 1], F32)
            mk = p.sb("mk_r", [128, c.E], F32)
            wd = p.sb("wd_r", [128, c.E], F32)
            ws = p.sb("ws_r", [128, 1], F32)
            for tz in range(c.NZ):
                for k in range(c.NK):
                    fw.op("pe", "matmul", ps_l[:, 0:c.E],
                          xn1[:, k * c.T + tz * 128:k * c.T + tz * 128 + 128],
                          rws[:, k * c.E:(k + 1) * c.E],
                          start=(k == 0), stop=(k == c.NK - 1))
                fw.op("dve", "tensor_tensor", lg[:, :], ps_l[:, 0:c.E],
                      rtb_bc[:, :], ALU.add)
                fw.op("dve", "tensor_reduce", nmx[:, :], lg[:, :], AX.X,
                      ALU.max, negate=True)
                fw.op("act", "activation", ex[:, :], lg[:, :], AF.Exp,
                      bias=nmx[:, 0:1], scale=1.0)
                fw.op("dve", "tensor_reduce", ssum[:, :], ex[:, :], AX.X, ALU.add)
                fw.op("dve", "reciprocal", ssum[:, :], ssum[:, :])
                fw.op("dve", "tensor_scalar", sc[:, :], ex[:, :], ssum[:, 0:1],
                      None, ALU.mult)
                topk_gate(p, sc, scw, thr, mk, c.KTOP)
                fw.op("dve", "tensor_mul", wd[:, :], mk[:, :], sc[:, :])
                fw.op("dve", "tensor_reduce", ws[:, :], wd[:, :], AX.X, ALU.add)
                fw.dma("pool", wsall[0:1, tz * 128:(tz + 1) * 128], ws[:, 0:1],
                       tag="fl_ws")
                fw.op("pe", "transpose", pst32r[0:c.E, 0:128], wd[:, :],
                      p.eye32[:, :], reads=(wd, p.eye32), writes=("k_wdt",))
                fw.op("dve", "tensor_copy", wdall[0:c.E, tz * 128:(tz + 1) * 128],
                      pst32r[0:c.E, 0:128], reads=("k_wdt",), writes=(wdall,))
            wsb = p.sb("wsb", [128, c.T], F32)
            for tt in range(c.T // 512):
                bcast(p, wsall[0:1, tt * 512:(tt + 1) * 512], 512,
                      out=wsb[:, tt * 512:(tt + 1) * 512])
            ps_b = p.ps("ps_b", [128, 512], F32)
            ps_s2 = p.ps("ps_s2", [1, c.T], F32)
            ps_q2 = p.ps("ps_q2", [1, c.T], F32)
            tsb = [p.sb(f"tsb{i}", [128, 512], F32) for i in range(2)]
            xo1 = [p.sb(f"xo1{i}", [128, 512], F32) for i in range(2)]
            x2s = [p.sb(f"x2s{i}", [128, 512], F32) for i in range(2)]
            sq2 = [p.sb(f"sq2{i}", [128, 512], BF16) for i in range(2)]
            xb2 = [p.sb(f"xb2{i}", [128, 512], BF16) for i in range(2)]
            for m in range(c.NK):
                for tt in range(c.T // 512):
                    i2 = tt % 2
                    fw.op("pe", "matmul", ps_b[:, 0:512],
                          ebs[:, m * 128:(m + 1) * 128],
                          wdall[:, tt * 512:(tt + 1) * 512],
                          start=True, stop=True)
                    fw.op("dve", "tensor_mul", tsb[i2][:, :],
                          xn1[:, m * c.T + tt * 512:m * c.T + tt * 512 + 512],
                          wsb[:, tt * 512:(tt + 1) * 512],
                          reads=(xn1, wsb), writes=(tsb[i2],))
                    fw.op("dve", "tensor_tensor", tsb[i2][:, :], tsb[i2][:, :],
                          ps_b[:, 0:512], ALU.add)
                    fw.dma("sp", xo1[i2][:, :],
                           p.d["x1"][m * 128:(m + 1) * 128, tt * 512:tt * 512 + 512],
                           tag="ld_x1")
                    fw.op("dve", "tensor_tensor", x2s[i2][:, :], tsb[i2][:, :],
                          xo1[i2][:, :], ALU.add)
                    fw.dma("sp", p.d["x2"][m * 128:(m + 1) * 128,
                                           tt * 512:tt * 512 + 512],
                           x2s[i2][:, :], tag="st_x2", sem="st_x2")
                    bsl = xb2[i2][:, :]
                    fw.op("act", "activation", bsl, x2s[i2][:, :], AF.Copy,
                          bias=0.0, scale=1.0, reads=(x2s[i2],), writes=(xb2[i2],))
                    fw.dma("sp", p.d["x2bf"][m * 128:(m + 1) * 128,
                                             tt * 512:tt * 512 + 512],
                           bsl, tag="st_x2bf")
                    fw.op("act", "activation", sq2[i2][:, :], bsl, AF.Square,
                          bias=0.0, scale=1.0, reads=(xb2[i2],), writes=(sq2[i2],))
                    fw.op("pe", "matmul", ps_s2[0:1, tt * 512:(tt + 1) * 512],
                          p.onescol[:, 0:1], bsl, start=(m == 0),
                          stop=(m == c.NK - 1), reads=(p.onescol, xb2[i2]),
                          writes=("k_lns2",))
                    fw.op("pe", "matmul", ps_q2[0:1, tt * 512:(tt + 1) * 512],
                          p.onescol[:, 0:1], sq2[i2][:, :], start=(m == 0),
                          stop=(m == c.NK - 1), reads=(p.onescol, sq2[i2]),
                          writes=("k_lnq2",))
            rows2 = ln_rows(p, ps_s2, ps_q2, "k_lns2", "k_lnq2")
        xn2 = p.sb("xn2_r", [128, c.NK * c.T], BF16)
        ln_apply_dram(p, rows2[0], rows2[1], "x2bf", xn2, store="xn2")



def ph_l1_inproj(p):
    c, fw = p.cfg, p.fw
    with p.scope():
        xns = p.sb("xns1", [128, c.NK * c.T], BF16)
        load_fm(p, xns, "xn2", c.D, c.T)
        gws = p.sb("gws", [128, c.NK * 512], BF16)
        psv = p.ps("l1psv", [128, 512], F32)
        ev = [p.sb(f"l1ev{i}", [128, 512], BF16) for i in range(2)]
        for n in range(c.INTER // 512):
            for k in range(c.NK):
                fw.dma("pool", gws[:, k * 512:(k + 1) * 512],
                       p.d["ipj"][k * 128:(k + 1) * 128, n * 512:(n + 1) * 512],
                       tag=f"ipg{n}_{k}")
            for tz in range(c.NZ):
                i2 = tz % 2
                for k in range(c.NK):
                    fw.op("pe", "matmul", psv[:, 0:512],
                          xns[:, k * c.T + tz * 128:k * c.T + tz * 128 + 128],
                          gws[:, k * 512:(k + 1) * 512],
                          start=(k == 0), stop=(k == c.NK - 1))
                fw.op("act", "activation", ev[i2][:, :], psv[:, 0:512],
                      AF.Silu, bias=0.0, scale=1.0)
                fw.dma("sp", p.d["ipg"][tz * 128:(tz + 1) * 128,
                                        n * 512:n * 512 + 512],
                       ev[i2][:, :], tag="st_ipg")
        wt = [p.sb(f"l1w{i}", [128, 128], BF16) for i in range(2)]
        ps2 = [p.ps(f"l1ps2{i}", [128, 512], F32) for i in range(2)]
        ev2 = [p.sb(f"l1eh{i}", [128, 512], BF16) for i in range(2)]
        nc3 = c.CONV // 128
        hal = p.sb("halsb", [128, nc3 * 3], F32)
        for m in range(c.CONV // 128):
            for k in range(c.NK):
                w = wt[k % 2]
                fw.dma("pool", w[:, :],
                       p.d["ipj"][k * 128:(k + 1) * 128,
                                  c.INTER + m * 128:c.INTER + (m + 1) * 128],
                       tag=f"iph{m}_{k}")
                for tt in range(c.T // 512):
                    fw.op("pe", "matmul", ps2[tt][:, 0:512], w[:, :],
                          xns[:, k * c.T + tt * 512:k * c.T + tt * 512 + 512],
                          start=(k == 0), stop=(k == c.NK - 1))
            for tt in range(c.T // 512):
                fw.op("dve", "tensor_copy", ev2[tt % 2][:, :], ps2[tt][:, 0:512])
                fw.dma("sp", p.d["iph"][m * 128:(m + 1) * 128,
                                        tt * 512:tt * 512 + 512],
                       ev2[tt % 2][:, :], tag="st_iph")
                if tt == c.T // 512 - 1:
                    fw.op("dve", "tensor_copy", hal[:, m * 3:(m + 1) * 3],
                          ev2[tt % 2][:, 509:512], reads=(ev2[tt % 2],),
                          writes=(hal,))
        h1 = c.H1
        wdt = [p.sb(f"l1wd{i}", [128, h1], BF16) for i in range(2)]
        psd = p.ps("l1psd", [128, 512], F32)
        dtb1c = p.sb("dtb1c", [h1, 1], F32)
        fw.dma("sp", dtb1c[:, :], p.d["dtb1"])
        acol = p.sb("acol1", [h1, 1], F32)
        fw.dma("sp", acol[:, :], p.d["alog1"])
        fw.op("act", "activation", acol[:, :], acol[:, :], AF.Exp,
              bias=0.0, scale=1.0)
        raw = p.sb("l1raw", [h1, 512], F32)
        s1 = p.sb("l1s1", [h1, 512], F32)
        s2 = p.sb("l1s2", [h1, 512], F32)
        dts = p.sb("l1dt", [h1, 512], F32)
        lnd = p.sb("l1ln", [h1, 512], F32)
        adt = p.sb("l1ad", [h1, 512], F32)
        for tt in range(c.T // 512):
            for k in range(c.NK):
                w = wdt[k % 2]
                fw.dma("pool", w[:, 0:h1],
                       p.d["ipj"][k * 128:(k + 1) * 128,
                                  c.INTER + c.CONV:c.INTER + c.CONV + h1],
                       tag=f"ipd{tt}_{k}")
                fw.op("pe", "matmul", psd[0:h1, 0:512], w[:, 0:h1],
                      xns[:, k * c.T + tt * 512:k * c.T + tt * 512 + 512],
                      start=(k == 0), stop=(k == c.NK - 1))
            fw.op("dve", "tensor_scalar", raw[:, :], psd[0:h1, 0:512],
                  dtb1c[:, 0:1], None, ALU.add, reads=(psd, dtb1c), writes=(raw,))
            softplus(p, dts[:, :], raw[:, :], s1[:, :], s2[:, :])
            fw.op("dve", "tensor_scalar_min", dts[:, :], dts[:, :], 100.0)
            fw.dma("sp", p.d["dt1"][:, tt * 512:(tt + 1) * 512], dts[:, :],
                   tag="st_dt1", sem="st_dt1")
            fw.op("act", "activation", lnd[:, :], dts[:, :], AF.Ln,
                  bias=1e-30, scale=1.0)
            fw.dma("sp", p.d["lndt1"][:, tt * 512:(tt + 1) * 512], lnd[:, :],
                   tag="st_lndt1", sem="st_lndt1")
            fw.op("dve", "tensor_scalar", adt[:, :], dts[:, :], acol[:, 0:1],
                  -1.0, ALU.mult, ALU.mult, reads=(dts, acol), writes=(adt,))
            fw.dma("sp", p.d["adt1"][:, tt * 512:(tt + 1) * 512], adt[:, :],
                   tag="st_adt1", sem="st_adt1")
        fw.op("dve", "tensor_scalar", hal[:, :], hal[:, :], p.smcol, None,
              ALU.mult)
        fw.dma("sp", p.d["halo"], hal[:, :], tag="st_halo")
        if c.fused:
            fw.cc("AllReduce", ALU.add, c.groups, p.d["halo"], p.d["haloin"])



def ph_l1_convcs(p):
    c, fw = p.cfg, p.fw
    h1 = c.H1
    nc3 = c.CONV // 128
    nhs = c.INTER // 128
    with p.scope():
        dtr = p.sb("dt1r", [h1, c.T], F32)
        fw.dma("sp", dtr[:, :], p.d["dt1"])
        lndr = p.sb("lnd1r", [h1, c.T], F32)
        fw.dma("sp", lndr[:, :], p.d["lndt1"])
        adtr = p.sb("adt1r", [h1, c.T], F32)
        fw.dma("sp", adtr[:, :], p.d["adt1"])
        cwsb = p.sb("cwsb", [128, nc3 * 4], F32)
        fw.dma("sp", cwsb[:, :], p.d["cw"])
        cbsb = p.sb("cbsb", [128, nc3], F32)
        fw.dma("sp", cbsb[:, :], p.d["cb"])
        halb = p.sb("halb", [128, nc3 * 3], BF16)
        hali = p.sb("hali", [128, nc3 * 3], F32)
        fw.dma("sp", hali[:, :], p.d["haloin"], tag="ld_haloin")
        fw.op("dve", "tensor_scalar", halb[:, :], hali[:, :], p.rmcol, None,
              ALU.mult)
        S = p.sb("S1", [128, h1 * c.P1], F32)
        fw.op("dve", "memset", S[:, :], 0.0)
        prow = p.sb("prow1", [1, h1], F32)
        fw.op("dve", "memset", prow[0:1, :], 1.0)
        pst32 = p.ps("l1pst32", [128, 128], F32)
        ps_ac = p.ps("l1psac", [128, 128], F32)
        pstb = p.ps("l1pstb", [128, 128], BF16)
        adtT = p.sb("adtT", [128, h1], F32)
        lndT = p.sb("lndT", [128, h1], F32)
        acum = p.sb("acum1", [128, h1], F32)
        b1s = p.sb("b1s", [128, h1], F32)
        expb1 = p.sb("expb1", [128, h1], F32)
        acfm = p.sb("acfm", [h1, 128], F32)
        acflr = p.sb("acflr", [1, h1 * 128], F32)
        decrow = p.sb("decrow1", [1, h1], F32)
        decbc = p.sb("decbc1", [128, h1], F32)
        scl = p.sb("scl1", [128, h1], F32)
        hb = [p.sb(f"hb{i}", [128, 131], BF16) for i in range(2)]
        acc = [p.sb(f"cacc{i}", [128, 128], F32) for i in range(2)]
        hstm = p.sb("hstm_sb", [128, c.INTER], BF16)
        bcz = p.sb("bcz", [128, (nc3 - nhs) * 128], BF16)
        bt = p.sb("bt1", [128, c.G1 * 128], BF16)
        bsc = [p.sb(f"bsc{i}", [128, 128], BF16) for i in range(2)]
        ps_cs = [p.ps(f"l1pscs{i}", [128, c.P1], F32) for i in range(2)]
        ssend = p.sb("ssend1", [128, h1 * c.P1], F32)
        for z in range(c.NZ):
            fw.op("pe", "transpose", pst32[:, 0:h1],
                  adtr[0:h1, z * 128:(z + 1) * 128], p.eye32[0:h1, 0:h1],
                  reads=(adtr, p.eye32), writes=(pst32,))
            fw.op("dve", "tensor_copy", adtT[:, :], pst32[:, 0:h1],
                  reads=(pst32,), writes=(adtT,))
            fw.op("pe", "matmul", ps_ac[:, 0:h1], p.triu[:, :], adtT[:, :],
                  start=True, stop=True)
            fw.op("dve", "tensor_copy", acum[:, :], ps_ac[:, 0:h1])
            fw.dma("sp", p.d["acumt"][z * 128:(z + 1) * 128, :], acum[:, :],
                   tag="st_acumt", sem="st_acumt")
            fw.op("pe", "transpose", pst32[:, 0:h1],
                  lndr[0:h1, z * 128:(z + 1) * 128], p.eye32[0:h1, 0:h1],
                  reads=(lndr, p.eye32), writes=(pst32,))
            fw.op("dve", "tensor_copy", lndT[:, :], pst32[:, 0:h1],
                  reads=(pst32,), writes=(lndT,))
            fw.op("dve", "tensor_sub", b1s[:, :], lndT[:, :], acum[:, :])
            fw.dma("sp", p.d["b1sp"][z * 128:(z + 1) * 128, :], b1s[:, :],
                   tag="st_b1sp", sem="st_b1sp")

            fw.op("pe", "transpose", pst32[0:h1, 0:128], acum[:, 0:h1],
                  p.eye32[:, :], reads=(acum, p.eye32), writes=(pst32,))
            fw.op("dve", "tensor_copy", acfm[0:h1, :], pst32[0:h1, :],
                  reads=(pst32,), writes=(acfm,))
            fw.dma("pool", acflr[0:1, :], acfm[0:h1, :], tag="fl_acfm")
            fw.dma("sp", p.d["acfl"][z:z + 1, :], acflr[0:1, :], tag="st_acfl",
                   sem="st_acfl")
            fw.dma("pool", decrow[0:1, :], acum[127:128, 0:h1], tag="fl_dec")
            glb1 = bcast(p, decrow, h1, out=expb1)
            fw.op("dve", "tensor_tensor", scl[:, :], b1s[:, :], glb1[:, :],
                  ALU.add)
            fw.op("act", "activation", scl[:, :], scl[:, :], AF.Exp,
                  bias=0.0, scale=1.0)
            fw.op("act", "activation", decrow[0:1, :], decrow[0:1, :], AF.Exp,
                  bias=0.0, scale=1.0)
            fw.dma("sp", p.d["p1"][z:z + 1, :], prow[0:1, :], tag="st_p1",
                   sem="st_p1")
            fw.op("dve", "tensor_mul", prow[0:1, :], prow[0:1, :], decrow[0:1, :])
            bcast(p, decrow, h1, out=decbc)
            for pc in range(nc3):
                i2 = pc % 2
                if z == 0:
                    fw.op("dve", "tensor_copy", hb[i2][:, 0:3],
                          halb[:, pc * 3:(pc + 1) * 3], reads=(halb,),
                          writes=(hb[i2],))
                    fw.dma("sp", hb[i2][:, 3:131],
                           p.d["iph"][pc * 128:(pc + 1) * 128, 0:128],
                           tag="ld_hb")
                else:
                    fw.dma("sp", hb[i2][:, 0:131],
                           p.d["iph"][pc * 128:(pc + 1) * 128,
                                      z * 128 - 3:z * 128 + 128],
                           tag="ld_hb")
                fw.op("dve", "tensor_scalar", acc[i2][:, :], hb[i2][:, 0:128],
                      cwsb[:, pc * 4:pc * 4 + 1], None, ALU.mult,
                      reads=(hb[i2], cwsb), writes=(acc[i2],))
                fw.op("dve", "scalar_tensor_tensor", acc[i2][:, :],
                      hb[i2][:, 1:129], cwsb[:, pc * 4 + 1:pc * 4 + 2],
                      acc[i2][:, :], ALU.mult, ALU.add,
                      reads=(hb[i2], cwsb, acc[i2]), writes=(acc[i2],))
                fw.op("dve", "scalar_tensor_tensor", acc[i2][:, :],
                      hb[i2][:, 2:130], cwsb[:, pc * 4 + 2:pc * 4 + 3],
                      acc[i2][:, :], ALU.mult, ALU.add,
                      reads=(hb[i2], cwsb, acc[i2]), writes=(acc[i2],))
                fw.op("dve", "scalar_tensor_tensor", acc[i2][:, :],
                      hb[i2][:, 3:131], cwsb[:, pc * 4 + 3:pc * 4 + 4],
                      acc[i2][:, :], ALU.mult, ALU.add,
                      reads=(hb[i2], cwsb, acc[i2]), writes=(acc[i2],))
                if pc < nhs:
                    fw.op("act", "activation", hb[i2][:, 0:128], acc[i2][:, :],
                          AF.Silu, bias=cbsb[:, pc:pc + 1], scale=1.0,
                          reads=(acc[i2], cbsb), writes=(hb[i2],))
                    fw.op("pe", "transpose", pstb[:, :], hb[i2][:, 0:128],
                          p.eyeb[:, :], reads=(hb[i2], p.eyeb), writes=(pstb,))
                    fw.op("dve", "tensor_copy", hstm[:, pc * 128:(pc + 1) * 128],
                          pstb[:, :], reads=(pstb,), writes=(hstm,))
                else:
                    fw.op("act", "activation",
                          bcz[:, (pc - nhs) * 128:(pc - nhs + 1) * 128],
                          acc[i2][:, :], AF.Silu, bias=cbsb[:, pc:pc + 1],
                          scale=1.0, reads=(acc[i2], cbsb), writes=(bcz,))
            fw.dma("sp", p.d["hstm"][z * 128:(z + 1) * 128, :], hstm[:, :],
                   tag="st_hstm", sem="st_hstm")
            for bcid in range(nc3 - nhs):
                fw.dma("sp", p.d["bcfm"][bcid * 128:(bcid + 1) * 128,
                                         z * 128:(z + 1) * 128],
                       bcz[:, bcid * 128:(bcid + 1) * 128], tag="st_bcfm",
                       sem="st_bcfm")
            for g in range(c.G1):
                fw.op("pe", "transpose", pstb[:, :], bcz[:, g * 128:(g + 1) * 128],
                      p.eyeb[:, :], reads=(bcz, p.eyeb), writes=(pstb,))
                fw.op("dve", "tensor_copy", bt[:, g * 128:(g + 1) * 128],
                      pstb[:, :], reads=(pstb,), writes=(bt,))
            fw.dma("pool", p.d["s1loc"][z * c.N1:(z + 1) * c.N1, :], S[:, :],
                   tag="st_s1loc", sem="s1loc")
            for h in range(h1):
                g = h // (h1 // c.G1)
                i2 = h % 2
                fw.op("dve", "tensor_scalar", bsc[i2][:, :],
                      bt[:, g * 128:(g + 1) * 128], scl[:, h:h + 1], None,
                      ALU.mult, reads=(bt, scl), writes=(bsc[i2],))
                fw.op("pe", "matmul", ps_cs[i2][:, 0:c.P1], bsc[i2][:, :],
                      hstm[:, h * c.P1:(h + 1) * c.P1], start=True, stop=True,
                      reads=(bsc[i2], hstm), writes=(f"k_cs{i2}",))
                hsl = S[:, h * c.P1:(h + 1) * c.P1]
                fw.op("dve", "tensor_scalar", hsl, hsl, decbc[:, h:h + 1],
                      None, ALU.mult, reads=(S, decbc), writes=(S,))
                fw.op("dve", "tensor_tensor", hsl, hsl, ps_cs[i2][:, 0:c.P1],
                      ALU.add, reads=(S, f"k_cs{i2}"), writes=(S,))
        fw.op("dve", "tensor_scalar", ssend[:, :], S[:, :], p.smcol, None,
              ALU.mult)
        fw.dma("sp", p.d["s1fin"], ssend[:, :], tag="st_s1fin")
        if c.fused:
            fw.cc("AllReduce", ALU.add, c.groups, p.d["s1fin"], p.d["s1init"])



def ph_l1_ssd(p):
    c, fw = p.cfg, p.fw
    h1 = c.H1
    nhs = c.INTER // 128
    nbc = 2 * c.G1 * c.N1 // 128
    with p.scope():
        Sini = p.sb("Sini1", [128, h1 * c.P1], BF16)
        sini32 = p.sb("sini32", [128, 512], F32)
        for o in range(0, h1 * c.P1, 512):
            fw.dma("sp", sini32[:, :], p.d["s1init"][:, o:o + 512],
                   tag="ld_s1init")
            fw.op("dve", "tensor_scalar", Sini[:, o:o + 512], sini32[:, :],
                  p.rmcol, None, ALU.mult, reads=(sini32, p.smrm),
                  writes=(Sini,))
        nwrow = p.sb("nwrow", [1, c.INTER], F32)
        fw.dma("sp", nwrow[0:1, :], p.d["nw"])
        nw_bc = p.sb("nw_bc", [128, c.INTER], BF16)
        for o in range(0, c.INTER, 512):
            fw.op("pe", "matmul", p.ps_bc[:, 0:512], p.ones1[0:1, 0:128],
                  nwrow[0:1, o:o + 512], start=True, stop=True,
                  reads=(p.ones1, nwrow), writes=(p.ps_bc,))
            fw.op("dve", "tensor_copy", nw_bc[:, o:o + 512], p.ps_bc[:, 0:512],
                  reads=(p.ps_bc,), writes=(nw_bc,))
        dvrow = p.sb("dvrow", [1, h1], F32)
        fw.dma("sp", dvrow[0:1, :], p.d["dvec"])
        dcols = bcast(p, dvrow, h1, name="dcols")
        slsb = p.sb("slsb1", [128, h1 * c.P1], BF16)
        hsz = p.sb("hsz", [128, c.INTER], BF16)
        bcch = p.sb("bcch", [128, nbc * 128], BF16)
        acumz = p.sb("acumz", [128, h1], F32)
        b1z = p.sb("b1z", [128, h1], F32)
        expA = p.sb("expA1", [128, h1], F32)
        p1row = p.sb("p1row", [1, h1], F32)
        pbc1 = p.sb("pbc1", [128, h1], F32)
        segw = min(4096, h1 * 128)
        acflr = p.sb("acflr_s", [1, segw], F32)
        szb = p.sb("szb1x", [128, h1 * c.P1], BF16)
        cbg = [p.sb(f"cbg{i}", [128, 128], F32) for i in range(2)]
        ps_cb = [p.ps(f"pscb{i}", [128, 128], F32) for i in range(2)]
        bankA1 = [p.ps(f"l1bankA{i}", [128, 128], F32) for i in range(2)]
        bankY = [p.ps(f"l1bankY{i}", [128, 128], F32) for i in range(2)]
        ps_a1 = [bankA1[i][:, 0:128] for i in range(2)]
        ps_yd = [bankY[i][:, 0:c.P1] for i in range(2)]
        ps_yo = [bankY[i][:, c.P1:2 * c.P1] for i in range(2)]
        dm = [p.sb(f"dm1{i}", [128, 128], F32) for i in range(2)]
        dec = dm
        mt = [p.sb(f"mt1{i}", [128, 128], BF16) for i in range(2)]
        u1 = [p.sb(f"u1_{i}", [128, c.P1], F32) for i in range(2)]
        u2 = u1
        ych = p.sb("ych1", [128, c.INTER], BF16)
        gch = p.sb("gch1", [128, c.INTER], BF16)
        gg = slsb
        ss = p.sb("ss1", [128, c.G1], F32)
        sqj = p.sb("sqj1", [128, c.INTER // c.G1], BF16)
        rstd = p.sb("rstd1", [128, c.G1], F32)
        ggn = ych
        pstb = p.ps("pstb1", [128, 128], BF16)
        tb = [p.sb(f"tb1{i}", [128, 128], BF16) for i in range(2)]
        for z in range(c.NZ):
            fw.dma("sp", slsb[:, :], p.d["s1loc"][z * c.N1:(z + 1) * c.N1, :],
                   tag="ld_s1loc")
            fw.dma("sp", hsz[:, :], p.d["hstm"][z * 128:(z + 1) * 128, :],
                   tag="ld_hstm")
            fw.dma("sp", bcch[:, :].rearrange("pp (cc t) -> pp cc t", t=128),
                   p.d["bcfm"].rearrange("(cc pp) t -> pp cc t", pp=128)[:, :, z * 128:(z + 1) * 128],
                   tag="ld_bcfm")
            fw.dma("sp", gch[:, :], p.d["ipg"][z * 128:(z + 1) * 128, :],
                   tag="ld_ipg")
            fw.dma("sp", acumz[:, :], p.d["acumt"][z * 128:(z + 1) * 128, :])
            fw.dma("sp", b1z[:, :], p.d["b1sp"][z * 128:(z + 1) * 128, :])
            fw.dma("sp", p1row[0:1, :], p.d["p1"][z:z + 1, :], tag="ld_p1")
            acfl_seg = [-1]
            fw.op("act", "activation", expA[:, :], acumz[:, :], AF.Exp,
                  bias=0.0, scale=1.0)
            bcast(p, p1row, h1, out=pbc1)
            for h in range(h1):
                hsl = slice(h * c.P1, (h + 1) * c.P1)
                fw.op("dve", "scalar_tensor_tensor", szb[:, hsl],
                      Sini[:, hsl], pbc1[:, h:h + 1], slsb[:, hsl],
                      ALU.mult, ALU.add, reads=(Sini, pbc1, slsb), writes=(szb,))
            for g in range(c.G1):
                ig = g % 2
                fw.op("pe", "matmul", ps_cb[ig][:, 0:128],
                      bcch[:, g * 128:(g + 1) * 128],
                      bcch[:, (c.G1 + g) * 128:(c.G1 + g + 1) * 128],
                      start=True, stop=True, reads=(bcch,),
                      writes=(f"k_cb{ig}",))
                fw.op("dve", "tensor_copy", cbg[ig][:, :], ps_cb[ig][:, 0:128],
                      reads=(f"k_cb{ig}",), writes=(cbg[ig],))
                for hh in range(h1 // c.G1):
                    h = g * (h1 // c.G1) + hh
                    i2 = h % 2
                    hsl = slice(h * c.P1, (h + 1) * c.P1)
                    seg = (h * 128) // segw
                    if seg != acfl_seg[0]:
                        fw.dma("sp", acflr[0:1, :],
                               p.d["acfl"][z:z + 1, seg * segw:(seg + 1) * segw],
                               tag="ld_acfl")
                        acfl_seg[0] = seg
                    ho = (h * 128) % segw
                    fw.op("pe", "matmul", ps_a1[i2], p.ones1[0:1, 0:128],
                          acflr[0:1, ho:ho + 128], start=True,
                          stop=True, reads=(p.ones1, acflr),
                          writes=(f"k1_a1{i2}",))
                    fw.op("dve", "tensor_tensor", dm[i2][:, :], ps_a1[i2],
                          p.mneg[:, :], ALU.add, reads=(f"k1_a1{i2}", p.mneg),
                          writes=(dm[i2],))
                    fw.op("act", "activation", dec[i2][:, :], dm[i2][:, :],
                          AF.Exp, bias=b1z[:, h:h + 1], scale=1.0)
                    fw.op("dve", "tensor_tensor", mt[i2][:, :], cbg[ig][:, :],
                          dec[i2][:, :], ALU.mult, reads=(cbg[ig], dec[i2]),
                          writes=(mt[i2],))
                    fw.op("pe", "matmul", ps_yd[i2], mt[i2][:, :],
                          hsz[:, hsl], start=True, stop=True,
                          reads=(mt[i2], hsz), writes=(f"k1_yd{i2}",))
                    fw.op("pe", "matmul", ps_yo[i2],
                          bcch[:, (c.G1 + g) * 128:(c.G1 + g + 1) * 128],
                          szb[:, hsl], start=True, stop=True,
                          reads=(bcch, szb), writes=(f"k1_yo{i2}",))
                    fw.op("dve", "tensor_scalar", u1[i2][:, :], ps_yo[i2],
                          expA[:, h:h + 1], None, ALU.mult,
                          reads=(f"k1_yo{i2}", f"k1_yd{i2}", expA),
                          writes=(u1[i2],))
                    fw.op("dve", "tensor_tensor", u2[i2][:, :], u1[i2][:, :],
                          ps_yd[i2], ALU.add, reads=(u1[i2], f"k1_yd{i2}"),
                          writes=(u2[i2],))
                    fw.op("dve", "scalar_tensor_tensor", ych[:, hsl],
                          hsz[:, hsl], dcols[:, h:h + 1], u2[i2][:, :],
                          ALU.mult, ALU.add, reads=(hsz, dcols, u2[i2]),
                          writes=(ych,))
            for j in range(c.INTER // 512):
                fw.op("dve", "tensor_tensor", gg[:, j * 512:(j + 1) * 512],
                      ych[:, j * 512:(j + 1) * 512],
                      gch[:, j * 512:(j + 1) * 512], ALU.mult,
                      reads=(ych, gch), writes=(gg,))
            gsz = c.INTER // c.G1
            for g in range(c.G1):
                fw.op("act", "activation", sqj[:, :], gg[:, g * gsz:(g + 1) * gsz],
                      AF.Square, bias=0.0, scale=1.0,
                      accum_out=ss[:, g:g + 1], reads=(gg,), writes=(sqj, ss))
            fw.op("act", "activation", rstd[:, :], ss[:, :], AF.Ln,
                  bias=1e-5, scale=1.0 / gsz)
            fw.op("act", "activation", rstd[:, :], rstd[:, :], AF.Exp,
                  bias=0.0, scale=-0.5)
            for g in range(c.G1):
                fw.op("dve", "scalar_tensor_tensor", ggn[:, g * gsz:(g + 1) * gsz],
                      gg[:, g * gsz:(g + 1) * gsz], rstd[:, g:g + 1],
                      nw_bc[:, g * gsz:(g + 1) * gsz], ALU.mult, ALU.mult,
                      reads=(gg, rstd, nw_bc), writes=(ggn,))
            for m in range(c.INTER // 128):
                fw.op("pe", "transpose", pstb[:, :], ggn[:, m * 128:(m + 1) * 128],
                      p.eyeb[:, :], reads=(ggn, p.eyeb), writes=("k_ggT",))
                fw.op("dve", "tensor_copy", tb[m % 2][:, :], pstb[:, :],
                      reads=("k_ggT",), writes=(tb[m % 2],))
                fw.dma("sp", p.d["ggfm"][m * 128:(m + 1) * 128,
                                         z * 128:(z + 1) * 128],
                       tb[m % 2][:, :], tag="st_ggfm")



def ph_l1_post(p):
    c, fw = p.cfg, p.fw
    nhs = c.INTER // 128
    with p.scope():
        rows3 = None
        with p.scope():
            ggr = p.sb("ggr", [128, nhs * c.T], BF16)
            load_fm(p, ggr, "ggfm", c.INTER, c.T)
            ps_s = p.ps("ps_s3", [1, c.T], F32)
            ps_q = p.ps("ps_q3", [1, c.T], F32)
            ps_o = [p.ps(f"ps3_o{i}", [128, 512], F32) for i in range(2)]
            wt = [p.sb(f"op_t{i}", [128, 128], BF16) for i in range(2)]
            xold = [p.sb(f"xo3{i}", [128, 512], F32) for i in range(2)]
            x3s = [p.sb(f"x3s{i}", [128, 512], F32) for i in range(2)]
            sqt = [p.sb(f"sq3{i}", [128, 512], BF16) for i in range(2)]
            xb3 = [p.sb(f"xb3{i}", [128, 512], BF16) for i in range(2)]
            for m in range(c.NK):
                for k in range(nhs):
                    w = wt[k % 2]
                    fw.dma("pool", w[:, :],
                           p.d["opj"][k * 128:(k + 1) * 128, m * 128:(m + 1) * 128],
                           tag=f"opj{m}_{k}")
                    for tt in range(c.T // 512):
                        fw.op("pe", "matmul", ps_o[tt][:, 0:512], w[:, :],
                              ggr[:, k * c.T + tt * 512:k * c.T + tt * 512 + 512],
                              start=(k == 0), stop=(k == nhs - 1))
                for tt in range(c.T // 512):
                    i2 = tt % 2
                    fw.dma("sp", xold[i2][:, :],
                           p.d["x2"][m * 128:(m + 1) * 128, tt * 512:tt * 512 + 512],
                           tag="ld_x2")
                    fw.op("dve", "tensor_tensor", x3s[i2][:, :], ps_o[tt][:, 0:512],
                          xold[i2][:, :], ALU.add)
                    fw.dma("sp", p.d["x3"][m * 128:(m + 1) * 128,
                                           tt * 512:tt * 512 + 512],
                           x3s[i2][:, :], tag="st_x3", sem="st_x3")
                    bsl = xb3[i2][:, :]
                    fw.op("act", "activation", bsl, x3s[i2][:, :], AF.Copy,
                          bias=0.0, scale=1.0, reads=(x3s[i2],), writes=(xb3[i2],))
                    fw.dma("sp", p.d["x3bf"][m * 128:(m + 1) * 128,
                                             tt * 512:tt * 512 + 512],
                           bsl, tag="st_x3bf")
                    fw.op("act", "activation", sqt[i2][:, :], bsl, AF.Square,
                          bias=0.0, scale=1.0, reads=(xb3[i2],), writes=(sqt[i2],))
                    fw.op("pe", "matmul", ps_s[0:1, tt * 512:(tt + 1) * 512],
                          p.onescol[:, 0:1], bsl, start=(m == 0),
                          stop=(m == c.NK - 1), reads=(p.onescol, xb3[i2]),
                          writes=("k_lns3",))
                    fw.op("pe", "matmul", ps_q[0:1, tt * 512:(tt + 1) * 512],
                          p.onescol[:, 0:1], sqt[i2][:, :], start=(m == 0),
                          stop=(m == c.NK - 1), reads=(p.onescol, sqt[i2]),
                          writes=("k_lnq3",))
            rows3 = ln_rows(p, ps_s, ps_q, "k_lns3", "k_lnq3")
        xn3 = p.sb("xn3_r", [128, c.NK * c.T], BF16)
        ln_apply_dram(p, rows3[0], rows3[1], "x3bf", xn3)
        Gall = p.sb("Gall", [128, c.NZ * c.MK], F32)
        with p.scope():
            wrs = p.sb("wrs", [128, c.NK * c.MK], BF16)
            for k in range(c.NK):
                fw.dma("pool", wrs[:, k * c.MK:(k + 1) * c.MK],
                       p.d["wr"][k * 128:(k + 1) * 128, :], tag=f"wr{k}")
            ps_l = p.ps("ps_l3", [128, 512], F32)
            lg = p.sb("lg3", [128, c.MK], F32)
            nmx = p.sb("nmx3", [128, 1], F32)
            ex = p.sb("ex3", [128, c.MK], F32)
            ssum = p.sb("ssum3", [128, 1], F32)
            scw = p.sb("scw3", [128, c.MK], F32)
            thr = p.sb("thr3", [128, 1], F32)
            mk = p.sb("mk3", [128, c.MK], F32)
            eg = p.sb("eg3", [128, c.MK], F32)
            for tz in range(c.NZ):
                for k in range(c.NK):
                    fw.op("pe", "matmul", ps_l[:, 0:c.MK],
                          xn3[:, k * c.T + tz * 128:k * c.T + tz * 128 + 128],
                          wrs[:, k * c.MK:(k + 1) * c.MK],
                          start=(k == 0), stop=(k == c.NK - 1))
                fw.op("dve", "tensor_copy", lg[:, :], ps_l[:, 0:c.MK])
                fw.op("dve", "tensor_reduce", nmx[:, :], lg[:, :], AX.X,
                      ALU.max, negate=True)
                fw.op("act", "activation", ex[:, :], lg[:, :], AF.Exp,
                      bias=nmx[:, 0:1], scale=1.0)
                topk_gate(p, ex, scw, thr, mk, c.MTOP)
                fw.op("dve", "tensor_mul", eg[:, :], ex[:, :], mk[:, :])
                fw.op("dve", "tensor_reduce", ssum[:, :], eg[:, :], AX.X, ALU.add)
                fw.op("dve", "reciprocal", ssum[:, :], ssum[:, :])
                fw.op("dve", "tensor_scalar", Gall[:, tz * c.MK:(tz + 1) * c.MK],
                      eg[:, :], ssum[:, 0:1], None, ALU.mult,
                      reads=(eg, ssum), writes=(Gall,))
        hgf = p.sb("hgf", [128, (c.KR // 128) * c.T], BF16)
        with p.scope():
            ast = p.sb("ast", [128, c.NK * 512], BF16)
            ps_h = p.ps("ps_h3", [128, 512], F32)
            hgt = p.sb("hgt", [128, 512], BF16)
            pstb3 = p.ps("pstb3", [128, 128], BF16)
            tb3 = [p.sb(f"tb3{i}", [128, 128], BF16) for i in range(2)]
            for n in range(c.KR // 512):
                for k in range(c.NK):
                    fw.dma("pool", ast[:, k * 512:(k + 1) * 512],
                           p.d["afl"][k * 128:(k + 1) * 128, n * 512:(n + 1) * 512],
                           tag=f"afl{n}_{k}")
                for tz in range(c.NZ):
                    for k in range(c.NK):
                        fw.op("pe", "matmul", ps_h[:, 0:512],
                              xn3[:, k * c.T + tz * 128:k * c.T + tz * 128 + 128],
                              ast[:, k * 512:(k + 1) * 512],
                              start=(k == 0), stop=(k == c.NK - 1))
                    nex = 512 // c.R
                    gv = Gall[:, tz * c.MK + n * nex: tz * c.MK + (n + 1) * nex]
                    fw.op("dve", "tensor_tensor",
                          hgt[:, :].rearrange("pp (e r) -> pp e r", r=c.R),
                          ps_h[:, 0:512].rearrange("pp (e r) -> pp e r", r=c.R),
                          gv.unsqueeze(2).broadcast_to([128, nex, c.R]),
                          ALU.mult, reads=(ps_h, Gall), writes=(hgt,))
                    for j in range(4):
                        fw.op("pe", "transpose", pstb3[:, :],
                              hgt[:, j * 128:(j + 1) * 128], p.eyeb[:, :],
                              reads=(hgt, p.eyeb), writes=("k_hgT",))
                        fw.op("dve", "tensor_copy", tb3[j % 2][:, :],
                              pstb3[:, :], reads=("k_hgT",), writes=(tb3[j % 2],))
                        kr = n * 4 + j
                        fw.op("dve", "tensor_copy",
                              hgf[:, kr * c.T + tz * 128: kr * c.T + tz * 128 + 128],
                              tb3[j % 2][:, :], reads=(tb3[j % 2],), writes=(hgf,))
        with p.scope():
            wt3 = [p.sb(f"bfl_t{i}", [128, 128], BF16) for i in range(2)]
            ps_f = [p.ps(f"ps_f{i}", [128, 512], F32) for i in range(2)]
            xo4 = [p.sb(f"xo4{i}", [128, 512], F32) for i in range(2)]
            yt = [p.sb(f"yt4{i}", [128, 512], F32) for i in range(2)]
            for m in range(c.NK):
                for k in range(c.KR // 128):
                    w = wt3[k % 2]
                    fw.dma("pool", w[:, :],
                           p.d["bfl"][k * 128:(k + 1) * 128, m * 128:(m + 1) * 128],
                           tag=f"bfl{m}_{k}")
                    for tt in range(c.T // 512):
                        fw.op("pe", "matmul", ps_f[tt][:, 0:512], w[:, :],
                              hgf[:, k * c.T + tt * 512:k * c.T + tt * 512 + 512],
                              start=(k == 0), stop=(k == c.KR // 128 - 1))
                for tt in range(c.T // 512):
                    i2 = tt % 2
                    fw.dma("sp", xo4[i2][:, :],
                           p.d["x3"][m * 128:(m + 1) * 128, tt * 512:tt * 512 + 512],
                           tag="ld_x3")
                    fw.op("dve", "scalar_tensor_tensor", yt[i2][:, :],
                          ps_f[tt][:, 0:512], 2.0, xo4[i2][:, :],
                          ALU.mult, ALU.add)
                    fw.dma("sp", p.d["y"][m * 128:(m + 1) * 128,
                                          tt * 512:tt * 512 + 512],
                           yt[i2][:, :], tag="st_y", sem="st_y")



def ph_ln0(p):
    ph_ln(p, None, "xn0", cast_from="x")


CONSTS = ["eye", "triu", "mneg", "smask", "rmask"]
PHASE_IO = {
    "ln0": (ph_ln0, ["x"], ["xn0"]),
    "l0proj": (ph_l0_proj, ["xn0", "wq", "wk", "wv", "wg", "wwin", "dtb0",
                            "alog0", "rcos", "rsin"],
               ["qsp", "ksp", "vsp", "g0sp", "dt0", "g0cum", "b0"]),
    "l0cs": (ph_l0_cs, ["ksp", "vsp", "g0cum", "b0"],
             ["s0loc", "p0", "s0fin"]),
    "l0att": (ph_l0_att, ["qsp", "ksp", "vsp", "g0sp", "g0cum", "b0", "p0",
                          "s0loc", "s0init", "gnw"], ["ygfm"]),
    "l0post": (ph_l0_post, ["ygfm", "x", "wo", "rw", "rb", "eb"],
               ["x1", "x2", "xn2", "x1bf", "x2bf"]),
    "l1inproj": (ph_l1_inproj, ["xn2", "ipj", "dtb1", "alog1"],
                 ["ipg", "iph", "dt1", "lndt1", "adt1", "halo"]),
    "l1convcs": (ph_l1_convcs, ["iph", "haloin", "cw", "cb", "dt1", "lndt1",
                                "adt1"],
                 ["hstm", "bcfm", "acumt", "b1sp", "acfl", "p1", "s1loc",
                  "s1fin"]),
    "l1ssd": (ph_l1_ssd, ["s1loc", "s1init", "hstm", "bcfm", "ipg", "acumt",
                          "b1sp", "acfl", "p1", "nw", "dvec"], ["ggfm"]),
    "l1post": (ph_l1_post, ["ggfm", "x2", "opj", "wr", "afl", "bfl"],
               ["x3", "y", "x3bf"]),
}
CC_ROUTE = {"s0init": "s0fin", "haloin": "halo", "s1init": "s1fin"}
CC_PROD = {"s0fin": "l0cs", "halo": "l1inproj", "s1fin": "l1convcs"}

LAUNCHES = [["ln0", "l0proj", "l0cs"], ["l0att"], ["l0post"],
            ["l1inproj"], ["l1convcs"], ["l1ssd"], ["l1post"]]

_BUILD_CACHE = {}


def build_launch(cfg, phases, key, out_force=()):
    if key in _BUILD_CACHE:
        return _BUILD_CACHE[key]
    reg = declare_tensors(cfg)
    nc = bass.Bass("TRN2", target_bir_lowering=False)
    writes_here = set()
    reads_here = set(CONSTS)
    for ph in phases:
        _, rds, wrs = PHASE_IO[ph]
        for r in rds:
            if r not in writes_here:
                reads_here.add(r)
        writes_here.update(wrs)
        if cfg.fused:
            for ini, fin in CC_ROUTE.items():
                if fin in wrs:
                    writes_here.add(ini)
    later_reads = set()
    if not cfg.fused:
        for ph, (fn, rds, wrs) in PHASE_IO.items():
            if ph in phases:
                continue
            later_reads.update(rds)
            for ini in CC_ROUTE:
                if ini in rds:
                    later_reads.add(CC_ROUTE[ini])
    ins, outs = [], []
    with contextlib.ExitStack() as es:
        p = P(nc, cfg, es)
        for name, (shape, dt) in reg.items():
            if name in reads_here and name not in writes_here:
                kind = "ExternalInput"
                ins.append(name)
            elif name in out_force and name in writes_here:
                kind = "ExternalOutput"
                outs.append(name)
            elif name == "y" and "y" in writes_here:
                kind = "ExternalOutput"
                outs.append(name)
            elif name in writes_here and (not cfg.fused) and (
                    name in later_reads or name in CC_ROUTE.values()):
                kind = "ExternalOutput"
                outs.append(name)
            elif name in writes_here or name in reads_here:
                kind = "Internal"
            else:
                continue
            p.d[name] = nc.dram_tensor(name, list(shape), dt, kind=kind).ap()
        ph_consts(p)
        for ph in phases:
            PHASE_IO[ph][0](p)
        p.fw.finish()
        p.fw.flush(es)
    _BUILD_CACHE[key] = (nc, ins, outs)
    return _BUILD_CACHE[key]


def rope_tables(L, D=128, theta=1e4):
    inv = 1.0 / theta ** (np.arange(0, D, 2, dtype=np.float32) / D)
    ang = np.arange(L, dtype=np.float32)[:, None] * inv
    cos = np.concatenate([np.cos(ang), np.cos(ang)], -1).astype(np.float32)
    sin = np.concatenate([np.sin(ang), np.sin(ang)], -1).astype(np.float32)
    return cos, sin


def prep_inputs(cfg, inp):
    c = cfg
    ct = lambda a: np.ascontiguousarray(a, dtype=np.float32)
    shared = {}
    shared["wq"] = ct(inp["L0_attn_Wq"].T)
    shared["wk"] = ct(inp["L0_attn_Wk"].T)
    shared["wv"] = ct(inp["L0_attn_Wv"].T)
    shared["wg"] = ct(inp["L0_attn_Wg"].T)
    shared["wwin"] = ct(inp["L0_attn_Win"].T)
    shared["dtb0"] = ct(inp["L0_attn_dt_bias"][None, :])
    shared["alog0"] = ct(inp["L0_attn_A_log_bias"][None, :])
    shared["gnw"] = ct(inp["L0_attn_gnorm_w"][None, :])
    shared["wo"] = ct(inp["L0_attn_Wo"].T)
    shared["rw"] = ct(inp["L0_ffn_router_weight"].T)
    shared["rb"] = ct(inp["L0_ffn_router_bias"][None, :])
    shared["eb"] = ct(inp["L0_ffn_expert_bias"])
    shared["ipj"] = ct(inp["L1_attn_in_proj_w"].T)
    cw = inp["L1_attn_conv_w"][:, 0, :]
    nc3 = c.CONV // 128
    shared["cw"] = ct(cw.reshape(nc3, 128, 4).transpose(1, 0, 2).reshape(128, nc3 * 4))
    shared["cb"] = ct(inp["L1_attn_conv_b"].reshape(nc3, 128).T)
    shared["dtb1"] = ct(inp["L1_attn_dt_bias"][:, None])
    shared["alog1"] = ct(inp["L1_attn_A_log"][:, None])
    shared["dvec"] = ct(inp["L1_attn_D"][None, :])
    shared["nw"] = ct(inp["L1_attn_norm_w"][None, :])
    shared["opj"] = ct(inp["L1_attn_out_proj_w"].T)
    shared["afl"] = ct(inp["L1_ffn_A"].reshape(c.KR, c.D).T)
    shared["bfl"] = ct(inp["L1_ffn_B"].transpose(0, 2, 1).reshape(c.KR, c.D))
    shared["wr"] = ct(inp["L1_ffn_Wr"].T)
    shared["eye"] = np.eye(128, dtype=np.float32)
    tri = np.triu(np.ones((128, 128), dtype=np.float32))
    shared["triu"] = tri
    shared["mneg"] = np.where(tri > 0, 0.0, -1e30).astype(np.float32)
    x = inp["L0_attn_hidden_states"]
    B, L, _ = x.shape
    cos, sin = rope_tables(L)
    percore = []
    for core in range(8):
        b, half = core // 2, core % 2
        pc = {}
        pc["x"] = ct(x[b, half * c.T:(half + 1) * c.T, :].T)
        pc["rcos"] = ct(cos[half * c.T:(half + 1) * c.T, :].T)
        pc["rsin"] = ct(sin[half * c.T:(half + 1) * c.T, :].T)
        pc["smask"] = np.full((1, 1), 1.0 if half == 0 else 0.0, np.float32)
        pc["rmask"] = np.full((1, 1), 0.0 if half == 0 else 1.0, np.float32)
        percore.append(pc)
    return shared, percore


def kernel(**inputs):
    cfg = Cfg(True)
    cfg.fused = FUSED
    shared, percore = prep_inputs(cfg, inputs)
    env = [dict(shared, **pc) for pc in percore]
    launches = [[ph for grp in LAUNCHES for ph in grp]] if cfg.fused else LAUNCHES
    for li, grp in enumerate(launches):
        nc, ins, outs = build_launch(cfg, grp, (cfg.fused, li))
        in_maps = [{k: env[core][k] for k in ins} for core in range(8)]
        res = run_bass_kernel_spmd(nc, in_maps, core_ids=list(range(8)))
        for core in range(8):
            for k in outs:
                env[core][k] = res.results[core][k]
        if not cfg.fused:
            for ini, fin in CC_ROUTE.items():
                if fin in outs:
                    for core in range(8):
                        env[core][ini] = res.results[core ^ 1][fin]
    out = np.empty((4, 2048, 4096), dtype=np.float32)
    for core in range(8):
        b, half = core // 2, core % 2
        out[b, half * cfg.T:(half + 1) * cfg.T, :] = env[core]["y"].T
    return out


# version 08
# speedup vs baseline: 1.6698x; 1.6664x over previous
import contextlib
import numpy as np

import concourse.bass as bass
import concourse.mybir as mybir
from concourse.bass_utils import run_bass_kernel_spmd

F32 = mybir.dt.float32
BF16 = mybir.dt.bfloat16
AX = mybir.AxisListType
ALU = mybir.AluOpType
AF = mybir.ActivationFunctionType

FUSED = True


ENGS = ("pe", "act", "dve", "pool", "sp")
_ENG_ATTR = {"pe": "tensor", "act": "scalar", "dve": "vector", "pool": "gpsimd",
             "sp": "sync"}
_SEM_ROT = 48000


class _Op:
    __slots__ = ("method", "args", "kwargs", "waits", "inc", "idx", "tag")

    def __init__(self, method, args, kwargs, idx, tag):
        self.method = method
        self.args = args
        self.kwargs = kwargs
        self.waits = []
        self.inc = None
        self.idx = idx
        self.tag = tag


class Fw:
    def __init__(self, nc):
        self.nc = nc
        self.ops = {e: [] for e in ENGS}
        self.ndma = {e: 0 for e in ENGS}
        self.named = {}
        self.barrier_totals = {e: 0 for e in ENGS}
        self.needed = {}
        self.state = {}
        self.waited = {e: {} for e in ENGS}
        self.last_compute = {e: None for e in ENGS}
        self.epoch = 0
        self.named_first = {}
        self.named_last = {}
        self.named_queue = {}
        self.verbose = False

    def _key(self, ap):
        if isinstance(ap, str):
            return ap
        t = getattr(ap, "tensor", ap)
        return getattr(t, "name", str(t))

    def _st(self, key):
        if key not in self.state:
            self.state[key] = {"w": None, "r": []}
        return self.state[key]

    def _need_wait(self, eng, dep):
        stream, idx = dep
        cur = self.waited[eng].get(stream, -1)
        if idx <= cur:
            return False
        self.waited[eng][stream] = idx
        return True

    def _resolve(self, eng, deps, o, what=""):
        for d in deps:
            stream, idx = d
            if stream == "eng:" + eng and eng == "pe":
                continue
            if stream == "eng:" + eng and idx == o.idx:
                continue
            if stream.startswith("dmabulk:"):
                q = stream.split(":")[1]
                if idx <= self.barrier_totals[q]:
                    continue
                raise RuntimeError(
                    f"dep on un-barriered bulk DMA {d} (from {what})")
            if self._need_wait(eng, d):
                o.waits.append(d)
                self.needed[d] = True

    def op(self, eng, method, *args, reads=None, writes=None, tag="", **kwargs):
        assert eng in ENGS
        _OUT_KW = ("accum_out",)
        if writes is None:
            writes = (args[0],)
            if kwargs.get("accum_out") is not None:
                writes = writes + (kwargs["accum_out"],)
        if reads is None:
            reads = tuple(a for a in args[1:] if hasattr(a, "tensor"))
            reads = reads + tuple(v for k, v in kwargs.items()
                                  if k not in _OUT_KW and hasattr(v, "tensor"))
        idx = len(self.ops[eng])
        o = _Op(method, args, kwargs, idx, tag)
        deps = []
        for ap in reads:
            st = self._st(self._key(ap))
            if st["w"] is not None:
                deps.append(st["w"])
        for ap in writes:
            st = self._st(self._key(ap))
            if st["w"] is not None:
                deps.append(st["w"])
            deps.extend(st["r"])
        self._resolve(eng, deps, o, tag or method)
        mytok = ("eng:" + eng, idx)
        for ap in reads:
            self._st(self._key(ap))["r"].append(mytok)
        for ap in writes:
            st = self._st(self._key(ap))
            st["w"] = mytok
            st["r"] = []
        self.ops[eng].append(o)
        self.last_compute[eng] = idx
        return mytok

    def dma(self, q, out, in_, sem=None, rkey=None, wkey=None, tag="", **kwargs):
        assert q in ENGS
        if sem is None:
            side = out if "SBTensor" in type(getattr(out, "tensor", out)).__name__ else in_
            sem = "b_" + self._key(side) + "@" + q
        elif sem is False:
            sem = None
        idx = len(self.ops[q])
        o = _Op("dma_start", (), dict(out=out, in_=in_, **kwargs), idx, tag)
        deps = []
        st_in = self._st(rkey if rkey is not None else self._key(in_))
        if st_in["w"] is not None:
            deps.append(st_in["w"])
        st_out = self._st(wkey if wkey is not None else self._key(out))
        if st_out["w"] is not None:
            deps.append(st_out["w"])
        deps.extend(st_out["r"])
        self._resolve(q, deps, o, tag or "dma")
        if sem is None:
            self.ndma[q] += 1
            tok = ("dmabulk:" + q, self.ndma[q])
        else:
            self.named[sem] = self.named.get(sem, 0) + 1
            self.named_first.setdefault(sem, self.epoch)
            self.named_last[sem] = self.epoch
            q0 = self.named_queue.setdefault(sem, q)
            assert q0 == q, f"named sem {sem} used from queues {q0} and {q}"
            tok = ("dsem:" + sem, self.named[sem])
        o.inc = tok
        st_in["r"].append(tok)
        st_out["w"] = tok
        st_out["r"] = []
        self.ops[q].append(o)
        return tok

    def cc(self, kind, op, groups, in_ap, out_ap):
        tok = self.op("pool", "collective_compute", kind, op,
                      reads=(in_ap,), writes=(out_ap,),
                      replica_groups=groups, ins=[in_ap.opt()],
                      outs=[out_ap.opt()], tag="cc")
        self.op("pool", "engine_nop", reads=(out_ap,), writes=("_ccnop",),
                tag="ccwait")
        return tok

    def barrier(self):
        toks = []
        for e in ENGS:
            if self.last_compute[e] is not None:
                toks.append(("eng:" + e, self.last_compute[e]))
            if self.ndma[e]:
                toks.append(("dmabulk:" + e, self.ndma[e]))
        for nm, c in self.named.items():
            if c:
                toks.append(("dsem:" + nm, c))
        for e in ENGS:
            o = _Op("engine_nop", (), {}, len(self.ops[e]), "barrier")
            for d in toks:
                if d[0] == "eng:" + e:
                    continue
                if self._need_wait(e, d):
                    o.waits.append(d)
                    self.needed[d] = True
            self.ops[e].append(o)
        for e in ENGS:
            self.barrier_totals[e] = self.ndma[e]
        self.epoch += 1

    def finish(self):
        self.barrier()

    def flush(self, es):
        nc = self.nc
        eng_incpts = {e: sorted(i for (s, i) in self.needed if s == "eng:" + e)
                      for e in ENGS}

        def mk_sems(name, nvals):
            n_sems = max(1, (nvals + _SEM_ROT - 1) // _SEM_ROT)
            return [es.enter_context(nc.semaphore(f"{name}{i}"))
                    for i in range(n_sems)]

        eng_sems = {e: mk_sems("S" + e, len(eng_incpts[e])) for e in ENGS}
        eng_rank = {e: {idx: r + 1 for r, idx in enumerate(eng_incpts[e])}
                    for e in ENGS}
        bulk_sems = {e: mk_sems("D" + e, self.ndma[e] * 16) for e in ENGS}
        slots = []
        slot_count = []
        name_slot = {}
        name_base = {}
        for nm in sorted(self.named, key=lambda n: (self.named_first[n],
                                                    self.named_last[n])):
            fe, le = self.named_first[nm], self.named_last[nm]
            qn = self.named_queue[nm]
            pick = None
            for si in range(len(slots)):
                if (slots[si][0] == qn and slots[si][1] < fe
                        and (slot_count[si] + self.named[nm]) * 16 < _SEM_ROT):
                    pick = si
                    break
            if pick is None:
                pick = len(slots)
                slots.append((qn, -1))
                slot_count.append(0)
            name_slot[nm] = pick
            name_base[nm] = slot_count[pick]
            slot_count[pick] += self.named[nm]
            slots[pick] = (qn, le)
        named_slot_sems = [es.enter_context(nc.semaphore(f"NS{si}"))
                           for si in range(len(slots))]
        rot = _SEM_ROT

        def sem_val(stream, idx):
            kind, name = stream.split(":", 1)
            if kind == "eng":
                r = eng_rank[name][idx]
                si, v = divmod(r - 1, rot)
                return eng_sems[name][si], v + 1
            if kind == "dmabulk":
                per = rot // 16
                si, v = divmod(idx - 1, per)
                return bulk_sems[name][si], (v + 1) * 16
            return (named_slot_sems[name_slot[name]],
                    (name_base[name] + idx) * 16)

        def inc_for_eng(e, idx):
            r = eng_rank[e].get(idx)
            if r is None:
                return None
            return eng_sems[e][(r - 1) // rot], 1

        def inc_for_dma(stream, idx):
            kind, name = stream.split(":", 1)
            if kind == "dmabulk":
                per = rot // 16
                return bulk_sems[name][(idx - 1) // per], 16
            return named_slot_sems[name_slot[name]], 16

        if self.verbose:
            stats = {e: (len(self.ops[e]), len(eng_incpts[e])) for e in ENGS}
            nw = sum(len(o.waits) for e in ENGS for o in self.ops[e])
            print("fw flush ops(incs):", stats, "names:", len(self.named),
                  "slots:", len(slots), "waits:", nw)

        with nc.Block() as block:
            for e in ENGS:
                ops = self.ops[e]
                if not ops:
                    continue

                def body(engine, ops=ops, e=e):
                    for o in ops:
                        for d in o.waits:
                            s, v = sem_val(*d)
                            engine.wait_ge(s, v)
                        if o.method == "engine_nop":
                            inst = engine.nop()
                        else:
                            inst = getattr(engine, o.method)(*o.args, **o.kwargs)
                        if o.inc is not None:
                            s, a = inc_for_dma(*o.inc)
                            inst.then_inc(s, a)
                        else:
                            ic = inc_for_eng(e, o.idx)
                            if ic is not None:
                                inst.then_inc(ic[0], ic[1])

                getattr(block, _ENG_ATTR[e])(body)



class Cfg:
    def __init__(self, full=True):
        if full:
            self.D = 4096
            self.T = 1024
            self.H0 = 32
            self.KVH = 8
            self.H1 = 128
            self.G1 = 8
            self.E = 64
            self.KTOP = 8
            self.MK = 64
            self.MTOP = 16
            self.R = 16
        else:
            self.D = 512
            self.T = 512
            self.H0 = 4
            self.KVH = 1
            self.H1 = 16
            self.G1 = 2
            self.E = 64
            self.KTOP = 8
            self.MK = 64
            self.MTOP = 16
            self.R = 16
        self.HD = 128
        self.P1 = 64
        self.N1 = 128
        self.CH = 128
        self.NZ = self.T // self.CH
        self.KD = self.KVH * self.HD
        self.QD = self.H0 * self.HD
        self.INTER = self.H1 * self.P1
        self.CONV = self.INTER + 2 * self.G1 * self.N1
        self.IPJ = self.INTER + self.CONV + self.H1
        self.KR = self.MK * self.R
        self.NK = self.D // 128
        self.fused = True
        self.groups = [[2 * i, 2 * i + 1] for i in range(4)]



def declare_tensors(cfg):
    c = cfg
    t = {}
    t["x"] = ([c.D, c.T], F32)
    t["wq"] = ([c.D, c.QD], F32)
    t["wk"] = ([c.D, c.KD], F32)
    t["wv"] = ([c.D, c.KD], F32)
    t["wg"] = ([c.D, c.QD], F32)
    t["wwin"] = ([c.D, c.H0], F32)
    t["dtb0"] = ([1, c.H0], F32)
    t["alog0"] = ([1, c.H0], F32)
    t["gnw"] = ([1, c.HD], F32)
    t["wo"] = ([c.QD, c.D], F32)
    t["rw"] = ([c.D, c.E], F32)
    t["rb"] = ([1, c.E], F32)
    t["eb"] = ([c.E, c.D], F32)
    t["ipj"] = ([c.D, c.IPJ], F32)
    t["cw"] = ([128, (c.CONV // 128) * 4], F32)
    t["cb"] = ([128, c.CONV // 128], F32)
    t["dtb1"] = ([c.H1, 1], F32)
    t["alog1"] = ([c.H1, 1], F32)
    t["dvec"] = ([1, c.H1], F32)
    t["nw"] = ([1, c.INTER], F32)
    t["opj"] = ([c.INTER, c.D], F32)
    t["afl"] = ([c.D, c.KR], F32)
    t["bfl"] = ([c.KR, c.D], F32)
    t["wr"] = ([c.D, c.MK], F32)
    def t4(K, M):
        return ([(M // 128) * (K // 512) * 128, 512], F32)
    t["wq4"] = t4(c.D, c.QD)
    t["wk4"] = t4(c.D, c.KD)
    t["wo4"] = t4(c.QD, c.D)
    t["opj4"] = t4(c.INTER, c.D)
    t["bfl4"] = t4(c.KR, c.D)
    t["iph4"] = t4(c.D, c.CONV)
    t["rcos"] = ([c.HD, c.T], F32)
    t["rsin"] = ([c.HD, c.T], F32)
    t["eye"] = ([128, 128], F32)
    t["triu"] = ([128, 128], F32)
    t["mneg"] = ([128, 128], F32)
    t["smask"] = ([1, 1], F32)
    t["rmask"] = ([1, 1], F32)
    t["y"] = ([c.D, c.T], F32)
    t["xn0"] = ([c.D, c.T], BF16)
    t["qsp"] = ([c.QD, c.T], BF16)
    t["ksp"] = ([c.KD, c.T], BF16)
    t["vsp"] = ([c.T, c.KD], BF16)
    t["g0sp"] = ([c.T, c.QD], BF16)
    t["dt0"] = ([c.T, c.H0], F32)
    t["g0cum"] = ([c.T, c.H0], F32)
    t["b0"] = ([c.T, c.H0], F32)
    t["s0loc"] = ([c.NZ * 128, c.H0 * c.HD], BF16)
    t["p0"] = ([c.NZ, c.H0], F32)
    t["s0fin"] = ([128, c.H0 * c.HD], F32)
    t["s0init"] = ([128, c.H0 * c.HD], F32)
    t["ygfm"] = ([c.QD, c.T], BF16)
    t["x1"] = ([c.D, c.T], F32)
    t["x1bf"] = ([c.D, c.T], BF16)
    t["xn1"] = ([c.D, c.T], BF16)
    t["x2"] = ([c.D, c.T], F32)
    t["x2bf"] = ([c.D, c.T], BF16)
    t["xn2"] = ([c.D, c.T], BF16)
    t["ipg"] = ([c.T, c.INTER], BF16)
    t["iph"] = ([c.CONV, c.T], BF16)
    t["dt1"] = ([c.H1, c.T], F32)
    t["lndt1"] = ([c.H1, c.T], F32)
    t["adt1"] = ([c.H1, c.T], F32)
    t["halo"] = ([128, (c.CONV // 128) * 3], F32)
    t["haloin"] = ([128, (c.CONV // 128) * 3], F32)
    t["hstm"] = ([c.T, c.INTER], BF16)
    t["bcfm"] = ([2 * c.G1 * c.N1, c.T], BF16)
    t["acumt"] = ([c.T, c.H1], F32)
    t["b1sp"] = ([c.T, c.H1], F32)
    t["acfl"] = ([c.NZ, c.H1 * 128], F32)
    t["p1"] = ([c.NZ, c.H1], F32)
    t["s1loc"] = ([c.NZ * c.N1, c.H1 * c.P1], BF16)
    t["s1fin"] = ([c.N1, c.H1 * c.P1], F32)
    t["s1init"] = ([c.N1, c.H1 * c.P1], F32)
    t["ggfm"] = ([c.INTER, c.T], BF16)
    t["x3"] = ([c.D, c.T], F32)
    t["x3bf"] = ([c.D, c.T], BF16)
    t["xn3"] = ([c.D, c.T], BF16)
    return t


INPUT_NAMES = ["x", "wq", "wk", "wv", "wg", "wwin", "dtb0", "alog0", "gnw",
               "wo", "rw", "rb", "eb", "ipj", "cw", "cb", "dtb1", "alog1",
               "dvec", "nw", "opj", "afl", "bfl", "wr", "rcos", "rsin",
               "eye", "triu", "mneg", "smask", "rmask"]



class P:

    def __init__(self, nc, cfg, es):
        self.nc = nc
        self.cfg = cfg
        self.scopes = [es]
        self.fw = Fw(nc)
        self.d = {}
        self.uid = 0

    def sb(self, name, shape, dt=F32):
        return self.scopes[-1].enter_context(self.nc.sbuf_tensor(name, shape, dt))

    def sb_root(self, name, shape, dt=F32):
        return self.scopes[0].enter_context(self.nc.sbuf_tensor(name, shape, dt))

    def ps(self, name, shape, dt=F32):
        return self.scopes[-1].enter_context(self.nc.psum_tensor(name, shape, dt))

    def u(self, base):
        self.uid += 1
        return f"{base}{self.uid}"

    @contextlib.contextmanager
    def scope(self):
        es = contextlib.ExitStack()
        self.scopes.append(es)
        try:
            yield
        finally:
            self.fw.barrier()
            self.scopes.pop()
            es.close()


def ph_consts(p):
    c, fw = p.cfg, p.fw
    p.ones1 = p.sb("ones1", [1, 128], F32)
    fw.op("dve", "memset", p.ones1[0:1, :], 1.0)
    p.onescol = p.sb("onescol", [128, 1], BF16)
    fw.op("dve", "memset", p.onescol[:, :], 1.0)
    p.eye32 = p.sb("c_eye32", [128, 128], F32)
    fw.dma("sp", p.eye32[:, :], p.d["eye"])
    p.eyeb = p.sb("c_eyeb", [128, 128], BF16)
    fw.dma("pool", p.eyeb[:, :], p.d["eye"])
    p.triu = p.sb("c_triu", [128, 128], F32)
    fw.dma("sp", p.triu[:, :], p.d["triu"])
    p.mneg = p.sb("c_mneg", [128, 128], F32)
    fw.dma("sp", p.mneg[:, :], p.d["mneg"])
    p.ps_bc = p.ps("ps_bc", [128, 512], F32)
    sm = p.sb("sm_row", [1, 2], F32)
    fw.dma("sp", sm[0:1, 0:1], p.d["smask"])
    fw.dma("sp", sm[0:1, 1:2], p.d["rmask"])
    p.smrm = p.sb("smrm", [128, 2], F32)
    fw.op("pe", "matmul", p.ps_bc[:, 0:2], p.ones1[0:1, 0:128], sm[0:1, 0:2],
          start=True, stop=True)
    fw.op("dve", "tensor_copy", p.smrm[:, :], p.ps_bc[:, 0:2])
    p.smcol = p.smrm[:, 0:1]
    p.rmcol = p.smrm[:, 1:2]
    consts = [1e-5, 1e-30, -0.5, 1.0 / c.D, 1.0 / c.HD,
              1.0 / (c.INTER // c.G1), 0.5]
    p.cst = p.sb("c_vals", [128, len(consts)], F32)
    p.rows = p.sb("c_rows", [1, 4 * c.T], F32)
    p.rows_off = 0
    for i, v in enumerate(consts):
        fw.op("dve", "memset", p.cst[:, i:i + 1], v, writes=(p.cst,))
        p.nc.const_aps.register(F32, v, p.cst[:, i:i + 1])
    fw.barrier()


def bcast(p, row_ap, w, out=None, name=None, read_key=None):
    fw = p.fw
    if out is None:
        out = p.sb(name or p.u("bc"), [128, w], F32)
    for o in range(0, w, 512):
        ww = min(512, w - o)
        kw = {}
        fw.op("pe", "matmul", p.ps_bc[:, 0:ww], p.ones1[0:1, 0:128],
              row_ap[0:1, o:o + ww], start=True, stop=True,
              reads=(p.ones1[:, :], read_key or row_ap,), writes=(p.ps_bc,))
        fw.op("dve", "tensor_copy", out[:, o:o + ww], p.ps_bc[:, 0:ww])
    return out


def load_fm(p, sb_tile, dram_name, nrows, T, q="sp", cast=False, cols=None):
    c0, cl = cols if cols else (0, T)
    src = p.d[dram_name]
    for cch in range(nrows // 128):
        dst = sb_tile[:, cch * cl:(cch + 1) * cl]
        s = src[cch * 128:(cch + 1) * 128, c0:c0 + cl]
        p.fw.dma("pool" if cast else q, dst, s, tag=f"ld_{dram_name}{cch}")


def mm_k4_stream(p, wname, m, KC, psums, rhs_fn, bufs, tag):
    fw = p.fw
    K4 = KC // 4
    for k4 in range(K4):
        w = bufs[(m * K4 + k4) % len(bufs)]
        fw.dma("pool", w[:, :],
               p.d[wname][(m * K4 + k4) * 128:(m * K4 + k4 + 1) * 128, :],
               tag=tag)
        for j in range(4):
            k = k4 * 4 + j
            for tt, ps in enumerate(psums):
                fw.op("pe", "matmul", ps, w[:, j * 128:(j + 1) * 128],
                      rhs_fn(k, tt), start=(k == 0), stop=(k == KC - 1))


def softplus(p, out, raw, tmp1, tmp2):
    fw = p.fw
    fw.op("act", "activation", tmp1, raw, AF.Abs, bias=0.0, scale=1.0)
    fw.op("act", "activation", tmp1, tmp1, AF.Exp, bias=0.0, scale=-1.0)
    fw.op("act", "activation", tmp1, tmp1, AF.Ln, bias=1.0, scale=1.0)
    fw.op("dve", "tensor_scalar", tmp2, raw, 0.0, None, ALU.max)
    fw.op("dve", "tensor_add", out, tmp2, tmp1)



def ph_ln(p, xbf_src, xn_dst, cast_from=None):
    c, fw = p.cfg, p.fw
    with p.scope():
        xbf = p.sb(p.u("lnxbf"), [128, c.NK * c.T], BF16)
        if cast_from is not None:
            load_fm(p, xbf, cast_from, c.D, c.T, cast=True)
        else:
            load_fm(p, xbf, xbf_src, c.D, c.T, q="sp")
        ps_sum = p.ps(p.u("lnps"), [1, c.T], F32)
        ps_sq = p.ps(p.u("lnpq"), [1, c.T], F32)
        sq = p.sb(p.u("lnsq"), [128, 512], BF16)
        for tt in range(c.T // 512):
            for cc in range(c.NK):
                xsl = xbf[:, cc * c.T + tt * 512: cc * c.T + tt * 512 + 512]
                fw.op("pe", "matmul", ps_sum[0:1, tt * 512:(tt + 1) * 512],
                      p.onescol[:, 0:1], xsl, start=(cc == 0), stop=(cc == c.NK - 1))
                fw.op("act", "activation", sq[:, :], xsl, AF.Square,
                      bias=0.0, scale=1.0, tag="lnsq")
                fw.op("pe", "matmul", ps_sq[0:1, tt * 512:(tt + 1) * 512],
                      p.onescol[:, 0:1], sq[:, :], start=(cc == 0), stop=(cc == c.NK - 1))
        mean = p.sb(p.u("lnm"), [1, c.T], F32)
        var = p.sb(p.u("lnv"), [1, c.T], F32)
        rstd = p.sb(p.u("lnr"), [1, c.T], F32)
        mrs = p.sb(p.u("lnmr"), [1, c.T], F32)
        fw.op("act", "activation", mean[0:1, :], ps_sum[0:1, :], AF.Copy,
              bias=0.0, scale=1.0 / c.D)
        fw.op("dve", "tensor_mul", var[0:1, :], mean[0:1, :], mean[0:1, :])
        fw.op("dve", "scalar_tensor_tensor", var[0:1, :], ps_sq[0:1, :],
              1.0 / c.D, var[0:1, :], ALU.mult, ALU.subtract)
        fw.op("act", "activation", rstd[0:1, :], var[0:1, :], AF.Ln,
              bias=1e-5, scale=1.0)
        fw.op("act", "activation", rstd[0:1, :], rstd[0:1, :], AF.Exp,
              bias=0.0, scale=-0.5)
        fw.op("dve", "tensor_mul", mrs[0:1, :], mean[0:1, :], rstd[0:1, :])
        tmp = p.sb(p.u("lntmp"), [128, 512], F32)
        xnt = p.sb(p.u("lnxn"), [128, 512], BF16)
        for tt in range(c.T // 512):
            rbc = bcast(p, rstd[0:1, tt * 512:(tt + 1) * 512], 512, name=p.u("lnrb"))
            mbc = bcast(p, mrs[0:1, tt * 512:(tt + 1) * 512], 512, name=p.u("lnmb"))
            for cc in range(c.NK):
                xsl = xbf[:, cc * c.T + tt * 512: cc * c.T + tt * 512 + 512]
                fw.op("dve", "tensor_mul", tmp[:, :], xsl, rbc[:, :])
                fw.op("dve", "tensor_sub", xnt[:, :], tmp[:, :], mbc[:, :])
                p.fw.dma("sp", p.d[xn_dst][cc * 128:(cc + 1) * 128,
                                           tt * 512:tt * 512 + 512],
                         xnt[:, :], tag="st_xn")



def ph_l0_proj(p):
    c, fw = p.cfg, p.fw
    parts = getattr(c, "l0proj_parts", "qk,vg,dt")
    with p.scope():
        xns = p.sb("xns", [128, c.NK * c.T], BF16)
        load_fm(p, xns, "xn0", c.D, c.T)
        cosf = p.sb("cosf", [128, c.T], F32)
        sinf = p.sb("sinf", [128, c.T], F32)
        fw.dma("sp", cosf[:, :], p.d["rcos"])
        fw.dma("sp", sinf[:, :], p.d["rsin"])

        wt = [p.sb(f"l0w{i}", [128, 512], BF16) for i in range(3)]
        psq = [p.ps(f"l0psq{i}", [128, 512], F32) for i in range(2)]
        t1 = p.sb("ropet1", [128, 512], F32)
        rot = p.sb("roperot", [128, 512], F32)
        t2 = p.sb("ropet2", [128, 512], F32)
        qout = p.sb("ropeq", [128, 512], BF16)

        def proj_rope(wname, mch, dst):
            for m in range(mch):
                mm_k4_stream(p, wname, m, c.NK,
                             [psq[tt][:, 0:512] for tt in range(c.T // 512)],
                             lambda k, tt: xns[:, k * c.T + tt * 512:k * c.T + tt * 512 + 512],
                             wt, f"{wname}w")
                for tt in range(c.T // 512):
                    if getattr(c, "norope", False):
                        fw.op("dve", "tensor_copy", qout[:, :], psq[tt][:, 0:512])
                        fw.dma("sp", p.d[dst][m * 128:(m + 1) * 128,
                                              tt * 512:tt * 512 + 512],
                               qout[:, :], tag=f"st_{dst}")
                        continue
                    cs = cosf[:, tt * 512:(tt + 1) * 512]
                    sn = sinf[:, tt * 512:(tt + 1) * 512]
                    rv = getattr(c, "ropevar", 0)
                    if rv == 1:
                        fw.op("dve", "tensor_mul", t1[:, :], psq[tt][:, 0:512], cs)
                        fw.op("act", "activation", qout[:, :], t1[:, :], AF.Copy,
                              bias=0.0, scale=1.0)
                        fw.dma("sp", p.d[dst][m * 128:(m + 1) * 128,
                                              tt * 512:tt * 512 + 512],
                               qout[:, :], tag=f"st_{dst}")
                        continue
                    if rv == 2:
                        fw.op("act", "activation", rot[0:64, :], psq[tt][64:128, 0:512],
                              AF.Copy, bias=0.0, scale=-1.0, writes=(rot,),
                              reads=(psq[tt],))
                        fw.op("act", "activation", rot[64:128, :], psq[tt][0:64, 0:512],
                              AF.Copy, bias=0.0, scale=1.0, writes=(rot,),
                              reads=(psq[tt],))
                        fw.op("dve", "tensor_mul", t2[:, :], rot[:, :], sn)
                        fw.op("act", "activation", qout[:, :], t2[:, :], AF.Copy,
                              bias=0.0, scale=1.0)
                        fw.dma("sp", p.d[dst][m * 128:(m + 1) * 128,
                                              tt * 512:tt * 512 + 512],
                               qout[:, :], tag=f"st_{dst}")
                        continue
                    fw.op("dve", "tensor_mul", t1[:, :], psq[tt][:, 0:512], cs)
                    fw.op("dve", "tensor_scalar", rot[0:64, :],
                          psq[tt][64:128, 0:512], -1.0, None, ALU.mult,
                          writes=(rot,), reads=(psq[tt],))
                    fw.op("dve", "tensor_copy", rot[64:128, :],
                          psq[tt][0:64, 0:512], writes=(rot,), reads=(psq[tt],))
                    fw.op("dve", "tensor_mul", t2[:, :], rot[:, :], sn)
                    fw.op("dve", "tensor_add", qout[:, :], t1[:, :], t2[:, :])
                    fw.dma("sp", p.d[dst][m * 128:(m + 1) * 128,
                                          tt * 512:tt * 512 + 512],
                           qout[:, :], tag=f"st_{dst}")

        if "qk" in parts:
            proj_rope("wq4", c.H0, "qsp")
            proj_rope("wk4", c.KVH, "ksp")

        psv = [p.ps(f"l0psv{i}", [128, 512], F32) for i in range(2)]
        vst = [p.sb(f"l0vst{i}", [128, c.NK * 512], BF16) for i in range(2)]
        ev = [p.sb(f"l0ev{i}", [128, 512], BF16) for i in range(2)]

        def proj_tm(wname, width, dst, silu, nbase=0):
            for ni, n0 in enumerate(range(0, width, 512)):
                w5 = min(512, width - n0)
                sbuf = vst[(nbase + ni) % 2]
                for k in range(c.NK):
                    fw.dma("pool", sbuf[:, k * 512:k * 512 + w5],
                           p.d[wname][k * 128:(k + 1) * 128, n0:n0 + w5],
                           tag=f"{wname}s{n0}_{k}")
                for tz in range(c.NZ):
                    pv = psv[tz % 2]
                    e2 = ev[tz % 2]
                    for k in range(c.NK):
                        fw.op("pe", "matmul", pv[:, 0:w5],
                              xns[:, k * c.T + tz * 128:k * c.T + tz * 128 + 128],
                              sbuf[:, k * 512:k * 512 + w5],
                              start=(k == 0), stop=(k == c.NK - 1))
                    if silu:
                        fw.op("act", "activation", e2[:, 0:w5], pv[:, 0:w5],
                              AF.Silu, bias=0.0, scale=1.0)
                    else:
                        fw.op("dve", "tensor_copy", e2[:, 0:w5], pv[:, 0:w5])
                    fw.dma("sp", p.d[dst][tz * 128:(tz + 1) * 128, n0:n0 + w5],
                           e2[:, 0:w5], tag=f"st_{dst}")

        if "vg" in parts:
            proj_tm("wv", c.KD, "vsp", False)
            proj_tm("wg", c.QD, "g0sp", True)

        if "dt" not in parts:
            return
        h0 = c.H0
        wdt = p.sb("l0wdt", [128, c.NK * h0], BF16)
        for k in range(c.NK):
            fw.dma("pool", wdt[:, k * h0:(k + 1) * h0],
                   p.d["wwin"][k * 128:(k + 1) * 128, :], tag=f"wwin{k}")
        dtbrow = p.sb("dtbrow", [1, h0], F32)
        fw.dma("sp", dtbrow[0:1, :], p.d["dtb0"])
        dtb_bc = bcast(p, dtbrow, h0, name="dtb_bc")
        alrow = p.sb("alrow", [1, h0], F32)
        fw.dma("sp", alrow[0:1, :], p.d["alog0"])
        fw.op("act", "activation", alrow[0:1, :], alrow[0:1, :], AF.Exp,
              bias=0.0, scale=1.0)
        a_bc = bcast(p, alrow, h0, name="a_bc")
        psd = p.ps("l0psd", [128, h0], F32)
        psg = p.ps("l0psg", [128, h0], F32)
        raw = p.sb("dtraw", [128, h0], F32)
        s1 = p.sb("dts1", [128, h0], F32)
        s2 = p.sb("dts2", [128, h0], F32)
        dts = p.sb("dts", [128, h0], F32)
        lnd = p.sb("dtln", [128, h0], F32)
        gts = p.sb("dtg", [128, h0], F32)
        Gs = p.sb("dtG", [128, h0], F32)
        b0s = p.sb("dtb0s", [128, h0], F32)
        for tz in range(c.NZ):
            for k in range(c.NK):
                fw.op("pe", "matmul", psd[:, 0:h0],
                      xns[:, k * c.T + tz * 128:k * c.T + tz * 128 + 128],
                      wdt[:, k * h0:(k + 1) * h0],
                      start=(k == 0), stop=(k == c.NK - 1))
            fw.op("dve", "tensor_add", raw[:, :], psd[:, 0:h0], dtb_bc[:, :])
            softplus(p, dts[:, :], raw[:, :], s1[:, :], s2[:, :])
            fw.dma("sp", p.d["dt0"][tz * 128:(tz + 1) * 128, :], dts[:, :],
                   tag="st_dt0")
            fw.op("act", "activation", lnd[:, :], dts[:, :], AF.Ln,
                  bias=1e-30, scale=1.0)
            fw.op("dve", "scalar_tensor_tensor", gts[:, :], dts[:, :], -1.0,
                  a_bc[:, :], ALU.mult, ALU.mult)
            fw.op("pe", "matmul", psg[:, 0:h0], p.triu[:, :], gts[:, :],
                  start=True, stop=True)
            fw.op("dve", "tensor_copy", Gs[:, :], psg[:, 0:h0])
            fw.dma("sp", p.d["g0cum"][tz * 128:(tz + 1) * 128, :], Gs[:, :],
                   tag="st_g0cum")
            fw.op("dve", "tensor_sub", b0s[:, :], lnd[:, :], Gs[:, :])
            fw.dma("sp", p.d["b0"][tz * 128:(tz + 1) * 128, :], b0s[:, :],
                   tag="st_b0")



def ph_l0_cs(p):
    c, fw = p.cfg, p.fw
    with p.scope():
        ks = p.sb("ks", [128, c.KVH * c.T], BF16)
        for kc in range(c.KVH):
            fw.dma("sp", ks[:, kc * c.T:(kc + 1) * c.T],
                   p.d["ksp"][kc * 128:(kc + 1) * 128, :], tag="ld_ks")
        S = p.sb("S0", [128, c.H0 * c.HD], F32)
        fw.op("dve", "memset", S[:, :], 0.0)
        prow = p.sb("prow0", [1, c.H0], F32)
        fw.op("dve", "memset", prow[0:1, :], 1.0)
        vch = p.sb("vch0", [128, c.KD], BF16)
        Gt = p.sb("Gt0", [128, c.H0], F32)
        b0c = p.sb("b0c0", [128, c.H0], F32)
        glrow = p.sb("glrow0", [1, c.H0], F32)
        expb = p.sb("expb0", [128, c.H0], F32)
        scl = p.sb("scl0", [128, c.H0], F32)
        decbc = p.sb("decbc0", [128, c.H0], F32)
        pst = p.ps("pst0", [128, 128], BF16)
        ktt = p.sb("ktt0", [128, c.KVH * 128], BF16)
        kst = [p.sb(f"kst0{i}", [128, 128], BF16) for i in range(2)]
        ps_cs = [p.ps(f"pscs0{i}", [128, c.HD], F32) for i in range(2)]
        ssend = p.sb("ssend0", [128, c.H0 * c.HD], F32)
        for z in range(c.NZ):
            fw.dma("sp", vch[:, :], p.d["vsp"][z * 128:(z + 1) * 128, :],
                   tag="ld_vch")
            fw.dma("sp", Gt[:, :], p.d["g0cum"][z * 128:(z + 1) * 128, :])
            fw.dma("sp", b0c[:, :], p.d["b0"][z * 128:(z + 1) * 128, :])
            fw.dma("pool", glrow[0:1, :], Gt[127:128, :], tag="fl_gl")
            glb = bcast(p, glrow, c.H0, out=expb)
            fw.op("dve", "tensor_tensor", scl[:, :], b0c[:, :], glb[:, :],
                  ALU.add)
            fw.op("act", "activation", scl[:, :], scl[:, :], AF.Exp,
                  bias=0.0, scale=1.0)
            fw.op("act", "activation", glrow[0:1, :], glrow[0:1, :], AF.Exp,
                  bias=0.0, scale=1.0)
            fw.dma("sp", p.d["p0"][z:z + 1, :], prow[0:1, :], tag="st_p0")
            fw.op("dve", "tensor_mul", prow[0:1, :], prow[0:1, :], glrow[0:1, :])
            bcast(p, glrow, c.H0, out=decbc)
            fw.dma("pool", p.d["s0loc"][z * 128:(z + 1) * 128, :], S[:, :],
                   tag="st_s0loc", sem="s0loc")
            for kc in range(c.KVH):
                fw.op("pe", "transpose", pst[:, :],
                      ks[:, kc * c.T + z * 128:kc * c.T + z * 128 + 128],
                      p.eyeb[:, :])
                fw.op("dve", "tensor_copy", ktt[:, kc * 128:(kc + 1) * 128],
                      pst[:, :])
            for h in range(c.H0):
                kc = h // (c.H0 // c.KVH)
                kb = kst[h % 2]
                pc = ps_cs[h % 2]
                fw.op("dve", "tensor_scalar", kb[:, :],
                      ktt[:, kc * 128:(kc + 1) * 128], scl[:, h:h + 1], None,
                      ALU.mult)
                fw.op("pe", "matmul", pc[:, 0:c.HD], kb[:, :],
                      vch[:, kc * c.HD:(kc + 1) * c.HD], start=True, stop=True)
                hsl = S[:, h * c.HD:(h + 1) * c.HD]
                fw.op("dve", "tensor_scalar", hsl, hsl, decbc[:, h:h + 1],
                      None, ALU.mult, reads=(S,), writes=(S,))
                fw.op("dve", "tensor_tensor", hsl, hsl, pc[:, 0:c.HD], ALU.add,
                      reads=(S, pc), writes=(S,))
        fw.op("dve", "tensor_scalar", ssend[:, :], S[:, :], p.smcol, None,
              ALU.mult)
        fw.dma("sp", p.d["s0fin"], ssend[:, :], tag="st_s0fin")
        if c.fused:
            fw.cc("AllReduce", ALU.add, p.cfg.groups, p.d["s0fin"], p.d["s0init"])



def ph_l0_att(p):
    c, fw = p.cfg, p.fw
    with p.scope():
        src = "s0init"
        Sini = p.sb("Sini0", [128, c.H0 * c.HD], F32)
        fw.dma("sp", Sini[:, :], p.d[src], tag="ld_s0init")
        fw.op("dve", "tensor_scalar", Sini[:, :], Sini[:, :], p.rmcol, None,
              ALU.mult)
        ks = p.sb("ks_a", [128, c.KVH * c.T], BF16)
        for kc in range(c.KVH):
            fw.dma("sp", ks[:, kc * c.T:(kc + 1) * c.T],
                   p.d["ksp"][kc * 128:(kc + 1) * 128, :], tag="ld_ksa")
        gnwrow = p.sb("gnwrow", [1, c.HD], F32)
        fw.dma("sp", gnwrow[0:1, :], p.d["gnw"])
        gnw_bc = bcast(p, gnwrow, c.HD, name="gnw_bc")
        qsb = p.sb("qsb", [128, c.H0 * 128], BF16)
        vch = p.sb("vch_a", [128, c.KD], BF16)
        gch = p.sb("gch_a", [128, c.QD], BF16)
        Gt = p.sb("Gt_a", [128, c.H0], F32)
        b0c = p.sb("b0c_a", [128, c.H0], F32)
        expG = p.sb("expG_a", [128, c.H0], F32)
        p0row = p.sb("p0row", [1, c.H0], F32)
        pbc = p.sb("pbc0", [128, c.H0], F32)
        slsb = p.sb("slsb0", [128, c.H0 * c.HD], BF16)
        pst32 = p.ps("pst32", [128, 128], F32)
        gfm = p.sb("gfm0", [c.H0, 128], F32)
        gflat = p.sb("gflat0", [1, c.H0 * 128], F32)
        szb = [p.sb(f"szb{i}", [128, c.HD], BF16) for i in range(2)]
        dm = [p.sb(f"dm0{i}", [128, 128], F32) for i in range(2)]
        dec = [p.sb(f"dec0{i}", [128, 128], F32) for i in range(2)]
        mt = [p.sb(f"mt0{i}", [128, 128], BF16) for i in range(2)]
        t1e = [p.sb(f"t1e{i}", [128, c.HD], F32) for i in range(2)]
        bankA = [p.ps(f"l0bankA{i}", [128, 256], F32) for i in range(2)]
        bankB = [p.ps(f"l0bankB{i}", [128, 256], F32) for i in range(2)]
        ps_sc = [bankA[i][:, 0:128] for i in range(2)]
        ps_a1 = [bankA[i][:, 128:256] for i in range(2)]
        ps_yi = [bankB[i][:, 0:c.HD] for i in range(2)]
        ps_ya = [bankB[i][:, 128:128 + c.HD] for i in range(2)]
        kq = [dict(sc=f"k_sc{i}", a1=f"k_a1{i}", yi=f"k_yi{i}", ya=f"k_ya{i}")
              for i in range(2)]
        ych = p.sb("ych", [128, c.QD], F32)
        ss = p.sb("ss0", [128, c.H0], F32)
        sqj = p.sb("sqj0", [128, 128], BF16)
        rstd = p.sb("rstd0", [128, c.H0], F32)
        yg = p.sb("yg0", [128, c.QD], BF16)
        yg2 = p.sb("yg20", [128, c.QD], BF16)
        pstb = p.ps("pstb0", [128, 128], BF16)
        tb = [p.sb(f"tb0{i}", [128, 128], BF16) for i in range(2)]
        for z in range(c.NZ):
            fw.dma("sp", qsb[:, :].rearrange("pp (c t) -> pp c t", t=128),
                   p.d["qsp"].rearrange("(c pp) t -> pp c t", pp=128)[:, :, z * 128:(z + 1) * 128],
                   tag="ld_qsb")
            fw.dma("sp", vch[:, :], p.d["vsp"][z * 128:(z + 1) * 128, :],
                   tag="ld_vcha")
            fw.dma("sp", gch[:, :], p.d["g0sp"][z * 128:(z + 1) * 128, :],
                   tag="ld_gch")
            fw.dma("sp", Gt[:, :], p.d["g0cum"][z * 128:(z + 1) * 128, :])
            fw.dma("sp", b0c[:, :], p.d["b0"][z * 128:(z + 1) * 128, :])
            fw.dma("sp", p0row[0:1, :], p.d["p0"][z:z + 1, :], tag="ld_p0")
            fw.dma("sp", slsb[:, :], p.d["s0loc"][z * 128:(z + 1) * 128, :],
                   tag="ld_s0loc")
            fw.op("act", "activation", expG[:, :], Gt[:, :], AF.Exp,
                  bias=0.0, scale=1.0)
            bcast(p, p0row, c.H0, out=pbc)
            fw.op("pe", "transpose", pst32[0:c.H0, 0:128], Gt[:, :],
                  p.eye32[:, :], writes=(pst32,))
            fw.op("dve", "tensor_copy", gfm[0:c.H0, :], pst32[0:c.H0, 0:128],
                  reads=(pst32,), writes=(gfm,))
            fw.dma("pool", gflat[0:1, :], gfm[0:c.H0, :], tag="fl_gfm")
            for h in range(c.H0):
                kc = h // (c.H0 // c.KVH)
                i2 = h % 2
                hsl = slice(h * c.HD, (h + 1) * c.HD)
                fw.op("dve", "scalar_tensor_tensor", szb[i2][:, :],
                      Sini[:, hsl], pbc[:, h:h + 1], slsb[:, hsl],
                      ALU.mult, ALU.add, reads=(Sini, pbc, slsb))
                fw.op("pe", "matmul", ps_a1[i2], p.ones1[0:1, 0:128],
                      gflat[0:1, h * 128:(h + 1) * 128], start=True, stop=True,
                      reads=(p.ones1, gflat), writes=(kq[i2]["a1"],))
                fw.op("pe", "matmul", ps_sc[i2],
                      ks[:, kc * c.T + z * 128:kc * c.T + z * 128 + 128],
                      qsb[:, h * 128:(h + 1) * 128], start=True, stop=True,
                      reads=(ks, qsb), writes=(kq[i2]["sc"],))
                fw.op("dve", "tensor_tensor", dm[i2][:, :], ps_a1[i2],
                      p.mneg[:, :], ALU.add,
                      reads=(kq[i2]["a1"], kq[i2]["sc"], p.mneg),
                      writes=(dm[i2],))
                fw.op("act", "activation", dec[i2][:, :], dm[i2][:, :], AF.Exp,
                      bias=b0c[:, h:h + 1], scale=1.0)
                fw.op("dve", "tensor_tensor", mt[i2][:, :], ps_sc[i2],
                      dec[i2][:, :], ALU.mult,
                      reads=(kq[i2]["sc"], dec[i2]), writes=(mt[i2],))
                fw.op("pe", "matmul", ps_yi[i2],
                      qsb[:, h * 128:(h + 1) * 128], szb[i2][:, :],
                      start=True, stop=True,
                      reads=(qsb, szb[i2]), writes=(kq[i2]["yi"],))
                fw.op("pe", "matmul", ps_ya[i2], mt[i2][:, :],
                      vch[:, kc * c.HD:(kc + 1) * c.HD], start=True, stop=True,
                      reads=(mt[i2], vch), writes=(kq[i2]["ya"],))
                fw.op("dve", "tensor_scalar", t1e[i2][:, :], ps_yi[i2],
                      expG[:, h:h + 1], None, ALU.mult,
                      reads=(kq[i2]["yi"], kq[i2]["ya"], expG), writes=(t1e[i2],))
                fw.op("dve", "tensor_tensor", ych[:, hsl], t1e[i2][:, :],
                      ps_ya[i2], ALU.add,
                      reads=(t1e[i2], kq[i2]["ya"]), writes=(ych,))
            for h in range(c.H0):
                hsl = slice(h * c.HD, (h + 1) * c.HD)
                fw.op("act", "activation", sqj[:, :], ych[:, hsl], AF.Square,
                      bias=0.0, scale=1.0, accum_out=ss[:, h:h + 1],
                      reads=(ych,), writes=(sqj, ss))
            fw.op("act", "activation", rstd[:, :], ss[:, :], AF.Ln,
                  bias=1e-5, scale=1.0 / c.HD)
            fw.op("act", "activation", rstd[:, :], rstd[:, :], AF.Exp,
                  bias=0.0, scale=-0.5)
            for h in range(c.H0):
                hsl = slice(h * c.HD, (h + 1) * c.HD)
                fw.op("dve", "scalar_tensor_tensor", yg[:, hsl], ych[:, hsl],
                      rstd[:, h:h + 1], gnw_bc[:, :], ALU.mult, ALU.mult,
                      reads=(ych, rstd, gnw_bc), writes=(yg,))
            for j in range(c.QD // 512):
                fw.op("dve", "tensor_tensor", yg2[:, j * 512:(j + 1) * 512],
                      yg[:, j * 512:(j + 1) * 512],
                      gch[:, j * 512:(j + 1) * 512], ALU.mult,
                      reads=(yg, gch), writes=(yg2,))
            for m in range(c.QD // 128):
                fw.op("pe", "transpose", pstb[:, :], yg2[:, m * 128:(m + 1) * 128],
                      p.eyeb[:, :], reads=(yg2, p.eyeb))
                fw.op("dve", "tensor_copy", tb[m % 2][:, :], pstb[:, :])
                fw.dma("sp", p.d["ygfm"][m * 128:(m + 1) * 128,
                                         z * 128:(z + 1) * 128],
                       tb[m % 2][:, :], tag="st_ygfm")



def ln_rows(p, ps_s, ps_q, ks, kq):
    c, fw = p.cfg, p.fw
    o = 0
    mean = p.rows[0:1, (o + 0) * c.T:(o + 1) * c.T]
    var = p.rows[0:1, (o + 1) * c.T:(o + 2) * c.T]
    rstd = p.rows[0:1, (o + 2) * c.T:(o + 3) * c.T]
    mrs = p.rows[0:1, (o + 3) * c.T:(o + 4) * c.T]
    fw.op("act", "activation", mean, ps_s[0:1, :], AF.Copy,
          bias=0.0, scale=1.0 / c.D, reads=(ks,), writes=(mean,))
    fw.op("dve", "tensor_mul", var, mean, mean)
    fw.op("dve", "scalar_tensor_tensor", var, ps_q[0:1, :],
          1.0 / c.D, var, ALU.mult, ALU.subtract,
          reads=(kq, var), writes=(var,))
    fw.op("act", "activation", rstd, var, AF.Ln, bias=1e-5, scale=1.0)
    fw.op("act", "activation", rstd, rstd, AF.Exp, bias=0.0, scale=-0.5)
    fw.op("dve", "tensor_mul", mrs, mean, rstd)
    return rstd, mrs


def ln_apply(p, rstd, mrs, xbf, xn, store=None):
    c, fw = p.cfg, p.fw
    tmp = p.sb(p.u("lap"), [128, 512], F32)
    for tt in range(c.T // 512):
        rbc = bcast(p, rstd[0:1, tt * 512:(tt + 1) * 512], 512, name=p.u("larb"))
        mbc = bcast(p, mrs[0:1, tt * 512:(tt + 1) * 512], 512, name=p.u("lamb"))
        for cc in range(c.NK):
            sl = slice(cc * c.T + tt * 512, cc * c.T + tt * 512 + 512)
            fw.op("dve", "tensor_mul", tmp[:, :], xbf[:, sl], rbc[:, :],
                  reads=(xbf, rbc), writes=(tmp,))
            fw.op("dve", "tensor_tensor", xn[:, sl], tmp[:, :], mbc[:, :],
                  ALU.subtract, reads=(tmp, mbc), writes=(xn,))
            if store:
                fw.dma("sp", p.d[store][cc * 128:(cc + 1) * 128,
                                        tt * 512:tt * 512 + 512],
                       xn[:, sl], tag="st_" + store, sem="st_" + store)


def ln_apply_dram(p, rstd, mrs, xbf_dram, xn, store=None):
    c, fw = p.cfg, p.fw
    tmp = p.sb(p.u("lap"), [128, 512], F32)
    xt = [p.sb(p.u("lax"), [128, 512], BF16) for _ in range(2)]
    for tt in range(c.T // 512):
        rbc = bcast(p, rstd[0:1, tt * 512:(tt + 1) * 512], 512, name=p.u("larb"))
        mbc = bcast(p, mrs[0:1, tt * 512:(tt + 1) * 512], 512, name=p.u("lamb"))
        for cc in range(c.NK):
            i2 = cc % 2
            fw.dma("sp", xt[i2][:, :],
                   p.d[xbf_dram][cc * 128:(cc + 1) * 128, tt * 512:tt * 512 + 512],
                   tag="ld_" + xbf_dram)
            sl = slice(cc * c.T + tt * 512, cc * c.T + tt * 512 + 512)
            fw.op("dve", "tensor_mul", tmp[:, :], xt[i2][:, :], rbc[:, :],
                  reads=(xt[i2], rbc), writes=(tmp,))
            fw.op("dve", "tensor_tensor", xn[:, sl], tmp[:, :], mbc[:, :],
                  ALU.subtract, reads=(tmp, mbc), writes=(xn,))
            if store:
                fw.dma("sp", p.d[store][cc * 128:(cc + 1) * 128,
                                        tt * 512:tt * 512 + 512],
                       xn[:, sl], tag="st_" + store, sem="st_" + store)


def topk_gate(p, sc, scw, thr, mk, kiter):
    fw = p.fw
    fw.op("dve", "tensor_copy", scw[:, :], sc[:, :])
    for it in range(kiter):
        fw.op("dve", "tensor_reduce", thr[:, :], scw[:, :], AX.X, ALU.max)
        if it < kiter - 1:
            fw.op("dve", "tensor_scalar", mk[:, :], scw[:, :], thr[:, 0:1],
                  None, ALU.is_ge)
            fw.op("dve", "scalar_tensor_tensor", scw[:, :], mk[:, :], -1e30,
                  scw[:, :], ALU.mult, ALU.add)
    fw.op("dve", "tensor_scalar", mk[:, :], sc[:, :], thr[:, 0:1], None,
          ALU.is_ge)



def ph_l0_post(p):
    c, fw = p.cfg, p.fw
    with p.scope():
        rows1 = []
        with p.scope():
            ygr = p.sb("ygr", [128, (c.QD // 128) * c.T], BF16)
            load_fm(p, ygr, "ygfm", c.QD, c.T)
            ps_s = p.ps("ps_s1", [1, c.T], F32)
            ps_q = p.ps("ps_q1", [1, c.T], F32)
            ps_o = [p.ps(f"ps_o{i}", [128, 512], F32) for i in range(2)]
            wt = [p.sb(f"wo_t{i}", [128, 512], BF16) for i in range(3)]
            xold = [p.sb(f"xo{i}", [128, 512], F32) for i in range(2)]
            x1s = [p.sb(f"x1s{i}", [128, 512], F32) for i in range(2)]
            sqt = [p.sb(f"sq1{i}", [128, 512], BF16) for i in range(2)]
            xb1 = [p.sb(f"xb1{i}", [128, 512], BF16) for i in range(2)]
            for m in range(c.NK):
                mm_k4_stream(p, "wo4", m, c.QD // 128,
                             [ps_o[tt][:, 0:512] for tt in range(c.T // 512)],
                             lambda k, tt: ygr[:, k * c.T + tt * 512:k * c.T + tt * 512 + 512],
                             wt, "wo4w")
                for tt in range(c.T // 512):
                    i2 = tt % 2
                    fw.dma("sp", xold[i2][:, :],
                           p.d["x"][m * 128:(m + 1) * 128, tt * 512:tt * 512 + 512],
                           tag="ld_x0")
                    fw.op("dve", "tensor_tensor", x1s[i2][:, :], ps_o[tt][:, 0:512],
                          xold[i2][:, :], ALU.add)
                    fw.dma("sp", p.d["x1"][m * 128:(m + 1) * 128,
                                           tt * 512:tt * 512 + 512],
                           x1s[i2][:, :], tag="st_x1", sem="st_x1")
                    bsl = xb1[i2][:, :]
                    fw.op("act", "activation", bsl, x1s[i2][:, :], AF.Copy,
                          bias=0.0, scale=1.0, reads=(x1s[i2],), writes=(xb1[i2],))
                    fw.dma("sp", p.d["x1bf"][m * 128:(m + 1) * 128,
                                             tt * 512:tt * 512 + 512],
                           bsl, tag="st_x1bf")
                    fw.op("act", "activation", sqt[i2][:, :], bsl, AF.Square,
                          bias=0.0, scale=1.0, reads=(xb1[i2],), writes=(sqt[i2],))
                    fw.op("pe", "matmul", ps_s[0:1, tt * 512:(tt + 1) * 512],
                          p.onescol[:, 0:1], bsl, start=(m == 0),
                          stop=(m == c.NK - 1), reads=(p.onescol, xb1[i2]),
                          writes=("k_lns1",))
                    fw.op("pe", "matmul", ps_q[0:1, tt * 512:(tt + 1) * 512],
                          p.onescol[:, 0:1], sqt[i2][:, :], start=(m == 0),
                          stop=(m == c.NK - 1), reads=(p.onescol, sqt[i2]),
                          writes=("k_lnq1",))
            rows1 = ln_rows(p, ps_s, ps_q, "k_lns1", "k_lnq1")
        xn1 = p.sb("xn1_r", [128, c.NK * c.T], BF16)
        ln_apply_dram(p, rows1[0], rows1[1], "x1bf", xn1)
        rows2 = None
        with p.scope():
            rws = p.sb("rws", [128, c.NK * c.E], BF16)
            for k in range(c.NK):
                fw.dma("pool", rws[:, k * c.E:(k + 1) * c.E],
                       p.d["rw"][k * 128:(k + 1) * 128, :], tag=f"rw{k}")
            rbrow = p.sb("rbrow", [1, c.E], F32)
            fw.dma("sp", rbrow[0:1, :], p.d["rb"])
            rtb_bc = bcast(p, rbrow, c.E, name="rtb_bc")
            ebs = p.sb("ebs", [c.E, c.D], BF16)
            fw.dma("pool", ebs[:, :], p.d["eb"], tag="ld_eb")
            wdall = p.sb("wdall", [c.E, c.T], BF16)
            wsall = p.sb("wsall", [1, c.T], F32)
            ps_l = p.ps("ps_l", [128, 512], F32)
            pst32r = p.ps("pst32r", [128, 128], F32)
            lg = p.sb("lg", [128, c.E], F32)
            nmx = p.sb("nmx", [128, 1], F32)
            ex = p.sb("ex", [128, c.E], F32)
            ssum = p.sb("ssum", [128, 1], F32)
            sc = p.sb("sc_r", [128, c.E], F32)
            scw = p.sb("scw_r", [128, c.E], F32)
            thr = p.sb("thr_r", [128, 1], F32)
            mk = p.sb("mk_r", [128, c.E], F32)
            wd = p.sb("wd_r", [128, c.E], F32)
            ws = p.sb("ws_r", [128, 1], F32)
            for tz in range(c.NZ):
                for k in range(c.NK):
                    fw.op("pe", "matmul", ps_l[:, 0:c.E],
                          xn1[:, k * c.T + tz * 128:k * c.T + tz * 128 + 128],
                          rws[:, k * c.E:(k + 1) * c.E],
                          start=(k == 0), stop=(k == c.NK - 1))
                fw.op("dve", "tensor_tensor", lg[:, :], ps_l[:, 0:c.E],
                      rtb_bc[:, :], ALU.add)
                fw.op("dve", "tensor_reduce", nmx[:, :], lg[:, :], AX.X,
                      ALU.max, negate=True)
                fw.op("act", "activation", ex[:, :], lg[:, :], AF.Exp,
                      bias=nmx[:, 0:1], scale=1.0)
                fw.op("dve", "tensor_reduce", ssum[:, :], ex[:, :], AX.X, ALU.add)
                fw.op("dve", "reciprocal", ssum[:, :], ssum[:, :])
                fw.op("dve", "tensor_scalar", sc[:, :], ex[:, :], ssum[:, 0:1],
                      None, ALU.mult)
                topk_gate(p, sc, scw, thr, mk, c.KTOP)
                fw.op("dve", "tensor_mul", wd[:, :], mk[:, :], sc[:, :])
                fw.op("dve", "tensor_reduce", ws[:, :], wd[:, :], AX.X, ALU.add)
                fw.dma("pool", wsall[0:1, tz * 128:(tz + 1) * 128], ws[:, 0:1],
                       tag="fl_ws")
                fw.op("pe", "transpose", pst32r[0:c.E, 0:128], wd[:, :],
                      p.eye32[:, :], reads=(wd, p.eye32), writes=("k_wdt",))
                fw.op("dve", "tensor_copy", wdall[0:c.E, tz * 128:(tz + 1) * 128],
                      pst32r[0:c.E, 0:128], reads=("k_wdt",), writes=(wdall,))
            wsb = p.sb("wsb", [128, c.T], F32)
            for tt in range(c.T // 512):
                bcast(p, wsall[0:1, tt * 512:(tt + 1) * 512], 512,
                      out=wsb[:, tt * 512:(tt + 1) * 512])
            ps_b = p.ps("ps_b", [128, 512], F32)
            ps_s2 = p.ps("ps_s2", [1, c.T], F32)
            ps_q2 = p.ps("ps_q2", [1, c.T], F32)
            tsb = [p.sb(f"tsb{i}", [128, 512], F32) for i in range(2)]
            xo1 = [p.sb(f"xo1{i}", [128, 512], F32) for i in range(2)]
            x2s = [p.sb(f"x2s{i}", [128, 512], F32) for i in range(2)]
            sq2 = [p.sb(f"sq2{i}", [128, 512], BF16) for i in range(2)]
            xb2 = [p.sb(f"xb2{i}", [128, 512], BF16) for i in range(2)]
            for m in range(c.NK):
                for tt in range(c.T // 512):
                    i2 = tt % 2
                    fw.op("pe", "matmul", ps_b[:, 0:512],
                          ebs[:, m * 128:(m + 1) * 128],
                          wdall[:, tt * 512:(tt + 1) * 512],
                          start=True, stop=True)
                    fw.op("dve", "tensor_mul", tsb[i2][:, :],
                          xn1[:, m * c.T + tt * 512:m * c.T + tt * 512 + 512],
                          wsb[:, tt * 512:(tt + 1) * 512],
                          reads=(xn1, wsb), writes=(tsb[i2],))
                    fw.op("dve", "tensor_tensor", tsb[i2][:, :], tsb[i2][:, :],
                          ps_b[:, 0:512], ALU.add)
                    fw.dma("sp", xo1[i2][:, :],
                           p.d["x1"][m * 128:(m + 1) * 128, tt * 512:tt * 512 + 512],
                           tag="ld_x1")
                    fw.op("dve", "tensor_tensor", x2s[i2][:, :], tsb[i2][:, :],
                          xo1[i2][:, :], ALU.add)
                    fw.dma("sp", p.d["x2"][m * 128:(m + 1) * 128,
                                           tt * 512:tt * 512 + 512],
                           x2s[i2][:, :], tag="st_x2", sem="st_x2")
                    bsl = xb2[i2][:, :]
                    fw.op("act", "activation", bsl, x2s[i2][:, :], AF.Copy,
                          bias=0.0, scale=1.0, reads=(x2s[i2],), writes=(xb2[i2],))
                    fw.dma("sp", p.d["x2bf"][m * 128:(m + 1) * 128,
                                             tt * 512:tt * 512 + 512],
                           bsl, tag="st_x2bf")
                    fw.op("act", "activation", sq2[i2][:, :], bsl, AF.Square,
                          bias=0.0, scale=1.0, reads=(xb2[i2],), writes=(sq2[i2],))
                    fw.op("pe", "matmul", ps_s2[0:1, tt * 512:(tt + 1) * 512],
                          p.onescol[:, 0:1], bsl, start=(m == 0),
                          stop=(m == c.NK - 1), reads=(p.onescol, xb2[i2]),
                          writes=("k_lns2",))
                    fw.op("pe", "matmul", ps_q2[0:1, tt * 512:(tt + 1) * 512],
                          p.onescol[:, 0:1], sq2[i2][:, :], start=(m == 0),
                          stop=(m == c.NK - 1), reads=(p.onescol, sq2[i2]),
                          writes=("k_lnq2",))
            rows2 = ln_rows(p, ps_s2, ps_q2, "k_lns2", "k_lnq2")
        xn2 = p.sb("xn2_r", [128, c.NK * c.T], BF16)
        ln_apply_dram(p, rows2[0], rows2[1], "x2bf", xn2, store="xn2")



def ph_l1_inproj(p):
    c, fw = p.cfg, p.fw
    with p.scope():
        xns = p.sb("xns1", [128, c.NK * c.T], BF16)
        load_fm(p, xns, "xn2", c.D, c.T)
        gws = [p.sb(f"gws{i}", [128, c.NK * 512], BF16) for i in range(2)]
        psv = [p.ps(f"l1psv{i}", [128, 512], F32) for i in range(2)]
        ev = [p.sb(f"l1ev{i}", [128, 512], BF16) for i in range(2)]
        for n in range(c.INTER // 512):
            sbuf = gws[n % 2]
            for k in range(c.NK):
                fw.dma("pool", sbuf[:, k * 512:(k + 1) * 512],
                       p.d["ipj"][k * 128:(k + 1) * 128, n * 512:(n + 1) * 512],
                       tag=f"ipg{n}_{k}")
            for tz in range(c.NZ):
                i2 = tz % 2
                for k in range(c.NK):
                    fw.op("pe", "matmul", psv[i2][:, 0:512],
                          xns[:, k * c.T + tz * 128:k * c.T + tz * 128 + 128],
                          sbuf[:, k * 512:(k + 1) * 512],
                          start=(k == 0), stop=(k == c.NK - 1))
                fw.op("act", "activation", ev[i2][:, :], psv[i2][:, 0:512],
                      AF.Silu, bias=0.0, scale=1.0)
                fw.dma("sp", p.d["ipg"][tz * 128:(tz + 1) * 128,
                                        n * 512:n * 512 + 512],
                       ev[i2][:, :], tag="st_ipg")
        wt = [p.sb(f"l1w{i}", [128, 512], BF16) for i in range(3)]
        ps2 = [p.ps(f"l1ps2{i}", [128, 512], F32) for i in range(2)]
        ev2 = [p.sb(f"l1eh{i}", [128, 512], BF16) for i in range(2)]
        nc3 = c.CONV // 128
        hal = p.sb("halsb", [128, nc3 * 3], F32)
        for m in range(c.CONV // 128):
            mm_k4_stream(p, "iph4", m, c.NK,
                         [ps2[tt][:, 0:512] for tt in range(c.T // 512)],
                         lambda k, tt: xns[:, k * c.T + tt * 512:k * c.T + tt * 512 + 512],
                         wt, "iph4w")
            for tt in range(c.T // 512):
                fw.op("dve", "tensor_copy", ev2[tt % 2][:, :], ps2[tt][:, 0:512])
                fw.dma("sp", p.d["iph"][m * 128:(m + 1) * 128,
                                        tt * 512:tt * 512 + 512],
                       ev2[tt % 2][:, :], tag="st_iph")
                if tt == c.T // 512 - 1:
                    fw.op("dve", "tensor_copy", hal[:, m * 3:(m + 1) * 3],
                          ev2[tt % 2][:, 509:512], reads=(ev2[tt % 2],),
                          writes=(hal,))
        h1 = c.H1
        wdt = [p.sb(f"l1wd{i}", [128, h1], BF16) for i in range(2)]
        psd = p.ps("l1psd", [128, 512], F32)
        dtb1c = p.sb("dtb1c", [h1, 1], F32)
        fw.dma("sp", dtb1c[:, :], p.d["dtb1"])
        acol = p.sb("acol1", [h1, 1], F32)
        fw.dma("sp", acol[:, :], p.d["alog1"])
        fw.op("act", "activation", acol[:, :], acol[:, :], AF.Exp,
              bias=0.0, scale=1.0)
        raw = p.sb("l1raw", [h1, 512], F32)
        s1 = p.sb("l1s1", [h1, 512], F32)
        s2 = p.sb("l1s2", [h1, 512], F32)
        dts = p.sb("l1dt", [h1, 512], F32)
        lnd = p.sb("l1ln", [h1, 512], F32)
        adt = p.sb("l1ad", [h1, 512], F32)
        for tt in range(c.T // 512):
            for k in range(c.NK):
                w = wdt[k % 2]
                fw.dma("pool", w[:, 0:h1],
                       p.d["ipj"][k * 128:(k + 1) * 128,
                                  c.INTER + c.CONV:c.INTER + c.CONV + h1],
                       tag=f"ipd{tt}_{k}")
                fw.op("pe", "matmul", psd[0:h1, 0:512], w[:, 0:h1],
                      xns[:, k * c.T + tt * 512:k * c.T + tt * 512 + 512],
                      start=(k == 0), stop=(k == c.NK - 1))
            fw.op("dve", "tensor_scalar", raw[:, :], psd[0:h1, 0:512],
                  dtb1c[:, 0:1], None, ALU.add, reads=(psd, dtb1c), writes=(raw,))
            softplus(p, dts[:, :], raw[:, :], s1[:, :], s2[:, :])
            fw.op("dve", "tensor_scalar_min", dts[:, :], dts[:, :], 100.0)
            fw.dma("sp", p.d["dt1"][:, tt * 512:(tt + 1) * 512], dts[:, :],
                   tag="st_dt1", sem="st_dt1")
            fw.op("act", "activation", lnd[:, :], dts[:, :], AF.Ln,
                  bias=1e-30, scale=1.0)
            fw.dma("sp", p.d["lndt1"][:, tt * 512:(tt + 1) * 512], lnd[:, :],
                   tag="st_lndt1", sem="st_lndt1")
            fw.op("dve", "tensor_scalar", adt[:, :], dts[:, :], acol[:, 0:1],
                  -1.0, ALU.mult, ALU.mult, reads=(dts, acol), writes=(adt,))
            fw.dma("sp", p.d["adt1"][:, tt * 512:(tt + 1) * 512], adt[:, :],
                   tag="st_adt1", sem="st_adt1")
        fw.op("dve", "tensor_scalar", hal[:, :], hal[:, :], p.smcol, None,
              ALU.mult)
        fw.dma("sp", p.d["halo"], hal[:, :], tag="st_halo")
        if c.fused:
            fw.cc("AllReduce", ALU.add, c.groups, p.d["halo"], p.d["haloin"])



def ph_l1_convcs(p):
    c, fw = p.cfg, p.fw
    h1 = c.H1
    nc3 = c.CONV // 128
    nhs = c.INTER // 128
    with p.scope():
        dtr = p.sb("dt1r", [h1, c.T], F32)
        fw.dma("sp", dtr[:, :], p.d["dt1"])
        lndr = p.sb("lnd1r", [h1, c.T], F32)
        fw.dma("sp", lndr[:, :], p.d["lndt1"])
        adtr = p.sb("adt1r", [h1, c.T], F32)
        fw.dma("sp", adtr[:, :], p.d["adt1"])
        cwsb = p.sb("cwsb", [128, nc3 * 4], F32)
        fw.dma("sp", cwsb[:, :], p.d["cw"])
        cbsb = p.sb("cbsb", [128, nc3], F32)
        fw.dma("sp", cbsb[:, :], p.d["cb"])
        halb = p.sb("halb", [128, nc3 * 3], BF16)
        hali = p.sb("hali", [128, nc3 * 3], F32)
        fw.dma("sp", hali[:, :], p.d["haloin"], tag="ld_haloin")
        fw.op("dve", "tensor_scalar", halb[:, :], hali[:, :], p.rmcol, None,
              ALU.mult)
        S = p.sb("S1", [128, h1 * c.P1], F32)
        fw.op("dve", "memset", S[:, :], 0.0)
        prow = p.sb("prow1", [1, h1], F32)
        fw.op("dve", "memset", prow[0:1, :], 1.0)
        pst32 = p.ps("l1pst32", [128, 128], F32)
        ps_ac = p.ps("l1psac", [128, 128], F32)
        pstb = p.ps("l1pstb", [128, 128], BF16)
        adtT = p.sb("adtT", [128, h1], F32)
        lndT = p.sb("lndT", [128, h1], F32)
        acum = p.sb("acum1", [128, h1], F32)
        b1s = p.sb("b1s", [128, h1], F32)
        expb1 = p.sb("expb1", [128, h1], F32)
        acfm = p.sb("acfm", [h1, 128], F32)
        acflr = p.sb("acflr", [1, h1 * 128], F32)
        decrow = p.sb("decrow1", [1, h1], F32)
        decbc = p.sb("decbc1", [128, h1], F32)
        scl = p.sb("scl1", [128, h1], F32)
        hb = [p.sb(f"hb{i}", [128, 131], BF16) for i in range(2)]
        acc = [p.sb(f"cacc{i}", [128, 128], F32) for i in range(2)]
        hstm = p.sb("hstm_sb", [128, c.INTER], BF16)
        bcz = p.sb("bcz", [128, (nc3 - nhs) * 128], BF16)
        bt = p.sb("bt1", [128, c.G1 * 128], BF16)
        bsc = [p.sb(f"bsc{i}", [128, 128], BF16) for i in range(2)]
        ps_cs = [p.ps(f"l1pscs{i}", [128, c.P1], F32) for i in range(2)]
        ssend = p.sb("ssend1", [128, h1 * c.P1], F32)
        for z in range(c.NZ):
            fw.op("pe", "transpose", pst32[:, 0:h1],
                  adtr[0:h1, z * 128:(z + 1) * 128], p.eye32[0:h1, 0:h1],
                  reads=(adtr, p.eye32), writes=(pst32,))
            fw.op("dve", "tensor_copy", adtT[:, :], pst32[:, 0:h1],
                  reads=(pst32,), writes=(adtT,))
            fw.op("pe", "matmul", ps_ac[:, 0:h1], p.triu[:, :], adtT[:, :],
                  start=True, stop=True)
            fw.op("dve", "tensor_copy", acum[:, :], ps_ac[:, 0:h1])
            fw.dma("sp", p.d["acumt"][z * 128:(z + 1) * 128, :], acum[:, :],
                   tag="st_acumt", sem="st_acumt")
            fw.op("pe", "transpose", pst32[:, 0:h1],
                  lndr[0:h1, z * 128:(z + 1) * 128], p.eye32[0:h1, 0:h1],
                  reads=(lndr, p.eye32), writes=(pst32,))
            fw.op("dve", "tensor_copy", lndT[:, :], pst32[:, 0:h1],
                  reads=(pst32,), writes=(lndT,))
            fw.op("dve", "tensor_sub", b1s[:, :], lndT[:, :], acum[:, :])
            fw.dma("sp", p.d["b1sp"][z * 128:(z + 1) * 128, :], b1s[:, :],
                   tag="st_b1sp", sem="st_b1sp")

            fw.op("pe", "transpose", pst32[0:h1, 0:128], acum[:, 0:h1],
                  p.eye32[:, :], reads=(acum, p.eye32), writes=(pst32,))
            fw.op("dve", "tensor_copy", acfm[0:h1, :], pst32[0:h1, :],
                  reads=(pst32,), writes=(acfm,))
            fw.dma("pool", acflr[0:1, :], acfm[0:h1, :], tag="fl_acfm")
            fw.dma("sp", p.d["acfl"][z:z + 1, :], acflr[0:1, :], tag="st_acfl",
                   sem="st_acfl")
            fw.dma("pool", decrow[0:1, :], acum[127:128, 0:h1], tag="fl_dec")
            glb1 = bcast(p, decrow, h1, out=expb1)
            fw.op("dve", "tensor_tensor", scl[:, :], b1s[:, :], glb1[:, :],
                  ALU.add)
            fw.op("act", "activation", scl[:, :], scl[:, :], AF.Exp,
                  bias=0.0, scale=1.0)
            fw.op("act", "activation", decrow[0:1, :], decrow[0:1, :], AF.Exp,
                  bias=0.0, scale=1.0)
            fw.dma("sp", p.d["p1"][z:z + 1, :], prow[0:1, :], tag="st_p1",
                   sem="st_p1")
            fw.op("dve", "tensor_mul", prow[0:1, :], prow[0:1, :], decrow[0:1, :])
            bcast(p, decrow, h1, out=decbc)
            for pc in range(nc3):
                i2 = pc % 2
                if z == 0:
                    fw.op("dve", "tensor_copy", hb[i2][:, 0:3],
                          halb[:, pc * 3:(pc + 1) * 3], reads=(halb,),
                          writes=(hb[i2],))
                    fw.dma("sp", hb[i2][:, 3:131],
                           p.d["iph"][pc * 128:(pc + 1) * 128, 0:128],
                           tag="ld_hb")
                else:
                    fw.dma("sp", hb[i2][:, 0:131],
                           p.d["iph"][pc * 128:(pc + 1) * 128,
                                      z * 128 - 3:z * 128 + 128],
                           tag="ld_hb")
                fw.op("dve", "tensor_scalar", acc[i2][:, :], hb[i2][:, 0:128],
                      cwsb[:, pc * 4:pc * 4 + 1], None, ALU.mult,
                      reads=(hb[i2], cwsb), writes=(acc[i2],))
                fw.op("dve", "scalar_tensor_tensor", acc[i2][:, :],
                      hb[i2][:, 1:129], cwsb[:, pc * 4 + 1:pc * 4 + 2],
                      acc[i2][:, :], ALU.mult, ALU.add,
                      reads=(hb[i2], cwsb, acc[i2]), writes=(acc[i2],))
                fw.op("dve", "scalar_tensor_tensor", acc[i2][:, :],
                      hb[i2][:, 2:130], cwsb[:, pc * 4 + 2:pc * 4 + 3],
                      acc[i2][:, :], ALU.mult, ALU.add,
                      reads=(hb[i2], cwsb, acc[i2]), writes=(acc[i2],))
                fw.op("dve", "scalar_tensor_tensor", acc[i2][:, :],
                      hb[i2][:, 3:131], cwsb[:, pc * 4 + 3:pc * 4 + 4],
                      acc[i2][:, :], ALU.mult, ALU.add,
                      reads=(hb[i2], cwsb, acc[i2]), writes=(acc[i2],))
                if pc < nhs:
                    fw.op("act", "activation", hb[i2][:, 0:128], acc[i2][:, :],
                          AF.Silu, bias=cbsb[:, pc:pc + 1], scale=1.0,
                          reads=(acc[i2], cbsb), writes=(hb[i2],))
                    fw.op("pe", "transpose", pstb[:, :], hb[i2][:, 0:128],
                          p.eyeb[:, :], reads=(hb[i2], p.eyeb), writes=(pstb,))
                    fw.op("dve", "tensor_copy", hstm[:, pc * 128:(pc + 1) * 128],
                          pstb[:, :], reads=(pstb,), writes=(hstm,))
                else:
                    fw.op("act", "activation",
                          bcz[:, (pc - nhs) * 128:(pc - nhs + 1) * 128],
                          acc[i2][:, :], AF.Silu, bias=cbsb[:, pc:pc + 1],
                          scale=1.0, reads=(acc[i2], cbsb), writes=(bcz,))
            fw.dma("sp", p.d["hstm"][z * 128:(z + 1) * 128, :], hstm[:, :],
                   tag="st_hstm", sem="st_hstm")
            for bcid in range(nc3 - nhs):
                fw.dma("sp", p.d["bcfm"][bcid * 128:(bcid + 1) * 128,
                                         z * 128:(z + 1) * 128],
                       bcz[:, bcid * 128:(bcid + 1) * 128], tag="st_bcfm",
                       sem="st_bcfm")
            for g in range(c.G1):
                fw.op("pe", "transpose", pstb[:, :], bcz[:, g * 128:(g + 1) * 128],
                      p.eyeb[:, :], reads=(bcz, p.eyeb), writes=(pstb,))
                fw.op("dve", "tensor_copy", bt[:, g * 128:(g + 1) * 128],
                      pstb[:, :], reads=(pstb,), writes=(bt,))
            fw.dma("pool", p.d["s1loc"][z * c.N1:(z + 1) * c.N1, :], S[:, :],
                   tag="st_s1loc", sem="s1loc")
            for h in range(h1):
                g = h // (h1 // c.G1)
                i2 = h % 2
                fw.op("dve", "tensor_scalar", bsc[i2][:, :],
                      bt[:, g * 128:(g + 1) * 128], scl[:, h:h + 1], None,
                      ALU.mult, reads=(bt, scl), writes=(bsc[i2],))
                fw.op("pe", "matmul", ps_cs[i2][:, 0:c.P1], bsc[i2][:, :],
                      hstm[:, h * c.P1:(h + 1) * c.P1], start=True, stop=True,
                      reads=(bsc[i2], hstm), writes=(f"k_cs{i2}",))
                hsl = S[:, h * c.P1:(h + 1) * c.P1]
                fw.op("dve", "tensor_scalar", hsl, hsl, decbc[:, h:h + 1],
                      None, ALU.mult, reads=(S, decbc), writes=(S,))
                fw.op("dve", "tensor_tensor", hsl, hsl, ps_cs[i2][:, 0:c.P1],
                      ALU.add, reads=(S, f"k_cs{i2}"), writes=(S,))
        fw.op("dve", "tensor_scalar", ssend[:, :], S[:, :], p.smcol, None,
              ALU.mult)
        fw.dma("sp", p.d["s1fin"], ssend[:, :], tag="st_s1fin")
        if c.fused:
            fw.cc("AllReduce", ALU.add, c.groups, p.d["s1fin"], p.d["s1init"])



def ph_l1_ssd(p):
    c, fw = p.cfg, p.fw
    h1 = c.H1
    nhs = c.INTER // 128
    nbc = 2 * c.G1 * c.N1 // 128
    with p.scope():
        Sini = p.sb("Sini1", [128, h1 * c.P1], BF16)
        sini32 = p.sb("sini32", [128, 512], F32)
        for o in range(0, h1 * c.P1, 512):
            fw.dma("sp", sini32[:, :], p.d["s1init"][:, o:o + 512],
                   tag="ld_s1init")
            fw.op("dve", "tensor_scalar", Sini[:, o:o + 512], sini32[:, :],
                  p.rmcol, None, ALU.mult, reads=(sini32, p.smrm),
                  writes=(Sini,))
        nwrow = p.sb("nwrow", [1, c.INTER], F32)
        fw.dma("sp", nwrow[0:1, :], p.d["nw"])
        nw_bc = p.sb("nw_bc", [128, c.INTER], BF16)
        for o in range(0, c.INTER, 512):
            fw.op("pe", "matmul", p.ps_bc[:, 0:512], p.ones1[0:1, 0:128],
                  nwrow[0:1, o:o + 512], start=True, stop=True,
                  reads=(p.ones1, nwrow), writes=(p.ps_bc,))
            fw.op("dve", "tensor_copy", nw_bc[:, o:o + 512], p.ps_bc[:, 0:512],
                  reads=(p.ps_bc,), writes=(nw_bc,))
        dvrow = p.sb("dvrow", [1, h1], F32)
        fw.dma("sp", dvrow[0:1, :], p.d["dvec"])
        dcols = bcast(p, dvrow, h1, name="dcols")
        slsb = p.sb("slsb1", [128, h1 * c.P1], BF16)
        hsz = p.sb("hsz", [128, c.INTER], BF16)
        bcch = p.sb("bcch", [128, nbc * 128], BF16)
        acumz = p.sb("acumz", [128, h1], F32)
        b1z = p.sb("b1z", [128, h1], F32)
        expA = p.sb("expA1", [128, h1], F32)
        p1row = p.sb("p1row", [1, h1], F32)
        pbc1 = p.sb("pbc1", [128, h1], F32)
        segw = min(4096, h1 * 128)
        acflr = p.sb("acflr_s", [1, segw], F32)
        szb = p.sb("szb1x", [128, h1 * c.P1], BF16)
        cbg = [p.sb(f"cbg{i}", [128, 128], F32) for i in range(2)]
        ps_cb = [p.ps(f"pscb{i}", [128, 128], F32) for i in range(2)]
        bankA1 = [p.ps(f"l1bankA{i}", [128, 128], F32) for i in range(2)]
        bankY = [p.ps(f"l1bankY{i}", [128, 128], F32) for i in range(2)]
        ps_a1 = [bankA1[i][:, 0:128] for i in range(2)]
        ps_yd = [bankY[i][:, 0:c.P1] for i in range(2)]
        ps_yo = [bankY[i][:, c.P1:2 * c.P1] for i in range(2)]
        dm = [p.sb(f"dm1{i}", [128, 128], F32) for i in range(2)]
        dec = dm
        mt = [p.sb(f"mt1{i}", [128, 128], BF16) for i in range(2)]
        u1 = [p.sb(f"u1_{i}", [128, c.P1], F32) for i in range(2)]
        u2 = u1
        ych = p.sb("ych1", [128, c.INTER], BF16)
        gch = p.sb("gch1", [128, c.INTER], BF16)
        gg = slsb
        ss = p.sb("ss1", [128, c.G1], F32)
        sqj = p.sb("sqj1", [128, c.INTER // c.G1], BF16)
        rstd = p.sb("rstd1", [128, c.G1], F32)
        ggn = ych
        pstb = p.ps("pstb1", [128, 128], BF16)
        tb = [p.sb(f"tb1{i}", [128, 128], BF16) for i in range(2)]
        for z in range(c.NZ):
            fw.dma("sp", slsb[:, :], p.d["s1loc"][z * c.N1:(z + 1) * c.N1, :],
                   tag="ld_s1loc")
            fw.dma("sp", hsz[:, :], p.d["hstm"][z * 128:(z + 1) * 128, :],
                   tag="ld_hstm")
            fw.dma("sp", bcch[:, :].rearrange("pp (cc t) -> pp cc t", t=128),
                   p.d["bcfm"].rearrange("(cc pp) t -> pp cc t", pp=128)[:, :, z * 128:(z + 1) * 128],
                   tag="ld_bcfm")
            fw.dma("sp", gch[:, :], p.d["ipg"][z * 128:(z + 1) * 128, :],
                   tag="ld_ipg")
            fw.dma("sp", acumz[:, :], p.d["acumt"][z * 128:(z + 1) * 128, :])
            fw.dma("sp", b1z[:, :], p.d["b1sp"][z * 128:(z + 1) * 128, :])
            fw.dma("sp", p1row[0:1, :], p.d["p1"][z:z + 1, :], tag="ld_p1")
            acfl_seg = [-1]
            fw.op("act", "activation", expA[:, :], acumz[:, :], AF.Exp,
                  bias=0.0, scale=1.0)
            bcast(p, p1row, h1, out=pbc1)
            for h in range(h1):
                hsl = slice(h * c.P1, (h + 1) * c.P1)
                fw.op("dve", "scalar_tensor_tensor", szb[:, hsl],
                      Sini[:, hsl], pbc1[:, h:h + 1], slsb[:, hsl],
                      ALU.mult, ALU.add, reads=(Sini, pbc1, slsb), writes=(szb,))
            for g in range(c.G1):
                ig = g % 2
                fw.op("pe", "matmul", ps_cb[ig][:, 0:128],
                      bcch[:, g * 128:(g + 1) * 128],
                      bcch[:, (c.G1 + g) * 128:(c.G1 + g + 1) * 128],
                      start=True, stop=True, reads=(bcch,),
                      writes=(f"k_cb{ig}",))
                fw.op("dve", "tensor_copy", cbg[ig][:, :], ps_cb[ig][:, 0:128],
                      reads=(f"k_cb{ig}",), writes=(cbg[ig],))
                for hh in range(h1 // c.G1):
                    h = g * (h1 // c.G1) + hh
                    i2 = h % 2
                    hsl = slice(h * c.P1, (h + 1) * c.P1)
                    seg = (h * 128) // segw
                    if seg != acfl_seg[0]:
                        fw.dma("sp", acflr[0:1, :],
                               p.d["acfl"][z:z + 1, seg * segw:(seg + 1) * segw],
                               tag="ld_acfl")
                        acfl_seg[0] = seg
                    ho = (h * 128) % segw
                    fw.op("pe", "matmul", ps_a1[i2], p.ones1[0:1, 0:128],
                          acflr[0:1, ho:ho + 128], start=True,
                          stop=True, reads=(p.ones1, acflr),
                          writes=(f"k1_a1{i2}",))
                    fw.op("dve", "tensor_tensor", dm[i2][:, :], ps_a1[i2],
                          p.mneg[:, :], ALU.add, reads=(f"k1_a1{i2}", p.mneg),
                          writes=(dm[i2],))
                    fw.op("act", "activation", dec[i2][:, :], dm[i2][:, :],
                          AF.Exp, bias=b1z[:, h:h + 1], scale=1.0)
                    fw.op("dve", "tensor_tensor", mt[i2][:, :], cbg[ig][:, :],
                          dec[i2][:, :], ALU.mult, reads=(cbg[ig], dec[i2]),
                          writes=(mt[i2],))
                    fw.op("pe", "matmul", ps_yd[i2], mt[i2][:, :],
                          hsz[:, hsl], start=True, stop=True,
                          reads=(mt[i2], hsz), writes=(f"k1_yd{i2}",))
                    fw.op("pe", "matmul", ps_yo[i2],
                          bcch[:, (c.G1 + g) * 128:(c.G1 + g + 1) * 128],
                          szb[:, hsl], start=True, stop=True,
                          reads=(bcch, szb), writes=(f"k1_yo{i2}",))
                    fw.op("dve", "tensor_scalar", u1[i2][:, :], ps_yo[i2],
                          expA[:, h:h + 1], None, ALU.mult,
                          reads=(f"k1_yo{i2}", f"k1_yd{i2}", expA),
                          writes=(u1[i2],))
                    fw.op("dve", "tensor_tensor", u2[i2][:, :], u1[i2][:, :],
                          ps_yd[i2], ALU.add, reads=(u1[i2], f"k1_yd{i2}"),
                          writes=(u2[i2],))
                    fw.op("dve", "scalar_tensor_tensor", ych[:, hsl],
                          hsz[:, hsl], dcols[:, h:h + 1], u2[i2][:, :],
                          ALU.mult, ALU.add, reads=(hsz, dcols, u2[i2]),
                          writes=(ych,))
            for j in range(c.INTER // 512):
                fw.op("dve", "tensor_tensor", gg[:, j * 512:(j + 1) * 512],
                      ych[:, j * 512:(j + 1) * 512],
                      gch[:, j * 512:(j + 1) * 512], ALU.mult,
                      reads=(ych, gch), writes=(gg,))
            gsz = c.INTER // c.G1
            for g in range(c.G1):
                fw.op("act", "activation", sqj[:, :], gg[:, g * gsz:(g + 1) * gsz],
                      AF.Square, bias=0.0, scale=1.0,
                      accum_out=ss[:, g:g + 1], reads=(gg,), writes=(sqj, ss))
            fw.op("act", "activation", rstd[:, :], ss[:, :], AF.Ln,
                  bias=1e-5, scale=1.0 / gsz)
            fw.op("act", "activation", rstd[:, :], rstd[:, :], AF.Exp,
                  bias=0.0, scale=-0.5)
            for g in range(c.G1):
                fw.op("dve", "scalar_tensor_tensor", ggn[:, g * gsz:(g + 1) * gsz],
                      gg[:, g * gsz:(g + 1) * gsz], rstd[:, g:g + 1],
                      nw_bc[:, g * gsz:(g + 1) * gsz], ALU.mult, ALU.mult,
                      reads=(gg, rstd, nw_bc), writes=(ggn,))
            for m in range(c.INTER // 128):
                fw.op("pe", "transpose", pstb[:, :], ggn[:, m * 128:(m + 1) * 128],
                      p.eyeb[:, :], reads=(ggn, p.eyeb), writes=("k_ggT",))
                fw.op("dve", "tensor_copy", tb[m % 2][:, :], pstb[:, :],
                      reads=("k_ggT",), writes=(tb[m % 2],))
                fw.dma("sp", p.d["ggfm"][m * 128:(m + 1) * 128,
                                         z * 128:(z + 1) * 128],
                       tb[m % 2][:, :], tag="st_ggfm")



def ph_l1_post(p):
    c, fw = p.cfg, p.fw
    nhs = c.INTER // 128
    with p.scope():
        rows3 = None
        with p.scope():
            ggr = p.sb("ggr", [128, nhs * c.T], BF16)
            load_fm(p, ggr, "ggfm", c.INTER, c.T)
            ps_s = p.ps("ps_s3", [1, c.T], F32)
            ps_q = p.ps("ps_q3", [1, c.T], F32)
            ps_o = [p.ps(f"ps3_o{i}", [128, 512], F32) for i in range(2)]
            wt = [p.sb(f"op_t{i}", [128, 512], BF16) for i in range(3)]
            xold = [p.sb(f"xo3{i}", [128, 512], F32) for i in range(2)]
            x3s = [p.sb(f"x3s{i}", [128, 512], F32) for i in range(2)]
            sqt = [p.sb(f"sq3{i}", [128, 512], BF16) for i in range(2)]
            xb3 = [p.sb(f"xb3{i}", [128, 512], BF16) for i in range(2)]
            for m in range(c.NK):
                mm_k4_stream(p, "opj4", m, nhs,
                             [ps_o[tt][:, 0:512] for tt in range(c.T // 512)],
                             lambda k, tt: ggr[:, k * c.T + tt * 512:k * c.T + tt * 512 + 512],
                             wt, "opj4w")
                for tt in range(c.T // 512):
                    i2 = tt % 2
                    fw.dma("sp", xold[i2][:, :],
                           p.d["x2"][m * 128:(m + 1) * 128, tt * 512:tt * 512 + 512],
                           tag="ld_x2")
                    fw.op("dve", "tensor_tensor", x3s[i2][:, :], ps_o[tt][:, 0:512],
                          xold[i2][:, :], ALU.add)
                    fw.dma("sp", p.d["x3"][m * 128:(m + 1) * 128,
                                           tt * 512:tt * 512 + 512],
                           x3s[i2][:, :], tag="st_x3", sem="st_x3")
                    bsl = xb3[i2][:, :]
                    fw.op("act", "activation", bsl, x3s[i2][:, :], AF.Copy,
                          bias=0.0, scale=1.0, reads=(x3s[i2],), writes=(xb3[i2],))
                    fw.dma("sp", p.d["x3bf"][m * 128:(m + 1) * 128,
                                             tt * 512:tt * 512 + 512],
                           bsl, tag="st_x3bf")
                    fw.op("act", "activation", sqt[i2][:, :], bsl, AF.Square,
                          bias=0.0, scale=1.0, reads=(xb3[i2],), writes=(sqt[i2],))
                    fw.op("pe", "matmul", ps_s[0:1, tt * 512:(tt + 1) * 512],
                          p.onescol[:, 0:1], bsl, start=(m == 0),
                          stop=(m == c.NK - 1), reads=(p.onescol, xb3[i2]),
                          writes=("k_lns3",))
                    fw.op("pe", "matmul", ps_q[0:1, tt * 512:(tt + 1) * 512],
                          p.onescol[:, 0:1], sqt[i2][:, :], start=(m == 0),
                          stop=(m == c.NK - 1), reads=(p.onescol, sqt[i2]),
                          writes=("k_lnq3",))
            rows3 = ln_rows(p, ps_s, ps_q, "k_lns3", "k_lnq3")
        xn3 = p.sb("xn3_r", [128, c.NK * c.T], BF16)
        ln_apply_dram(p, rows3[0], rows3[1], "x3bf", xn3)
        Gall = p.sb("Gall", [128, c.NZ * c.MK], F32)
        with p.scope():
            wrs = p.sb("wrs", [128, c.NK * c.MK], BF16)
            for k in range(c.NK):
                fw.dma("pool", wrs[:, k * c.MK:(k + 1) * c.MK],
                       p.d["wr"][k * 128:(k + 1) * 128, :], tag=f"wr{k}")
            ps_l = p.ps("ps_l3", [128, 512], F32)
            lg = p.sb("lg3", [128, c.MK], F32)
            nmx = p.sb("nmx3", [128, 1], F32)
            ex = p.sb("ex3", [128, c.MK], F32)
            ssum = p.sb("ssum3", [128, 1], F32)
            scw = p.sb("scw3", [128, c.MK], F32)
            thr = p.sb("thr3", [128, 1], F32)
            mk = p.sb("mk3", [128, c.MK], F32)
            eg = p.sb("eg3", [128, c.MK], F32)
            for tz in range(c.NZ):
                for k in range(c.NK):
                    fw.op("pe", "matmul", ps_l[:, 0:c.MK],
                          xn3[:, k * c.T + tz * 128:k * c.T + tz * 128 + 128],
                          wrs[:, k * c.MK:(k + 1) * c.MK],
                          start=(k == 0), stop=(k == c.NK - 1))
                fw.op("dve", "tensor_copy", lg[:, :], ps_l[:, 0:c.MK])
                fw.op("dve", "tensor_reduce", nmx[:, :], lg[:, :], AX.X,
                      ALU.max, negate=True)
                fw.op("act", "activation", ex[:, :], lg[:, :], AF.Exp,
                      bias=nmx[:, 0:1], scale=1.0)
                topk_gate(p, ex, scw, thr, mk, c.MTOP)
                fw.op("dve", "tensor_mul", eg[:, :], ex[:, :], mk[:, :])
                fw.op("dve", "tensor_reduce", ssum[:, :], eg[:, :], AX.X, ALU.add)
                fw.op("dve", "reciprocal", ssum[:, :], ssum[:, :])
                fw.op("dve", "tensor_scalar", Gall[:, tz * c.MK:(tz + 1) * c.MK],
                      eg[:, :], ssum[:, 0:1], None, ALU.mult,
                      reads=(eg, ssum), writes=(Gall,))
        hgf = p.sb("hgf", [128, (c.KR // 128) * c.T], BF16)
        with p.scope():
            ast = [p.sb(f"ast{i}", [128, c.NK * 512], BF16) for i in range(2)]
            ps_h = [p.ps(f"ps_h3{i}", [128, 512], F32) for i in range(2)]
            hgt = [p.sb(f"hgt{i}", [128, 512], BF16) for i in range(2)]
            pstb3 = p.ps("pstb3", [128, 128], BF16)
            tb3 = [p.sb(f"tb3{i}", [128, 128], BF16) for i in range(2)]
            for n in range(c.KR // 512):
                sbuf = ast[n % 2]
                for k in range(c.NK):
                    fw.dma("pool", sbuf[:, k * 512:(k + 1) * 512],
                           p.d["afl"][k * 128:(k + 1) * 128, n * 512:(n + 1) * 512],
                           tag=f"afl{n}_{k}")
                for tz in range(c.NZ):
                    i2 = tz % 2
                    for k in range(c.NK):
                        fw.op("pe", "matmul", ps_h[i2][:, 0:512],
                              xn3[:, k * c.T + tz * 128:k * c.T + tz * 128 + 128],
                              sbuf[:, k * 512:(k + 1) * 512],
                              start=(k == 0), stop=(k == c.NK - 1))
                    nex = 512 // c.R
                    gv = Gall[:, tz * c.MK + n * nex: tz * c.MK + (n + 1) * nex]
                    fw.op("dve", "tensor_tensor",
                          hgt[i2][:, :].rearrange("pp (e r) -> pp e r", r=c.R),
                          ps_h[i2][:, 0:512].rearrange("pp (e r) -> pp e r", r=c.R),
                          gv.unsqueeze(2).broadcast_to([128, nex, c.R]),
                          ALU.mult, reads=(ps_h[i2], Gall), writes=(hgt[i2],))
                    for j in range(4):
                        fw.op("pe", "transpose", pstb3[:, :],
                              hgt[i2][:, j * 128:(j + 1) * 128], p.eyeb[:, :],
                              reads=(hgt[i2], p.eyeb), writes=(pstb3,))
                        kr = n * 4 + j
                        fw.op("dve", "tensor_copy",
                              hgf[:, kr * c.T + tz * 128: kr * c.T + tz * 128 + 128],
                              pstb3[:, :], reads=(pstb3,), writes=(hgf,))
        with p.scope():
            wt3 = [p.sb(f"bfl_t{i}", [128, 512], BF16) for i in range(3)]
            ps_f = [p.ps(f"ps_f{i}", [128, 512], F32) for i in range(2)]
            xo4 = [p.sb(f"xo4{i}", [128, 512], F32) for i in range(2)]
            yt = [p.sb(f"yt4{i}", [128, 512], F32) for i in range(2)]
            for m in range(c.NK):
                mm_k4_stream(p, "bfl4", m, c.KR // 128,
                             [ps_f[tt][:, 0:512] for tt in range(c.T // 512)],
                             lambda k, tt: hgf[:, k * c.T + tt * 512:k * c.T + tt * 512 + 512],
                             wt3, "bfl4w")
                for tt in range(c.T // 512):
                    i2 = tt % 2
                    fw.dma("sp", xo4[i2][:, :],
                           p.d["x3"][m * 128:(m + 1) * 128, tt * 512:tt * 512 + 512],
                           tag="ld_x3")
                    fw.op("dve", "scalar_tensor_tensor", yt[i2][:, :],
                          ps_f[tt][:, 0:512], 2.0, xo4[i2][:, :],
                          ALU.mult, ALU.add)
                    fw.dma("sp", p.d["y"][m * 128:(m + 1) * 128,
                                          tt * 512:tt * 512 + 512],
                           yt[i2][:, :], tag="st_y", sem="st_y")



def ph_ln0(p):
    ph_ln(p, None, "xn0", cast_from="x")


CONSTS = ["eye", "triu", "mneg", "smask", "rmask"]
PHASE_IO = {
    "ln0": (ph_ln0, ["x"], ["xn0"]),
    "l0proj": (ph_l0_proj, ["xn0", "wq4", "wk4", "wv", "wg", "wwin", "dtb0",
                            "alog0", "rcos", "rsin"],
               ["qsp", "ksp", "vsp", "g0sp", "dt0", "g0cum", "b0"]),
    "l0cs": (ph_l0_cs, ["ksp", "vsp", "g0cum", "b0"],
             ["s0loc", "p0", "s0fin"]),
    "l0att": (ph_l0_att, ["qsp", "ksp", "vsp", "g0sp", "g0cum", "b0", "p0",
                          "s0loc", "s0init", "gnw"], ["ygfm"]),
    "l0post": (ph_l0_post, ["ygfm", "x", "wo4", "rw", "rb", "eb"],
               ["x1", "x2", "xn2", "x1bf", "x2bf"]),
    "l1inproj": (ph_l1_inproj, ["xn2", "ipj", "iph4", "dtb1", "alog1"],
                 ["ipg", "iph", "dt1", "lndt1", "adt1", "halo"]),
    "l1convcs": (ph_l1_convcs, ["iph", "haloin", "cw", "cb", "dt1", "lndt1",
                                "adt1"],
                 ["hstm", "bcfm", "acumt", "b1sp", "acfl", "p1", "s1loc",
                  "s1fin"]),
    "l1ssd": (ph_l1_ssd, ["s1loc", "s1init", "hstm", "bcfm", "ipg", "acumt",
                          "b1sp", "acfl", "p1", "nw", "dvec"], ["ggfm"]),
    "l1post": (ph_l1_post, ["ggfm", "x2", "opj4", "wr", "afl", "bfl4"],
               ["x3", "y", "x3bf"]),
}
CC_ROUTE = {"s0init": "s0fin", "haloin": "halo", "s1init": "s1fin"}
CC_PROD = {"s0fin": "l0cs", "halo": "l1inproj", "s1fin": "l1convcs"}

LAUNCHES = [["ln0", "l0proj", "l0cs"], ["l0att"], ["l0post"],
            ["l1inproj"], ["l1convcs"], ["l1ssd"], ["l1post"]]

_BUILD_CACHE = {}


def build_launch(cfg, phases, key, out_force=()):
    if key in _BUILD_CACHE:
        return _BUILD_CACHE[key]
    reg = declare_tensors(cfg)
    nc = bass.Bass("TRN2", target_bir_lowering=False)
    writes_here = set()
    reads_here = set(CONSTS)
    for ph in phases:
        _, rds, wrs = PHASE_IO[ph]
        for r in rds:
            if r not in writes_here:
                reads_here.add(r)
        writes_here.update(wrs)
        if cfg.fused:
            for ini, fin in CC_ROUTE.items():
                if fin in wrs:
                    writes_here.add(ini)
    later_reads = set()
    if not cfg.fused:
        for ph, (fn, rds, wrs) in PHASE_IO.items():
            if ph in phases:
                continue
            later_reads.update(rds)
            for ini in CC_ROUTE:
                if ini in rds:
                    later_reads.add(CC_ROUTE[ini])
    ins, outs = [], []
    with contextlib.ExitStack() as es:
        p = P(nc, cfg, es)
        for name, (shape, dt) in reg.items():
            if name in reads_here and name not in writes_here:
                kind = "ExternalInput"
                ins.append(name)
            elif name in out_force and name in writes_here:
                kind = "ExternalOutput"
                outs.append(name)
            elif name == "y" and "y" in writes_here:
                kind = "ExternalOutput"
                outs.append(name)
            elif name in writes_here and (not cfg.fused) and (
                    name in later_reads or name in CC_ROUTE.values()):
                kind = "ExternalOutput"
                outs.append(name)
            elif name in writes_here or name in reads_here:
                kind = "Internal"
            else:
                continue
            p.d[name] = nc.dram_tensor(name, list(shape), dt, kind=kind).ap()
        ph_consts(p)
        for ph in phases:
            PHASE_IO[ph][0](p)
        p.fw.finish()
        p.fw.flush(es)
    _BUILD_CACHE[key] = (nc, ins, outs)
    return _BUILD_CACHE[key]


def rope_tables(L, D=128, theta=1e4):
    inv = 1.0 / theta ** (np.arange(0, D, 2, dtype=np.float32) / D)
    ang = np.arange(L, dtype=np.float32)[:, None] * inv
    cos = np.concatenate([np.cos(ang), np.cos(ang)], -1).astype(np.float32)
    sin = np.concatenate([np.sin(ang), np.sin(ang)], -1).astype(np.float32)
    return cos, sin


def prep_inputs(cfg, inp):
    c = cfg
    ct = lambda a: np.ascontiguousarray(a, dtype=np.float32)
    shared = {}
    shared["wq"] = ct(inp["L0_attn_Wq"].T)
    shared["wk"] = ct(inp["L0_attn_Wk"].T)
    shared["wv"] = ct(inp["L0_attn_Wv"].T)
    shared["wg"] = ct(inp["L0_attn_Wg"].T)
    shared["wwin"] = ct(inp["L0_attn_Win"].T)
    shared["dtb0"] = ct(inp["L0_attn_dt_bias"][None, :])
    shared["alog0"] = ct(inp["L0_attn_A_log_bias"][None, :])
    shared["gnw"] = ct(inp["L0_attn_gnorm_w"][None, :])
    shared["wo"] = ct(inp["L0_attn_Wo"].T)
    shared["rw"] = ct(inp["L0_ffn_router_weight"].T)
    shared["rb"] = ct(inp["L0_ffn_router_bias"][None, :])
    shared["eb"] = ct(inp["L0_ffn_expert_bias"])
    shared["ipj"] = ct(inp["L1_attn_in_proj_w"].T)
    cw = inp["L1_attn_conv_w"][:, 0, :]
    nc3 = c.CONV // 128
    shared["cw"] = ct(cw.reshape(nc3, 128, 4).transpose(1, 0, 2).reshape(128, nc3 * 4))
    shared["cb"] = ct(inp["L1_attn_conv_b"].reshape(nc3, 128).T)
    shared["dtb1"] = ct(inp["L1_attn_dt_bias"][:, None])
    shared["alog1"] = ct(inp["L1_attn_A_log"][:, None])
    shared["dvec"] = ct(inp["L1_attn_D"][None, :])
    shared["nw"] = ct(inp["L1_attn_norm_w"][None, :])
    shared["opj"] = ct(inp["L1_attn_out_proj_w"].T)
    shared["afl"] = ct(inp["L1_ffn_A"].reshape(c.KR, c.D).T)
    shared["bfl"] = ct(inp["L1_ffn_B"].transpose(0, 2, 1).reshape(c.KR, c.D))
    shared["wr"] = ct(inp["L1_ffn_Wr"].T)
    def t4(Wt):
        K, M = Wt.shape
        return ct(Wt.reshape(K // 512, 4, 128, M // 128, 128)
                  .transpose(3, 0, 2, 1, 4).reshape(-1, 512))
    shared["wq4"] = t4(shared["wq"])
    shared["wk4"] = t4(shared["wk"])
    shared["wo4"] = t4(shared["wo"])
    shared["opj4"] = t4(shared["opj"])
    shared["bfl4"] = t4(shared["bfl"])
    shared["iph4"] = t4(shared["ipj"][:, c.INTER:c.INTER + c.CONV])
    shared["eye"] = np.eye(128, dtype=np.float32)
    tri = np.triu(np.ones((128, 128), dtype=np.float32))
    shared["triu"] = tri
    shared["mneg"] = np.where(tri > 0, 0.0, -1e30).astype(np.float32)
    x = inp["L0_attn_hidden_states"]
    B, L, _ = x.shape
    cos, sin = rope_tables(L)
    percore = []
    for core in range(8):
        b, half = core // 2, core % 2
        pc = {}
        pc["x"] = ct(x[b, half * c.T:(half + 1) * c.T, :].T)
        pc["rcos"] = ct(cos[half * c.T:(half + 1) * c.T, :].T)
        pc["rsin"] = ct(sin[half * c.T:(half + 1) * c.T, :].T)
        pc["smask"] = np.full((1, 1), 1.0 if half == 0 else 0.0, np.float32)
        pc["rmask"] = np.full((1, 1), 0.0 if half == 0 else 1.0, np.float32)
        percore.append(pc)
    return shared, percore


def kernel(**inputs):
    cfg = Cfg(True)
    cfg.fused = FUSED
    shared, percore = prep_inputs(cfg, inputs)
    env = [dict(shared, **pc) for pc in percore]
    launches = [[ph for grp in LAUNCHES for ph in grp]] if cfg.fused else LAUNCHES
    for li, grp in enumerate(launches):
        nc, ins, outs = build_launch(cfg, grp, (cfg.fused, li))
        in_maps = [{k: env[core][k] for k in ins} for core in range(8)]
        res = run_bass_kernel_spmd(nc, in_maps, core_ids=list(range(8)))
        for core in range(8):
            for k in outs:
                env[core][k] = res.results[core][k]
        if not cfg.fused:
            for ini, fin in CC_ROUTE.items():
                if fin in outs:
                    for core in range(8):
                        env[core][ini] = res.results[core ^ 1][fin]
    out = np.empty((4, 2048, 4096), dtype=np.float32)
    for core in range(8):
        b, half = core // 2, core % 2
        out[b, half * cfg.T:(half + 1) * cfg.T, :] = env[core]["y"].T
    return out


# version 11
# speedup vs baseline: 1.6976x; 1.0167x over previous
import contextlib
import numpy as np

import concourse.bass as bass
import concourse.mybir as mybir
from concourse.bass_utils import run_bass_kernel_spmd

F32 = mybir.dt.float32
BF16 = mybir.dt.bfloat16
AX = mybir.AxisListType
ALU = mybir.AluOpType
AF = mybir.ActivationFunctionType

FUSED = True


ENGS = ("pe", "act", "dve", "pool", "sp")
_ENG_ATTR = {"pe": "tensor", "act": "scalar", "dve": "vector", "pool": "gpsimd",
             "sp": "sync"}
_SEM_ROT = 48000


class _Op:
    __slots__ = ("method", "args", "kwargs", "waits", "inc", "idx", "tag")

    def __init__(self, method, args, kwargs, idx, tag):
        self.method = method
        self.args = args
        self.kwargs = kwargs
        self.waits = []
        self.inc = None
        self.idx = idx
        self.tag = tag


class Fw:
    def __init__(self, nc):
        self.nc = nc
        self.ops = {e: [] for e in ENGS}
        self.ndma = {e: 0 for e in ENGS}
        self.named = {}
        self.barrier_totals = {e: 0 for e in ENGS}
        self.needed = {}
        self.state = {}
        self.waited = {e: {} for e in ENGS}
        self.last_compute = {e: None for e in ENGS}
        self.epoch = 0
        self.named_first = {}
        self.named_last = {}
        self.named_queue = {}
        self.verbose = False

    def _key(self, ap):
        if isinstance(ap, str):
            return ap
        t = getattr(ap, "tensor", ap)
        return getattr(t, "name", str(t))

    def _st(self, key):
        if key not in self.state:
            self.state[key] = {"w": None, "r": []}
        return self.state[key]

    def _need_wait(self, eng, dep):
        stream, idx = dep
        cur = self.waited[eng].get(stream, -1)
        if idx <= cur:
            return False
        self.waited[eng][stream] = idx
        return True

    def _resolve(self, eng, deps, o, what=""):
        for d in deps:
            stream, idx = d
            if stream == "eng:" + eng and eng == "pe":
                continue
            if stream == "eng:" + eng and idx == o.idx:
                continue
            if stream.startswith("dmabulk:"):
                q = stream.split(":")[1]
                if idx <= self.barrier_totals[q]:
                    continue
                raise RuntimeError(
                    f"dep on un-barriered bulk DMA {d} (from {what})")
            if self._need_wait(eng, d):
                o.waits.append(d)
                self.needed[d] = True

    def op(self, eng, method, *args, reads=None, writes=None, tag="", **kwargs):
        assert eng in ENGS
        _OUT_KW = ("accum_out",)
        if writes is None:
            writes = (args[0],)
            if kwargs.get("accum_out") is not None:
                writes = writes + (kwargs["accum_out"],)
        if reads is None:
            reads = tuple(a for a in args[1:] if hasattr(a, "tensor"))
            reads = reads + tuple(v for k, v in kwargs.items()
                                  if k not in _OUT_KW and hasattr(v, "tensor"))
        idx = len(self.ops[eng])
        o = _Op(method, args, kwargs, idx, tag)
        deps = []
        for ap in reads:
            st = self._st(self._key(ap))
            if st["w"] is not None:
                deps.append(st["w"])
        for ap in writes:
            st = self._st(self._key(ap))
            if st["w"] is not None:
                deps.append(st["w"])
            deps.extend(st["r"])
        self._resolve(eng, deps, o, tag or method)
        mytok = ("eng:" + eng, idx)
        for ap in reads:
            self._st(self._key(ap))["r"].append(mytok)
        for ap in writes:
            st = self._st(self._key(ap))
            st["w"] = mytok
            st["r"] = []
        self.ops[eng].append(o)
        self.last_compute[eng] = idx
        return mytok

    def dma(self, q, out, in_, sem=None, rkey=None, wkey=None, tag="", **kwargs):
        assert q in ENGS
        if sem is None:
            side = out if "SBTensor" in type(getattr(out, "tensor", out)).__name__ else in_
            sem = "b_" + self._key(side) + "@" + q
        elif sem is False:
            sem = None
        idx = len(self.ops[q])
        o = _Op("dma_start", (), dict(out=out, in_=in_, **kwargs), idx, tag)
        deps = []
        st_in = self._st(rkey if rkey is not None else self._key(in_))
        if st_in["w"] is not None:
            deps.append(st_in["w"])
        st_out = self._st(wkey if wkey is not None else self._key(out))
        if st_out["w"] is not None:
            deps.append(st_out["w"])
        deps.extend(st_out["r"])
        self._resolve(q, deps, o, tag or "dma")
        if sem is None:
            self.ndma[q] += 1
            tok = ("dmabulk:" + q, self.ndma[q])
        else:
            self.named[sem] = self.named.get(sem, 0) + 1
            self.named_first.setdefault(sem, self.epoch)
            self.named_last[sem] = self.epoch
            q0 = self.named_queue.setdefault(sem, q)
            assert q0 == q, f"named sem {sem} used from queues {q0} and {q}"
            tok = ("dsem:" + sem, self.named[sem])
        o.inc = tok
        st_in["r"].append(tok)
        st_out["w"] = tok
        st_out["r"] = []
        self.ops[q].append(o)
        return tok

    def cc(self, kind, op, groups, in_ap, out_ap):
        tok = self.op("pool", "collective_compute", kind, op,
                      reads=(in_ap,), writes=(out_ap,),
                      replica_groups=groups, ins=[in_ap.opt()],
                      outs=[out_ap.opt()], tag="cc")
        self.op("pool", "engine_nop", reads=(out_ap,), writes=("_ccnop",),
                tag="ccwait")
        return tok

    def barrier(self):
        toks = []
        for e in ENGS:
            if self.last_compute[e] is not None:
                toks.append(("eng:" + e, self.last_compute[e]))
            if self.ndma[e]:
                toks.append(("dmabulk:" + e, self.ndma[e]))
        for nm, c in self.named.items():
            if c:
                toks.append(("dsem:" + nm, c))
        for e in ENGS:
            o = _Op("engine_nop", (), {}, len(self.ops[e]), "barrier")
            for d in toks:
                if d[0] == "eng:" + e:
                    continue
                if self._need_wait(e, d):
                    o.waits.append(d)
                    self.needed[d] = True
            self.ops[e].append(o)
        for e in ENGS:
            self.barrier_totals[e] = self.ndma[e]
        self.epoch += 1

    def finish(self):
        self.barrier()

    def flush(self, es):
        nc = self.nc
        eng_incpts = {e: sorted(i for (s, i) in self.needed if s == "eng:" + e)
                      for e in ENGS}

        def mk_sems(name, nvals):
            n_sems = max(1, (nvals + _SEM_ROT - 1) // _SEM_ROT)
            return [es.enter_context(nc.semaphore(f"{name}{i}"))
                    for i in range(n_sems)]

        eng_sems = {e: mk_sems("S" + e, len(eng_incpts[e])) for e in ENGS}
        eng_rank = {e: {idx: r + 1 for r, idx in enumerate(eng_incpts[e])}
                    for e in ENGS}
        bulk_sems = {e: mk_sems("D" + e, self.ndma[e] * 16) for e in ENGS}
        slots = []
        slot_count = []
        name_slot = {}
        name_base = {}
        for nm in sorted(self.named, key=lambda n: (self.named_first[n],
                                                    self.named_last[n])):
            fe, le = self.named_first[nm], self.named_last[nm]
            qn = self.named_queue[nm]
            pick = None
            for si in range(len(slots)):
                if (slots[si][0] == qn and slots[si][1] < fe
                        and (slot_count[si] + self.named[nm]) * 16 < _SEM_ROT):
                    pick = si
                    break
            if pick is None:
                pick = len(slots)
                slots.append((qn, -1))
                slot_count.append(0)
            name_slot[nm] = pick
            name_base[nm] = slot_count[pick]
            slot_count[pick] += self.named[nm]
            slots[pick] = (qn, le)
        named_slot_sems = [es.enter_context(nc.semaphore(f"NS{si}"))
                           for si in range(len(slots))]
        rot = _SEM_ROT

        def sem_val(stream, idx):
            kind, name = stream.split(":", 1)
            if kind == "eng":
                r = eng_rank[name][idx]
                si, v = divmod(r - 1, rot)
                return eng_sems[name][si], v + 1
            if kind == "dmabulk":
                per = rot // 16
                si, v = divmod(idx - 1, per)
                return bulk_sems[name][si], (v + 1) * 16
            return (named_slot_sems[name_slot[name]],
                    (name_base[name] + idx) * 16)

        def inc_for_eng(e, idx):
            r = eng_rank[e].get(idx)
            if r is None:
                return None
            return eng_sems[e][(r - 1) // rot], 1

        def inc_for_dma(stream, idx):
            kind, name = stream.split(":", 1)
            if kind == "dmabulk":
                per = rot // 16
                return bulk_sems[name][(idx - 1) // per], 16
            return named_slot_sems[name_slot[name]], 16

        if self.verbose:
            stats = {e: (len(self.ops[e]), len(eng_incpts[e])) for e in ENGS}
            nw = sum(len(o.waits) for e in ENGS for o in self.ops[e])
            print("fw flush ops(incs):", stats, "names:", len(self.named),
                  "slots:", len(slots), "waits:", nw)

        with nc.Block() as block:
            for e in ENGS:
                ops = self.ops[e]
                if not ops:
                    continue

                def body(engine, ops=ops, e=e):
                    for o in ops:
                        for d in o.waits:
                            s, v = sem_val(*d)
                            engine.wait_ge(s, v)
                        if o.method == "engine_nop":
                            inst = engine.nop()
                        else:
                            inst = getattr(engine, o.method)(*o.args, **o.kwargs)
                        if o.inc is not None:
                            s, a = inc_for_dma(*o.inc)
                            inst.then_inc(s, a)
                        else:
                            ic = inc_for_eng(e, o.idx)
                            if ic is not None:
                                inst.then_inc(ic[0], ic[1])

                getattr(block, _ENG_ATTR[e])(body)



class Cfg:
    def __init__(self, full=True):
        if full:
            self.D = 4096
            self.T = 1024
            self.H0 = 32
            self.KVH = 8
            self.H1 = 128
            self.G1 = 8
            self.E = 64
            self.KTOP = 8
            self.MK = 64
            self.MTOP = 16
            self.R = 16
        else:
            self.D = 512
            self.T = 512
            self.H0 = 4
            self.KVH = 1
            self.H1 = 16
            self.G1 = 2
            self.E = 64
            self.KTOP = 8
            self.MK = 64
            self.MTOP = 16
            self.R = 16
        self.HD = 128
        self.P1 = 64
        self.N1 = 128
        self.CH = 128
        self.NZ = self.T // self.CH
        self.KD = self.KVH * self.HD
        self.QD = self.H0 * self.HD
        self.INTER = self.H1 * self.P1
        self.CONV = self.INTER + 2 * self.G1 * self.N1
        self.IPJ = self.INTER + self.CONV + self.H1
        self.KR = self.MK * self.R
        self.NK = self.D // 128
        self.fused = True
        self.groups = [[2 * i, 2 * i + 1] for i in range(4)]



def declare_tensors(cfg):
    c = cfg
    t = {}
    t["x"] = ([c.D, c.T], F32)
    t["wq"] = ([c.D, c.QD], F32)
    t["wk"] = ([c.D, c.KD], F32)
    t["wv"] = ([c.D, c.KD], F32)
    t["wg"] = ([c.D, c.QD], F32)
    t["wwin"] = ([c.D, c.H0], F32)
    t["dtb0"] = ([1, c.H0], F32)
    t["alog0"] = ([1, c.H0], F32)
    t["gnw"] = ([1, c.HD], F32)
    t["wo"] = ([c.QD, c.D], F32)
    t["rw"] = ([c.D, c.E], F32)
    t["rb"] = ([1, c.E], F32)
    t["eb"] = ([c.E, c.D], F32)
    t["ipj"] = ([c.D, c.IPJ], F32)
    t["cw"] = ([128, (c.CONV // 128) * 4], F32)
    t["cb"] = ([128, c.CONV // 128], F32)
    t["dtb1"] = ([c.H1, 1], F32)
    t["alog1"] = ([c.H1, 1], F32)
    t["dvec"] = ([1, c.H1], F32)
    t["nw"] = ([1, c.INTER], F32)
    t["opj"] = ([c.INTER, c.D], F32)
    t["afl"] = ([c.D, c.KR], F32)
    t["bfl"] = ([c.KR, c.D], F32)
    t["wr"] = ([c.D, c.MK], F32)
    def t4(K, M):
        return ([(M // 128) * (K // 512) * 128, 512], F32)
    t["wq4"] = t4(c.D, c.QD)
    t["wk4"] = t4(c.D, c.KD)
    t["wo4"] = t4(c.QD, c.D)
    t["opj4"] = t4(c.INTER, c.D)
    t["bfl4"] = t4(c.KR, c.D)
    t["iph4"] = t4(c.D, c.CONV)
    t["rcos"] = ([c.HD, c.T], F32)
    t["rsin"] = ([c.HD, c.T], F32)
    t["eye"] = ([128, 128], F32)
    t["triu"] = ([128, 128], F32)
    t["mneg"] = ([128, 128], F32)
    t["smask"] = ([1, 1], F32)
    t["rmask"] = ([1, 1], F32)
    t["y"] = ([c.D, c.T], F32)
    t["xn0"] = ([c.D, c.T], BF16)
    t["qsp"] = ([c.QD, c.T], BF16)
    t["ksp"] = ([c.KD, c.T], BF16)
    t["vsp"] = ([c.T, c.KD], BF16)
    t["g0sp"] = ([c.T, c.QD], BF16)
    t["dt0"] = ([c.T, c.H0], F32)
    t["g0cum"] = ([c.T, c.H0], F32)
    t["b0"] = ([c.T, c.H0], F32)
    t["s0loc"] = ([c.NZ * 128, c.H0 * c.HD], BF16)
    t["p0"] = ([c.NZ, c.H0], F32)
    t["s0fin"] = ([128, c.H0 * c.HD], F32)
    t["s0init"] = ([128, c.H0 * c.HD], F32)
    t["ygfm"] = ([c.QD, c.T], BF16)
    t["x1"] = ([c.D, c.T], F32)
    t["x1bf"] = ([c.D, c.T], BF16)
    t["xn1"] = ([c.D, c.T], BF16)
    t["x2"] = ([c.D, c.T], F32)
    t["x2bf"] = ([c.D, c.T], BF16)
    t["xn2"] = ([c.D, c.T], BF16)
    t["ipg"] = ([c.T, c.INTER], BF16)
    t["iph"] = ([c.CONV, c.T], BF16)
    t["dt1"] = ([c.H1, c.T], F32)
    t["lndt1"] = ([c.H1, c.T], F32)
    t["adt1"] = ([c.H1, c.T], F32)
    t["halo"] = ([128, (c.CONV // 128) * 3], F32)
    t["haloin"] = ([128, (c.CONV // 128) * 3], F32)
    t["hstm"] = ([c.T, c.INTER], BF16)
    t["bcfm"] = ([2 * c.G1 * c.N1, c.T], BF16)
    t["acumt"] = ([c.T, c.H1], F32)
    t["b1sp"] = ([c.T, c.H1], F32)
    t["acfl"] = ([c.NZ, c.H1 * 128], F32)
    t["p1"] = ([c.NZ, c.H1], F32)
    t["s1loc"] = ([c.NZ * c.N1, c.H1 * c.P1], BF16)
    t["s1fin"] = ([c.N1, c.H1 * c.P1], F32)
    t["s1init"] = ([c.N1, c.H1 * c.P1], F32)
    t["ggfm"] = ([c.INTER, c.T], BF16)
    t["x3"] = ([c.D, c.T], F32)
    t["x3bf"] = ([c.D, c.T], BF16)
    t["xn3"] = ([c.D, c.T], BF16)
    return t


INPUT_NAMES = ["x", "wq", "wk", "wv", "wg", "wwin", "dtb0", "alog0", "gnw",
               "wo", "rw", "rb", "eb", "ipj", "cw", "cb", "dtb1", "alog1",
               "dvec", "nw", "opj", "afl", "bfl", "wr", "rcos", "rsin",
               "eye", "triu", "mneg", "smask", "rmask"]



class P:

    def __init__(self, nc, cfg, es):
        self.nc = nc
        self.cfg = cfg
        self.scopes = [es]
        self.fw = Fw(nc)
        self.d = {}
        self.uid = 0

    def sb(self, name, shape, dt=F32):
        return self.scopes[-1].enter_context(self.nc.sbuf_tensor(name, shape, dt))

    def sb_root(self, name, shape, dt=F32):
        return self.scopes[0].enter_context(self.nc.sbuf_tensor(name, shape, dt))

    def ps(self, name, shape, dt=F32):
        return self.scopes[-1].enter_context(self.nc.psum_tensor(name, shape, dt))

    def u(self, base):
        self.uid += 1
        return f"{base}{self.uid}"

    @contextlib.contextmanager
    def scope(self):
        es = contextlib.ExitStack()
        self.scopes.append(es)
        try:
            yield
        finally:
            self.fw.barrier()
            self.scopes.pop()
            es.close()


def ph_consts(p):
    c, fw = p.cfg, p.fw
    p.ones1 = p.sb("ones1", [1, 128], F32)
    fw.op("dve", "memset", p.ones1[0:1, :], 1.0)
    p.onescol = p.sb("onescol", [128, 1], BF16)
    fw.op("dve", "memset", p.onescol[:, :], 1.0)
    p.eye32 = p.sb("c_eye32", [128, 128], F32)
    fw.dma("sp", p.eye32[:, :], p.d["eye"])
    p.eyeb = p.sb("c_eyeb", [128, 128], BF16)
    fw.dma("pool", p.eyeb[:, :], p.d["eye"])
    p.triu = p.sb("c_triu", [128, 128], F32)
    fw.dma("sp", p.triu[:, :], p.d["triu"])
    p.mneg = p.sb("c_mneg", [128, 128], F32)
    fw.dma("sp", p.mneg[:, :], p.d["mneg"])
    p.ps_bc = p.ps("ps_bc", [128, 512], F32)
    sm = p.sb("sm_row", [1, 2], F32)
    fw.dma("sp", sm[0:1, 0:1], p.d["smask"])
    fw.dma("sp", sm[0:1, 1:2], p.d["rmask"])
    p.smrm = p.sb("smrm", [128, 2], F32)
    fw.op("pe", "matmul", p.ps_bc[:, 0:2], p.ones1[0:1, 0:128], sm[0:1, 0:2],
          start=True, stop=True)
    fw.op("dve", "tensor_copy", p.smrm[:, :], p.ps_bc[:, 0:2])
    p.smcol = p.smrm[:, 0:1]
    p.rmcol = p.smrm[:, 1:2]
    consts = [1e-5, 1e-30, -0.5, 1.0 / c.D, 1.0 / c.HD,
              1.0 / (c.INTER // c.G1), 0.5]
    p.cst = p.sb("c_vals", [128, len(consts)], F32)
    p.rows = p.sb("c_rows", [1, 4 * c.T], F32)
    p.rows_off = 0
    for i, v in enumerate(consts):
        fw.op("dve", "memset", p.cst[:, i:i + 1], v, writes=(p.cst,))
        p.nc.const_aps.register(F32, v, p.cst[:, i:i + 1])
    fw.barrier()


def bcast(p, row_ap, w, out=None, name=None, read_key=None):
    fw = p.fw
    if out is None:
        out = p.sb(name or p.u("bc"), [128, w], F32)
    for o in range(0, w, 512):
        ww = min(512, w - o)
        kw = {}
        fw.op("pe", "matmul", p.ps_bc[:, 0:ww], p.ones1[0:1, 0:128],
              row_ap[0:1, o:o + ww], start=True, stop=True,
              reads=(p.ones1[:, :], read_key or row_ap,), writes=(p.ps_bc,))
        fw.op("dve", "tensor_copy", out[:, o:o + ww], p.ps_bc[:, 0:ww])
    return out


def load_fm(p, sb_tile, dram_name, nrows, T, q="sp", cast=False, cols=None):
    c0, cl = cols if cols else (0, T)
    src = p.d[dram_name]
    for cch in range(nrows // 128):
        dst = sb_tile[:, cch * cl:(cch + 1) * cl]
        s = src[cch * 128:(cch + 1) * 128, c0:c0 + cl]
        p.fw.dma("pool" if cast else q, dst, s, tag=f"ld_{dram_name}{cch}")


def mm_k4_stream(p, wname, m, KC, psums, rhs_fn, bufs, tag):
    fw = p.fw
    K4 = KC // 4
    for k4 in range(K4):
        w = bufs[(m * K4 + k4) % len(bufs)]
        fw.dma("pool", w[:, :],
               p.d[wname][(m * K4 + k4) * 128:(m * K4 + k4 + 1) * 128, :],
               tag=tag)
        for j in range(4):
            k = k4 * 4 + j
            for tt, ps in enumerate(psums):
                fw.op("pe", "matmul", ps, w[:, j * 128:(j + 1) * 128],
                      rhs_fn(k, tt), start=(k == 0), stop=(k == KC - 1))


def softplus(p, out, raw, tmp1, tmp2):
    fw = p.fw
    fw.op("act", "activation", tmp1, raw, AF.Abs, bias=0.0, scale=1.0)
    fw.op("act", "activation", tmp1, tmp1, AF.Exp, bias=0.0, scale=-1.0)
    fw.op("act", "activation", tmp1, tmp1, AF.Ln, bias=1.0, scale=1.0)
    fw.op("dve", "tensor_scalar", tmp2, raw, 0.0, None, ALU.max)
    fw.op("dve", "tensor_add", out, tmp2, tmp1)



def ph_ln(p, xbf_src, xn_dst, cast_from=None):
    c, fw = p.cfg, p.fw
    with p.scope():
        xbf = p.sb(p.u("lnxbf"), [128, c.NK * c.T], BF16)
        if cast_from is not None:
            load_fm(p, xbf, cast_from, c.D, c.T, cast=True)
        else:
            load_fm(p, xbf, xbf_src, c.D, c.T, q="sp")
        ps_sum = p.ps(p.u("lnps"), [1, c.T], F32)
        ps_sq = p.ps(p.u("lnpq"), [1, c.T], F32)
        sq = p.sb(p.u("lnsq"), [128, 512], BF16)
        for tt in range(c.T // 512):
            for cc in range(c.NK):
                xsl = xbf[:, cc * c.T + tt * 512: cc * c.T + tt * 512 + 512]
                fw.op("pe", "matmul", ps_sum[0:1, tt * 512:(tt + 1) * 512],
                      p.onescol[:, 0:1], xsl, start=(cc == 0), stop=(cc == c.NK - 1))
                fw.op("act", "activation", sq[:, :], xsl, AF.Square,
                      bias=0.0, scale=1.0, tag="lnsq")
                fw.op("pe", "matmul", ps_sq[0:1, tt * 512:(tt + 1) * 512],
                      p.onescol[:, 0:1], sq[:, :], start=(cc == 0), stop=(cc == c.NK - 1))
        mean = p.sb(p.u("lnm"), [1, c.T], F32)
        var = p.sb(p.u("lnv"), [1, c.T], F32)
        rstd = p.sb(p.u("lnr"), [1, c.T], F32)
        mrs = p.sb(p.u("lnmr"), [1, c.T], F32)
        fw.op("act", "activation", mean[0:1, :], ps_sum[0:1, :], AF.Copy,
              bias=0.0, scale=1.0 / c.D)
        fw.op("dve", "tensor_mul", var[0:1, :], mean[0:1, :], mean[0:1, :])
        fw.op("dve", "scalar_tensor_tensor", var[0:1, :], ps_sq[0:1, :],
              1.0 / c.D, var[0:1, :], ALU.mult, ALU.subtract)
        fw.op("act", "activation", rstd[0:1, :], var[0:1, :], AF.Ln,
              bias=1e-5, scale=1.0)
        fw.op("act", "activation", rstd[0:1, :], rstd[0:1, :], AF.Exp,
              bias=0.0, scale=-0.5)
        fw.op("dve", "tensor_mul", mrs[0:1, :], mean[0:1, :], rstd[0:1, :])
        tmp = p.sb(p.u("lntmp"), [128, 512], F32)
        xnt = p.sb(p.u("lnxn"), [128, 512], BF16)
        for tt in range(c.T // 512):
            rbc = bcast(p, rstd[0:1, tt * 512:(tt + 1) * 512], 512, name=p.u("lnrb"))
            mbc = bcast(p, mrs[0:1, tt * 512:(tt + 1) * 512], 512, name=p.u("lnmb"))
            for cc in range(c.NK):
                xsl = xbf[:, cc * c.T + tt * 512: cc * c.T + tt * 512 + 512]
                fw.op("dve", "tensor_mul", tmp[:, :], xsl, rbc[:, :])
                fw.op("dve", "tensor_sub", xnt[:, :], tmp[:, :], mbc[:, :])
                p.fw.dma("sp", p.d[xn_dst][cc * 128:(cc + 1) * 128,
                                           tt * 512:tt * 512 + 512],
                         xnt[:, :], tag="st_xn")



def ph_l0_proj(p):
    c, fw = p.cfg, p.fw
    parts = getattr(c, "l0proj_parts", "qk,vg,dt")
    with p.scope():
        xns = p.sb("xns", [128, c.NK * c.T], BF16)
        load_fm(p, xns, "xn0", c.D, c.T)
        cosf = p.sb("cosf", [128, c.T], F32)
        sinf = p.sb("sinf", [128, c.T], F32)
        fw.dma("sp", cosf[:, :], p.d["rcos"])
        fw.dma("sp", sinf[:, :], p.d["rsin"])

        wt = [p.sb(f"l0w{i}", [128, 512], BF16) for i in range(3)]
        psq = [p.ps(f"l0psq{i}", [128, 512], F32) for i in range(2)]
        t1 = p.sb("ropet1", [128, 512], F32)
        rot = p.sb("roperot", [128, 512], F32)
        t2 = p.sb("ropet2", [128, 512], F32)
        qout = p.sb("ropeq", [128, 512], BF16)

        def proj_rope(wname, mch, dst):
            for m in range(mch):
                mm_k4_stream(p, wname, m, c.NK,
                             [psq[tt][:, 0:512] for tt in range(c.T // 512)],
                             lambda k, tt: xns[:, k * c.T + tt * 512:k * c.T + tt * 512 + 512],
                             wt, f"{wname}w")
                for tt in range(c.T // 512):
                    if getattr(c, "norope", False):
                        fw.op("dve", "tensor_copy", qout[:, :], psq[tt][:, 0:512])
                        fw.dma("sp", p.d[dst][m * 128:(m + 1) * 128,
                                              tt * 512:tt * 512 + 512],
                               qout[:, :], tag=f"st_{dst}")
                        continue
                    cs = cosf[:, tt * 512:(tt + 1) * 512]
                    sn = sinf[:, tt * 512:(tt + 1) * 512]
                    rv = getattr(c, "ropevar", 0)
                    if rv == 1:
                        fw.op("dve", "tensor_mul", t1[:, :], psq[tt][:, 0:512], cs)
                        fw.op("act", "activation", qout[:, :], t1[:, :], AF.Copy,
                              bias=0.0, scale=1.0)
                        fw.dma("sp", p.d[dst][m * 128:(m + 1) * 128,
                                              tt * 512:tt * 512 + 512],
                               qout[:, :], tag=f"st_{dst}")
                        continue
                    if rv == 2:
                        fw.op("act", "activation", rot[0:64, :], psq[tt][64:128, 0:512],
                              AF.Copy, bias=0.0, scale=-1.0, writes=(rot,),
                              reads=(psq[tt],))
                        fw.op("act", "activation", rot[64:128, :], psq[tt][0:64, 0:512],
                              AF.Copy, bias=0.0, scale=1.0, writes=(rot,),
                              reads=(psq[tt],))
                        fw.op("dve", "tensor_mul", t2[:, :], rot[:, :], sn)
                        fw.op("act", "activation", qout[:, :], t2[:, :], AF.Copy,
                              bias=0.0, scale=1.0)
                        fw.dma("sp", p.d[dst][m * 128:(m + 1) * 128,
                                              tt * 512:tt * 512 + 512],
                               qout[:, :], tag=f"st_{dst}")
                        continue
                    fw.op("dve", "tensor_mul", t1[:, :], psq[tt][:, 0:512], cs)
                    fw.op("dve", "tensor_scalar", rot[0:64, :],
                          psq[tt][64:128, 0:512], -1.0, None, ALU.mult,
                          writes=(rot,), reads=(psq[tt],))
                    fw.op("dve", "tensor_copy", rot[64:128, :],
                          psq[tt][0:64, 0:512], writes=(rot,), reads=(psq[tt],))
                    fw.op("dve", "tensor_mul", t2[:, :], rot[:, :], sn)
                    fw.op("dve", "tensor_add", qout[:, :], t1[:, :], t2[:, :])
                    fw.dma("sp", p.d[dst][m * 128:(m + 1) * 128,
                                          tt * 512:tt * 512 + 512],
                           qout[:, :], tag=f"st_{dst}")

        if "qk" in parts:
            proj_rope("wq4", c.H0, "qsp")
            proj_rope("wk4", c.KVH, "ksp")

        psv = [p.ps(f"l0psv{i}", [128, 512], F32) for i in range(2)]
        vst = [p.sb(f"l0vst{i}", [128, c.NK * 512], BF16) for i in range(2)]
        ev = [p.sb(f"l0ev{i}", [128, 512], BF16) for i in range(2)]

        def proj_tm(wname, width, dst, silu, nbase=0):
            for ni, n0 in enumerate(range(0, width, 512)):
                w5 = min(512, width - n0)
                sbuf = vst[(nbase + ni) % 2]
                for k in range(c.NK):
                    fw.dma("pool", sbuf[:, k * 512:k * 512 + w5],
                           p.d[wname][k * 128:(k + 1) * 128, n0:n0 + w5],
                           tag=f"{wname}s{n0}_{k}")
                for tz in range(c.NZ):
                    pv = psv[tz % 2]
                    e2 = ev[tz % 2]
                    for k in range(c.NK):
                        fw.op("pe", "matmul", pv[:, 0:w5],
                              xns[:, k * c.T + tz * 128:k * c.T + tz * 128 + 128],
                              sbuf[:, k * 512:k * 512 + w5],
                              start=(k == 0), stop=(k == c.NK - 1))
                    if silu:
                        fw.op("act", "activation", e2[:, 0:w5], pv[:, 0:w5],
                              AF.Silu, bias=0.0, scale=1.0)
                    else:
                        fw.op("dve", "tensor_copy", e2[:, 0:w5], pv[:, 0:w5])
                    fw.dma("sp", p.d[dst][tz * 128:(tz + 1) * 128, n0:n0 + w5],
                           e2[:, 0:w5], tag=f"st_{dst}")

        if "vg" in parts:
            proj_tm("wv", c.KD, "vsp", False)
            proj_tm("wg", c.QD, "g0sp", True)

        if "dt" not in parts:
            return
        h0 = c.H0
        wdt = p.sb("l0wdt", [128, c.NK * h0], BF16)
        for k in range(c.NK):
            fw.dma("pool", wdt[:, k * h0:(k + 1) * h0],
                   p.d["wwin"][k * 128:(k + 1) * 128, :], tag=f"wwin{k}")
        dtbrow = p.sb("dtbrow", [1, h0], F32)
        fw.dma("sp", dtbrow[0:1, :], p.d["dtb0"])
        dtb_bc = bcast(p, dtbrow, h0, name="dtb_bc")
        alrow = p.sb("alrow", [1, h0], F32)
        fw.dma("sp", alrow[0:1, :], p.d["alog0"])
        fw.op("act", "activation", alrow[0:1, :], alrow[0:1, :], AF.Exp,
              bias=0.0, scale=1.0)
        a_bc = bcast(p, alrow, h0, name="a_bc")
        psd = p.ps("l0psd", [128, h0], F32)
        psg = p.ps("l0psg", [128, h0], F32)
        raw = p.sb("dtraw", [128, h0], F32)
        s1 = p.sb("dts1", [128, h0], F32)
        s2 = p.sb("dts2", [128, h0], F32)
        dts = p.sb("dts", [128, h0], F32)
        lnd = p.sb("dtln", [128, h0], F32)
        gts = p.sb("dtg", [128, h0], F32)
        Gs = p.sb("dtG", [128, h0], F32)
        b0s = p.sb("dtb0s", [128, h0], F32)
        for tz in range(c.NZ):
            for k in range(c.NK):
                fw.op("pe", "matmul", psd[:, 0:h0],
                      xns[:, k * c.T + tz * 128:k * c.T + tz * 128 + 128],
                      wdt[:, k * h0:(k + 1) * h0],
                      start=(k == 0), stop=(k == c.NK - 1))
            fw.op("dve", "tensor_add", raw[:, :], psd[:, 0:h0], dtb_bc[:, :])
            softplus(p, dts[:, :], raw[:, :], s1[:, :], s2[:, :])
            fw.dma("sp", p.d["dt0"][tz * 128:(tz + 1) * 128, :], dts[:, :],
                   tag="st_dt0")
            fw.op("act", "activation", lnd[:, :], dts[:, :], AF.Ln,
                  bias=1e-30, scale=1.0)
            fw.op("dve", "scalar_tensor_tensor", gts[:, :], dts[:, :], -1.0,
                  a_bc[:, :], ALU.mult, ALU.mult)
            fw.op("pe", "matmul", psg[:, 0:h0], p.triu[:, :], gts[:, :],
                  start=True, stop=True)
            fw.op("dve", "tensor_copy", Gs[:, :], psg[:, 0:h0])
            fw.dma("sp", p.d["g0cum"][tz * 128:(tz + 1) * 128, :], Gs[:, :],
                   tag="st_g0cum")
            fw.op("dve", "tensor_sub", b0s[:, :], lnd[:, :], Gs[:, :])
            fw.dma("sp", p.d["b0"][tz * 128:(tz + 1) * 128, :], b0s[:, :],
                   tag="st_b0")



def ph_l0_cs(p):
    c, fw = p.cfg, p.fw
    with p.scope():
        ks = p.sb("ks", [128, c.KVH * c.T], BF16)
        for kc in range(c.KVH):
            fw.dma("sp", ks[:, kc * c.T:(kc + 1) * c.T],
                   p.d["ksp"][kc * 128:(kc + 1) * 128, :], tag="ld_ks")
        S = p.sb("S0", [128, c.H0 * c.HD], F32)
        fw.op("dve", "memset", S[:, :], 0.0)
        prow = p.sb("prow0", [1, c.H0], F32)
        fw.op("dve", "memset", prow[0:1, :], 1.0)
        vch = p.sb("vch0", [128, c.KD], BF16)
        Gt = p.sb("Gt0", [128, c.H0], F32)
        b0c = p.sb("b0c0", [128, c.H0], F32)
        glrow = p.sb("glrow0", [1, c.H0], F32)
        expb = p.sb("expb0", [128, c.H0], F32)
        scl = p.sb("scl0", [128, c.H0], F32)
        decbc = p.sb("decbc0", [128, c.H0], F32)
        pst = p.ps("pst0", [128, 128], BF16)
        ktt = p.sb("ktt0", [128, c.KVH * 128], BF16)
        kst = [p.sb(f"kst0{i}", [128, 128], BF16) for i in range(2)]
        ps_cs = [p.ps(f"pscs0{i}", [128, c.HD], F32) for i in range(2)]
        ssend = p.sb("ssend0", [128, c.H0 * c.HD], F32)
        for z in range(c.NZ):
            fw.dma("sp", vch[:, :], p.d["vsp"][z * 128:(z + 1) * 128, :],
                   tag="ld_vch")
            fw.dma("sp", Gt[:, :], p.d["g0cum"][z * 128:(z + 1) * 128, :])
            fw.dma("sp", b0c[:, :], p.d["b0"][z * 128:(z + 1) * 128, :])
            fw.dma("pool", glrow[0:1, :], Gt[127:128, :], tag="fl_gl")
            glb = bcast(p, glrow, c.H0, out=expb)
            fw.op("dve", "tensor_tensor", scl[:, :], b0c[:, :], glb[:, :],
                  ALU.add)
            fw.op("act", "activation", scl[:, :], scl[:, :], AF.Exp,
                  bias=0.0, scale=1.0)
            fw.op("act", "activation", glrow[0:1, :], glrow[0:1, :], AF.Exp,
                  bias=0.0, scale=1.0)
            fw.dma("sp", p.d["p0"][z:z + 1, :], prow[0:1, :], tag="st_p0")
            fw.op("dve", "tensor_mul", prow[0:1, :], prow[0:1, :], glrow[0:1, :])
            bcast(p, glrow, c.H0, out=decbc)
            fw.dma("pool", p.d["s0loc"][z * 128:(z + 1) * 128, :], S[:, :],
                   tag="st_s0loc", sem="s0loc")
            for kc in range(c.KVH):
                fw.op("pe", "transpose", pst[:, :],
                      ks[:, kc * c.T + z * 128:kc * c.T + z * 128 + 128],
                      p.eyeb[:, :])
                fw.op("dve", "tensor_copy", ktt[:, kc * 128:(kc + 1) * 128],
                      pst[:, :])
            for h in range(c.H0):
                kc = h // (c.H0 // c.KVH)
                kb = kst[h % 2]
                pc = ps_cs[h % 2]
                fw.op("dve", "tensor_scalar", kb[:, :],
                      ktt[:, kc * 128:(kc + 1) * 128], scl[:, h:h + 1], None,
                      ALU.mult)
                fw.op("pe", "matmul", pc[:, 0:c.HD], kb[:, :],
                      vch[:, kc * c.HD:(kc + 1) * c.HD], start=True, stop=True)
                hsl = S[:, h * c.HD:(h + 1) * c.HD]
                fw.op("dve", "tensor_scalar", hsl, hsl, decbc[:, h:h + 1],
                      None, ALU.mult, reads=(S,), writes=(S,))
                fw.op("dve", "tensor_tensor", hsl, hsl, pc[:, 0:c.HD], ALU.add,
                      reads=(S, pc), writes=(S,))
        fw.op("dve", "tensor_scalar", ssend[:, :], S[:, :], p.smcol, None,
              ALU.mult)
        fw.dma("sp", p.d["s0fin"], ssend[:, :], tag="st_s0fin")
        if c.fused:
            fw.cc("AllReduce", ALU.add, p.cfg.groups, p.d["s0fin"], p.d["s0init"])



def ph_l0_att(p):
    c, fw = p.cfg, p.fw
    with p.scope():
        src = "s0init"
        Sini = p.sb("Sini0", [128, c.H0 * c.HD], F32)
        fw.dma("sp", Sini[:, :], p.d[src], tag="ld_s0init")
        fw.op("dve", "tensor_scalar", Sini[:, :], Sini[:, :], p.rmcol, None,
              ALU.mult)
        ks = p.sb("ks_a", [128, c.KVH * c.T], BF16)
        for kc in range(c.KVH):
            fw.dma("sp", ks[:, kc * c.T:(kc + 1) * c.T],
                   p.d["ksp"][kc * 128:(kc + 1) * 128, :], tag="ld_ksa")
        gnwrow = p.sb("gnwrow", [1, c.HD], F32)
        fw.dma("sp", gnwrow[0:1, :], p.d["gnw"])
        gnw_bc = bcast(p, gnwrow, c.HD, name="gnw_bc")
        qsb = p.sb("qsb", [128, c.H0 * 128], BF16)
        vch = p.sb("vch_a", [128, c.KD], BF16)
        gch = p.sb("gch_a", [128, c.QD], BF16)
        Gt = p.sb("Gt_a", [128, c.H0], F32)
        b0c = p.sb("b0c_a", [128, c.H0], F32)
        expG = p.sb("expG_a", [128, c.H0], F32)
        p0row = p.sb("p0row", [1, c.H0], F32)
        pbc = p.sb("pbc0", [128, c.H0], F32)
        slsb = p.sb("slsb0", [128, c.H0 * c.HD], BF16)
        pst32 = p.ps("pst32", [128, 128], F32)
        gfm = p.sb("gfm0", [c.H0, 128], F32)
        gflat = p.sb("gflat0", [1, c.H0 * 128], F32)
        szb = [p.sb(f"szb{i}", [128, c.HD], BF16) for i in range(2)]
        dm = [p.sb(f"dm0{i}", [128, 128], F32) for i in range(2)]
        dec = [p.sb(f"dec0{i}", [128, 128], F32) for i in range(2)]
        mt = [p.sb(f"mt0{i}", [128, 128], BF16) for i in range(2)]
        t1e = [p.sb(f"t1e{i}", [128, c.HD], F32) for i in range(2)]
        bankA = [p.ps(f"l0bankA{i}", [128, 256], F32) for i in range(2)]
        bankB = [p.ps(f"l0bankB{i}", [128, 256], F32) for i in range(2)]
        ps_sc = [bankA[i][:, 0:128] for i in range(2)]
        ps_a1 = [bankA[i][:, 128:256] for i in range(2)]
        ps_yi = [bankB[i][:, 0:c.HD] for i in range(2)]
        ps_ya = [bankB[i][:, 128:128 + c.HD] for i in range(2)]
        kq = [dict(sc=f"k_sc{i}", a1=f"k_a1{i}", yi=f"k_yi{i}", ya=f"k_ya{i}")
              for i in range(2)]
        ych = p.sb("ych", [128, c.QD], F32)
        ss = p.sb("ss0", [128, c.H0], F32)
        sqj = p.sb("sqj0", [128, 128], BF16)
        rstd = p.sb("rstd0", [128, c.H0], F32)
        yg = p.sb("yg0", [128, c.QD], BF16)
        yg2 = p.sb("yg20", [128, c.QD], BF16)
        pstb = p.ps("pstb0", [128, 128], BF16)
        tb = [p.sb(f"tb0{i}", [128, 128], BF16) for i in range(2)]
        for z in range(c.NZ):
            fw.dma("sp", qsb[:, :].rearrange("pp (c t) -> pp c t", t=128),
                   p.d["qsp"].rearrange("(c pp) t -> pp c t", pp=128)[:, :, z * 128:(z + 1) * 128],
                   tag="ld_qsb")
            fw.dma("sp", vch[:, :], p.d["vsp"][z * 128:(z + 1) * 128, :],
                   tag="ld_vcha")
            fw.dma("sp", gch[:, :], p.d["g0sp"][z * 128:(z + 1) * 128, :],
                   tag="ld_gch")
            fw.dma("sp", Gt[:, :], p.d["g0cum"][z * 128:(z + 1) * 128, :])
            fw.dma("sp", b0c[:, :], p.d["b0"][z * 128:(z + 1) * 128, :])
            fw.dma("sp", p0row[0:1, :], p.d["p0"][z:z + 1, :], tag="ld_p0")
            fw.dma("sp", slsb[:, :], p.d["s0loc"][z * 128:(z + 1) * 128, :],
                   tag="ld_s0loc")
            fw.op("act", "activation", expG[:, :], Gt[:, :], AF.Exp,
                  bias=0.0, scale=1.0)
            bcast(p, p0row, c.H0, out=pbc)
            fw.op("pe", "transpose", pst32[0:c.H0, 0:128], Gt[:, :],
                  p.eye32[:, :], writes=(pst32,))
            fw.op("dve", "tensor_copy", gfm[0:c.H0, :], pst32[0:c.H0, 0:128],
                  reads=(pst32,), writes=(gfm,))
            fw.dma("pool", gflat[0:1, :], gfm[0:c.H0, :], tag="fl_gfm")
            for h in range(c.H0):
                kc = h // (c.H0 // c.KVH)
                i2 = h % 2
                hsl = slice(h * c.HD, (h + 1) * c.HD)
                fw.op("dve", "scalar_tensor_tensor", szb[i2][:, :],
                      Sini[:, hsl], pbc[:, h:h + 1], slsb[:, hsl],
                      ALU.mult, ALU.add, reads=(Sini, pbc, slsb))
                fw.op("pe", "matmul", ps_a1[i2], p.ones1[0:1, 0:128],
                      gflat[0:1, h * 128:(h + 1) * 128], start=True, stop=True,
                      reads=(p.ones1, gflat), writes=(kq[i2]["a1"],))
                fw.op("pe", "matmul", ps_sc[i2],
                      ks[:, kc * c.T + z * 128:kc * c.T + z * 128 + 128],
                      qsb[:, h * 128:(h + 1) * 128], start=True, stop=True,
                      reads=(ks, qsb), writes=(kq[i2]["sc"],))
                fw.op("dve", "tensor_tensor", dm[i2][:, :], ps_a1[i2],
                      p.mneg[:, :], ALU.add,
                      reads=(kq[i2]["a1"], kq[i2]["sc"], p.mneg),
                      writes=(dm[i2],))
                fw.op("act", "activation", dec[i2][:, :], dm[i2][:, :], AF.Exp,
                      bias=b0c[:, h:h + 1], scale=1.0)
                fw.op("dve", "tensor_tensor", mt[i2][:, :], ps_sc[i2],
                      dec[i2][:, :], ALU.mult,
                      reads=(kq[i2]["sc"], dec[i2]), writes=(mt[i2],))
                fw.op("pe", "matmul", ps_yi[i2],
                      qsb[:, h * 128:(h + 1) * 128], szb[i2][:, :],
                      start=True, stop=True,
                      reads=(qsb, szb[i2]), writes=(kq[i2]["yi"],))
                fw.op("pe", "matmul", ps_ya[i2], mt[i2][:, :],
                      vch[:, kc * c.HD:(kc + 1) * c.HD], start=True, stop=True,
                      reads=(mt[i2], vch), writes=(kq[i2]["ya"],))
                fw.op("dve", "tensor_scalar", t1e[i2][:, :], ps_yi[i2],
                      expG[:, h:h + 1], None, ALU.mult,
                      reads=(kq[i2]["yi"], kq[i2]["ya"], expG), writes=(t1e[i2],))
                fw.op("dve", "tensor_tensor", ych[:, hsl], t1e[i2][:, :],
                      ps_ya[i2], ALU.add,
                      reads=(t1e[i2], kq[i2]["ya"]), writes=(ych,))
            for h in range(c.H0):
                hsl = slice(h * c.HD, (h + 1) * c.HD)
                fw.op("act", "activation", sqj[:, :], ych[:, hsl], AF.Square,
                      bias=0.0, scale=1.0, accum_out=ss[:, h:h + 1],
                      reads=(ych,), writes=(sqj, ss))
            fw.op("act", "activation", rstd[:, :], ss[:, :], AF.Ln,
                  bias=1e-5, scale=1.0 / c.HD)
            fw.op("act", "activation", rstd[:, :], rstd[:, :], AF.Exp,
                  bias=0.0, scale=-0.5)
            for h in range(c.H0):
                hsl = slice(h * c.HD, (h + 1) * c.HD)
                fw.op("dve", "scalar_tensor_tensor", yg[:, hsl], ych[:, hsl],
                      rstd[:, h:h + 1], gnw_bc[:, :], ALU.mult, ALU.mult,
                      reads=(ych, rstd, gnw_bc), writes=(yg,))
            for j in range(c.QD // 512):
                fw.op("dve", "tensor_tensor", yg2[:, j * 512:(j + 1) * 512],
                      yg[:, j * 512:(j + 1) * 512],
                      gch[:, j * 512:(j + 1) * 512], ALU.mult,
                      reads=(yg, gch), writes=(yg2,))
            for m in range(c.QD // 128):
                fw.op("pe", "transpose", pstb[:, :], yg2[:, m * 128:(m + 1) * 128],
                      p.eyeb[:, :], reads=(yg2, p.eyeb))
                fw.op("dve", "tensor_copy", tb[m % 2][:, :], pstb[:, :])
                fw.dma("sp", p.d["ygfm"][m * 128:(m + 1) * 128,
                                         z * 128:(z + 1) * 128],
                       tb[m % 2][:, :], tag="st_ygfm")



def ln_rows(p, ps_s, ps_q, ks, kq):
    c, fw = p.cfg, p.fw
    o = 0
    mean = p.rows[0:1, (o + 0) * c.T:(o + 1) * c.T]
    var = p.rows[0:1, (o + 1) * c.T:(o + 2) * c.T]
    rstd = p.rows[0:1, (o + 2) * c.T:(o + 3) * c.T]
    mrs = p.rows[0:1, (o + 3) * c.T:(o + 4) * c.T]
    fw.op("act", "activation", mean, ps_s[0:1, :], AF.Copy,
          bias=0.0, scale=1.0 / c.D, reads=(ks,), writes=(mean,))
    fw.op("dve", "tensor_mul", var, mean, mean)
    fw.op("dve", "scalar_tensor_tensor", var, ps_q[0:1, :],
          1.0 / c.D, var, ALU.mult, ALU.subtract,
          reads=(kq, var), writes=(var,))
    fw.op("act", "activation", rstd, var, AF.Ln, bias=1e-5, scale=1.0)
    fw.op("act", "activation", rstd, rstd, AF.Exp, bias=0.0, scale=-0.5)
    fw.op("dve", "tensor_mul", mrs, mean, rstd)
    return rstd, mrs


def ln_apply(p, rstd, mrs, xbf, xn, store=None):
    c, fw = p.cfg, p.fw
    tmp = p.sb(p.u("lap"), [128, 512], F32)
    for tt in range(c.T // 512):
        rbc = bcast(p, rstd[0:1, tt * 512:(tt + 1) * 512], 512, name=p.u("larb"))
        mbc = bcast(p, mrs[0:1, tt * 512:(tt + 1) * 512], 512, name=p.u("lamb"))
        for cc in range(c.NK):
            sl = slice(cc * c.T + tt * 512, cc * c.T + tt * 512 + 512)
            fw.op("dve", "tensor_mul", tmp[:, :], xbf[:, sl], rbc[:, :],
                  reads=(xbf, rbc), writes=(tmp,))
            fw.op("dve", "tensor_tensor", xn[:, sl], tmp[:, :], mbc[:, :],
                  ALU.subtract, reads=(tmp, mbc), writes=(xn,))
            if store:
                fw.dma("sp", p.d[store][cc * 128:(cc + 1) * 128,
                                        tt * 512:tt * 512 + 512],
                       xn[:, sl], tag="st_" + store, sem="st_" + store)


def ln_apply_dram(p, rstd, mrs, xbf_dram, xn, store=None):
    c, fw = p.cfg, p.fw
    tmp = p.sb(p.u("lap"), [128, 512], F32)
    xt = [p.sb(p.u("lax"), [128, 512], BF16) for _ in range(2)]
    for tt in range(c.T // 512):
        rbc = bcast(p, rstd[0:1, tt * 512:(tt + 1) * 512], 512, name=p.u("larb"))
        mbc = bcast(p, mrs[0:1, tt * 512:(tt + 1) * 512], 512, name=p.u("lamb"))
        for cc in range(c.NK):
            i2 = cc % 2
            fw.dma("sp", xt[i2][:, :],
                   p.d[xbf_dram][cc * 128:(cc + 1) * 128, tt * 512:tt * 512 + 512],
                   tag="ld_" + xbf_dram)
            sl = slice(cc * c.T + tt * 512, cc * c.T + tt * 512 + 512)
            fw.op("dve", "tensor_mul", tmp[:, :], xt[i2][:, :], rbc[:, :],
                  reads=(xt[i2], rbc), writes=(tmp,))
            fw.op("dve", "tensor_tensor", xn[:, sl], tmp[:, :], mbc[:, :],
                  ALU.subtract, reads=(tmp, mbc), writes=(xn,))
            if store:
                fw.dma("sp", p.d[store][cc * 128:(cc + 1) * 128,
                                        tt * 512:tt * 512 + 512],
                       xn[:, sl], tag="st_" + store, sem="st_" + store)


def topk_gate(p, sc, scw, thr, mk, kiter):
    fw = p.fw
    fw.op("dve", "tensor_copy", scw[:, :], sc[:, :])
    for it in range(kiter):
        fw.op("dve", "tensor_reduce", thr[:, :], scw[:, :], AX.X, ALU.max)
        if it < kiter - 1:
            fw.op("dve", "tensor_scalar", mk[:, :], scw[:, :], thr[:, 0:1],
                  None, ALU.is_ge)
            fw.op("dve", "scalar_tensor_tensor", scw[:, :], mk[:, :], -1e30,
                  scw[:, :], ALU.mult, ALU.add)
    fw.op("dve", "tensor_scalar", mk[:, :], sc[:, :], thr[:, 0:1], None,
          ALU.is_ge)



def ph_l0_post(p):
    c, fw = p.cfg, p.fw
    with p.scope():
        rows1 = []
        with p.scope():
            ygr = p.sb("ygr", [128, (c.QD // 128) * c.T], BF16)
            load_fm(p, ygr, "ygfm", c.QD, c.T)
            ps_s = p.ps("ps_s1", [1, c.T], F32)
            ps_q = p.ps("ps_q1", [1, c.T], F32)
            ps_o = [p.ps(f"ps_o{i}", [128, 512], F32) for i in range(2)]
            wt = [p.sb(f"wo_t{i}", [128, 512], BF16) for i in range(3)]
            xold = [p.sb(f"xo{i}", [128, 512], F32) for i in range(2)]
            x1s = [p.sb(f"x1s{i}", [128, 512], F32) for i in range(2)]
            sqt = [p.sb(f"sq1{i}", [128, 512], BF16) for i in range(2)]
            xb1 = [p.sb(f"xb1{i}", [128, 512], BF16) for i in range(2)]
            for m in range(c.NK):
                mm_k4_stream(p, "wo4", m, c.QD // 128,
                             [ps_o[tt][:, 0:512] for tt in range(c.T // 512)],
                             lambda k, tt: ygr[:, k * c.T + tt * 512:k * c.T + tt * 512 + 512],
                             wt, "wo4w")
                for tt in range(c.T // 512):
                    i2 = tt % 2
                    fw.dma("sp", xold[i2][:, :],
                           p.d["x"][m * 128:(m + 1) * 128, tt * 512:tt * 512 + 512],
                           tag="ld_x0")
                    fw.op("dve", "tensor_tensor", x1s[i2][:, :], ps_o[tt][:, 0:512],
                          xold[i2][:, :], ALU.add)
                    fw.dma("sp", p.d["x1"][m * 128:(m + 1) * 128,
                                           tt * 512:tt * 512 + 512],
                           x1s[i2][:, :], tag="st_x1", sem="st_x1")
                    bsl = xb1[i2][:, :]
                    fw.op("act", "activation", bsl, x1s[i2][:, :], AF.Copy,
                          bias=0.0, scale=1.0, reads=(x1s[i2],), writes=(xb1[i2],))
                    fw.dma("sp", p.d["x1bf"][m * 128:(m + 1) * 128,
                                             tt * 512:tt * 512 + 512],
                           bsl, tag="st_x1bf")
                    fw.op("act", "activation", sqt[i2][:, :], bsl, AF.Square,
                          bias=0.0, scale=1.0, reads=(xb1[i2],), writes=(sqt[i2],))
                    fw.op("pe", "matmul", ps_s[0:1, tt * 512:(tt + 1) * 512],
                          p.onescol[:, 0:1], bsl, start=(m == 0),
                          stop=(m == c.NK - 1), reads=(p.onescol, xb1[i2]),
                          writes=("k_lns1",))
                    fw.op("pe", "matmul", ps_q[0:1, tt * 512:(tt + 1) * 512],
                          p.onescol[:, 0:1], sqt[i2][:, :], start=(m == 0),
                          stop=(m == c.NK - 1), reads=(p.onescol, sqt[i2]),
                          writes=("k_lnq1",))
            rows1 = ln_rows(p, ps_s, ps_q, "k_lns1", "k_lnq1")
        xn1 = p.sb("xn1_r", [128, c.NK * c.T], BF16)
        ln_apply_dram(p, rows1[0], rows1[1], "x1bf", xn1)
        rows2 = None
        with p.scope():
            rws = p.sb("rws", [128, c.NK * c.E], BF16)
            for k in range(c.NK):
                fw.dma("pool", rws[:, k * c.E:(k + 1) * c.E],
                       p.d["rw"][k * 128:(k + 1) * 128, :], tag=f"rw{k}")
            rbrow = p.sb("rbrow", [1, c.E], F32)
            fw.dma("sp", rbrow[0:1, :], p.d["rb"])
            rtb_bc = bcast(p, rbrow, c.E, name="rtb_bc")
            ebs = p.sb("ebs", [c.E, c.D], BF16)
            fw.dma("pool", ebs[:, :], p.d["eb"], tag="ld_eb")
            wdall = p.sb("wdall", [c.E, c.T], BF16)
            wsall = p.sb("wsall", [1, c.T], F32)
            ps_l = p.ps("ps_l", [128, 512], F32)
            pst32r = p.ps("pst32r", [128, 128], F32)
            lg = p.sb("lg", [128, c.E], F32)
            nmx = p.sb("nmx", [128, 1], F32)
            ex = p.sb("ex", [128, c.E], F32)
            ssum = p.sb("ssum", [128, 1], F32)
            sc = p.sb("sc_r", [128, c.E], F32)
            scw = p.sb("scw_r", [128, c.E], F32)
            thr = p.sb("thr_r", [128, 1], F32)
            mk = p.sb("mk_r", [128, c.E], F32)
            wd = p.sb("wd_r", [128, c.E], F32)
            ws = p.sb("ws_r", [128, 1], F32)
            for tz in range(c.NZ):
                for k in range(c.NK):
                    fw.op("pe", "matmul", ps_l[:, 0:c.E],
                          xn1[:, k * c.T + tz * 128:k * c.T + tz * 128 + 128],
                          rws[:, k * c.E:(k + 1) * c.E],
                          start=(k == 0), stop=(k == c.NK - 1))
                fw.op("dve", "tensor_tensor", lg[:, :], ps_l[:, 0:c.E],
                      rtb_bc[:, :], ALU.add)
                fw.op("dve", "tensor_reduce", nmx[:, :], lg[:, :], AX.X,
                      ALU.max, negate=True)
                fw.op("act", "activation", ex[:, :], lg[:, :], AF.Exp,
                      bias=nmx[:, 0:1], scale=1.0)
                fw.op("dve", "tensor_reduce", ssum[:, :], ex[:, :], AX.X, ALU.add)
                fw.op("dve", "reciprocal", ssum[:, :], ssum[:, :])
                fw.op("dve", "tensor_scalar", sc[:, :], ex[:, :], ssum[:, 0:1],
                      None, ALU.mult)
                topk_gate(p, sc, scw, thr, mk, c.KTOP)
                fw.op("dve", "tensor_mul", wd[:, :], mk[:, :], sc[:, :])
                fw.op("dve", "tensor_reduce", ws[:, :], wd[:, :], AX.X, ALU.add)
                fw.dma("pool", wsall[0:1, tz * 128:(tz + 1) * 128], ws[:, 0:1],
                       tag="fl_ws")
                fw.op("pe", "transpose", pst32r[0:c.E, 0:128], wd[:, :],
                      p.eye32[:, :], reads=(wd, p.eye32), writes=("k_wdt",))
                fw.op("dve", "tensor_copy", wdall[0:c.E, tz * 128:(tz + 1) * 128],
                      pst32r[0:c.E, 0:128], reads=("k_wdt",), writes=(wdall,))
            wsb = p.sb("wsb", [128, c.T], F32)
            for tt in range(c.T // 512):
                bcast(p, wsall[0:1, tt * 512:(tt + 1) * 512], 512,
                      out=wsb[:, tt * 512:(tt + 1) * 512])
            ps_b = p.ps("ps_b", [128, 512], F32)
            ps_s2 = p.ps("ps_s2", [1, c.T], F32)
            ps_q2 = p.ps("ps_q2", [1, c.T], F32)
            tsb = [p.sb(f"tsb{i}", [128, 512], F32) for i in range(2)]
            xo1 = [p.sb(f"xo1{i}", [128, 512], F32) for i in range(2)]
            x2s = [p.sb(f"x2s{i}", [128, 512], F32) for i in range(2)]
            sq2 = [p.sb(f"sq2{i}", [128, 512], BF16) for i in range(2)]
            xb2 = [p.sb(f"xb2{i}", [128, 512], BF16) for i in range(2)]
            for m in range(c.NK):
                for tt in range(c.T // 512):
                    i2 = tt % 2
                    fw.op("pe", "matmul", ps_b[:, 0:512],
                          ebs[:, m * 128:(m + 1) * 128],
                          wdall[:, tt * 512:(tt + 1) * 512],
                          start=True, stop=True)
                    fw.op("dve", "tensor_mul", tsb[i2][:, :],
                          xn1[:, m * c.T + tt * 512:m * c.T + tt * 512 + 512],
                          wsb[:, tt * 512:(tt + 1) * 512],
                          reads=(xn1, wsb), writes=(tsb[i2],))
                    fw.op("dve", "tensor_tensor", tsb[i2][:, :], tsb[i2][:, :],
                          ps_b[:, 0:512], ALU.add)
                    fw.dma("sp", xo1[i2][:, :],
                           p.d["x1"][m * 128:(m + 1) * 128, tt * 512:tt * 512 + 512],
                           tag="ld_x1")
                    fw.op("dve", "tensor_tensor", x2s[i2][:, :], tsb[i2][:, :],
                          xo1[i2][:, :], ALU.add)
                    fw.dma("sp", p.d["x2"][m * 128:(m + 1) * 128,
                                           tt * 512:tt * 512 + 512],
                           x2s[i2][:, :], tag="st_x2", sem="st_x2")
                    bsl = xb2[i2][:, :]
                    fw.op("act", "activation", bsl, x2s[i2][:, :], AF.Copy,
                          bias=0.0, scale=1.0, reads=(x2s[i2],), writes=(xb2[i2],))
                    fw.dma("sp", p.d["x2bf"][m * 128:(m + 1) * 128,
                                             tt * 512:tt * 512 + 512],
                           bsl, tag="st_x2bf")
                    fw.op("act", "activation", sq2[i2][:, :], bsl, AF.Square,
                          bias=0.0, scale=1.0, reads=(xb2[i2],), writes=(sq2[i2],))
                    fw.op("pe", "matmul", ps_s2[0:1, tt * 512:(tt + 1) * 512],
                          p.onescol[:, 0:1], bsl, start=(m == 0),
                          stop=(m == c.NK - 1), reads=(p.onescol, xb2[i2]),
                          writes=("k_lns2",))
                    fw.op("pe", "matmul", ps_q2[0:1, tt * 512:(tt + 1) * 512],
                          p.onescol[:, 0:1], sq2[i2][:, :], start=(m == 0),
                          stop=(m == c.NK - 1), reads=(p.onescol, sq2[i2]),
                          writes=("k_lnq2",))
            rows2 = ln_rows(p, ps_s2, ps_q2, "k_lns2", "k_lnq2")
        xn2 = p.sb("xn2_r", [128, c.NK * c.T], BF16)
        ln_apply_dram(p, rows2[0], rows2[1], "x2bf", xn2, store="xn2")



def ph_l1_inproj(p):
    c, fw = p.cfg, p.fw
    with p.scope():
        xns = p.sb("xns1", [128, c.NK * c.T], BF16)
        load_fm(p, xns, "xn2", c.D, c.T)
        gws = [p.sb(f"gws{i}", [128, c.NK * 512], BF16) for i in range(2)]
        psv = [p.ps(f"l1psv{i}", [128, 512], F32) for i in range(2)]
        ev = [p.sb(f"l1ev{i}", [128, 512], BF16) for i in range(2)]
        for n in range(c.INTER // 512):
            sbuf = gws[n % 2]
            for k in range(c.NK):
                fw.dma("pool", sbuf[:, k * 512:(k + 1) * 512],
                       p.d["ipj"][k * 128:(k + 1) * 128, n * 512:(n + 1) * 512],
                       tag=f"ipg{n}_{k}")
            for tz in range(c.NZ):
                i2 = tz % 2
                for k in range(c.NK):
                    fw.op("pe", "matmul", psv[i2][:, 0:512],
                          xns[:, k * c.T + tz * 128:k * c.T + tz * 128 + 128],
                          sbuf[:, k * 512:(k + 1) * 512],
                          start=(k == 0), stop=(k == c.NK - 1))
                fw.op("act", "activation", ev[i2][:, :], psv[i2][:, 0:512],
                      AF.Silu, bias=0.0, scale=1.0)
                fw.dma("sp", p.d["ipg"][tz * 128:(tz + 1) * 128,
                                        n * 512:n * 512 + 512],
                       ev[i2][:, :], tag="st_ipg")
        wt = [p.sb(f"l1w{i}", [128, 512], BF16) for i in range(3)]
        ps2 = [p.ps(f"l1ps2{i}", [128, 512], F32) for i in range(2)]
        ev2 = [p.sb(f"l1eh{i}", [128, 512], BF16) for i in range(2)]
        nc3 = c.CONV // 128
        hal = p.sb("halsb", [128, nc3 * 3], F32)
        for m in range(c.CONV // 128):
            mm_k4_stream(p, "iph4", m, c.NK,
                         [ps2[tt][:, 0:512] for tt in range(c.T // 512)],
                         lambda k, tt: xns[:, k * c.T + tt * 512:k * c.T + tt * 512 + 512],
                         wt, "iph4w")
            for tt in range(c.T // 512):
                fw.op("dve", "tensor_copy", ev2[tt % 2][:, :], ps2[tt][:, 0:512])
                fw.dma("sp", p.d["iph"][m * 128:(m + 1) * 128,
                                        tt * 512:tt * 512 + 512],
                       ev2[tt % 2][:, :], tag="st_iph")
                if tt == c.T // 512 - 1:
                    fw.op("dve", "tensor_copy", hal[:, m * 3:(m + 1) * 3],
                          ev2[tt % 2][:, 509:512], reads=(ev2[tt % 2],),
                          writes=(hal,))
        h1 = c.H1
        wdt = [p.sb(f"l1wd{i}", [128, h1], BF16) for i in range(2)]
        psd = p.ps("l1psd", [128, 512], F32)
        dtb1c = p.sb("dtb1c", [h1, 1], F32)
        fw.dma("sp", dtb1c[:, :], p.d["dtb1"])
        acol = p.sb("acol1", [h1, 1], F32)
        fw.dma("sp", acol[:, :], p.d["alog1"])
        fw.op("act", "activation", acol[:, :], acol[:, :], AF.Exp,
              bias=0.0, scale=1.0)
        raw = p.sb("l1raw", [h1, 512], F32)
        s1 = p.sb("l1s1", [h1, 512], F32)
        s2 = p.sb("l1s2", [h1, 512], F32)
        dts = p.sb("l1dt", [h1, 512], F32)
        lnd = p.sb("l1ln", [h1, 512], F32)
        adt = p.sb("l1ad", [h1, 512], F32)
        for tt in range(c.T // 512):
            for k in range(c.NK):
                w = wdt[k % 2]
                fw.dma("pool", w[:, 0:h1],
                       p.d["ipj"][k * 128:(k + 1) * 128,
                                  c.INTER + c.CONV:c.INTER + c.CONV + h1],
                       tag=f"ipd{tt}_{k}")
                fw.op("pe", "matmul", psd[0:h1, 0:512], w[:, 0:h1],
                      xns[:, k * c.T + tt * 512:k * c.T + tt * 512 + 512],
                      start=(k == 0), stop=(k == c.NK - 1))
            fw.op("dve", "tensor_scalar", raw[:, :], psd[0:h1, 0:512],
                  dtb1c[:, 0:1], None, ALU.add, reads=(psd, dtb1c), writes=(raw,))
            softplus(p, dts[:, :], raw[:, :], s1[:, :], s2[:, :])
            fw.op("dve", "tensor_scalar_min", dts[:, :], dts[:, :], 100.0)
            fw.dma("sp", p.d["dt1"][:, tt * 512:(tt + 1) * 512], dts[:, :],
                   tag="st_dt1", sem="st_dt1")
            fw.op("act", "activation", lnd[:, :], dts[:, :], AF.Ln,
                  bias=1e-30, scale=1.0)
            fw.dma("sp", p.d["lndt1"][:, tt * 512:(tt + 1) * 512], lnd[:, :],
                   tag="st_lndt1", sem="st_lndt1")
            fw.op("dve", "tensor_scalar", adt[:, :], dts[:, :], acol[:, 0:1],
                  -1.0, ALU.mult, ALU.mult, reads=(dts, acol), writes=(adt,))
            fw.dma("sp", p.d["adt1"][:, tt * 512:(tt + 1) * 512], adt[:, :],
                   tag="st_adt1", sem="st_adt1")
        fw.op("dve", "tensor_scalar", hal[:, :], hal[:, :], p.smcol, None,
              ALU.mult)
        fw.dma("sp", p.d["halo"], hal[:, :], tag="st_halo")
        if c.fused:
            fw.cc("AllReduce", ALU.add, c.groups, p.d["halo"], p.d["haloin"])



def ph_l1_convcs(p):
    c, fw = p.cfg, p.fw
    h1 = c.H1
    nc3 = c.CONV // 128
    nhs = c.INTER // 128
    with p.scope():
        dtr = p.sb("dt1r", [h1, c.T], F32)
        fw.dma("sp", dtr[:, :], p.d["dt1"])
        lndr = p.sb("lnd1r", [h1, c.T], F32)
        fw.dma("sp", lndr[:, :], p.d["lndt1"])
        adtr = p.sb("adt1r", [h1, c.T], F32)
        fw.dma("sp", adtr[:, :], p.d["adt1"])
        cwsb = p.sb("cwsb", [128, nc3 * 4], F32)
        fw.dma("sp", cwsb[:, :], p.d["cw"])
        cbsb = p.sb("cbsb", [128, nc3], F32)
        fw.dma("sp", cbsb[:, :], p.d["cb"])
        halb = p.sb("halb", [128, nc3 * 3], BF16)
        hali = p.sb("hali", [128, nc3 * 3], F32)
        fw.dma("sp", hali[:, :], p.d["haloin"], tag="ld_haloin")
        fw.op("dve", "tensor_scalar", halb[:, :], hali[:, :], p.rmcol, None,
              ALU.mult)
        S = p.sb("S1", [128, h1 * c.P1], F32)
        fw.op("dve", "memset", S[:, :], 0.0)
        prow = p.sb("prow1", [1, h1], F32)
        fw.op("dve", "memset", prow[0:1, :], 1.0)
        pst32 = p.ps("l1pst32", [128, 128], F32)
        ps_ac = p.ps("l1psac", [128, 128], F32)
        pstb = p.ps("l1pstb", [128, 128], BF16)
        adtT = p.sb("adtT", [128, h1], F32)
        lndT = p.sb("lndT", [128, h1], F32)
        acum = p.sb("acum1", [128, h1], F32)
        b1s = p.sb("b1s", [128, h1], F32)
        expb1 = p.sb("expb1", [128, h1], F32)
        acfm = p.sb("acfm", [h1, 128], F32)
        acflr = p.sb("acflr", [1, h1 * 128], F32)
        decrow = p.sb("decrow1", [1, h1], F32)
        decbc = p.sb("decbc1", [128, h1], F32)
        scl = p.sb("scl1", [128, h1], F32)
        hb = [p.sb(f"hb{i}", [128, 131], BF16) for i in range(2)]
        acc = [p.sb(f"cacc{i}", [128, 128], F32) for i in range(2)]
        hstm = p.sb("hstm_sb", [128, c.INTER], BF16)
        bcz = p.sb("bcz", [128, (nc3 - nhs) * 128], BF16)
        bt = p.sb("bt1", [128, c.G1 * 128], BF16)
        bsc = [p.sb(f"bsc{i}", [128, 128], BF16) for i in range(2)]
        ps_cs = [p.ps(f"l1pscs{i}", [128, c.P1], F32) for i in range(2)]
        ssend = p.sb("ssend1", [128, h1 * c.P1], F32)
        for z in range(c.NZ):
            fw.op("pe", "transpose", pst32[:, 0:h1],
                  adtr[0:h1, z * 128:(z + 1) * 128], p.eye32[0:h1, 0:h1],
                  reads=(adtr, p.eye32), writes=(pst32,))
            fw.op("dve", "tensor_copy", adtT[:, :], pst32[:, 0:h1],
                  reads=(pst32,), writes=(adtT,))
            fw.op("pe", "matmul", ps_ac[:, 0:h1], p.triu[:, :], adtT[:, :],
                  start=True, stop=True)
            fw.op("dve", "tensor_copy", acum[:, :], ps_ac[:, 0:h1])
            fw.dma("sp", p.d["acumt"][z * 128:(z + 1) * 128, :], acum[:, :],
                   tag="st_acumt", sem="st_acumt")
            fw.op("pe", "transpose", pst32[:, 0:h1],
                  lndr[0:h1, z * 128:(z + 1) * 128], p.eye32[0:h1, 0:h1],
                  reads=(lndr, p.eye32), writes=(pst32,))
            fw.op("dve", "tensor_copy", lndT[:, :], pst32[:, 0:h1],
                  reads=(pst32,), writes=(lndT,))
            fw.op("dve", "tensor_sub", b1s[:, :], lndT[:, :], acum[:, :])
            fw.dma("sp", p.d["b1sp"][z * 128:(z + 1) * 128, :], b1s[:, :],
                   tag="st_b1sp", sem="st_b1sp")

            fw.op("pe", "transpose", pst32[0:h1, 0:128], acum[:, 0:h1],
                  p.eye32[:, :], reads=(acum, p.eye32), writes=(pst32,))
            fw.op("dve", "tensor_copy", acfm[0:h1, :], pst32[0:h1, :],
                  reads=(pst32,), writes=(acfm,))
            fw.dma("pool", acflr[0:1, :], acfm[0:h1, :], tag="fl_acfm")
            fw.dma("sp", p.d["acfl"][z:z + 1, :], acflr[0:1, :], tag="st_acfl",
                   sem="st_acfl")
            fw.dma("pool", decrow[0:1, :], acum[127:128, 0:h1], tag="fl_dec")
            glb1 = bcast(p, decrow, h1, out=expb1)
            fw.op("dve", "tensor_tensor", scl[:, :], b1s[:, :], glb1[:, :],
                  ALU.add)
            fw.op("act", "activation", scl[:, :], scl[:, :], AF.Exp,
                  bias=0.0, scale=1.0)
            fw.op("act", "activation", decrow[0:1, :], decrow[0:1, :], AF.Exp,
                  bias=0.0, scale=1.0)
            fw.dma("sp", p.d["p1"][z:z + 1, :], prow[0:1, :], tag="st_p1",
                   sem="st_p1")
            fw.op("dve", "tensor_mul", prow[0:1, :], prow[0:1, :], decrow[0:1, :])
            bcast(p, decrow, h1, out=decbc)
            for pc in range(nc3):
                i2 = pc % 2
                if z == 0:
                    fw.op("dve", "tensor_copy", hb[i2][:, 0:3],
                          halb[:, pc * 3:(pc + 1) * 3], reads=(halb,),
                          writes=(hb[i2],))
                    fw.dma("sp", hb[i2][:, 3:131],
                           p.d["iph"][pc * 128:(pc + 1) * 128, 0:128],
                           tag="ld_hb")
                else:
                    fw.dma("sp", hb[i2][:, 0:131],
                           p.d["iph"][pc * 128:(pc + 1) * 128,
                                      z * 128 - 3:z * 128 + 128],
                           tag="ld_hb")
                fw.op("dve", "tensor_scalar", acc[i2][:, :], hb[i2][:, 0:128],
                      cwsb[:, pc * 4:pc * 4 + 1], None, ALU.mult,
                      reads=(hb[i2], cwsb), writes=(acc[i2],))
                fw.op("dve", "scalar_tensor_tensor", acc[i2][:, :],
                      hb[i2][:, 1:129], cwsb[:, pc * 4 + 1:pc * 4 + 2],
                      acc[i2][:, :], ALU.mult, ALU.add,
                      reads=(hb[i2], cwsb, acc[i2]), writes=(acc[i2],))
                fw.op("dve", "scalar_tensor_tensor", acc[i2][:, :],
                      hb[i2][:, 2:130], cwsb[:, pc * 4 + 2:pc * 4 + 3],
                      acc[i2][:, :], ALU.mult, ALU.add,
                      reads=(hb[i2], cwsb, acc[i2]), writes=(acc[i2],))
                fw.op("dve", "scalar_tensor_tensor", acc[i2][:, :],
                      hb[i2][:, 3:131], cwsb[:, pc * 4 + 3:pc * 4 + 4],
                      acc[i2][:, :], ALU.mult, ALU.add,
                      reads=(hb[i2], cwsb, acc[i2]), writes=(acc[i2],))
                if pc < nhs:
                    fw.op("act", "activation", hb[i2][:, 0:128], acc[i2][:, :],
                          AF.Silu, bias=cbsb[:, pc:pc + 1], scale=1.0,
                          reads=(acc[i2], cbsb), writes=(hb[i2],))
                    fw.op("pe", "transpose", pstb[:, :], hb[i2][:, 0:128],
                          p.eyeb[:, :], reads=(hb[i2], p.eyeb), writes=(pstb,))
                    fw.op("dve", "tensor_copy", hstm[:, pc * 128:(pc + 1) * 128],
                          pstb[:, :], reads=(pstb,), writes=(hstm,))
                else:
                    fw.op("act", "activation",
                          bcz[:, (pc - nhs) * 128:(pc - nhs + 1) * 128],
                          acc[i2][:, :], AF.Silu, bias=cbsb[:, pc:pc + 1],
                          scale=1.0, reads=(acc[i2], cbsb), writes=(bcz,))
            fw.dma("sp", p.d["hstm"][z * 128:(z + 1) * 128, :], hstm[:, :],
                   tag="st_hstm", sem="st_hstm")
            for bcid in range(nc3 - nhs):
                fw.dma("sp", p.d["bcfm"][bcid * 128:(bcid + 1) * 128,
                                         z * 128:(z + 1) * 128],
                       bcz[:, bcid * 128:(bcid + 1) * 128], tag="st_bcfm",
                       sem="st_bcfm")
            for g in range(c.G1):
                fw.op("pe", "transpose", pstb[:, :], bcz[:, g * 128:(g + 1) * 128],
                      p.eyeb[:, :], reads=(bcz, p.eyeb), writes=(pstb,))
                fw.op("dve", "tensor_copy", bt[:, g * 128:(g + 1) * 128],
                      pstb[:, :], reads=(pstb,), writes=(bt,))
            fw.dma("pool", p.d["s1loc"][z * c.N1:(z + 1) * c.N1, :], S[:, :],
                   tag="st_s1loc", sem="s1loc")
            for h in range(h1):
                g = h // (h1 // c.G1)
                i2 = h % 2
                fw.op("act", "activation", bsc[i2][:, :],
                      bt[:, g * 128:(g + 1) * 128], AF.Copy, bias=0.0,
                      scale=scl[:, h:h + 1], reads=(bt, scl), writes=(bsc[i2],))
                fw.op("pe", "matmul", ps_cs[i2][:, 0:c.P1], bsc[i2][:, :],
                      hstm[:, h * c.P1:(h + 1) * c.P1], start=True, stop=True,
                      reads=(bsc[i2], hstm), writes=(f"k_cs{i2}",))
                hsl = S[:, h * c.P1:(h + 1) * c.P1]
                fw.op("dve", "scalar_tensor_tensor", hsl, hsl,
                      decbc[:, h:h + 1], ps_cs[i2][:, 0:c.P1],
                      ALU.mult, ALU.add, reads=(S, decbc, f"k_cs{i2}"),
                      writes=(S,))
        fw.op("dve", "tensor_scalar", ssend[:, :], S[:, :], p.smcol, None,
              ALU.mult)
        fw.dma("sp", p.d["s1fin"], ssend[:, :], tag="st_s1fin")
        if c.fused:
            fw.cc("AllReduce", ALU.add, c.groups, p.d["s1fin"], p.d["s1init"])



def ph_l1_ssd(p):
    c, fw = p.cfg, p.fw
    h1 = c.H1
    nhs = c.INTER // 128
    nbc = 2 * c.G1 * c.N1 // 128
    with p.scope():
        Sini = p.sb("Sini1", [128, h1 * c.P1], BF16)
        sini32 = p.sb("sini32", [128, 512], F32)
        for o in range(0, h1 * c.P1, 512):
            fw.dma("sp", sini32[:, :], p.d["s1init"][:, o:o + 512],
                   tag="ld_s1init")
            fw.op("dve", "tensor_scalar", Sini[:, o:o + 512], sini32[:, :],
                  p.rmcol, None, ALU.mult, reads=(sini32, p.smrm),
                  writes=(Sini,))
        nwrow = p.sb("nwrow", [1, c.INTER], F32)
        fw.dma("sp", nwrow[0:1, :], p.d["nw"])
        nw_bc = p.sb("nw_bc", [128, c.INTER], BF16)
        for o in range(0, c.INTER, 512):
            fw.op("pe", "matmul", p.ps_bc[:, 0:512], p.ones1[0:1, 0:128],
                  nwrow[0:1, o:o + 512], start=True, stop=True,
                  reads=(p.ones1, nwrow), writes=(p.ps_bc,))
            fw.op("dve", "tensor_copy", nw_bc[:, o:o + 512], p.ps_bc[:, 0:512],
                  reads=(p.ps_bc,), writes=(nw_bc,))
        dvrow = p.sb("dvrow", [1, h1], F32)
        fw.dma("sp", dvrow[0:1, :], p.d["dvec"])
        dcols = bcast(p, dvrow, h1, name="dcols")
        slsb = p.sb("slsb1", [128, h1 * c.P1], BF16)
        hsz = p.sb("hsz", [128, c.INTER], BF16)
        bcch = p.sb("bcch", [128, nbc * 128], BF16)
        acumz = p.sb("acumz", [128, h1], F32)
        b1z = p.sb("b1z", [128, h1], F32)
        expA = p.sb("expA1", [128, h1], F32)
        p1row = p.sb("p1row", [1, h1], F32)
        pbc1 = p.sb("pbc1", [128, h1], F32)
        segw = min(4096, h1 * 128)
        acflr = p.sb("acflr_s", [1, segw], F32)
        szb = p.sb("szb1x", [128, h1 * c.P1], BF16)
        cbg = [p.sb(f"cbg{i}", [128, 128], F32) for i in range(2)]
        ps_cb = [p.ps(f"pscb{i}", [128, 128], F32) for i in range(2)]
        bankY = [p.ps(f"l1bankY{i}", [128, 128], F32) for i in range(2)]
        ps_yd = [bankY[i][:, 0:c.P1] for i in range(2)]
        ps_yo = [bankY[i][:, c.P1:2 * c.P1] for i in range(2)]
        ps_a4 = [p.ps(f"l1bankA{i}", [128, 512], F32) for i in range(2)]
        dm4 = [p.sb(f"dm4_{i}", [128, 512], F32) for i in range(2)]
        mneg4 = p.sb("mneg4", [128, 512], F32)
        for q4 in range(4):
            fw.op("dve", "tensor_copy", mneg4[:, q4 * 128:(q4 + 1) * 128],
                  p.mneg[:, :], reads=(p.mneg,), writes=(mneg4,))
        dec2 = [p.sb(f"dec2_{i}", [128, 128], F32) for i in range(2)]
        mt = [p.sb(f"mt1{i}", [128, 128], BF16) for i in range(2)]
        u1 = [p.sb(f"u1_{i}", [128, c.P1], F32) for i in range(2)]
        u2 = u1
        ych = p.sb("ych1", [128, c.INTER], BF16)
        gch = p.sb("gch1", [128, c.INTER], BF16)
        gg = slsb
        ss = p.sb("ss1", [128, c.G1], F32)
        sqj = p.sb("sqj1", [128, c.INTER // c.G1], BF16)
        rstd = p.sb("rstd1", [128, c.G1], F32)
        ggn = ych
        pstb = p.ps("pstb1", [128, 128], BF16)
        tb = [p.sb(f"tb1{i}", [128, 128], BF16) for i in range(2)]
        for z in range(c.NZ):
            fw.dma("sp", slsb[:, :], p.d["s1loc"][z * c.N1:(z + 1) * c.N1, :],
                   tag="ld_s1loc")
            fw.dma("sp", hsz[:, :], p.d["hstm"][z * 128:(z + 1) * 128, :],
                   tag="ld_hstm")
            fw.dma("sp", bcch[:, :].rearrange("pp (cc t) -> pp cc t", t=128),
                   p.d["bcfm"].rearrange("(cc pp) t -> pp cc t", pp=128)[:, :, z * 128:(z + 1) * 128],
                   tag="ld_bcfm")
            fw.dma("sp", gch[:, :], p.d["ipg"][z * 128:(z + 1) * 128, :],
                   tag="ld_ipg")
            fw.dma("sp", acumz[:, :], p.d["acumt"][z * 128:(z + 1) * 128, :])
            fw.dma("sp", b1z[:, :], p.d["b1sp"][z * 128:(z + 1) * 128, :])
            fw.dma("sp", p1row[0:1, :], p.d["p1"][z:z + 1, :], tag="ld_p1")
            acfl_seg = [-1]
            fw.op("act", "activation", expA[:, :], acumz[:, :], AF.Exp,
                  bias=0.0, scale=1.0)
            bcast(p, p1row, h1, out=pbc1)
            for h in range(h1):
                hsl = slice(h * c.P1, (h + 1) * c.P1)
                fw.op("dve", "scalar_tensor_tensor", szb[:, hsl],
                      Sini[:, hsl], pbc1[:, h:h + 1], slsb[:, hsl],
                      ALU.mult, ALU.add, reads=(Sini, pbc1, slsb), writes=(szb,))
            for g in range(c.G1):
                ig = g % 2
                fw.op("pe", "matmul", ps_cb[ig][:, 0:128],
                      bcch[:, g * 128:(g + 1) * 128],
                      bcch[:, (c.G1 + g) * 128:(c.G1 + g + 1) * 128],
                      start=True, stop=True, reads=(bcch,),
                      writes=(f"k_cb{ig}",))
                fw.op("dve", "tensor_tensor", cbg[ig][:, :], ps_cb[ig][:, 0:128],
                      p.triu[:, :], ALU.mult, reads=(f"k_cb{ig}", p.triu),
                      writes=(cbg[ig],))
                for hh in range(h1 // c.G1):
                    h = g * (h1 // c.G1) + hh
                    i2 = h % 2
                    hsl = slice(h * c.P1, (h + 1) * c.P1)
                    if hh % 4 == 0:
                        seg = (h * 128) // segw
                        if seg != acfl_seg[0]:
                            fw.dma("sp", acflr[0:1, :],
                                   p.d["acfl"][z:z + 1,
                                               seg * segw:(seg + 1) * segw],
                                   tag="ld_acfl")
                            acfl_seg[0] = seg
                        ho = (h * 128) % segw
                        q2 = (hh // 4) % 2
                        fw.op("pe", "matmul", ps_a4[q2][:, 0:512],
                              p.ones1[0:1, 0:128], acflr[0:1, ho:ho + 512],
                              start=True, stop=True, reads=(p.ones1, acflr),
                              writes=(f"k1_a4{q2}",))
                        fw.op("dve", "tensor_tensor", dm4[q2][:, :],
                              ps_a4[q2][:, 0:512], mneg4[:, :], ALU.add,
                              reads=(f"k1_a4{q2}", mneg4), writes=(dm4[q2],))
                    q2 = (hh // 4) % 2
                    dsl = slice((hh % 4) * 128, (hh % 4 + 1) * 128)
                    fw.op("act", "activation", dec2[i2][:, :], dm4[q2][:, dsl],
                          AF.Exp, bias=b1z[:, h:h + 1], scale=1.0,
                          reads=(dm4[q2], b1z), writes=(dec2[i2],))
                    fw.op("dve", "tensor_tensor", mt[i2][:, :], cbg[ig][:, :],
                          dec2[i2][:, :], ALU.mult,
                          reads=(cbg[ig], dec2[i2]), writes=(mt[i2],))
                    fw.op("pe", "matmul", ps_yd[i2], mt[i2][:, :],
                          hsz[:, hsl], start=True, stop=True,
                          reads=(mt[i2], hsz), writes=(f"k1_yd{i2}",))
                    fw.op("pe", "matmul", ps_yo[i2],
                          bcch[:, (c.G1 + g) * 128:(c.G1 + g + 1) * 128],
                          szb[:, hsl], start=True, stop=True,
                          reads=(bcch, szb), writes=(f"k1_yo{i2}",))
                    fw.op("act", "activation", u1[i2][:, :], ps_yo[i2],
                          AF.Copy, bias=0.0, scale=expA[:, h:h + 1],
                          reads=(f"k1_yo{i2}", expA), writes=(u1[i2],))
                    fw.op("dve", "tensor_tensor", u2[i2][:, :], ps_yd[i2],
                          u1[i2][:, :], ALU.add,
                          reads=(f"k1_yd{i2}", u1[i2]), writes=(u2[i2],))
                    fw.op("dve", "scalar_tensor_tensor", ych[:, hsl],
                          hsz[:, hsl], dcols[:, h:h + 1], u2[i2][:, :],
                          ALU.mult, ALU.add, reads=(hsz, dcols, u2[i2]),
                          writes=(ych,))
            for j in range(c.INTER // 512):
                fw.op("dve", "tensor_tensor", gg[:, j * 512:(j + 1) * 512],
                      ych[:, j * 512:(j + 1) * 512],
                      gch[:, j * 512:(j + 1) * 512], ALU.mult,
                      reads=(ych, gch), writes=(gg,))
            gsz = c.INTER // c.G1
            for g in range(c.G1):
                fw.op("act", "activation", sqj[:, :], gg[:, g * gsz:(g + 1) * gsz],
                      AF.Square, bias=0.0, scale=1.0,
                      accum_out=ss[:, g:g + 1], reads=(gg,), writes=(sqj, ss))
            fw.op("act", "activation", rstd[:, :], ss[:, :], AF.Ln,
                  bias=1e-5, scale=1.0 / gsz)
            fw.op("act", "activation", rstd[:, :], rstd[:, :], AF.Exp,
                  bias=0.0, scale=-0.5)
            for g in range(c.G1):
                fw.op("dve", "scalar_tensor_tensor", ggn[:, g * gsz:(g + 1) * gsz],
                      gg[:, g * gsz:(g + 1) * gsz], rstd[:, g:g + 1],
                      nw_bc[:, g * gsz:(g + 1) * gsz], ALU.mult, ALU.mult,
                      reads=(gg, rstd, nw_bc), writes=(ggn,))
            for m in range(c.INTER // 128):
                fw.op("pe", "transpose", pstb[:, :], ggn[:, m * 128:(m + 1) * 128],
                      p.eyeb[:, :], reads=(ggn, p.eyeb), writes=("k_ggT",))
                fw.op("dve", "tensor_copy", tb[m % 2][:, :], pstb[:, :],
                      reads=("k_ggT",), writes=(tb[m % 2],))
                fw.dma("sp", p.d["ggfm"][m * 128:(m + 1) * 128,
                                         z * 128:(z + 1) * 128],
                       tb[m % 2][:, :], tag="st_ggfm")



def ph_l1_post(p):
    c, fw = p.cfg, p.fw
    nhs = c.INTER // 128
    with p.scope():
        rows3 = None
        with p.scope():
            ggr = p.sb("ggr", [128, nhs * c.T], BF16)
            load_fm(p, ggr, "ggfm", c.INTER, c.T)
            ps_s = p.ps("ps_s3", [1, c.T], F32)
            ps_q = p.ps("ps_q3", [1, c.T], F32)
            ps_o = [p.ps(f"ps3_o{i}", [128, 512], F32) for i in range(2)]
            wt = [p.sb(f"op_t{i}", [128, 512], BF16) for i in range(3)]
            xold = [p.sb(f"xo3{i}", [128, 512], F32) for i in range(2)]
            x3s = [p.sb(f"x3s{i}", [128, 512], F32) for i in range(2)]
            sqt = [p.sb(f"sq3{i}", [128, 512], BF16) for i in range(2)]
            xb3 = [p.sb(f"xb3{i}", [128, 512], BF16) for i in range(2)]
            for m in range(c.NK):
                mm_k4_stream(p, "opj4", m, nhs,
                             [ps_o[tt][:, 0:512] for tt in range(c.T // 512)],
                             lambda k, tt: ggr[:, k * c.T + tt * 512:k * c.T + tt * 512 + 512],
                             wt, "opj4w")
                for tt in range(c.T // 512):
                    i2 = tt % 2
                    fw.dma("sp", xold[i2][:, :],
                           p.d["x2"][m * 128:(m + 1) * 128, tt * 512:tt * 512 + 512],
                           tag="ld_x2")
                    fw.op("dve", "tensor_tensor", x3s[i2][:, :], ps_o[tt][:, 0:512],
                          xold[i2][:, :], ALU.add)
                    fw.dma("sp", p.d["x3"][m * 128:(m + 1) * 128,
                                           tt * 512:tt * 512 + 512],
                           x3s[i2][:, :], tag="st_x3", sem="st_x3")
                    bsl = xb3[i2][:, :]
                    fw.op("act", "activation", bsl, x3s[i2][:, :], AF.Copy,
                          bias=0.0, scale=1.0, reads=(x3s[i2],), writes=(xb3[i2],))
                    fw.dma("sp", p.d["x3bf"][m * 128:(m + 1) * 128,
                                             tt * 512:tt * 512 + 512],
                           bsl, tag="st_x3bf")
                    fw.op("act", "activation", sqt[i2][:, :], bsl, AF.Square,
                          bias=0.0, scale=1.0, reads=(xb3[i2],), writes=(sqt[i2],))
                    fw.op("pe", "matmul", ps_s[0:1, tt * 512:(tt + 1) * 512],
                          p.onescol[:, 0:1], bsl, start=(m == 0),
                          stop=(m == c.NK - 1), reads=(p.onescol, xb3[i2]),
                          writes=("k_lns3",))
                    fw.op("pe", "matmul", ps_q[0:1, tt * 512:(tt + 1) * 512],
                          p.onescol[:, 0:1], sqt[i2][:, :], start=(m == 0),
                          stop=(m == c.NK - 1), reads=(p.onescol, sqt[i2]),
                          writes=("k_lnq3",))
            rows3 = ln_rows(p, ps_s, ps_q, "k_lns3", "k_lnq3")
        xn3 = p.sb("xn3_r", [128, c.NK * c.T], BF16)
        ln_apply_dram(p, rows3[0], rows3[1], "x3bf", xn3)
        Gall = p.sb("Gall", [128, c.NZ * c.MK], F32)
        with p.scope():
            wrs = p.sb("wrs", [128, c.NK * c.MK], BF16)
            for k in range(c.NK):
                fw.dma("pool", wrs[:, k * c.MK:(k + 1) * c.MK],
                       p.d["wr"][k * 128:(k + 1) * 128, :], tag=f"wr{k}")
            ps_l = p.ps("ps_l3", [128, 512], F32)
            lg = p.sb("lg3", [128, c.MK], F32)
            nmx = p.sb("nmx3", [128, 1], F32)
            ex = p.sb("ex3", [128, c.MK], F32)
            ssum = p.sb("ssum3", [128, 1], F32)
            scw = p.sb("scw3", [128, c.MK], F32)
            thr = p.sb("thr3", [128, 1], F32)
            mk = p.sb("mk3", [128, c.MK], F32)
            eg = p.sb("eg3", [128, c.MK], F32)
            for tz in range(c.NZ):
                for k in range(c.NK):
                    fw.op("pe", "matmul", ps_l[:, 0:c.MK],
                          xn3[:, k * c.T + tz * 128:k * c.T + tz * 128 + 128],
                          wrs[:, k * c.MK:(k + 1) * c.MK],
                          start=(k == 0), stop=(k == c.NK - 1))
                fw.op("dve", "tensor_copy", lg[:, :], ps_l[:, 0:c.MK])
                fw.op("dve", "tensor_reduce", nmx[:, :], lg[:, :], AX.X,
                      ALU.max, negate=True)
                fw.op("act", "activation", ex[:, :], lg[:, :], AF.Exp,
                      bias=nmx[:, 0:1], scale=1.0)
                topk_gate(p, ex, scw, thr, mk, c.MTOP)
                fw.op("dve", "tensor_mul", eg[:, :], ex[:, :], mk[:, :])
                fw.op("dve", "tensor_reduce", ssum[:, :], eg[:, :], AX.X, ALU.add)
                fw.op("dve", "reciprocal", ssum[:, :], ssum[:, :])
                fw.op("dve", "tensor_scalar", Gall[:, tz * c.MK:(tz + 1) * c.MK],
                      eg[:, :], ssum[:, 0:1], None, ALU.mult,
                      reads=(eg, ssum), writes=(Gall,))
        hgf = p.sb("hgf", [128, (c.KR // 128) * c.T], BF16)
        with p.scope():
            ast = [p.sb(f"ast{i}", [128, c.NK * 512], BF16) for i in range(2)]
            ps_h = [p.ps(f"ps_h3{i}", [128, 512], F32) for i in range(2)]
            hgt = [p.sb(f"hgt{i}", [128, 512], BF16) for i in range(2)]
            pstb3 = p.ps("pstb3", [128, 128], BF16)
            tb3 = [p.sb(f"tb3{i}", [128, 128], BF16) for i in range(2)]
            for n in range(c.KR // 512):
                sbuf = ast[n % 2]
                for k in range(c.NK):
                    fw.dma("pool", sbuf[:, k * 512:(k + 1) * 512],
                           p.d["afl"][k * 128:(k + 1) * 128, n * 512:(n + 1) * 512],
                           tag=f"afl{n}_{k}")
                for tz in range(c.NZ):
                    i2 = tz % 2
                    for k in range(c.NK):
                        fw.op("pe", "matmul", ps_h[i2][:, 0:512],
                              xn3[:, k * c.T + tz * 128:k * c.T + tz * 128 + 128],
                              sbuf[:, k * 512:(k + 1) * 512],
                              start=(k == 0), stop=(k == c.NK - 1))
                    nex = 512 // c.R
                    gv = Gall[:, tz * c.MK + n * nex: tz * c.MK + (n + 1) * nex]
                    fw.op("dve", "tensor_tensor",
                          hgt[i2][:, :].rearrange("pp (e r) -> pp e r", r=c.R),
                          ps_h[i2][:, 0:512].rearrange("pp (e r) -> pp e r", r=c.R),
                          gv.unsqueeze(2).broadcast_to([128, nex, c.R]),
                          ALU.mult, reads=(ps_h[i2], Gall), writes=(hgt[i2],))
                    for j in range(4):
                        fw.op("pe", "transpose", pstb3[:, :],
                              hgt[i2][:, j * 128:(j + 1) * 128], p.eyeb[:, :],
                              reads=(hgt[i2], p.eyeb), writes=(pstb3,))
                        kr = n * 4 + j
                        fw.op("dve", "tensor_copy",
                              hgf[:, kr * c.T + tz * 128: kr * c.T + tz * 128 + 128],
                              pstb3[:, :], reads=(pstb3,), writes=(hgf,))
        with p.scope():
            wt3 = [p.sb(f"bfl_t{i}", [128, 512], BF16) for i in range(3)]
            ps_f = [p.ps(f"ps_f{i}", [128, 512], F32) for i in range(2)]
            xo4 = [p.sb(f"xo4{i}", [128, 512], F32) for i in range(2)]
            yt = [p.sb(f"yt4{i}", [128, 512], F32) for i in range(2)]
            for m in range(c.NK):
                mm_k4_stream(p, "bfl4", m, c.KR // 128,
                             [ps_f[tt][:, 0:512] for tt in range(c.T // 512)],
                             lambda k, tt: hgf[:, k * c.T + tt * 512:k * c.T + tt * 512 + 512],
                             wt3, "bfl4w")
                for tt in range(c.T // 512):
                    i2 = tt % 2
                    fw.dma("sp", xo4[i2][:, :],
                           p.d["x3"][m * 128:(m + 1) * 128, tt * 512:tt * 512 + 512],
                           tag="ld_x3")
                    fw.op("dve", "scalar_tensor_tensor", yt[i2][:, :],
                          ps_f[tt][:, 0:512], 2.0, xo4[i2][:, :],
                          ALU.mult, ALU.add)
                    fw.dma("sp", p.d["y"][m * 128:(m + 1) * 128,
                                          tt * 512:tt * 512 + 512],
                           yt[i2][:, :], tag="st_y", sem="st_y")



def ph_ln0(p):
    ph_ln(p, None, "xn0", cast_from="x")


CONSTS = ["eye", "triu", "mneg", "smask", "rmask"]
PHASE_IO = {
    "ln0": (ph_ln0, ["x"], ["xn0"]),
    "l0proj": (ph_l0_proj, ["xn0", "wq4", "wk4", "wv", "wg", "wwin", "dtb0",
                            "alog0", "rcos", "rsin"],
               ["qsp", "ksp", "vsp", "g0sp", "dt0", "g0cum", "b0"]),
    "l0cs": (ph_l0_cs, ["ksp", "vsp", "g0cum", "b0"],
             ["s0loc", "p0", "s0fin"]),
    "l0att": (ph_l0_att, ["qsp", "ksp", "vsp", "g0sp", "g0cum", "b0", "p0",
                          "s0loc", "s0init", "gnw"], ["ygfm"]),
    "l0post": (ph_l0_post, ["ygfm", "x", "wo4", "rw", "rb", "eb"],
               ["x1", "x2", "xn2", "x1bf", "x2bf"]),
    "l1inproj": (ph_l1_inproj, ["xn2", "ipj", "iph4", "dtb1", "alog1"],
                 ["ipg", "iph", "dt1", "lndt1", "adt1", "halo"]),
    "l1convcs": (ph_l1_convcs, ["iph", "haloin", "cw", "cb", "dt1", "lndt1",
                                "adt1"],
                 ["hstm", "bcfm", "acumt", "b1sp", "acfl", "p1", "s1loc",
                  "s1fin"]),
    "l1ssd": (ph_l1_ssd, ["s1loc", "s1init", "hstm", "bcfm", "ipg", "acumt",
                          "b1sp", "acfl", "p1", "nw", "dvec"], ["ggfm"]),
    "l1post": (ph_l1_post, ["ggfm", "x2", "opj4", "wr", "afl", "bfl4"],
               ["x3", "y", "x3bf"]),
}
CC_ROUTE = {"s0init": "s0fin", "haloin": "halo", "s1init": "s1fin"}
CC_PROD = {"s0fin": "l0cs", "halo": "l1inproj", "s1fin": "l1convcs"}

LAUNCHES = [["ln0", "l0proj", "l0cs"], ["l0att"], ["l0post"],
            ["l1inproj"], ["l1convcs"], ["l1ssd"], ["l1post"]]

_BUILD_CACHE = {}


def build_launch(cfg, phases, key, out_force=()):
    if key in _BUILD_CACHE:
        return _BUILD_CACHE[key]
    reg = declare_tensors(cfg)
    nc = bass.Bass("TRN2", target_bir_lowering=False)
    writes_here = set()
    reads_here = set(CONSTS)
    for ph in phases:
        _, rds, wrs = PHASE_IO[ph]
        for r in rds:
            if r not in writes_here:
                reads_here.add(r)
        writes_here.update(wrs)
        if cfg.fused:
            for ini, fin in CC_ROUTE.items():
                if fin in wrs:
                    writes_here.add(ini)
    later_reads = set()
    if not cfg.fused:
        for ph, (fn, rds, wrs) in PHASE_IO.items():
            if ph in phases:
                continue
            later_reads.update(rds)
            for ini in CC_ROUTE:
                if ini in rds:
                    later_reads.add(CC_ROUTE[ini])
    ins, outs = [], []
    with contextlib.ExitStack() as es:
        p = P(nc, cfg, es)
        for name, (shape, dt) in reg.items():
            if name in reads_here and name not in writes_here:
                kind = "ExternalInput"
                ins.append(name)
            elif name in out_force and name in writes_here:
                kind = "ExternalOutput"
                outs.append(name)
            elif name == "y" and "y" in writes_here:
                kind = "ExternalOutput"
                outs.append(name)
            elif name in writes_here and (not cfg.fused) and (
                    name in later_reads or name in CC_ROUTE.values()):
                kind = "ExternalOutput"
                outs.append(name)
            elif name in writes_here or name in reads_here:
                kind = "Internal"
            else:
                continue
            p.d[name] = nc.dram_tensor(name, list(shape), dt, kind=kind).ap()
        ph_consts(p)
        for ph in phases:
            PHASE_IO[ph][0](p)
        p.fw.finish()
        p.fw.flush(es)
    _BUILD_CACHE[key] = (nc, ins, outs)
    return _BUILD_CACHE[key]


def rope_tables(L, D=128, theta=1e4):
    inv = 1.0 / theta ** (np.arange(0, D, 2, dtype=np.float32) / D)
    ang = np.arange(L, dtype=np.float32)[:, None] * inv
    cos = np.concatenate([np.cos(ang), np.cos(ang)], -1).astype(np.float32)
    sin = np.concatenate([np.sin(ang), np.sin(ang)], -1).astype(np.float32)
    return cos, sin


def prep_inputs(cfg, inp):
    c = cfg
    ct = lambda a: np.ascontiguousarray(a, dtype=np.float32)
    shared = {}
    shared["wq"] = ct(inp["L0_attn_Wq"].T)
    shared["wk"] = ct(inp["L0_attn_Wk"].T)
    shared["wv"] = ct(inp["L0_attn_Wv"].T)
    shared["wg"] = ct(inp["L0_attn_Wg"].T)
    shared["wwin"] = ct(inp["L0_attn_Win"].T)
    shared["dtb0"] = ct(inp["L0_attn_dt_bias"][None, :])
    shared["alog0"] = ct(inp["L0_attn_A_log_bias"][None, :])
    shared["gnw"] = ct(inp["L0_attn_gnorm_w"][None, :])
    shared["wo"] = ct(inp["L0_attn_Wo"].T)
    shared["rw"] = ct(inp["L0_ffn_router_weight"].T)
    shared["rb"] = ct(inp["L0_ffn_router_bias"][None, :])
    shared["eb"] = ct(inp["L0_ffn_expert_bias"])
    shared["ipj"] = ct(inp["L1_attn_in_proj_w"].T)
    cw = inp["L1_attn_conv_w"][:, 0, :]
    nc3 = c.CONV // 128
    shared["cw"] = ct(cw.reshape(nc3, 128, 4).transpose(1, 0, 2).reshape(128, nc3 * 4))
    shared["cb"] = ct(inp["L1_attn_conv_b"].reshape(nc3, 128).T)
    shared["dtb1"] = ct(inp["L1_attn_dt_bias"][:, None])
    shared["alog1"] = ct(inp["L1_attn_A_log"][:, None])
    shared["dvec"] = ct(inp["L1_attn_D"][None, :])
    shared["nw"] = ct(inp["L1_attn_norm_w"][None, :])
    shared["opj"] = ct(inp["L1_attn_out_proj_w"].T)
    shared["afl"] = ct(inp["L1_ffn_A"].reshape(c.KR, c.D).T)
    shared["bfl"] = ct(inp["L1_ffn_B"].transpose(0, 2, 1).reshape(c.KR, c.D))
    shared["wr"] = ct(inp["L1_ffn_Wr"].T)
    def t4(Wt):
        K, M = Wt.shape
        return ct(Wt.reshape(K // 512, 4, 128, M // 128, 128)
                  .transpose(3, 0, 2, 1, 4).reshape(-1, 512))
    shared["wq4"] = t4(shared["wq"])
    shared["wk4"] = t4(shared["wk"])
    shared["wo4"] = t4(shared["wo"])
    shared["opj4"] = t4(shared["opj"])
    shared["bfl4"] = t4(shared["bfl"])
    shared["iph4"] = t4(shared["ipj"][:, c.INTER:c.INTER + c.CONV])
    shared["eye"] = np.eye(128, dtype=np.float32)
    tri = np.triu(np.ones((128, 128), dtype=np.float32))
    shared["triu"] = tri
    shared["mneg"] = np.where(tri > 0, 0.0, -1e30).astype(np.float32)
    x = inp["L0_attn_hidden_states"]
    B, L, _ = x.shape
    cos, sin = rope_tables(L)
    percore = []
    for core in range(8):
        b, half = core // 2, core % 2
        pc = {}
        pc["x"] = ct(x[b, half * c.T:(half + 1) * c.T, :].T)
        pc["rcos"] = ct(cos[half * c.T:(half + 1) * c.T, :].T)
        pc["rsin"] = ct(sin[half * c.T:(half + 1) * c.T, :].T)
        pc["smask"] = np.full((1, 1), 1.0 if half == 0 else 0.0, np.float32)
        pc["rmask"] = np.full((1, 1), 0.0 if half == 0 else 1.0, np.float32)
        percore.append(pc)
    return shared, percore


def kernel(**inputs):
    cfg = Cfg(True)
    cfg.fused = FUSED
    shared, percore = prep_inputs(cfg, inputs)
    env = [dict(shared, **pc) for pc in percore]
    launches = [[ph for grp in LAUNCHES for ph in grp]] if cfg.fused else LAUNCHES
    for li, grp in enumerate(launches):
        nc, ins, outs = build_launch(cfg, grp, (cfg.fused, li))
        in_maps = [{k: env[core][k] for k in ins} for core in range(8)]
        res = run_bass_kernel_spmd(nc, in_maps, core_ids=list(range(8)))
        for core in range(8):
            for k in outs:
                env[core][k] = res.results[core][k]
        if not cfg.fused:
            for ini, fin in CC_ROUTE.items():
                if fin in outs:
                    for core in range(8):
                        env[core][ini] = res.results[core ^ 1][fin]
    out = np.empty((4, 2048, 4096), dtype=np.float32)
    for core in range(8):
        b, half = core // 2, core % 2
        out[b, half * cfg.T:(half + 1) * cfg.T, :] = env[core]["y"].T
    return out
